# version 05
# baseline (speedup 1.0000x reference)
import contextlib
import numpy as np
import ml_dtypes
import concourse.bass as bass
import concourse.mybir as mybir
from concourse.bass_utils import run_bass_kernel_spmd
from concourse.library_config import mlp

F32 = mybir.dt.float32
BF16 = mybir.dt.bfloat16
I32 = mybir.dt.int32
I16 = mybir.dt.int16
AF = mybir.ActivationFunctionType
ALU = mybir.AluOpType
AX = mybir.AxisListType

NCORES = 8
S = 8192
D = 4096
TPC = 1024
NDT = D // 128
BF = ml_dtypes.bfloat16
NQW = S // 512


def core_token_ids(c):
    return np.arange(TPC * c, TPC * (c + 1))




def l0_tile_mask(q0, k0):
    k = np.arange(k0, k0 + 128)[:, None]
    q = np.arange(q0, q0 + 512)[None, :]
    qb = q // 64
    kb = k // 64
    act = (kb <= qb) & ((qb - kb < 16) | ((kb + 1) % 8 == 0))
    return act & (k <= q)


def l1_tile_mask(q0, k0):
    k = np.arange(k0, k0 + 128)[:, None]
    q = np.arange(q0, q0 + 512)[None, :]
    return k <= q


def build_tile_plan(layer):
    plan = []
    masks = {}
    order = []
    for qw in range(NQW):
        q0 = 512 * qw
        tiles = []
        for k0 in range(0, q0 + 512, 128):
            m = l0_tile_mask(q0, k0) if layer == 0 else l1_tile_mask(q0, k0)
            if not m.any():
                continue
            if m.all():
                tiles.append((k0, None))
            else:
                key = m.tobytes()
                if key not in masks:
                    masks[key] = len(order)
                    order.append(m)
                tiles.append((k0, masks[key]))
        plan.append(tiles)
    marr = np.stack(order).astype(BF) if order else np.zeros((1, 128, 512), BF)
    return plan, marr


def rope_tables():
    hd = 128
    inv = 1.0 / (1000000.0 ** (np.arange(0, hd, 2, dtype=np.float64) / hd))
    t = np.arange(S, dtype=np.float64)
    fr = np.outer(t, inv)
    emb = np.concatenate([fr, fr], axis=1)
    cos = np.cos(emb).T.astype(np.float32)
    sin = np.sin(emb).T.astype(np.float32)
    sinS = sin.copy()
    sinS[:64] *= -1.0
    return np.ascontiguousarray(cos), np.ascontiguousarray(sinS)




class Sem:
    def __init__(self, stack, nc, name):
        self.h = stack.enter_context(nc.semaphore(name))
        self.n = 0


class Prog:
    ENGS = ("sync", "tensor", "vector", "scalar", "gpsimd")

    def __init__(self, nc, stack):
        self.nc = nc
        self.stack = stack
        self.ops = {e: [] for e in self.ENGS}
        self.sems = {}
        self.gate = ()
        self.gated = set()

    def sem(self, name):
        if name not in self.sems:
            self.sems[name] = Sem(self.stack, self.nc, name)
        return self.sems[name]

    def stage_gate(self):
        self.gate = tuple((nm, s.n) for nm, s in self.sems.items() if s.n > 0)
        self.gated = set()

    def add(self, eng, fn, after=(), inc=None, k=1, serial=True):
        if self.gate and eng not in self.gated:
            self.gated.add(eng)
            after = tuple(t for t in after if t is not None) + self.gate
        s = self.sem(inc) if inc else None
        tok = None
        extra = ()
        if s is not None:
            if serial and s.n > 0:
                extra = ((inc, s.n),)
            s.n += k
            tok = (inc, s.n)
        after = tuple(t for t in after if t is not None) + extra
        self.ops[eng].append((after, fn, s, k))
        return tok

    def dma(self, eng, fn, after=()):
        return self.add(eng, fn, after, inc="dma_" + eng, k=16)

    def comp(self, eng, fn, after=()):
        return self.add(eng, fn, after, inc="c_" + eng, k=1)

    def barrier_tok(self, eng_sem):
        return (eng_sem, self.sem(eng_sem).n)

    def emit(self, blk):
        def make_body(name):
            def body(eng):
                wm = {}
                own = "c_" + name
                for (waits, fn, s, k) in self.ops[name]:
                    for (snm, val) in waits:
                        if name == "tensor" and snm == own:
                            continue
                        if wm.get(snm, 0) >= val:
                            continue
                        eng.wait_ge(self.sems[snm].h, val)
                        wm[snm] = val
                    ins = fn(eng)
                    if s is not None:
                        assert ins is not None, f"op on {name} returned None"
                        ins.then_inc(s.h, k)
            return body

        for name in self.ENGS:
            if self.ops[name]:
                getattr(blk, name)(make_body(name))




def emit_attn(p, io, layer, pre):
    plan, masks_np = build_tile_plan(layer)
    nm = masks_np.shape[0]
    sm_scale = (1.0 / 128.0) if layer == 0 else float(1.0 / np.sqrt(128.0))

    nc = p.nc
    x_d = io["x"]
    wq_d = io["wq"]
    wkv_d = io["wkv"]
    wo_d = io["wo"]
    masks_d = io["masks"]
    eyeb_d = io["eyeb"]
    x1_d = io["xout"]
    xstage_d = io["xstage"]
    xag_d = io["xag"]
    rsi_d = io["rsi"]
    rso_d = io["rso"]
    if layer == 0:
        bq_d = io["bq"]
        bo_d = io["bo"]
        cos_d = io["cos"]
        sin_d = io["sin"]
    else:
        kvstage_d = io["kvs"]
        kvag_d = io["kvag"]

    ctx = contextlib.ExitStack()
    with ctx:
        sb = lambda name, shape, dt: ctx.enter_context(
            nc.sbuf_tensor(pre + name, shape, dt))
        xw_sb = sb("xw_sb", [128, NDT * 512], BF16)
        K_sb = sb("K_sb", [128, S], BF16)
        V_sb = sb("V_sb", [128, S], BF16)
        wqb_sb = sb("wqb_sb", [128, NDT * 512], BF16)
        wkvb_sb = sb("wkvb_sb", [128, NDT * 256], BF16)
        masks_sb = sb("masks_sb", [128, nm * 512], BF16)
        eyeb_sb = sb("eyeb_sb", [128, 128], BF16)
        onesf_sb = sb("onesf_sb", [128, 128], F32)
        onesb_sb = sb("onesb_sb", [128, 4], BF16)
        xt_sb = sb("xt_sb", [128, 2 * TPC], F32)
        xb_sb = sb("xb_sb", [128, 2 * TPC], BF16)
        sq_sb = sb("sq_sb", [128, TPC], F32)
        stat_sb = sb("stat_sb", [1, 4 * TPC], F32)
        bc_sb = sb("bc_sb", [128, 2 * TPC], F32)
        if layer == 0:
            bq_sb = sb("bq_sb", [128, 6], F32)
            bo_sb = sb("bo_sb", [128, 32], F32)
            cosw_sb = sb("cosw_sb", [128, 512], F32)
            sinw_sb = sb("sinw_sb", [128, 512], F32)
        wot_sb = sb("wot_sb", [128, 2 * 512], BF16)
        wotf_sb = sb("wotf_sb", [128, 2 * 512], F32)
        po_sb = sb("po_sb", [128, 2 * 512], F32)
        ctxh_sb = sb("ctxh_sb", [128, 4 * 512], BF16)
        kvf_sb = sb("kvf_sb", [128, 2 * 512], F32)
        vb_sb = sb("vb_sb", [128, 512], BF16)
        q_sb = sb("q_sb", [128, 4 * 512], BF16)
        qf_sb = sb("qf_sb", [128, 512], F32)
        kn_sb = sb("kn_sb", [1, 64], F32)
        nrm_sb = sb("nrm_sb", [1, 16], F32)
        cc_sb = sb("cc_sb", [128, 1], F32)
        pt_sb = sb("pt_sb", [128, 2 * 512], BF16)
        den_sb = sb("den_sb", [1, 512], F32)
        ctxf_sb = sb("ctxf_sb", [128, 512], F32)
        xr_sb = sb("xr_sb", [128, 2 * 512], F32)
        xo_sb = sb("xo_sb", [128, 2 * 512], F32)

        psb = lambda name, shape, dt: ctx.enter_context(
            nc.psum_tensor(pre + name, shape, dt))
        pb = psb("pb", [128, TPC], F32)
        pA = [psb(f"pA{i}", [128, 512], F32) for i in range(5)]
        ptr = psb("ptr", [128, 512], BF16)

        rg = [list(range(NCORES))]

        for n in range(nm):
            p.dma("sync", lambda e, n=n: e.dma_start(
                out=masks_sb[:, n * 512:(n + 1) * 512], in_=masks_d[n]))
        t_eye = p.dma("sync", lambda e: e.dma_start(out=eyeb_sb[:], in_=eyeb_d))
        if layer == 0:
            p.dma("sync", lambda e: e.dma_start(out=bq_sb[:], in_=bq_d))
            t_bo = p.dma("sync", lambda e: e.dma_start(out=bo_sb[:], in_=bo_d))
        else:
            t_bo = None
        t_const = p.barrier_tok("dma_sync")
        t_ones = p.comp("vector", lambda e: e.memset(onesf_sb[:], 1.0))
        t_onesb = p.comp("vector", lambda e: e.memset(onesb_sb[:], 1.0))

        pm0, pm1, pv0, pv1 = pA[0], pA[1], pA[2], pA[3]
        mm_tok = None
        slot_free = [None, None]
        for ic in range(NDT):
            sl = ic % 2
            t_ld = p.dma("sync", lambda e, ic=ic, sl=sl: e.dma_start(
                out=xt_sb[:, sl * TPC:(sl + 1) * TPC],
                in_=x_d[ic * 128:(ic + 1) * 128, :]), after=[slot_free[sl]])
            t_sq = p.comp("scalar", lambda e, sl=sl: e.activation(
                sq_sb[:], xt_sb[:, sl * TPC:(sl + 1) * TPC], AF.Square),
                after=[t_ld, mm_tok])
            for h in range(2):
                p.comp("tensor", lambda e, sl=sl, h=h, ic=ic: e.matmul(
                    (pm0 if h == 0 else pm1)[0:1, :], onesf_sb[:, 0:1],
                    xt_sb[:, sl * TPC + h * 512:sl * TPC + (h + 1) * 512],
                    start=(ic == 0), stop=(ic == NDT - 1)),
                    after=[t_ld, t_ones])
                mm_tok = p.comp("tensor", lambda e, h=h, ic=ic: e.matmul(
                    (pv0 if h == 0 else pv1)[0:1, :], onesf_sb[:, 0:1],
                    sq_sb[:, h * 512:(h + 1) * 512],
                    start=(ic == 0), stop=(ic == NDT - 1)),
                    after=[t_sq])
            slot_free[sl] = mm_tok
        m_ = stat_sb[0:1, 0:TPC]
        msq = stat_sb[0:1, TPC:2 * TPC]
        m2 = stat_sb[0:1, 2 * TPC:3 * TPC]
        var = stat_sb[0:1, 3 * TPC:4 * TPC]
        r_ = stat_sb[0:1, 3 * TPC:4 * TPC]
        mr = stat_sb[0:1, 2 * TPC:3 * TPC]
        t1 = p.comp("scalar", lambda e: e.activation(
            m_[:, 0:512], pm0[0:1, :], AF.Copy, scale=1.0 / D), after=[mm_tok])
        t1 = p.comp("scalar", lambda e: e.activation(
            m_[:, 512:1024], pm1[0:1, :], AF.Copy, scale=1.0 / D), after=[t1])
        t2 = p.comp("scalar", lambda e: e.activation(
            msq[:, 0:512], pv0[0:1, :], AF.Copy, scale=1.0 / D), after=[t1])
        t2 = p.comp("scalar", lambda e: e.activation(
            msq[:, 512:1024], pv1[0:1, :], AF.Copy, scale=1.0 / D), after=[t2])
        t3 = p.comp("scalar", lambda e: e.activation(m2[:], m_[:], AF.Square),
                    after=[t2])
        t4 = p.comp("vector", lambda e: e.tensor_sub(var[:], msq[:], m2[:]),
                    after=[t3])
        t4b = p.comp("vector", lambda e: e.tensor_scalar_add(var[:], var[:],
                                                             1e-5), after=[t4])
        t5 = p.comp("scalar", lambda e: e.activation(var[:], var[:], AF.Sqrt),
                    after=[t4b])
        t6 = p.comp("vector", lambda e: e.reciprocal(r_[:], var[:]), after=[t5])
        t7 = p.comp("vector", lambda e: e.tensor_mul(mr[:], m_[:], r_[:]),
                    after=[t6])
        bts = [p.comp("tensor", lambda e, h=h: e.matmul(
            pb[:, h * 512:(h + 1) * 512], onesf_sb[0:1, :],
            r_[:, h * 512:(h + 1) * 512], start=True, stop=True),
            after=[t6]) for h in range(2)]
        t8 = p.comp("scalar", lambda e: e.activation(bc_sb[:, 0:TPC], pb[:],
                                                     AF.Copy), after=bts)
        bts2 = [p.comp("tensor", lambda e, h=h: e.matmul(
            pb[:, h * 512:(h + 1) * 512], onesf_sb[0:1, :],
            mr[:, h * 512:(h + 1) * 512], start=True, stop=True),
            after=[t7, t8]) for h in range(2)]
        t9 = p.comp("scalar", lambda e: e.activation(bc_sb[:, TPC:2 * TPC], pb[:],
                                                     AF.Copy), after=bts2)
        rb = bc_sb[:, 0:TPC]
        mrb = bc_sb[:, TPC:2 * TPC]
        xst_toks = []
        slot_free = [None, None]
        for ic in range(NDT):
            sl = ic % 2
            t_ld = p.dma("sync", lambda e, ic=ic, sl=sl: e.dma_start(
                out=xt_sb[:, sl * TPC:(sl + 1) * TPC],
                in_=x_d[ic * 128:(ic + 1) * 128, :]), after=[t8, slot_free[sl]])
            tm = p.comp("vector", lambda e, sl=sl: e.tensor_mul(
                xt_sb[:, sl * TPC:(sl + 1) * TPC],
                xt_sb[:, sl * TPC:(sl + 1) * TPC], rb), after=[t_ld, t8])
            tsb = p.comp("vector", lambda e, sl=sl: e.tensor_sub(
                xb_sb[:, sl * TPC:(sl + 1) * TPC],
                xt_sb[:, sl * TPC:(sl + 1) * TPC], mrb), after=[tm, t9])
            tst = p.dma("scalar", lambda e, ic=ic, sl=sl: e.dma_start(
                out=xstage_d[ic * 128:(ic + 1) * 128, :],
                in_=xb_sb[:, sl * TPC:(sl + 1) * TPC]), after=[tsb])
            xst_toks.append(tst)
            slot_free[sl] = tst
        t_agx = p.add("gpsimd", lambda e: e.collective_compute(
            "AllGather", ALU.bypass, rg, [xstage_d], [xag_d]),
            after=[xst_toks[-1]], inc="ag", k=1)
        t_ln_done = t9

        def stream_slab_to(w_dram, col0, ncols, dst_sb, dst_off, after=()):
            return p.dma("gpsimd", lambda e: e.dma_start(
                out=dst_sb[:, dst_off:dst_off + NDT * ncols].rearrange(
                    "p (a o) -> p a o", o=ncols),
                in_=w_dram[:, col0:col0 + ncols].rearrange(
                    "(a p) o -> p a o", p=128)), after=after)

        wq_toks = []
        prev = None
        for j in range(4):
            prev = stream_slab_to(wq_d, j * 128, 128, wqb_sb, j * NDT * 128,
                                  after=[prev])
            wq_toks.append(prev)
        t_wkv = stream_slab_to(wkv_d, 0, 256, wkvb_sb, 0, after=[prev])

        def wq_tile(j, ic):
            return wqb_sb[:, j * NDT * 128 + ic * 128:j * NDT * 128 + (ic + 1) * 128]

        def wkv_tile(which, ic):
            o = ic * 256 + which * 128
            return wkvb_sb[:, o:o + 128]

        kv_ready = None
        kn_done = {}
        xw_init_gate = None
        if layer == 1:
            own_done = []
            ldt_last = None
            for h in range(2):
                ldt = []
                for ic in range(NDT):
                    ldt.append(p.dma("sync", lambda e, ic=ic, h=h: e.dma_start(
                        out=xw_sb[:, ic * 512:(ic + 1) * 512],
                        in_=xstage_d[ic * 128:(ic + 1) * 128,
                                     h * 512:(h + 1) * 512]),
                        after=[xst_toks[ic],
                               own_done[-1] if own_done else None]))
                mk = mv = None
                for ic in range(NDT):
                    mk = p.comp("tensor", lambda e, ic=ic: e.matmul(
                        pA[0][:], wkv_tile(0, ic),
                        xw_sb[:, ic * 512:(ic + 1) * 512],
                        start=(ic == 0), stop=(ic == NDT - 1)),
                        after=[t_wkv, t_ln_done, ldt[ic]] if ic == 0
                        else [ldt[ic]])
                for ic in range(NDT):
                    mv = p.comp("tensor", lambda e, ic=ic: e.matmul(
                        pA[1][:], wkv_tile(1, ic),
                        xw_sb[:, ic * 512:(ic + 1) * 512],
                        start=(ic == 0), stop=(ic == NDT - 1)), after=[])
                tk = p.comp("scalar", lambda e, h=h: e.activation(
                    xb_sb[:, h * 512:(h + 1) * 512], pA[0][:], AF.Copy),
                    after=[mk, mv])
                tv = p.comp("scalar", lambda e, h=h: e.activation(
                    xb_sb[:, TPC + h * 512:TPC + (h + 1) * 512], pA[1][:],
                    AF.Copy), after=[tk])
                own_done.append(tv)
                ldt_last = ldt[-1]
            tsk = p.dma("scalar", lambda e: e.dma_start(
                out=kvstage_d[0:128, :], in_=xb_sb[:, 0:TPC]), after=own_done)
            tsv = p.dma("scalar", lambda e: e.dma_start(
                out=kvstage_d[128:256, :], in_=xb_sb[:, TPC:2 * TPC]),
                after=own_done)
            t_agkv = p.add("gpsimd", lambda e: e.collective_compute(
                "AllGather", ALU.bypass, rg, [kvstage_d], [kvag_d]),
                after=[tsk, tsv], inc="ag", k=1)
            kl = []
            for r in range(NCORES):
                kl.append(p.dma("sync", lambda e, r=r: e.dma_start(
                    out=K_sb[:, r * TPC:(r + 1) * TPC],
                    in_=kvag_d[r * 256:r * 256 + 128, :]), after=[t_agkv]))
                kl.append(p.dma("sync", lambda e, r=r: e.dma_start(
                    out=V_sb[:, r * TPC:(r + 1) * TPC],
                    in_=kvag_d[r * 256 + 128:r * 256 + 256, :]),
                    after=[t_agkv]))
            tprev = None
            for kt in range(64):
                ta = p.comp("tensor", lambda e, kt=kt: e.transpose(
                    ptr[:, 0:128], V_sb[:, kt * 128:(kt + 1) * 128], eyeb_sb[:]),
                    after=[kl[2 * (kt // 8) + 1], t_eye, tprev])
                tprev = p.comp("vector", lambda e, kt=kt: e.tensor_copy(
                    V_sb[:, kt * 128:(kt + 1) * 128], ptr[:, 0:128]), after=[ta])
            kv_ready = tprev
            knt = None
            for kt in range(64):
                tsq2 = p.comp("scalar", lambda e, kt=kt: e.activation(
                    sq_sb[:, 0:128], K_sb[:, kt * 128:(kt + 1) * 128],
                    AF.Square), after=[kl[2 * (kt // 8)], knt])
                tm1 = p.comp("tensor", lambda e: e.matmul(
                    pA[4][0:1, 0:128], onesf_sb[:, 0:1], sq_sb[:, 0:128],
                    start=True, stop=True), after=[tsq2])
                knt = p.comp("vector", lambda e, kt=kt: e.tensor_reduce(
                    kn_sb[0:1, kt:kt + 1], pA[4][0:1, 0:128], AX.X, ALU.max),
                    after=[tm1])
                kn_done[kt] = knt
            xw_init_gate = ldt_last
            xw_init_gate = tsv

        NH = 4
        kwin_done = {}
        if layer == 1:
            for w in range(NQW):
                kwin_done[w] = kv_ready
        xw_free = xw_init_gate
        prev_qnorm = None
        prev_qproj = None
        qevac_prev = [None, None]
        prev_cx0 = kn_done.get(63) if layer == 1 else None
        prev_ctx_st = None
        prev_cf = None
        prev_cx = kn_done.get(63) if layer == 1 else None
        prev_d1 = None
        prev_d3 = None
        for qw in range(NQW):
            r = qw // 2
            hh = qw % 2
            t_ldw = p.dma("sync", lambda e, r=r, hh=hh: e.dma_start(
                out=xw_sb[:].rearrange("p (a t) -> p a t", t=512),
                in_=xag_d[r * D:(r + 1) * D,
                          hh * 512:(hh + 1) * 512].rearrange(
                    "(a p) t -> p a t", p=128)),
                after=[t_agx, xw_free])
            ldw = [t_ldw] * NDT
            if layer == 0:
                t_cs = p.dma("sync", lambda e, qw=qw: e.dma_start(
                    out=cosw_sb[:], in_=cos_d[:, qw * 512:(qw + 1) * 512]),
                    after=[prev_qproj, t_const])
                t_sn = p.dma("sync", lambda e, qw=qw: e.dma_start(
                    out=sinw_sb[:], in_=sin_d[:, qw * 512:(qw + 1) * 512]),
                    after=[prev_qproj])
                mk = mv = None
                for ic in range(NDT):
                    mk = p.comp("tensor", lambda e, ic=ic: e.matmul(
                        pA[0][:], wkv_tile(0, ic),
                        xw_sb[:, ic * 512:(ic + 1) * 512],
                        start=(ic == 0), stop=(ic == NDT - 1)),
                        after=[t_wkv, t_ln_done, ldw[ic], prev_ctx_st]
                        if ic == 0 else [ldw[ic]])
                for ic in range(NDT):
                    mv = p.comp("tensor", lambda e, ic=ic: e.matmul(
                        pA[1][:], wkv_tile(1, ic),
                        xw_sb[:, ic * 512:(ic + 1) * 512],
                        start=(ic == 0), stop=(ic == NDT - 1)), after=[])
                tke = p.comp("scalar", lambda e: e.activation(
                    kvf_sb[:, 0:512], pA[0][:], AF.Identity, bias=bq_sb[:, 4:5]),
                    after=[mk, mv])
                tr1 = p.comp("vector", lambda e: e.tensor_mul(
                    kvf_sb[:, 512:1024], kvf_sb[:, 0:512], cosw_sb[:]),
                    after=[tke, t_cs])
                tr2 = p.comp("vector", lambda e: e.tensor_mul(
                    qf_sb[0:64, :], kvf_sb[64:128, 0:512], sinw_sb[64:128, :]),
                    after=[tke, t_sn])
                tr3 = p.comp("vector", lambda e: e.tensor_mul(
                    qf_sb[64:128, :], kvf_sb[0:64, 0:512], sinw_sb[0:64, :]),
                    after=[tke])
                tk_bf = p.comp("vector", lambda e, qw=qw: e.tensor_add(
                    K_sb[:, qw * 512:(qw + 1) * 512], kvf_sb[:, 512:1024],
                    qf_sb[:, 0:512]), after=[tr1, tr2, tr3])
                tve = p.comp("scalar", lambda e: e.activation(
                    vb_sb[:], pA[1][:], AF.Identity, bias=bq_sb[:, 5:6]),
                    after=[tke])
                tprev = None
                for j in range(4):
                    ta = p.comp("tensor", lambda e, j=j: e.transpose(
                        ptr[:, 0:128], vb_sb[:, j * 128:(j + 1) * 128],
                        eyeb_sb[:]), after=[tve, t_eye, tprev])
                    tprev = p.comp("vector", lambda e, j=j, qw=qw: e.tensor_copy(
                        V_sb[:, qw * 512 + j * 128:qw * 512 + (j + 1) * 128],
                        ptr[:, 0:128]), after=[ta])
                for j in range(4):
                    kt = qw * 4 + j
                    tsq2 = p.comp("scalar", lambda e, qw=qw, j=j: e.activation(
                        sq_sb[:, 0:128],
                        K_sb[:, qw * 512 + j * 128:qw * 512 + (j + 1) * 128],
                        AF.Square),
                        after=[tk_bf, kn_done.get(kt - 1), prev_qnorm])
                    tm1 = p.comp("tensor", lambda e: e.matmul(
                        pA[4][0:1, 0:128], onesf_sb[:, 0:1], sq_sb[:, 0:128],
                        start=True, stop=True), after=[tsq2, prev_cx])
                    kn_done[kt] = p.comp(
                        "vector", lambda e, kt=kt: e.tensor_reduce(
                            kn_sb[0:1, kt:kt + 1], pA[4][0:1, 0:128], AX.X,
                            ALU.max), after=[tm1])
                kwin_done[qw] = tprev

            q_toks = []
            qn_toks = []
            for j in range(NH):
                qh = j % 2
                mq = None
                for ic in range(NDT):
                    mq = p.comp("tensor", lambda e, j=j, ic=ic, qh=qh: e.matmul(
                        pb[:, qh * 512:(qh + 1) * 512], wq_tile(j, ic),
                        xw_sb[:, ic * 512:(ic + 1) * 512],
                        start=(ic == 0), stop=(ic == NDT - 1)),
                        after=[wq_toks[j], t_ln_done, ldw[ic], prev_qproj,
                               qevac_prev[qh]]
                        if ic == 0 else [ldw[ic]])
                if layer == 0:
                    tqe = p.comp("scalar", lambda e, j=j, qh=qh: e.activation(
                        qf_sb[:], pb[:, qh * 512:(qh + 1) * 512], AF.Identity,
                        bias=bq_sb[:, j:j + 1]),
                        after=[mq])
                    qevac_prev[qh] = tqe
                    r1 = p.comp("vector", lambda e: e.tensor_mul(
                        kvf_sb[:, 0:512], qf_sb[:], cosw_sb[:]), after=[tqe])
                    r2 = p.comp("vector", lambda e: e.tensor_mul(
                        kvf_sb[0:64, 512:1024], qf_sb[64:128, :],
                        sinw_sb[64:128, :]), after=[tqe])
                    r3 = p.comp("vector", lambda e: e.tensor_mul(
                        kvf_sb[64:128, 512:1024], qf_sb[0:64, :],
                        sinw_sb[0:64, :]), after=[tqe])
                    tq = p.comp("vector", lambda e, j=j: e.tensor_add(
                        q_sb[:, j * 512:(j + 1) * 512], kvf_sb[:, 0:512],
                        kvf_sb[:, 512:1024]), after=[r1, r2, r3])
                else:
                    tq = p.comp("scalar", lambda e, j=j, qh=qh: e.activation(
                        q_sb[:, j * 512:(j + 1) * 512],
                        pb[:, qh * 512:(qh + 1) * 512], AF.Copy),
                        after=[mq])
                    qevac_prev[qh] = tq
                q_toks.append(tq)
                tsq3 = p.comp("scalar", lambda e, j=j: e.activation(
                    sq_sb[:, 128:640], q_sb[:, j * 512:(j + 1) * 512],
                    AF.Square), after=[tq, qn_toks[-1] if qn_toks else
                                       prev_qnorm])
                tm1 = p.comp("tensor", lambda e: e.matmul(
                    pA[4][0:1, 0:512], onesf_sb[:, 0:1], sq_sb[:, 128:640],
                    start=True, stop=True), after=[tsq3, prev_cx])
                qn_toks.append(p.comp("vector", lambda e, j=j: e.tensor_reduce(
                    nrm_sb[0:1, j:j + 1], pA[4][0:1, 0:512], AX.X, ALU.max),
                    after=[tm1]))
            prev_qproj = q_toks[-1]
            prev_qnorm = qn_toks[-1]

            tiles = plan[qw]
            last_kt = max(k0 for (k0, _m) in tiles) // 128
            tqm = p.comp("vector", lambda e: e.tensor_reduce(
                nrm_sb[0:1, 8:9], nrm_sb[0:1, 0:4], AX.X, ALU.max),
                after=qn_toks)
            tkm = p.comp("vector", lambda e, last_kt=last_kt: e.tensor_reduce(
                nrm_sb[0:1, 9:10], kn_sb[0:1, 0:last_kt + 1], AX.X, ALU.max),
                after=[kn_done[last_kt]])
            tc1 = p.comp("vector", lambda e: e.tensor_mul(
                nrm_sb[0:1, 10:11], nrm_sb[0:1, 8:9], nrm_sb[0:1, 9:10]),
                after=[tqm, tkm])
            tc2 = p.comp("scalar", lambda e: e.activation(
                nrm_sb[0:1, 11:12], nrm_sb[0:1, 10:11], AF.Sqrt), after=[tc1])
            tc3 = p.comp("scalar", lambda e: e.mul(
                nrm_sb[0:1, 12:13], nrm_sb[0:1, 11:12], -1.001 * sm_scale),
                after=[tc2])
            tc4 = p.comp("tensor", lambda e: e.matmul(
                pA[4][:, 0:1], onesf_sb[0:1, :], nrm_sb[0:1, 12:13],
                start=True, stop=True), after=[tc3, prev_cx])
            tc5 = p.comp("scalar", lambda e: e.activation(
                cc_sb[:], pA[4][:, 0:1], AF.Copy), after=[tc4])

            ctx_head_toks = []
            for j in range(NH):
                n_t = len(tiles)
                exp_prev = [None, None]
                pt_readers = [None, None]
                pv_prev = None
                den_prev = None
                kreq = kwin_done[qw]
                for i, (k0, mid) in enumerate(tiles):
                    bk = i % 2
                    t_s = p.comp("tensor", lambda e, k0=k0, bk=bk, j=j: e.matmul(
                        pA[bk][:], K_sb[:, k0:k0 + 128],
                        q_sb[:, j * 512:(j + 1) * 512], start=True, stop=True),
                        after=[exp_prev[bk], q_toks[j], kreq] if i < 2
                        else [exp_prev[bk]])
                    t_e = p.comp("scalar", lambda e, bk=bk: e.activation(
                        pt_sb[:, bk * 512:(bk + 1) * 512], pA[bk][:], AF.Exp,
                        bias=cc_sb[:, 0:1], scale=sm_scale),
                        after=[t_s, tc5, pt_readers[bk]])
                    exp_prev[bk] = t_e
                    t_m = t_e
                    if mid is not None:
                        t_m = p.comp("vector", lambda e, bk=bk, mid=mid:
                                     e.tensor_mul(
                                         pt_sb[:, bk * 512:(bk + 1) * 512],
                                         pt_sb[:, bk * 512:(bk + 1) * 512],
                                         masks_sb[:, mid * 512:(mid + 1) * 512]),
                                     after=[t_e, t_const])
                    t_pv = p.comp("tensor", lambda e, k0=k0, bk=bk, i=i, n_t=n_t:
                                  e.matmul(pA[2][:], V_sb[:, k0:k0 + 128],
                                           pt_sb[:, bk * 512:(bk + 1) * 512],
                                           start=(i == 0), stop=(i == n_t - 1)),
                                  after=[t_m, prev_cf] if i == 0 else [t_m])
                    pv_prev = t_pv
                    den_prev = p.comp("tensor", lambda e, bk=bk, i=i, n_t=n_t:
                                      e.matmul(pA[3][0:1, :], onesb_sb[:, 0:1],
                                               pt_sb[:, bk * 512:(bk + 1) * 512],
                                               start=(i == 0), stop=(i == n_t - 1)),
                                      after=[t_m, t_onesb, prev_d1] if i == 0
                                      else [])
                    pt_readers[bk] = den_prev
                t_d1 = p.comp("scalar", lambda e: e.activation(
                    den_sb[:], pA[3][0:1, :], AF.Copy), after=[den_prev])
                prev_d1 = t_d1
                t_d2 = p.comp("vector", lambda e: e.reciprocal(
                    den_sb[:], den_sb[:]), after=[t_d1])
                t_d3 = p.comp("tensor", lambda e: e.matmul(
                    pA[4][:], onesf_sb[0:1, :], den_sb[:], start=True,
                    stop=True), after=[t_d2, prev_cx])
                prev_d3 = t_d3
                t_cf = p.comp("scalar", lambda e: e.activation(
                    ctxf_sb[:], pA[2][:], AF.Copy), after=[pv_prev, t_d1,
                                                           prev_cx])
                prev_cf = t_cf
                t_cx = p.comp("vector", lambda e, j=j: e.tensor_mul(
                    ctxh_sb[:, j * 512:(j + 1) * 512], ctxf_sb[:], pA[4][:]),
                    after=[t_cf, t_d3, prev_ctx_st])
                prev_cx = t_cx
                ctx_head_toks.append(t_cx)
            st_toks = []
            wslot = [None, None]
            wcv = [None, None]
            pe_prev = [None, None]
            for ot in range(NDT):
                sl = ot % 2
                t_wf = p.dma("sync", lambda e, ot=ot, sl=sl: e.dma_start(
                    out=wotf_sb[:, sl * 512:(sl + 1) * 512].rearrange(
                        "p (j o) -> p j o", o=128),
                    in_=wo_d[:, ot * 128:(ot + 1) * 128].rearrange(
                        "(j p) o -> p j o", p=128)),
                    after=[wcv[sl] if True else None])
                t_wl = p.comp("vector", lambda e, sl=sl: e.tensor_copy(
                    wot_sb[:, sl * 512:(sl + 1) * 512],
                    wotf_sb[:, sl * 512:(sl + 1) * 512]),
                    after=[t_wf, wslot[sl]])
                wcv[sl] = t_wl
                mo = None
                for j in range(4):
                    mo = p.comp("tensor", lambda e, sl=sl, j=j: e.matmul(
                        pb[:, sl * 512:(sl + 1) * 512],
                        wot_sb[:, sl * 512 + j * 128:sl * 512 + (j + 1) * 128],
                        ctxh_sb[:, j * 512:(j + 1) * 512],
                        start=(j == 0), stop=(j == 3)),
                        after=[t_wl, pe_prev[sl]] + ctx_head_toks
                        if j == 0 else [])
                wslot[sl] = mo
                t_pe = p.comp("scalar", lambda e, sl=sl: e.activation(
                    po_sb[:, sl * 512:(sl + 1) * 512],
                    pb[:, sl * 512:(sl + 1) * 512], AF.Copy),
                    after=[mo, st_toks[-2] if len(st_toks) >= 2 else None])
                pe_prev[sl] = t_pe
                t_st = p.dma("scalar", lambda e, qw=qw, ot=ot, sl=sl: e.dma_start(
                    out=rsi_d[(qw // 2) * D + ot * 128:
                              (qw // 2) * D + (ot + 1) * 128,
                              (qw % 2) * 512:(qw % 2 + 1) * 512],
                    in_=po_sb[:, sl * 512:(sl + 1) * 512]), after=[t_pe])
                st_toks.append(t_st)
            prev_ctx_st = st_toks[-1]
            xw_free = prev_ctx_st

        t_rs = p.add("gpsimd", lambda e: e.collective_compute(
            "ReduceScatter", ALU.add, rg, [rsi_d], [rso_d]),
            after=[prev_ctx_st], inc="ag", k=1)

        res_toks = []
        for half in range(2):
            for ot in range(NDT):
                sl = ot % 2
                t_x = p.dma("sync", lambda e, ot=ot, half=half, sl=sl:
                            e.dma_start(
                                out=xr_sb[:, sl * 512:(sl + 1) * 512],
                                in_=x_d[ot * 128:(ot + 1) * 128,
                                        half * 512:(half + 1) * 512]),
                            after=[res_toks[-2] if len(res_toks) >= 2 else None])
                t_p = p.dma("sync", lambda e, ot=ot, half=half, sl=sl:
                            e.dma_start(
                                out=po_sb[:, sl * 512:(sl + 1) * 512],
                                in_=rso_d[ot * 128:(ot + 1) * 128,
                                          half * 512:(half + 1) * 512]),
                            after=[t_rs,
                                   res_toks[-2] if len(res_toks) >= 2 else None])
                if layer == 0:
                    t_r = p.comp("vector", lambda e, ot=ot, sl=sl:
                                 e.scalar_tensor_tensor(
                                     xo_sb[:, sl * 512:(sl + 1) * 512],
                                     po_sb[:, sl * 512:(sl + 1) * 512],
                                     bo_sb[:, ot:ot + 1],
                                     xr_sb[:, sl * 512:(sl + 1) * 512],
                                     ALU.add, ALU.add), after=[t_x, t_p, t_bo])
                else:
                    t_r = p.comp("vector", lambda e, sl=sl: e.tensor_add(
                        xo_sb[:, sl * 512:(sl + 1) * 512],
                        po_sb[:, sl * 512:(sl + 1) * 512],
                        xr_sb[:, sl * 512:(sl + 1) * 512]), after=[t_x, t_p])
                t_o = p.dma("scalar", lambda e, ot=ot, half=half, sl=sl:
                            e.dma_start(
                                out=x1_d[ot * 128:(ot + 1) * 128,
                                         half * 512:(half + 1) * 512],
                                in_=xo_sb[:, sl * 512:(sl + 1) * 512]),
                            after=[t_r])
                res_toks.append(t_o)

    return res_toks[-1]




def prep_attn_maps(maps, inputs, layer, pre):
    if layer == 0:
        wqkvT = np.ascontiguousarray(np.asarray(inputs["L0_attn_w_qkv"]).T)
        bqkv = np.asarray(inputs["L0_attn_b_qkv"])
        wdT = np.asarray(inputs["L0_attn_w_dense"]).T
        boT = np.ascontiguousarray(
            np.asarray(inputs["L0_attn_b_dense"]).reshape(32, 128).T)
        cos, sinS = rope_tables()
    else:
        wq = np.asarray(inputs["L1_attn_wq"])
        wk = np.asarray(inputs["L1_attn_wk"])
        wv = np.asarray(inputs["L1_attn_wv"])
        wo1 = np.asarray(inputs["L1_attn_wo"])
        wkv = np.ascontiguousarray(np.concatenate([wk, wv], axis=1))
    _plan, masks = build_tile_plan(layer)
    for c in range(NCORES):
        m = maps[c]
        wsl = wdT if layer == 0 else wo1
        m[pre + "wo"] = np.ascontiguousarray(wsl[512 * c:512 * (c + 1), :])
        m[pre + "masks"] = masks
        if layer == 0:
            m[pre + "wq"] = np.ascontiguousarray(wqkvT[:, 768 * c:768 * c + 512])
            m[pre + "wkv"] = np.ascontiguousarray(
                wqkvT[:, 768 * c + 512:768 * (c + 1)])
            m[pre + "bq"] = np.ascontiguousarray(
                bqkv[768 * c:768 * (c + 1)].reshape(6, 128).T)
            m[pre + "bo"] = boT
            m["cosT"] = cos
            m["sinSW"] = np.ascontiguousarray(np.roll(sinS, 64, axis=0))
        else:
            m[pre + "wq"] = np.ascontiguousarray(wq[:, 512 * c:512 * (c + 1)])
            m[pre + "wkv"] = wkv




def emit_ln(p, nc, x_d, xt_sb, sq_sb, stat_sb, bc_sb, onesf_sb, pm, pb,
            t_ones, write_fn):
    TPCl = TPC
    mm_tok = None
    slot_free = [None, None]
    for ic in range(NDT):
        sl = ic % 2
        t_ld = p.dma("sync", lambda e, ic=ic, sl=sl: e.dma_start(
            out=xt_sb[:, sl * TPCl:(sl + 1) * TPCl],
            in_=x_d[ic * 128:(ic + 1) * 128, :]), after=[slot_free[sl]])
        t_sq = p.comp("scalar", lambda e, sl=sl: e.activation(
            sq_sb[:, 0:TPCl], xt_sb[:, sl * TPCl:(sl + 1) * TPCl], AF.Square),
            after=[t_ld, mm_tok])
        for h in range(2):
            p.comp("tensor", lambda e, sl=sl, h=h, ic=ic: e.matmul(
                pm[h][0:1, :], onesf_sb[:, 0:1],
                xt_sb[:, sl * TPCl + h * 512:sl * TPCl + (h + 1) * 512],
                start=(ic == 0), stop=(ic == NDT - 1)),
                after=[t_ld, t_ones])
            mm_tok = p.comp("tensor", lambda e, h=h, ic=ic: e.matmul(
                pm[2 + h][0:1, :], onesf_sb[:, 0:1],
                sq_sb[:, h * 512:(h + 1) * 512],
                start=(ic == 0), stop=(ic == NDT - 1)),
                after=[t_sq])
        slot_free[sl] = mm_tok
    m_ = stat_sb[0:1, 0:TPCl]
    msq = stat_sb[0:1, TPCl:2 * TPCl]
    m2 = stat_sb[0:1, 2 * TPCl:3 * TPCl]
    var = stat_sb[0:1, 3 * TPCl:4 * TPCl]
    r_ = stat_sb[0:1, 3 * TPCl:4 * TPCl]
    mr = stat_sb[0:1, 2 * TPCl:3 * TPCl]
    t1 = p.comp("scalar", lambda e: e.activation(
        m_[:, 0:512], pm[0][0:1, :], AF.Copy, scale=1.0 / D), after=[mm_tok])
    t1 = p.comp("scalar", lambda e: e.activation(
        m_[:, 512:1024], pm[1][0:1, :], AF.Copy, scale=1.0 / D), after=[t1])
    t2 = p.comp("scalar", lambda e: e.activation(
        msq[:, 0:512], pm[2][0:1, :], AF.Copy, scale=1.0 / D), after=[t1])
    t2 = p.comp("scalar", lambda e: e.activation(
        msq[:, 512:1024], pm[3][0:1, :], AF.Copy, scale=1.0 / D), after=[t2])
    t3 = p.comp("scalar", lambda e: e.activation(m2[:], m_[:], AF.Square),
                after=[t2])
    t4 = p.comp("vector", lambda e: e.tensor_sub(var[:], msq[:], m2[:]),
                after=[t3])
    t4b = p.comp("vector", lambda e: e.tensor_scalar_add(var[:], var[:], 1e-5),
                 after=[t4])
    t5 = p.comp("scalar", lambda e: e.activation(var[:], var[:], AF.Sqrt),
                after=[t4b])
    t6 = p.comp("vector", lambda e: e.reciprocal(r_[:], var[:]), after=[t5])
    t7 = p.comp("vector", lambda e: e.tensor_mul(mr[:], m_[:], r_[:]),
                after=[t6])
    bts = [p.comp("tensor", lambda e, h=h: e.matmul(
        pb[:, h * 512:(h + 1) * 512], onesf_sb[0:1, :],
        r_[:, h * 512:(h + 1) * 512], start=True, stop=True),
        after=[t6]) for h in range(2)]
    t8 = p.comp("scalar", lambda e: e.activation(bc_sb[:, 0:TPCl], pb[:],
                                                 AF.Copy), after=bts)
    bts2 = [p.comp("tensor", lambda e, h=h: e.matmul(
        pb[:, h * 512:(h + 1) * 512], onesf_sb[0:1, :],
        mr[:, h * 512:(h + 1) * 512], start=True, stop=True),
        after=[t7, t8]) for h in range(2)]
    t9 = p.comp("scalar", lambda e: e.activation(bc_sb[:, TPCl:2 * TPCl], pb[:],
                                                 AF.Copy), after=bts2)
    rb = bc_sb[:, 0:TPCl]
    mrb = bc_sb[:, TPCl:2 * TPCl]
    toks = []
    slot_free = [None, None]
    for ic in range(NDT):
        sl = ic % 2
        t_ld = p.dma("sync", lambda e, ic=ic, sl=sl: e.dma_start(
            out=xt_sb[:, sl * TPCl:(sl + 1) * TPCl],
            in_=x_d[ic * 128:(ic + 1) * 128, :]), after=[t8, slot_free[sl]])
        tm = p.comp("vector", lambda e, sl=sl: e.tensor_mul(
            xt_sb[:, sl * TPCl:(sl + 1) * TPCl],
            xt_sb[:, sl * TPCl:(sl + 1) * TPCl], rb), after=[t_ld, t8])
        tok = write_fn(ic, sl, tm, t9, mrb)
        toks.append(tok)
        slot_free[sl] = tok
    return toks, bts2[-1]




def emit_l1_ffn(p, io, pre):
    nc = p.nc
    x_d = io["x"]
    bw_d = io["bw"]
    bb_d = io["bb"]
    at_d = io["at"]
    bs_d = io["bs"]
    rw_d = io["rw"]
    eyeb_d = io["eyeb"]
    x1_d = io["xout"]

    ctx = contextlib.ExitStack()
    with ctx:
        sb = lambda name, shape, dt: ctx.enter_context(
            nc.sbuf_tensor(pre + name, shape, dt))
        xln_sb = sb("xln_sb", [128, NDT * TPC], BF16)
        bwb_sb = sb("bwb_sb", [128, 2 * NDT * 128], BF16)
        bsb_sb = sb("bsb_sb", [128, 2 * 8 * 128], BF16)
        atb_sb = sb("atb_sb", [128, NDT * 64], BF16)
        rwb_sb = sb("rwb_sb", [128, NDT * 16], BF16)
        eyeb_sb = sb("eyeb_sb", [128, 128], BF16)
        onesf_sb = sb("onesf_sb", [128, 128], F32)
        xt_sb = sb("xt_sb", [128, 2 * TPC], F32)
        sq_sb = sb("sq_sb", [128, TPC], F32)
        stat_sb = sb("stat_sb", [1, 4 * TPC], F32)
        bc_sb = sb("bc_sb", [128, 2 * TPC], F32)
        bb_sb = sb("bb_sb", [128, 32], F32)
        pr_sb = sb("pr_sb", [128, 16], F32)
        sc_sb = sb("sc_sb", [128, 8], F32)
        comba_sb = sb("comba_sb", [128, 8 * 16], BF16)
        rowE_sb = sb("rowE_sb", [1, TPC], F32)
        rhT_sb = sb("rhT_sb", [64, TPC], F32)
        RH_sb = sb("RH_sb", [128, 8 * TPC], BF16)
        xr_sb = sb("xr_sb", [128, 2 * 512], F32)
        xo_sb = sb("xo_sb", [128, 2 * 512], F32)

        psb = lambda name, shape, dt: ctx.enter_context(
            nc.psum_tensor(pre + name, shape, dt))
        pb = psb("pb", [128, TPC], F32)
        pA = [psb(f"pA{i}", [128, 512], F32) for i in range(4)]
        ptr = psb("ptr", [128, 512], BF16)

        t_eye = p.dma("sync", lambda e: e.dma_start(out=eyeb_sb[:], in_=eyeb_d))
        t_bb = p.dma("sync", lambda e: e.dma_start(out=bb_sb[:], in_=bb_d))
        t_ones = p.comp("vector", lambda e: e.memset(onesf_sb[:], 1.0))
        t_at = p.dma("gpsimd", lambda e: e.dma_start(
            out=atb_sb[:].rearrange("p (a o) -> p a o", o=64),
            in_=at_d.rearrange("(a p) o -> p a o", p=128)))
        t_rw = p.dma("gpsimd", lambda e: e.dma_start(
            out=rwb_sb[:].rearrange("p (a o) -> p a o", o=16),
            in_=rw_d.rearrange("(a p) o -> p a o", p=128)))

        def xw(ic, sl, tm, t9, mrb):
            return p.comp("vector", lambda e, ic=ic, sl=sl: e.tensor_sub(
                xln_sb[:, ic * TPC:(ic + 1) * TPC],
                xt_sb[:, sl * TPC:(sl + 1) * TPC], mrb), after=[tm, t9])

        xln_toks, t_lnmm = emit_ln(p, nc, x_d, xt_sb, sq_sb, stat_sb, bc_sb,
                                   onesf_sb, pA, pb, t_ones, xw)
        xln_done = xln_toks[-1]

        def xln(ic):
            return xln_sb[:, ic * TPC:(ic + 1) * TPC]

        mrh = None
        for h in range(2):
            for ic in range(NDT):
                mrh = p.comp("tensor", lambda e, ic=ic, h=h: e.matmul(
                    pb[0:64, h * 512:(h + 1) * 512],
                    atb_sb[:, ic * 64:(ic + 1) * 64],
                    xln(ic)[:, h * 512:(h + 1) * 512],
                    start=(ic == 0), stop=(ic == NDT - 1)),
                    after=[t_at, xln_done, t_lnmm] if (ic == 0 and h == 0)
                    else [])
        t_rh = p.comp("scalar", lambda e: e.activation(rhT_sb[:], pb[0:64, :],
                                                       AF.Copy), after=[mrh])

        tr_toks = []
        for tt in range(8):
            mro = None
            for ic in range(NDT):
                mro = p.comp("tensor", lambda e, ic=ic, tt=tt: e.matmul(
                    pA[0][:, 0:16],
                    xln_sb[:, ic * TPC + tt * 128:ic * TPC + (tt + 1) * 128],
                    rwb_sb[:, ic * 16:(ic + 1) * 16],
                    start=(ic == 0), stop=(ic == NDT - 1)),
                    after=[t_rw, xln_done, t_rh,
                           tr_toks[-1] if tr_toks else None]
                    if ic == 0 else [])
            c0 = p.comp("scalar", lambda e: e.activation(
                pr_sb[:], pA[0][:, 0:16], AF.Copy),
                after=[mro, tr_toks[-1] if tr_toks else None])
            c1 = p.comp("vector", lambda e: e.tensor_reduce(
                sc_sb[:, 0:1], pr_sb[:], AX.X, ALU.max), after=[c0])
            c2 = p.comp("vector", lambda e: e.tensor_scalar_mul(
                sc_sb[:, 1:2], sc_sb[:, 0:1], -1.0), after=[c1])
            c3 = p.comp("scalar", lambda e: e.activation(
                pr_sb[:], pr_sb[:], AF.Exp, bias=sc_sb[:, 1:2]), after=[c2])
            c4 = p.comp("vector", lambda e: e.tensor_reduce(
                sc_sb[:, 2:3], pr_sb[:], AX.X, ALU.add), after=[c3])
            c5 = p.comp("vector", lambda e: e.reciprocal(
                sc_sb[:, 3:4], sc_sb[:, 2:3]), after=[c4])
            c6 = p.comp("vector", lambda e: e.tensor_scalar_mul(
                pr_sb[:], pr_sb[:], sc_sb[:, 3:4]), after=[c5])
            c7 = p.comp("vector", lambda e: e.tensor_reduce(
                sc_sb[:, 0:1], pr_sb[:], AX.X, ALU.max), after=[c6])
            c8 = p.comp("vector", lambda e: e.tensor_scalar(
                sq_sb[:, 0:16], pr_sb[:], sc_sb[:, 0:1], None, ALU.is_equal),
                after=[c7])
            c9 = p.comp("vector", lambda e: e.scalar_tensor_tensor(
                sq_sb[:, 16:32], sq_sb[:, 0:16], -2.0, pr_sb[:],
                ALU.mult, ALU.add), after=[c8])
            c10 = p.comp("vector", lambda e: e.tensor_reduce(
                sc_sb[:, 1:2], sq_sb[:, 16:32], AX.X, ALU.max), after=[c9])
            c11 = p.comp("vector", lambda e: e.tensor_scalar(
                sq_sb[:, 32:48], pr_sb[:], sc_sb[:, 1:2], None, ALU.is_equal),
                after=[c10])
            c12 = p.comp("vector", lambda e: e.tensor_add(
                sq_sb[:, 0:16], sq_sb[:, 0:16], sq_sb[:, 32:48]), after=[c11])
            c13 = p.comp("vector", lambda e: e.tensor_add(
                sc_sb[:, 4:5], sc_sb[:, 0:1], sc_sb[:, 1:2]), after=[c10])
            c14 = p.comp("vector", lambda e: e.tensor_scalar_add(
                sc_sb[:, 4:5], sc_sb[:, 4:5], 1e-6), after=[c13])
            c15 = p.comp("vector", lambda e: e.reciprocal(
                sc_sb[:, 5:6], sc_sb[:, 4:5]), after=[c14])
            c16 = p.comp("vector", lambda e: e.tensor_scalar_mul(
                sc_sb[:, 5:6], sc_sb[:, 5:6], 0.5), after=[c15])
            c17 = p.comp("vector", lambda e: e.tensor_mul(
                sq_sb[:, 0:16], sq_sb[:, 0:16], pr_sb[:]), after=[c12, c6])
            c18 = p.comp("vector", lambda e, tt=tt: e.tensor_scalar_mul(
                comba_sb[:, tt * 16:(tt + 1) * 16], sq_sb[:, 0:16],
                sc_sb[:, 5:6]), after=[c16, c17])
            tr_toks.append(c18)

        rh_toks = []
        prev_tr = None
        prev_row = None
        for e_ in range(16):
            row_toks = []
            for tt in range(8):
                ta = p.comp("tensor", lambda e, e_=e_, tt=tt: e.transpose(
                    ptr[0:1, 0:128],
                    comba_sb[:, tt * 16 + e_:tt * 16 + e_ + 1], eyeb_sb[:]),
                    after=[tr_toks[tt], t_eye, prev_tr,
                           rh_toks[-1] if rh_toks else None])
                prev_tr = p.comp("scalar", lambda e, tt=tt: e.activation(
                    rowE_sb[0:1, tt * 128:(tt + 1) * 128], ptr[0:1, 0:128],
                    AF.Copy), after=[ta, prev_row])
                row_toks.append(prev_tr)
            bc1 = p.comp("tensor", lambda e: e.matmul(
                pA[1][0:64, :], onesf_sb[0:1, 0:64], rowE_sb[0:1, 0:512],
                start=True, stop=True), after=row_toks + [prev_row])
            bc2 = p.comp("tensor", lambda e: e.matmul(
                pA[2][0:64, :], onesf_sb[0:1, 0:64], rowE_sb[0:1, 512:1024],
                start=True, stop=True), after=row_toks)
            cc = e_ // 2
            half = (e_ % 2) * 64
            m1 = p.comp("vector", lambda e, cc=cc, half=half: e.tensor_mul(
                RH_sb[half:half + 64, cc * TPC:cc * TPC + 512],
                pA[1][0:64, :], rhT_sb[0:64, 0:512]), after=[bc1, t_rh])
            m2 = p.comp("vector", lambda e, cc=cc, half=half: e.tensor_mul(
                RH_sb[half:half + 64, cc * TPC + 512:(cc + 1) * TPC],
                pA[2][0:64, :], rhT_sb[0:64, 512:1024]), after=[bc2, t_rh])
            prev_row = m2
            rh_toks += [m1, m2]

        res_toks = []
        wslot = [None, None]
        bslot = [None, None]
        for ot in range(NDT):
            sl = ot % 2
            t_wl = p.dma("gpsimd", lambda e, ot=ot, sl=sl: e.dma_start(
                out=bwb_sb[:, sl * NDT * 128:(sl + 1) * NDT * 128].rearrange(
                    "p (a o) -> p a o", o=128),
                in_=bw_d[:, ot * 128:(ot + 1) * 128].rearrange(
                    "(a p) o -> p a o", p=128)), after=[wslot[sl]])
            t_bs = p.dma("gpsimd", lambda e, ot=ot, sl=sl: e.dma_start(
                out=bsb_sb[:, sl * 8 * 128:(sl + 1) * 8 * 128].rearrange(
                    "p (a o) -> p a o", o=128),
                in_=bs_d[:, ot * 128:(ot + 1) * 128].rearrange(
                    "(a p) o -> p a o", p=128)), after=[bslot[sl]])
            tok = None
            for h in range(2):
                for ic in range(NDT):
                    tok = p.comp("tensor", lambda e, sl=sl, ic=ic, h=h: e.matmul(
                        pb[:, h * 512:(h + 1) * 512],
                        bwb_sb[:, sl * NDT * 128 + ic * 128:
                               sl * NDT * 128 + (ic + 1) * 128],
                        xln(ic)[:, h * 512:(h + 1) * 512],
                        start=(ic == 0), stop=False),
                        after=[t_wl, xln_done,
                               res_toks[-2] if len(res_toks) >= 2 else None,
                               rh_toks[-1]]
                        if (ic == 0 and h == 0) else [])
                for cc in range(8):
                    tok = p.comp("tensor", lambda e, sl=sl, cc=cc, h=h: e.matmul(
                        pb[:, h * 512:(h + 1) * 512],
                        bsb_sb[:, sl * 8 * 128 + cc * 128:
                               sl * 8 * 128 + (cc + 1) * 128],
                        RH_sb[:, cc * TPC + h * 512:cc * TPC + (h + 1) * 512],
                        start=False, stop=(cc == 7)),
                        after=[t_bs, rh_toks[-1]] if (cc == 0 and h == 0)
                        else [])
            wslot[sl] = tok
            bslot[sl] = tok
            t_r_prev = None
            for half in range(2):
                t_x = p.dma("sync", lambda e, ot=ot, half=half, sl=sl:
                            e.dma_start(
                                out=xr_sb[:, sl * 512:(sl + 1) * 512],
                                in_=x_d[ot * 128:(ot + 1) * 128,
                                        half * 512:(half + 1) * 512]),
                            after=[t_r_prev,
                                   res_toks[-2] if len(res_toks) >= 2
                                   else None])
                t_r = p.comp("vector", lambda e, ot=ot, half=half, sl=sl:
                             e.scalar_tensor_tensor(
                                 xo_sb[:, sl * 512:(sl + 1) * 512],
                                 pb[:, half * 512:(half + 1) * 512],
                                 bb_sb[:, ot:ot + 1],
                                 xr_sb[:, sl * 512:(sl + 1) * 512],
                                 ALU.add, ALU.add),
                             after=[tok, t_x, t_bb,
                                    res_toks[-1] if res_toks else None])
                t_r_prev = t_r
                t_o = p.dma("gpsimd", lambda e, ot=ot, half=half, sl=sl:
                            e.dma_start(
                                out=x1_d[ot * 128:(ot + 1) * 128,
                                         half * 512:(half + 1) * 512],
                                in_=xo_sb[:, sl * 512:(sl + 1) * 512]),
                            after=[t_r])
                res_toks.append(t_o)
    return res_toks[-1]


def prep_l1_ffn_maps(maps, inputs, pre):
    bwT = np.ascontiguousarray(np.asarray(inputs["L1_ffn_base_w"]).T)
    bbT = np.ascontiguousarray(np.asarray(inputs["L1_ffn_base_b"]).reshape(32, 128).T)
    AT = np.ascontiguousarray(np.asarray(inputs["L1_ffn_A"]).T)
    BsT = np.ascontiguousarray(
        np.asarray(inputs["L1_ffn_B"]).transpose(0, 2, 1).reshape(1024, D))
    rwT = np.ascontiguousarray(np.asarray(inputs["L1_ffn_router_w"]).T)
    for c in range(NCORES):
        maps[c].update({pre + "bw": bwT, pre + "bb": bbT, pre + "at": AT,
                        pre + "bs": BsT, pre + "rw": rwT})



NCHUNK = 11
NREAL_IB = 86


NPAR = 3


def emit_l0_ffn(p, io, pre, n_chunks=NCHUNK):
    nc = p.nc
    x_d = io["x"]
    cb_d = io["cb"]
    iw_d = io["iw"]
    idm_d = io["idm"]
    x1_d = io["xout"]
    cbp_d = io["cbp"]
    wgst_d = io["wgst"]
    wust_d = io["wust"]
    wdst_d = io["wdst"]
    wgag_d = io["wgag"]
    wuag_d = io["wuag"]
    wdag_d = io["wdag"]

    ctx = contextlib.ExitStack()
    with ctx:
        sb = lambda name, shape, dt: ctx.enter_context(
            nc.sbuf_tensor(pre + name, shape, dt))
        xln_sb = sb("xln_sb", [128, NDT * TPC], BF16)
        ggb_sb = sb("ggb_sb", [128, 4096], BF16)
        gub_sb = sb("gub_sb", [128, 4096], BF16)
        dstf_sb = sb("dstf_sb", [128, 32, 128], BF16)
        i16w_sb = sb("i16w_sb", [32, 4096], I16)
        i32w_sb = sb("i32w_sb", [32, 2048], I32)
        i32d_sb = sb("i32d_sb", [128, 512], I32)
        mk_sb = sb("mk_sb", [128, 3 * 512], F32)
        tmpf_sb = sb("tmpf_sb", [128, 32, 8], F32)
        wgT_sb = sb("wgT_sb", [128, NDT * 256], BF16)
        wuT_sb = sb("wuT_sb", [128, NDT * 256], BF16)
        wdT_sb = sb("wdT_sb", [128, 2 * 384], BF16)
        h_sb = sb("h_sb", [128, 8 * TPC], BF16)
        hg_sb = sb("hg_sb", [128, TPC], BF16)
        onesf_sb = sb("onesf_sb", [128, 128], F32)
        xt_sb = sb("xt_sb", [128, 2 * TPC], F32)
        sq_sb = sb("sq_sb", [128, TPC], F32)
        stat_sb = sb("stat_sb", [1, 4 * TPC], F32)
        bc_sb = sb("bc_sb", [128, 2 * TPC], F32)
        xr_sb = xt_sb[:, 0:TPC]
        xo_sb = xt_sb[:, TPC:2 * TPC]

        psb = lambda name, shape, dt: ctx.enter_context(
            nc.psum_tensor(pre + name, shape, dt))
        pb = psb("pb", [128, TPC], F32)
        pA = [psb(f"pA{i}", [128, 512], F32) for i in range(6)]

        rg = [list(range(NCORES))]

        t_ones = p.comp("vector", lambda e: e.memset(onesf_sb[:], 1.0))

        t_init = p.dma("sync", lambda e: e.dma_start(out=x1_d, in_=x_d))

        t_cbp = None
        for r0 in range(0, 32768, 4096):
            t_cbp = p.dma("gpsimd", lambda e, r0=r0: e.dma_start(
                out=cbp_d[r0:r0 + 4096, 0:16],
                in_=cb_d[2 * r0:2 * r0 + 8192, :].rearrange(
                    "(r t) e -> r (t e)", t=2)), after=[t_cbp])

        def xw(ic, sl, tm, t9, mrb):
            return p.comp("vector", lambda e, ic=ic, sl=sl: e.tensor_sub(
                xln_sb[:, ic * TPC:(ic + 1) * TPC],
                xt_sb[:, sl * TPC:(sl + 1) * TPC], mrb), after=[tm, t9])

        xln_toks, t_lnmm = emit_ln(p, nc, x_d, xt_sb, sq_sb, stat_sb, bc_sb,
                                   onesf_sb, pA, pb, t_ones, xw)
        xln_done = xln_toks[-1]

        def xln(ic):
            return xln_sb[:, ic * TPC:(ic + 1) * TPC]

        ag_toks = []
        xacc_tok = [t_init] * 64
        xacc_ld = [None] * 64
        compute_done = []

        NPC = 16
        regcell = {}

        def setup_reg(e):
            regcell["r"] = e.alloc_register(pre + "nidx")
            e.reg_mov(regcell["r"], 4096)

        p.add("gpsimd", setup_reg)
        t_i16_prev = None
        t_mb_prev = None
        sel_last = [None, None, None]
        stage_prev = {0: None, 1: None, 2: None}
        last_gth = None
        sel_prev = None
        for c in range(n_chunks):
            par = c % 3
            stage_toks = []
            for m in range(3):
                tgt_sb = gub_sb if m == 1 else ggb_sb
                t_i16 = None
                for hf in range(2):
                    t_lw = p.dma("scalar", lambda e, m=m, c=c, hf=hf:
                                 e.dma_start(
                                     out=i32w_sb[:],
                                     in_=iw_d[m, c, :,
                                              hf * 2048:(hf + 1) * 2048]),
                                 after=[t_i16_prev, t_i16])
                    t_sh = p.comp("vector", lambda e: e.tensor_scalar(
                        i32w_sb[:], i32w_sb[:], 1, None,
                        ALU.logical_shift_right), after=[t_lw])
                    t_i16 = p.comp("vector", lambda e, hf=hf: e.tensor_copy(
                        i16w_sb[:, hf * 2048:(hf + 1) * 2048], i32w_sb[:]),
                        after=[t_sh, last_gth if hf == 0 else None])
                t_i16_prev = t_i16
                t_lm = p.dma("scalar", lambda e, m=m, c=c: e.dma_start(
                    out=i32d_sb[:], in_=idm_d[m, c]), after=[t_mb_prev])
                t_and = p.comp("vector", lambda e: e.tensor_scalar(
                    i32d_sb[:], i32d_sb[:], 1, None, ALU.bitwise_and),
                    after=[t_lm, t_mb_prev])
                t_mb = p.comp("vector", lambda e, m=m: e.tensor_copy(
                    mk_sb[:, m * 512:(m + 1) * 512], i32d_sb[:]),
                    after=[t_and, sel_last[m]])
                t_mb_prev = t_mb
                war = stage_prev[0 if m == 2 else (2 if m == 0 else 1)]
                for pi in range(NPC):
                    t_g = p.add("gpsimd", lambda e, pi=pi: e.dma_gather(
                        dstf_sb[:], cbp_d, i16w_sb[:, pi * 256:(pi + 1) * 256],
                        4096, regcell["r"], 128, elem_step=128,
                        single_packet=False),
                        after=[t_i16, t_cbp, sel_prev] if pi == 0
                        else [sel_prev], inc="gth", k=16, serial=False)
                    last_gth = t_g
                    s1 = p.comp("vector", lambda e: e.tensor_sub(
                        tmpf_sb[:], dstf_sb[:, :, 8:16], dstf_sb[:, :, 0:8]),
                        after=[t_g, sel_prev])
                    m8 = mk_sb[:, m * 512 + pi * 32:m * 512 + (pi + 1) * 32]
                    s2 = p.comp("vector", lambda e, m8=m8: e.tensor_mul(
                        tmpf_sb[:], tmpf_sb[:],
                        m8.rearrange("p (j o) -> p j o", o=1).broadcast_to(
                            [128, 32, 8])), after=[s1, t_mb])
                    sel_prev = p.comp(
                        "vector", lambda e, pi=pi, tgt_sb=tgt_sb:
                        e.tensor_add(
                            tgt_sb[:, pi * 256:(pi + 1) * 256]
                            .rearrange("p (j e) -> p j e", e=8),
                            tmpf_sb[:], dstf_sb[:, :, 0:8]),
                        after=[s2, war if pi == 0 else None])
                sel_last[m] = sel_prev
                agg = ag_toks[-3] if len(ag_toks) >= 3 else None
                if m == 0:
                    st = p.dma("gpsimd", lambda e, par=par: e.dma_start(
                        out=wgst_d[par], in_=ggb_sb[:]),
                        after=[sel_prev, agg])
                elif m == 1:
                    st = p.dma("gpsimd", lambda e, par=par: e.dma_start(
                        out=wust_d[par], in_=gub_sb[:]),
                        after=[sel_prev, agg])
                else:
                    st = p.dma("gpsimd", lambda e, par=par: e.dma_start(
                        out=wdst_d[par].rearrange("(s p) g -> p s g", p=128),
                        in_=ggb_sb[:].rearrange("p (s g) -> p s g", g=1024)),
                        after=[sel_prev, agg])
                stage_prev[m] = st
                stage_toks.append(st)
            t_sg, t_su, t_sd = stage_toks
            cgate = compute_done[c - 3] if c >= 3 else None
            a1 = p.add("gpsimd", lambda e, par=par: e.collective_compute(
                "AllGather", ALU.bypass, rg, [wgst_d[par]], [wgag_d[par]]),
                after=[t_sg, cgate], inc="ag", k=1)
            a2 = p.add("gpsimd", lambda e, par=par: e.collective_compute(
                "AllGather", ALU.bypass, rg, [wust_d[par]], [wuag_d[par]]),
                after=[t_su], inc="ag", k=1)
            a3 = p.add("gpsimd", lambda e, par=par: e.collective_compute(
                "AllGather", ALU.bypass, rg, [wdst_d[par]], [wdag_d[par]]),
                after=[t_sd], inc="ag", k=1)
            ag_toks.append(a3)

            n_ib = min(8, NREAL_IB - c * 8)
            h_toks = []
            prev_pair_mm = compute_done[-1] if compute_done else None
            up_last = None
            for pp in range((n_ib + 1) // 2):
                tw = []
                for ic in range(NDT):
                    tw.append(p.dma("sync", lambda e, c=c, par=par, pp=pp,
                                    ic=ic: e.dma_start(
                                        out=wgT_sb[:, ic * 256:(ic + 1) * 256],
                                        in_=wgag_d[par][pp * 256:(pp + 1) * 256,
                                                        ic * 128:(ic + 1) * 128],
                                        transpose=True),
                                    after=[a1, prev_pair_mm] if ic == 0
                                    else []))
                    tw.append(p.dma("sync", lambda e, c=c, par=par, pp=pp,
                                    ic=ic: e.dma_start(
                                        out=wuT_sb[:, ic * 256:(ic + 1) * 256],
                                        in_=wuag_d[par][pp * 256:(pp + 1) * 256,
                                                        ic * 128:(ic + 1) * 128],
                                        transpose=True),
                                    after=[a2] if ic == 0 else []))
                for jj in range(2):
                    ib = pp * 2 + jj
                    if ib >= n_ib:
                        continue
                    mg = mu = None
                    for h in range(2):
                        for ic in range(NDT):
                            mg = p.comp("tensor", lambda e, ic=ic, jj=jj, h=h:
                                        e.matmul(
                                            pb[:, h * 512:(h + 1) * 512],
                                            wgT_sb[:, ic * 256 + jj * 128:
                                                   ic * 256 + (jj + 1) * 128],
                                            xln(ic)[:, h * 512:(h + 1) * 512],
                                            start=(ic == 0),
                                            stop=(ic == NDT - 1)),
                                        after=([tw[2 * ic], xln_done, t_lnmm,
                                                up_last]
                                               if (h == 0 and ic == 0) else
                                               [tw[2 * ic]] if jj == 0
                                               else []))
                    for h in range(2):
                        for ic in range(NDT):
                            mu = p.comp("tensor", lambda e, ic=ic, jj=jj, h=h:
                                        e.matmul(
                                            pA[h][:],
                                            wuT_sb[:, ic * 256 + jj * 128:
                                                   ic * 256 + (jj + 1) * 128],
                                            xln(ic)[:, h * 512:(h + 1) * 512],
                                            start=(ic == 0),
                                            stop=(ic == NDT - 1)),
                                        after=([tw[2 * ic + 1], mg, up_last]
                                               if (h == 0 and ic == 0) else
                                               [tw[2 * ic + 1]] if jj == 0
                                               else [mg] if ic == 0 else []))
                    ts1 = p.comp("scalar", lambda e: e.activation(
                        hg_sb[:, 0:512], pb[:, 0:512], AF.Silu), after=[mg, mu])
                    ts2 = p.comp("scalar", lambda e: e.activation(
                        hg_sb[:, 512:1024], pb[:, 512:1024], AF.Silu),
                        after=[ts1])
                    ts3 = p.comp("scalar", lambda e, ib=ib: e.activation(
                        h_sb[:, ib * TPC:ib * TPC + 512], pA[0][:], AF.Copy),
                        after=[ts2, h_toks[-1] if h_toks else None])
                    ts4 = p.comp("scalar", lambda e, ib=ib: e.activation(
                        h_sb[:, ib * TPC + 512:(ib + 1) * TPC], pA[1][:],
                        AF.Copy), after=[ts3])
                    tm = p.comp("vector", lambda e, ib=ib: e.tensor_mul(
                        h_sb[:, ib * TPC:(ib + 1) * TPC],
                        h_sb[:, ib * TPC:(ib + 1) * TPC], hg_sb[:]),
                        after=[ts4])
                    h_toks.append(tm)
                    up_last = tm
                prev_pair_mm = up_last
            dn_last = None
            ev_last = None
            for hg0 in range(0, NDT, 3):
                ntile = min(3, NDT - hg0)
                slot_mm = [None, None]
                twd = []
                for ib in range(n_ib):
                    t_wd = p.dma("sync", lambda e, c=c, par=par, hg0=hg0,
                                 ib=ib, ntile=ntile: e.dma_start(
                                     out=wdT_sb[:, (ib % 2) * 384:
                                                (ib % 2) * 384 + ntile * 128],
                                     in_=wdag_d[par][hg0 * 128:
                                                     hg0 * 128 + ntile * 128,
                                                     ib * 128:(ib + 1) * 128],
                                     transpose=True),
                                 after=[a3, slot_mm[ib % 2],
                                        h_toks[-1] if ib == 0 else None,
                                        ev_last if ib == 0 else None])
                    twd.append(t_wd)
                    for t in range(ntile):
                        for h in range(2):
                            dn_last = p.comp(
                                "tensor",
                                lambda e, ib=ib, t=t, h=h, n_ib=n_ib:
                                e.matmul(
                                    pA[t * 2 + h][:],
                                    wdT_sb[:, (ib % 2) * 384 + t * 128:
                                           (ib % 2) * 384 + (t + 1) * 128],
                                    h_sb[:, ib * TPC + h * 512:
                                         ib * TPC + (h + 1) * 512],
                                    start=(ib == 0), stop=(ib == n_ib - 1)),
                                after=[t_wd, h_toks[ib]] +
                                ([ev_last] if ib == 0 else [])
                                if (t == 0 and h == 0) else [])
                    slot_mm[ib % 2] = dn_last
                for t in range(ntile):
                    ot = hg0 + t
                    for h in range(2):
                        sl = (2 * t + h) % 2
                        ti = ot * 2 + h
                        t_x = p.dma("sync", lambda e, ot=ot, h=h, sl=sl:
                                    e.dma_start(
                                        out=xr_sb[:, sl * 512:(sl + 1) * 512],
                                        in_=x1_d[ot * 128:(ot + 1) * 128,
                                                 h * 512:(h + 1) * 512]),
                                    after=[xacc_tok[ti], dn_last,
                                           xacc_ld[ti]])
                        t_r = p.comp("vector", lambda e, t=t, h=h, sl=sl:
                                     e.tensor_add(
                                         xo_sb[:, sl * 512:(sl + 1) * 512],
                                         pA[t * 2 + h][:],
                                         xr_sb[:, sl * 512:(sl + 1) * 512]),
                                     after=[t_x, dn_last])
                        t_o = p.dma("sync", lambda e, ot=ot, h=h, sl=sl:
                                    e.dma_start(
                                        out=x1_d[ot * 128:(ot + 1) * 128,
                                                 h * 512:(h + 1) * 512],
                                        in_=xo_sb[:, sl * 512:(sl + 1) * 512]),
                                    after=[t_r])
                        xacc_tok[ti] = t_o
                        xacc_ld[ti] = t_r
                        ev_last = t_r
            compute_done.append(ev_last)

    return xacc_tok[-1]


def _wrap16(mat):
    lin = np.ascontiguousarray(mat.T).reshape(-1)
    w16 = np.ascontiguousarray(lin.reshape(-1, 16).T)
    return np.concatenate([w16, w16], axis=0)


def prep_l0_ffn_maps(maps, inputs, pre):
    cb = np.ascontiguousarray(np.asarray(inputs["L0_ffn_codebook"], np.float32))
    ig = np.asarray(inputs["L0_ffn_idx_gate"])
    iu = np.asarray(inputs["L0_ffn_idx_up"])
    idn = np.asarray(inputs["L0_ffn_idx_down"])
    for c in range(NCORES):
        iw = np.zeros((3, NCHUNK, 32, 4096), np.int32)
        idm = np.zeros((3, NCHUNK, 128, 512), np.int32)
        for ch in range(NCHUNK):
            tile = ch * 8 + c
            if tile < NREAL_IB:
                rows = slice(tile * 128, (tile + 1) * 128)
                idm[0, ch] = ig[rows]
                iw[0, ch] = _wrap16(ig[rows])
                idm[1, ch] = iu[rows]
                iw[1, ch] = _wrap16(iu[rows])
            gd0 = ch * 128
            gd1 = min(gd0 + 128, idn.shape[1])
            dtile = np.zeros((4, 128, 128), np.int32)
            for s_ in range(4):
                hr = slice(512 * c + s_ * 128, 512 * c + (s_ + 1) * 128)
                dtile[s_, :, 0:gd1 - gd0] = idn[hr, gd0:gd1]
            dmat = np.ascontiguousarray(
                dtile.transpose(1, 0, 2).reshape(128, 512))
            idm[2, ch] = dmat
            iw[2, ch] = _wrap16(dmat)
        maps[c].update({"cb": cb, pre + "iw": iw, pre + "idm": idm})




def build_mega():
    nc = bass.Bass("TRN2", target_bir_lowering=False)

    def dt(name, shape, dtyp, kind):
        return nc.dram_tensor(name, shape, dtyp, kind=kind).ap()

    x_d = dt("x", [D, TPC], F32, "ExternalInput")
    eyeb_d = dt("eyeb", [128, 128], BF16, "ExternalInput")
    nm0 = build_tile_plan(0)[1].shape[0]
    nm1 = build_tile_plan(1)[1].shape[0]

    a0 = dict(
        x=x_d, eyeb=eyeb_d,
        wq=dt("a0_wq", [D, 512], F32, "ExternalInput"),
        wkv=dt("a0_wkv", [D, 256], F32, "ExternalInput"),
        wo=dt("a0_wo", [512, D], F32, "ExternalInput"),
        masks=dt("a0_masks", [nm0, 128, 512], BF16, "ExternalInput"),
        bq=dt("a0_bq", [128, 6], F32, "ExternalInput"),
        bo=dt("a0_bo", [128, 32], F32, "ExternalInput"),
        cos=dt("cosT", [128, S], F32, "ExternalInput"),
        sin=dt("sinSW", [128, S], F32, "ExternalInput"),
        xout=dt("xa", [D, TPC], F32, "Internal"),
        xstage=dt("a0_xstage", [D, TPC], BF16, "Internal"),
        xag=dt("a0_xag", [NCORES * D, TPC], BF16, "Internal"),
        rsi=dt("a0_rsi", [NCORES * D, TPC], F32, "Internal"),
        rso=dt("a0_rso", [D, TPC], F32, "Internal"),
    )
    f0 = dict(
        x=a0["xout"],
        cb=dt("cb", [65536, 8], F32, "ExternalInput"),
        iw=dt("f0_iw", [3, NCHUNK, 32, 4096], I32, "ExternalInput"),
        idm=dt("f0_idm", [3, NCHUNK, 128, 512], I32, "ExternalInput"),
        cbp=dt("f0_cbp", [32768, 128], BF16, "Internal"),
        xout=dt("xb", [D, TPC], F32, "Internal"),
        wgst=[dt(f"f0_wgst{i}", [128, 4096], BF16, "Internal")
              for i in range(NPAR)],
        wust=[dt(f"f0_wust{i}", [128, 4096], BF16, "Internal")
              for i in range(NPAR)],
        wdst=[dt(f"f0_wdst{i}", [512, 1024], BF16, "Internal")
              for i in range(NPAR)],
        wgag=[dt(f"f0_wgag{i}", [1024, 4096], BF16, "Internal")
              for i in range(NPAR)],
        wuag=[dt(f"f0_wuag{i}", [1024, 4096], BF16, "Internal")
              for i in range(NPAR)],
        wdag=[dt(f"f0_wdag{i}", [4096, 1024], BF16, "Internal")
              for i in range(NPAR)],
    )
    a1 = dict(
        x=f0["xout"], eyeb=eyeb_d,
        wq=dt("a1_wq", [D, 512], F32, "ExternalInput"),
        wkv=dt("a1_wkv", [D, 256], F32, "ExternalInput"),
        wo=dt("a1_wo", [512, D], F32, "ExternalInput"),
        masks=dt("a1_masks", [nm1, 128, 512], BF16, "ExternalInput"),
        xout=dt("xc", [D, TPC], F32, "Internal"),
        xstage=dt("a1_xstage", [D, TPC], BF16, "Internal"),
        xag=dt("a1_xag", [NCORES * D, TPC], BF16, "Internal"),
        rsi=dt("a1_rsi", [NCORES * D, TPC], F32, "Internal"),
        rso=dt("a1_rso", [D, TPC], F32, "Internal"),
        kvs=dt("a1_kvs", [256, TPC], BF16, "Internal"),
        kvag=dt("a1_kvag", [NCORES * 256, TPC], BF16, "Internal"),
    )
    f1 = dict(
        x=a1["xout"], eyeb=eyeb_d,
        bw=dt("f1_bw", [D, D], F32, "ExternalInput"),
        bb=dt("f1_bb", [128, 32], F32, "ExternalInput"),
        at=dt("f1_at", [D, 64], F32, "ExternalInput"),
        bs=dt("f1_bs", [1024, D], F32, "ExternalInput"),
        rw=dt("f1_rw", [D, 16], F32, "ExternalInput"),
        xout=dt("x1", [D, TPC], F32, "ExternalOutput"),
    )

    ctx = contextlib.ExitStack()
    with ctx:
        p = Prog(nc, ctx)
        blk = ctx.enter_context(nc.Block())
        p.add("gpsimd", lambda e: e.load_library(mlp))
        emit_attn(p, a0, 0, "a0")
        p.stage_gate()
        emit_l0_ffn(p, f0, "f0")
        p.stage_gate()
        emit_attn(p, a1, 1, "a1")
        p.stage_gate()
        last = emit_l1_ffn(p, f1, "f1")
        p.add("gpsimd", lambda e: e.engine_nop(), after=[last])
        p.emit(blk)
    mybir.codegen_inst_isa_subclasses(nc)
    return nc




def kernel(**inputs):
    x0 = np.asarray(inputs["L0_attn_hidden_states"], np.float32).reshape(S, D)
    eyeb = np.eye(128, dtype=np.float32).astype(BF)
    maps = [{"x": np.ascontiguousarray(x0[core_token_ids(c)].T), "eyeb": eyeb}
            for c in range(NCORES)]
    prep_attn_maps(maps, inputs, 0, "a0_")
    prep_l0_ffn_maps(maps, inputs, "f0_")
    prep_attn_maps(maps, inputs, 1, "a1_")
    prep_l1_ffn_maps(maps, inputs, "f1_")
    nc = build_mega()
    res = run_bass_kernel_spmd(nc, maps, core_ids=list(range(NCORES)))
    out = np.zeros((S, D), dtype=np.float32)
    for c in range(NCORES):
        out[core_token_ids(c)] = res.results[c]["x1"].T
    return out.reshape(1, S, D)


# version 06
# speedup vs baseline: 1.2724x; 1.2724x over previous
import contextlib
import numpy as np
import ml_dtypes
import concourse.bass as bass
import concourse.mybir as mybir
from concourse.bass_utils import run_bass_kernel_spmd
from concourse.library_config import mlp

F32 = mybir.dt.float32
BF16 = mybir.dt.bfloat16
I32 = mybir.dt.int32
I16 = mybir.dt.int16
AF = mybir.ActivationFunctionType
ALU = mybir.AluOpType
AX = mybir.AxisListType

NCORES = 8
S = 8192
D = 4096
TPC = 1024
NDT = D // 128
BF = ml_dtypes.bfloat16
NQW = S // 512


def core_token_ids(c):
    return np.arange(TPC * c, TPC * (c + 1))




def l0_tile_mask(q0, k0):
    k = np.arange(k0, k0 + 128)[:, None]
    q = np.arange(q0, q0 + 512)[None, :]
    qb = q // 64
    kb = k // 64
    act = (kb <= qb) & ((qb - kb < 16) | ((kb + 1) % 8 == 0))
    return act & (k <= q)


def l1_tile_mask(q0, k0):
    k = np.arange(k0, k0 + 128)[:, None]
    q = np.arange(q0, q0 + 512)[None, :]
    return k <= q


def build_tile_plan(layer):
    plan = []
    masks = {}
    order = []
    for qw in range(NQW):
        q0 = 512 * qw
        tiles = []
        for k0 in range(0, q0 + 512, 128):
            m = l0_tile_mask(q0, k0) if layer == 0 else l1_tile_mask(q0, k0)
            if not m.any():
                continue
            if m.all():
                tiles.append((k0, None))
            else:
                key = m.tobytes()
                if key not in masks:
                    masks[key] = len(order)
                    order.append(m)
                tiles.append((k0, masks[key]))
        plan.append(tiles)
    marr = np.stack(order).astype(BF) if order else np.zeros((1, 128, 512), BF)
    return plan, marr


def rope_tables():
    hd = 128
    inv = 1.0 / (1000000.0 ** (np.arange(0, hd, 2, dtype=np.float64) / hd))
    t = np.arange(S, dtype=np.float64)
    fr = np.outer(t, inv)
    emb = np.concatenate([fr, fr], axis=1)
    cos = np.cos(emb).T.astype(np.float32)
    sin = np.sin(emb).T.astype(np.float32)
    sinS = sin.copy()
    sinS[:64] *= -1.0
    return np.ascontiguousarray(cos), np.ascontiguousarray(sinS)




class Sem:
    def __init__(self, stack, nc, name):
        self.h = stack.enter_context(nc.semaphore(name))
        self.n = 0


class Prog:
    ENGS = ("sync", "tensor", "vector", "scalar", "gpsimd")

    def __init__(self, nc, stack):
        self.nc = nc
        self.stack = stack
        self.ops = {e: [] for e in self.ENGS}
        self.sems = {}
        self.gate = ()
        self.gated = set()

    def sem(self, name):
        if name not in self.sems:
            self.sems[name] = Sem(self.stack, self.nc, name)
        return self.sems[name]

    def stage_gate(self):
        self.gate = tuple((nm, s.n) for nm, s in self.sems.items() if s.n > 0)
        self.gated = set()

    def add(self, eng, fn, after=(), inc=None, k=1, serial=True):
        if self.gate and eng not in self.gated:
            self.gated.add(eng)
            after = tuple(t for t in after if t is not None) + self.gate
        s = self.sem(inc) if inc else None
        tok = None
        extra = ()
        if s is not None:
            if serial and s.n > 0:
                extra = ((inc, s.n),)
            s.n += k
            tok = (inc, s.n)
        after = tuple(t for t in after if t is not None) + extra
        self.ops[eng].append((after, fn, s, k))
        return tok

    def dma(self, eng, fn, after=()):
        return self.add(eng, fn, after, inc="dma_" + eng, k=16)

    def comp(self, eng, fn, after=()):
        return self.add(eng, fn, after, inc="c_" + eng, k=1)

    def barrier_tok(self, eng_sem):
        return (eng_sem, self.sem(eng_sem).n)

    def emit(self, blk):
        def make_body(name):
            def body(eng):
                wm = {}
                own = "c_" + name
                for (waits, fn, s, k) in self.ops[name]:
                    for (snm, val) in waits:
                        if name == "tensor" and snm == own:
                            continue
                        if wm.get(snm, 0) >= val:
                            continue
                        eng.wait_ge(self.sems[snm].h, val)
                        wm[snm] = val
                    ins = fn(eng)
                    if s is not None:
                        assert ins is not None, f"op on {name} returned None"
                        ins.then_inc(s.h, k)
            return body

        for name in self.ENGS:
            if self.ops[name]:
                getattr(blk, name)(make_body(name))




def emit_attn(p, io, layer, pre):
    plan, masks_np = build_tile_plan(layer)
    nm = masks_np.shape[0]
    sm_scale = (1.0 / 128.0) if layer == 0 else float(1.0 / np.sqrt(128.0))

    nc = p.nc
    x_d = io["x"]
    wq_d = io["wq"]
    wkv_d = io["wkv"]
    wo_d = io["wo"]
    masks_d = io["masks"]
    eyeb_d = io["eyeb"]
    x1_d = io["xout"]
    xstage_d = io["xstage"]
    xag_d = io["xag"]
    rsi_d = io["rsi"]
    rso_d = io["rso"]
    if layer == 0:
        bq_d = io["bq"]
        bo_d = io["bo"]
        cos_d = io["cos"]
        sin_d = io["sin"]
    else:
        kvstage_d = io["kvs"]
        kvag_d = io["kvag"]

    ctx = contextlib.ExitStack()
    with ctx:
        sb = lambda name, shape, dt: ctx.enter_context(
            nc.sbuf_tensor(pre + name, shape, dt))
        xw_sb = sb("xw_sb", [128, NDT * 512], BF16)
        K_sb = sb("K_sb", [128, S], BF16)
        V_sb = sb("V_sb", [128, S], BF16)
        wqb_sb = sb("wqb_sb", [128, NDT * 512], BF16)
        wkvb_sb = sb("wkvb_sb", [128, NDT * 256], BF16)
        masks_sb = sb("masks_sb", [128, nm * 512], BF16)
        eyeb_sb = sb("eyeb_sb", [128, 128], BF16)
        onesf_sb = sb("onesf_sb", [128, 128], F32)
        onesb_sb = sb("onesb_sb", [128, 4], BF16)
        xt_sb = sb("xt_sb", [128, 2 * TPC], F32)
        xb_sb = sb("xb_sb", [128, 2 * TPC], BF16)
        sq_sb = sb("sq_sb", [128, TPC], F32)
        stat_sb = sb("stat_sb", [1, 4 * TPC], F32)
        bc_sb = sb("bc_sb", [128, 2 * TPC], F32)
        if layer == 0:
            bq_sb = sb("bq_sb", [128, 6], F32)
            bo_sb = sb("bo_sb", [128, 32], F32)
            cosw_sb = sb("cosw_sb", [128, 512], F32)
            sinw_sb = sb("sinw_sb", [128, 512], F32)
        wot_sb = sb("wot_sb", [128, 2 * 512], BF16)
        wotf_sb = sb("wotf_sb", [128, 2 * 512], F32)
        po_sb = sb("po_sb", [128, 2 * 512], F32)
        ctxh_sb = sb("ctxh_sb", [128, 4 * 512], BF16)
        kvf_sb = sb("kvf_sb", [128, 2 * 512], F32)
        vb_sb = sb("vb_sb", [128, 512], BF16)
        q_sb = sb("q_sb", [128, 4 * 512], BF16)
        qf_sb = sb("qf_sb", [128, 512], F32)
        kn_sb = sb("kn_sb", [1, 64], F32)
        nrm_sb = sb("nrm_sb", [1, 16], F32)
        cc_sb = sb("cc_sb", [128, 1], F32)
        pt_sb = sb("pt_sb", [128, 2 * 512], BF16)
        den_sb = sb("den_sb", [1, 512], F32)
        ctxf_sb = sb("ctxf_sb", [128, 512], F32)
        xr_sb = sb("xr_sb", [128, 2 * 512], F32)
        xo_sb = sb("xo_sb", [128, 2 * 512], F32)

        psb = lambda name, shape, dt: ctx.enter_context(
            nc.psum_tensor(pre + name, shape, dt))
        pb = psb("pb", [128, TPC], F32)
        pA = [psb(f"pA{i}", [128, 512], F32) for i in range(5)]
        ptr = psb("ptr", [128, 512], BF16)

        rg = [list(range(NCORES))]

        for n in range(nm):
            p.dma("sync", lambda e, n=n: e.dma_start(
                out=masks_sb[:, n * 512:(n + 1) * 512], in_=masks_d[n]))
        t_eye = p.dma("sync", lambda e: e.dma_start(out=eyeb_sb[:], in_=eyeb_d))
        if layer == 0:
            p.dma("sync", lambda e: e.dma_start(out=bq_sb[:], in_=bq_d))
            t_bo = p.dma("sync", lambda e: e.dma_start(out=bo_sb[:], in_=bo_d))
        else:
            t_bo = None
        t_const = p.barrier_tok("dma_sync")
        t_ones = p.comp("vector", lambda e: e.memset(onesf_sb[:], 1.0))
        t_onesb = p.comp("vector", lambda e: e.memset(onesb_sb[:], 1.0))

        pm0, pm1, pv0, pv1 = pA[0], pA[1], pA[2], pA[3]
        mm_tok = None
        slot_free = [None, None]
        for ic in range(NDT):
            sl = ic % 2
            t_ld = p.dma("sync", lambda e, ic=ic, sl=sl: e.dma_start(
                out=xt_sb[:, sl * TPC:(sl + 1) * TPC],
                in_=x_d[ic * 128:(ic + 1) * 128, :]), after=[slot_free[sl]])
            t_sq = p.comp("scalar", lambda e, sl=sl: e.activation(
                sq_sb[:], xt_sb[:, sl * TPC:(sl + 1) * TPC], AF.Square),
                after=[t_ld, mm_tok])
            for h in range(2):
                p.comp("tensor", lambda e, sl=sl, h=h, ic=ic: e.matmul(
                    (pm0 if h == 0 else pm1)[0:1, :], onesf_sb[:, 0:1],
                    xt_sb[:, sl * TPC + h * 512:sl * TPC + (h + 1) * 512],
                    start=(ic == 0), stop=(ic == NDT - 1)),
                    after=[t_ld, t_ones])
                mm_tok = p.comp("tensor", lambda e, h=h, ic=ic: e.matmul(
                    (pv0 if h == 0 else pv1)[0:1, :], onesf_sb[:, 0:1],
                    sq_sb[:, h * 512:(h + 1) * 512],
                    start=(ic == 0), stop=(ic == NDT - 1)),
                    after=[t_sq])
            slot_free[sl] = mm_tok
        m_ = stat_sb[0:1, 0:TPC]
        msq = stat_sb[0:1, TPC:2 * TPC]
        m2 = stat_sb[0:1, 2 * TPC:3 * TPC]
        var = stat_sb[0:1, 3 * TPC:4 * TPC]
        r_ = stat_sb[0:1, 3 * TPC:4 * TPC]
        mr = stat_sb[0:1, 2 * TPC:3 * TPC]
        t1 = p.comp("scalar", lambda e: e.activation(
            m_[:, 0:512], pm0[0:1, :], AF.Copy, scale=1.0 / D), after=[mm_tok])
        t1 = p.comp("scalar", lambda e: e.activation(
            m_[:, 512:1024], pm1[0:1, :], AF.Copy, scale=1.0 / D), after=[t1])
        t2 = p.comp("scalar", lambda e: e.activation(
            msq[:, 0:512], pv0[0:1, :], AF.Copy, scale=1.0 / D), after=[t1])
        t2 = p.comp("scalar", lambda e: e.activation(
            msq[:, 512:1024], pv1[0:1, :], AF.Copy, scale=1.0 / D), after=[t2])
        t3 = p.comp("scalar", lambda e: e.activation(m2[:], m_[:], AF.Square),
                    after=[t2])
        t4 = p.comp("vector", lambda e: e.tensor_sub(var[:], msq[:], m2[:]),
                    after=[t3])
        t4b = p.comp("vector", lambda e: e.tensor_scalar_add(var[:], var[:],
                                                             1e-5), after=[t4])
        t5 = p.comp("scalar", lambda e: e.activation(var[:], var[:], AF.Sqrt),
                    after=[t4b])
        t6 = p.comp("vector", lambda e: e.reciprocal(r_[:], var[:]), after=[t5])
        t7 = p.comp("vector", lambda e: e.tensor_mul(mr[:], m_[:], r_[:]),
                    after=[t6])
        bts = [p.comp("tensor", lambda e, h=h: e.matmul(
            pb[:, h * 512:(h + 1) * 512], onesf_sb[0:1, :],
            r_[:, h * 512:(h + 1) * 512], start=True, stop=True),
            after=[t6]) for h in range(2)]
        t8 = p.comp("scalar", lambda e: e.activation(bc_sb[:, 0:TPC], pb[:],
                                                     AF.Copy), after=bts)
        bts2 = [p.comp("tensor", lambda e, h=h: e.matmul(
            pb[:, h * 512:(h + 1) * 512], onesf_sb[0:1, :],
            mr[:, h * 512:(h + 1) * 512], start=True, stop=True),
            after=[t7, t8]) for h in range(2)]
        t9 = p.comp("scalar", lambda e: e.activation(bc_sb[:, TPC:2 * TPC], pb[:],
                                                     AF.Copy), after=bts2)
        rb = bc_sb[:, 0:TPC]
        mrb = bc_sb[:, TPC:2 * TPC]
        xst_toks = []
        slot_free = [None, None]
        for ic in range(NDT):
            sl = ic % 2
            t_ld = p.dma("sync", lambda e, ic=ic, sl=sl: e.dma_start(
                out=xt_sb[:, sl * TPC:(sl + 1) * TPC],
                in_=x_d[ic * 128:(ic + 1) * 128, :]), after=[t8, slot_free[sl]])
            tm = p.comp("vector", lambda e, sl=sl: e.tensor_mul(
                xt_sb[:, sl * TPC:(sl + 1) * TPC],
                xt_sb[:, sl * TPC:(sl + 1) * TPC], rb), after=[t_ld, t8])
            tsb = p.comp("vector", lambda e, sl=sl: e.tensor_sub(
                xb_sb[:, sl * TPC:(sl + 1) * TPC],
                xt_sb[:, sl * TPC:(sl + 1) * TPC], mrb), after=[tm, t9])
            tst = p.dma("scalar", lambda e, ic=ic, sl=sl: e.dma_start(
                out=xstage_d[ic * 128:(ic + 1) * 128, :],
                in_=xb_sb[:, sl * TPC:(sl + 1) * TPC]), after=[tsb])
            xst_toks.append(tst)
            slot_free[sl] = tst
        t_agx = p.add("gpsimd", lambda e: e.collective_compute(
            "AllGather", ALU.bypass, rg, [xstage_d], [xag_d]),
            after=[xst_toks[-1]], inc="ag", k=1)
        t_ln_done = t9

        def stream_slab_to(w_dram, col0, ncols, dst_sb, dst_off, after=()):
            return p.dma("gpsimd", lambda e: e.dma_start(
                out=dst_sb[:, dst_off:dst_off + NDT * ncols].rearrange(
                    "p (a o) -> p a o", o=ncols),
                in_=w_dram[:, col0:col0 + ncols].rearrange(
                    "(a p) o -> p a o", p=128)), after=after)

        wq_toks = []
        prev = None
        for j in range(4):
            prev = stream_slab_to(wq_d, j * 128, 128, wqb_sb, j * NDT * 128,
                                  after=[prev])
            wq_toks.append(prev)
        t_wkv = stream_slab_to(wkv_d, 0, 256, wkvb_sb, 0, after=[prev])

        def wq_tile(j, ic):
            return wqb_sb[:, j * NDT * 128 + ic * 128:j * NDT * 128 + (ic + 1) * 128]

        def wkv_tile(which, ic):
            o = ic * 256 + which * 128
            return wkvb_sb[:, o:o + 128]

        kv_ready = None
        kn_done = {}
        xw_init_gate = None
        if layer == 1:
            own_done = []
            ldt_last = None
            for h in range(2):
                ldt = []
                for ic in range(NDT):
                    ldt.append(p.dma("sync", lambda e, ic=ic, h=h: e.dma_start(
                        out=xw_sb[:, ic * 512:(ic + 1) * 512],
                        in_=xstage_d[ic * 128:(ic + 1) * 128,
                                     h * 512:(h + 1) * 512]),
                        after=[xst_toks[ic],
                               own_done[-1] if own_done else None]))
                mk = mv = None
                for ic in range(NDT):
                    mk = p.comp("tensor", lambda e, ic=ic: e.matmul(
                        pA[0][:], wkv_tile(0, ic),
                        xw_sb[:, ic * 512:(ic + 1) * 512],
                        start=(ic == 0), stop=(ic == NDT - 1)),
                        after=[t_wkv, t_ln_done, ldt[ic]] if ic == 0
                        else [ldt[ic]])
                for ic in range(NDT):
                    mv = p.comp("tensor", lambda e, ic=ic: e.matmul(
                        pA[1][:], wkv_tile(1, ic),
                        xw_sb[:, ic * 512:(ic + 1) * 512],
                        start=(ic == 0), stop=(ic == NDT - 1)), after=[])
                tk = p.comp("scalar", lambda e, h=h: e.activation(
                    xb_sb[:, h * 512:(h + 1) * 512], pA[0][:], AF.Copy),
                    after=[mk, mv])
                tv = p.comp("scalar", lambda e, h=h: e.activation(
                    xb_sb[:, TPC + h * 512:TPC + (h + 1) * 512], pA[1][:],
                    AF.Copy), after=[tk])
                own_done.append(tv)
                ldt_last = ldt[-1]
            tsk = p.dma("scalar", lambda e: e.dma_start(
                out=kvstage_d[0:128, :], in_=xb_sb[:, 0:TPC]), after=own_done)
            tsv = p.dma("scalar", lambda e: e.dma_start(
                out=kvstage_d[128:256, :], in_=xb_sb[:, TPC:2 * TPC]),
                after=own_done)
            t_agkv = p.add("gpsimd", lambda e: e.collective_compute(
                "AllGather", ALU.bypass, rg, [kvstage_d], [kvag_d]),
                after=[tsk, tsv], inc="ag", k=1)
            kl = []
            for r in range(NCORES):
                kl.append(p.dma("sync", lambda e, r=r: e.dma_start(
                    out=K_sb[:, r * TPC:(r + 1) * TPC],
                    in_=kvag_d[r * 256:r * 256 + 128, :]), after=[t_agkv]))
                kl.append(p.dma("sync", lambda e, r=r: e.dma_start(
                    out=V_sb[:, r * TPC:(r + 1) * TPC],
                    in_=kvag_d[r * 256 + 128:r * 256 + 256, :]),
                    after=[t_agkv]))
            tprev = None
            for kt in range(64):
                ta = p.comp("tensor", lambda e, kt=kt: e.transpose(
                    ptr[:, 0:128], V_sb[:, kt * 128:(kt + 1) * 128], eyeb_sb[:]),
                    after=[kl[2 * (kt // 8) + 1], t_eye, tprev])
                tprev = p.comp("vector", lambda e, kt=kt: e.tensor_copy(
                    V_sb[:, kt * 128:(kt + 1) * 128], ptr[:, 0:128]), after=[ta])
            kv_ready = tprev
            knt = None
            for kt in range(64):
                tsq2 = p.comp("scalar", lambda e, kt=kt: e.activation(
                    sq_sb[:, 0:128], K_sb[:, kt * 128:(kt + 1) * 128],
                    AF.Square), after=[kl[2 * (kt // 8)], knt])
                tm1 = p.comp("tensor", lambda e: e.matmul(
                    pA[4][0:1, 0:128], onesf_sb[:, 0:1], sq_sb[:, 0:128],
                    start=True, stop=True), after=[tsq2])
                knt = p.comp("vector", lambda e, kt=kt: e.tensor_reduce(
                    kn_sb[0:1, kt:kt + 1], pA[4][0:1, 0:128], AX.X, ALU.max),
                    after=[tm1])
                kn_done[kt] = knt
            xw_init_gate = ldt_last
            xw_init_gate = tsv

        NH = 4
        kwin_done = {}
        if layer == 1:
            for w in range(NQW):
                kwin_done[w] = kv_ready
        xw_free = xw_init_gate
        prev_qnorm = None
        prev_qproj = None
        qevac_prev = [None, None]
        prev_cx0 = kn_done.get(63) if layer == 1 else None
        prev_ctx_st = None
        prev_cf = None
        prev_cx = kn_done.get(63) if layer == 1 else None
        prev_d1 = None
        prev_d3 = None
        for qw in range(NQW):
            r = qw // 2
            hh = qw % 2
            t_ldw = p.dma("sync", lambda e, r=r, hh=hh: e.dma_start(
                out=xw_sb[:].rearrange("p (a t) -> p a t", t=512),
                in_=xag_d[r * D:(r + 1) * D,
                          hh * 512:(hh + 1) * 512].rearrange(
                    "(a p) t -> p a t", p=128)),
                after=[t_agx, xw_free])
            ldw = [t_ldw] * NDT
            if layer == 0:
                t_cs = p.dma("sync", lambda e, qw=qw: e.dma_start(
                    out=cosw_sb[:], in_=cos_d[:, qw * 512:(qw + 1) * 512]),
                    after=[prev_qproj, t_const])
                t_sn = p.dma("sync", lambda e, qw=qw: e.dma_start(
                    out=sinw_sb[:], in_=sin_d[:, qw * 512:(qw + 1) * 512]),
                    after=[prev_qproj])
                mk = mv = None
                for ic in range(NDT):
                    mk = p.comp("tensor", lambda e, ic=ic: e.matmul(
                        pA[0][:], wkv_tile(0, ic),
                        xw_sb[:, ic * 512:(ic + 1) * 512],
                        start=(ic == 0), stop=(ic == NDT - 1)),
                        after=[t_wkv, t_ln_done, ldw[ic], prev_ctx_st]
                        if ic == 0 else [ldw[ic]])
                for ic in range(NDT):
                    mv = p.comp("tensor", lambda e, ic=ic: e.matmul(
                        pA[1][:], wkv_tile(1, ic),
                        xw_sb[:, ic * 512:(ic + 1) * 512],
                        start=(ic == 0), stop=(ic == NDT - 1)), after=[])
                tke = p.comp("scalar", lambda e: e.activation(
                    kvf_sb[:, 0:512], pA[0][:], AF.Identity, bias=bq_sb[:, 4:5]),
                    after=[mk, mv])
                tr1 = p.comp("vector", lambda e: e.tensor_mul(
                    kvf_sb[:, 512:1024], kvf_sb[:, 0:512], cosw_sb[:]),
                    after=[tke, t_cs])
                tr2 = p.comp("vector", lambda e: e.tensor_mul(
                    qf_sb[0:64, :], kvf_sb[64:128, 0:512], sinw_sb[64:128, :]),
                    after=[tke, t_sn])
                tr3 = p.comp("vector", lambda e: e.tensor_mul(
                    qf_sb[64:128, :], kvf_sb[0:64, 0:512], sinw_sb[0:64, :]),
                    after=[tke])
                tk_bf = p.comp("vector", lambda e, qw=qw: e.tensor_add(
                    K_sb[:, qw * 512:(qw + 1) * 512], kvf_sb[:, 512:1024],
                    qf_sb[:, 0:512]), after=[tr1, tr2, tr3])
                tve = p.comp("scalar", lambda e: e.activation(
                    vb_sb[:], pA[1][:], AF.Identity, bias=bq_sb[:, 5:6]),
                    after=[tke])
                tprev = None
                for j in range(4):
                    ta = p.comp("tensor", lambda e, j=j: e.transpose(
                        ptr[:, 0:128], vb_sb[:, j * 128:(j + 1) * 128],
                        eyeb_sb[:]), after=[tve, t_eye, tprev])
                    tprev = p.comp("vector", lambda e, j=j, qw=qw: e.tensor_copy(
                        V_sb[:, qw * 512 + j * 128:qw * 512 + (j + 1) * 128],
                        ptr[:, 0:128]), after=[ta])
                for j in range(4):
                    kt = qw * 4 + j
                    tsq2 = p.comp("scalar", lambda e, qw=qw, j=j: e.activation(
                        sq_sb[:, 0:128],
                        K_sb[:, qw * 512 + j * 128:qw * 512 + (j + 1) * 128],
                        AF.Square),
                        after=[tk_bf, kn_done.get(kt - 1), prev_qnorm])
                    tm1 = p.comp("tensor", lambda e: e.matmul(
                        pA[4][0:1, 0:128], onesf_sb[:, 0:1], sq_sb[:, 0:128],
                        start=True, stop=True), after=[tsq2, prev_cx])
                    kn_done[kt] = p.comp(
                        "vector", lambda e, kt=kt: e.tensor_reduce(
                            kn_sb[0:1, kt:kt + 1], pA[4][0:1, 0:128], AX.X,
                            ALU.max), after=[tm1])
                kwin_done[qw] = tprev

            q_toks = []
            qn_toks = []
            for j in range(NH):
                qh = j % 2
                mq = None
                for ic in range(NDT):
                    mq = p.comp("tensor", lambda e, j=j, ic=ic, qh=qh: e.matmul(
                        pb[:, qh * 512:(qh + 1) * 512], wq_tile(j, ic),
                        xw_sb[:, ic * 512:(ic + 1) * 512],
                        start=(ic == 0), stop=(ic == NDT - 1)),
                        after=[wq_toks[j], t_ln_done, ldw[ic], prev_qproj,
                               qevac_prev[qh]]
                        if ic == 0 else [ldw[ic]])
                if layer == 0:
                    tqe = p.comp("scalar", lambda e, j=j, qh=qh: e.activation(
                        qf_sb[:], pb[:, qh * 512:(qh + 1) * 512], AF.Identity,
                        bias=bq_sb[:, j:j + 1]),
                        after=[mq])
                    qevac_prev[qh] = tqe
                    r1 = p.comp("vector", lambda e: e.tensor_mul(
                        kvf_sb[:, 0:512], qf_sb[:], cosw_sb[:]), after=[tqe])
                    r2 = p.comp("vector", lambda e: e.tensor_mul(
                        kvf_sb[0:64, 512:1024], qf_sb[64:128, :],
                        sinw_sb[64:128, :]), after=[tqe])
                    r3 = p.comp("vector", lambda e: e.tensor_mul(
                        kvf_sb[64:128, 512:1024], qf_sb[0:64, :],
                        sinw_sb[0:64, :]), after=[tqe])
                    tq = p.comp("vector", lambda e, j=j: e.tensor_add(
                        q_sb[:, j * 512:(j + 1) * 512], kvf_sb[:, 0:512],
                        kvf_sb[:, 512:1024]), after=[r1, r2, r3])
                else:
                    tq = p.comp("scalar", lambda e, j=j, qh=qh: e.activation(
                        q_sb[:, j * 512:(j + 1) * 512],
                        pb[:, qh * 512:(qh + 1) * 512], AF.Copy),
                        after=[mq])
                    qevac_prev[qh] = tq
                q_toks.append(tq)
                tsq3 = p.comp("scalar", lambda e, j=j: e.activation(
                    sq_sb[:, 128:640], q_sb[:, j * 512:(j + 1) * 512],
                    AF.Square), after=[tq, qn_toks[-1] if qn_toks else
                                       prev_qnorm])
                tm1 = p.comp("tensor", lambda e: e.matmul(
                    pA[4][0:1, 0:512], onesf_sb[:, 0:1], sq_sb[:, 128:640],
                    start=True, stop=True), after=[tsq3, prev_cx])
                qn_toks.append(p.comp("vector", lambda e, j=j: e.tensor_reduce(
                    nrm_sb[0:1, j:j + 1], pA[4][0:1, 0:512], AX.X, ALU.max),
                    after=[tm1]))
            prev_qproj = q_toks[-1]
            prev_qnorm = qn_toks[-1]

            tiles = plan[qw]
            last_kt = max(k0 for (k0, _m) in tiles) // 128
            tqm = p.comp("vector", lambda e: e.tensor_reduce(
                nrm_sb[0:1, 8:9], nrm_sb[0:1, 0:4], AX.X, ALU.max),
                after=qn_toks)
            tkm = p.comp("vector", lambda e, last_kt=last_kt: e.tensor_reduce(
                nrm_sb[0:1, 9:10], kn_sb[0:1, 0:last_kt + 1], AX.X, ALU.max),
                after=[kn_done[last_kt]])
            tc1 = p.comp("vector", lambda e: e.tensor_mul(
                nrm_sb[0:1, 10:11], nrm_sb[0:1, 8:9], nrm_sb[0:1, 9:10]),
                after=[tqm, tkm])
            tc2 = p.comp("scalar", lambda e: e.activation(
                nrm_sb[0:1, 11:12], nrm_sb[0:1, 10:11], AF.Sqrt), after=[tc1])
            tc3 = p.comp("scalar", lambda e: e.mul(
                nrm_sb[0:1, 12:13], nrm_sb[0:1, 11:12], -1.001 * sm_scale),
                after=[tc2])
            tc4 = p.comp("tensor", lambda e: e.matmul(
                pA[4][:, 0:1], onesf_sb[0:1, :], nrm_sb[0:1, 12:13],
                start=True, stop=True), after=[tc3, prev_cx])
            tc5 = p.comp("scalar", lambda e: e.activation(
                cc_sb[:], pA[4][:, 0:1], AF.Copy), after=[tc4])

            ctx_head_toks = []
            for j in range(NH):
                n_t = len(tiles)
                exp_prev = [None, None]
                pt_readers = [None, None]
                pv_prev = None
                den_prev = None
                kreq = kwin_done[qw]
                for i, (k0, mid) in enumerate(tiles):
                    bk = i % 2
                    t_s = p.comp("tensor", lambda e, k0=k0, bk=bk, j=j: e.matmul(
                        pA[bk][:], K_sb[:, k0:k0 + 128],
                        q_sb[:, j * 512:(j + 1) * 512], start=True, stop=True),
                        after=[exp_prev[bk], q_toks[j], kreq] if i < 2
                        else [exp_prev[bk]])
                    t_e = p.comp("scalar", lambda e, bk=bk: e.activation(
                        pt_sb[:, bk * 512:(bk + 1) * 512], pA[bk][:], AF.Exp,
                        bias=cc_sb[:, 0:1], scale=sm_scale),
                        after=[t_s, tc5, pt_readers[bk]])
                    exp_prev[bk] = t_e
                    t_m = t_e
                    if mid is not None:
                        t_m = p.comp("vector", lambda e, bk=bk, mid=mid:
                                     e.tensor_mul(
                                         pt_sb[:, bk * 512:(bk + 1) * 512],
                                         pt_sb[:, bk * 512:(bk + 1) * 512],
                                         masks_sb[:, mid * 512:(mid + 1) * 512]),
                                     after=[t_e, t_const])
                    t_pv = p.comp("tensor", lambda e, k0=k0, bk=bk, i=i, n_t=n_t:
                                  e.matmul(pA[2][:], V_sb[:, k0:k0 + 128],
                                           pt_sb[:, bk * 512:(bk + 1) * 512],
                                           start=(i == 0), stop=(i == n_t - 1)),
                                  after=[t_m, prev_cf] if i == 0 else [t_m])
                    pv_prev = t_pv
                    den_prev = p.comp("tensor", lambda e, bk=bk, i=i, n_t=n_t:
                                      e.matmul(pA[3][0:1, :], onesb_sb[:, 0:1],
                                               pt_sb[:, bk * 512:(bk + 1) * 512],
                                               start=(i == 0), stop=(i == n_t - 1)),
                                      after=[t_m, t_onesb, prev_d1] if i == 0
                                      else [])
                    pt_readers[bk] = den_prev
                t_d1 = p.comp("scalar", lambda e: e.activation(
                    den_sb[:], pA[3][0:1, :], AF.Copy), after=[den_prev])
                prev_d1 = t_d1
                t_d2 = p.comp("vector", lambda e: e.reciprocal(
                    den_sb[:], den_sb[:]), after=[t_d1])
                t_d3 = p.comp("tensor", lambda e: e.matmul(
                    pA[4][:], onesf_sb[0:1, :], den_sb[:], start=True,
                    stop=True), after=[t_d2, prev_cx])
                prev_d3 = t_d3
                t_cf = p.comp("scalar", lambda e: e.activation(
                    ctxf_sb[:], pA[2][:], AF.Copy), after=[pv_prev, t_d1,
                                                           prev_cx])
                prev_cf = t_cf
                t_cx = p.comp("vector", lambda e, j=j: e.tensor_mul(
                    ctxh_sb[:, j * 512:(j + 1) * 512], ctxf_sb[:], pA[4][:]),
                    after=[t_cf, t_d3, prev_ctx_st])
                prev_cx = t_cx
                ctx_head_toks.append(t_cx)
            st_toks = []
            wslot = [None, None]
            wcv = [None, None]
            pe_prev = [None, None]
            for ot in range(NDT):
                sl = ot % 2
                t_wf = p.dma("sync", lambda e, ot=ot, sl=sl: e.dma_start(
                    out=wotf_sb[:, sl * 512:(sl + 1) * 512].rearrange(
                        "p (j o) -> p j o", o=128),
                    in_=wo_d[:, ot * 128:(ot + 1) * 128].rearrange(
                        "(j p) o -> p j o", p=128)),
                    after=[wcv[sl] if True else None])
                t_wl = p.comp("vector", lambda e, sl=sl: e.tensor_copy(
                    wot_sb[:, sl * 512:(sl + 1) * 512],
                    wotf_sb[:, sl * 512:(sl + 1) * 512]),
                    after=[t_wf, wslot[sl]])
                wcv[sl] = t_wl
                mo = None
                for j in range(4):
                    mo = p.comp("tensor", lambda e, sl=sl, j=j: e.matmul(
                        pb[:, sl * 512:(sl + 1) * 512],
                        wot_sb[:, sl * 512 + j * 128:sl * 512 + (j + 1) * 128],
                        ctxh_sb[:, j * 512:(j + 1) * 512],
                        start=(j == 0), stop=(j == 3)),
                        after=[t_wl, pe_prev[sl]] + ctx_head_toks
                        if j == 0 else [])
                wslot[sl] = mo
                t_pe = p.comp("scalar", lambda e, sl=sl: e.activation(
                    po_sb[:, sl * 512:(sl + 1) * 512],
                    pb[:, sl * 512:(sl + 1) * 512], AF.Copy),
                    after=[mo, st_toks[-2] if len(st_toks) >= 2 else None])
                pe_prev[sl] = t_pe
                t_st = p.dma("scalar", lambda e, qw=qw, ot=ot, sl=sl: e.dma_start(
                    out=rsi_d[(qw // 2) * D + ot * 128:
                              (qw // 2) * D + (ot + 1) * 128,
                              (qw % 2) * 512:(qw % 2 + 1) * 512],
                    in_=po_sb[:, sl * 512:(sl + 1) * 512]), after=[t_pe])
                st_toks.append(t_st)
            prev_ctx_st = st_toks[-1]
            xw_free = prev_ctx_st

        t_rs = p.add("gpsimd", lambda e: e.collective_compute(
            "ReduceScatter", ALU.add, rg, [rsi_d], [rso_d]),
            after=[prev_ctx_st], inc="ag", k=1)

        res_toks = []
        for half in range(2):
            for ot in range(NDT):
                sl = ot % 2
                t_x = p.dma("sync", lambda e, ot=ot, half=half, sl=sl:
                            e.dma_start(
                                out=xr_sb[:, sl * 512:(sl + 1) * 512],
                                in_=x_d[ot * 128:(ot + 1) * 128,
                                        half * 512:(half + 1) * 512]),
                            after=[res_toks[-2] if len(res_toks) >= 2 else None])
                t_p = p.dma("sync", lambda e, ot=ot, half=half, sl=sl:
                            e.dma_start(
                                out=po_sb[:, sl * 512:(sl + 1) * 512],
                                in_=rso_d[ot * 128:(ot + 1) * 128,
                                          half * 512:(half + 1) * 512]),
                            after=[t_rs,
                                   res_toks[-2] if len(res_toks) >= 2 else None])
                if layer == 0:
                    t_r = p.comp("vector", lambda e, ot=ot, sl=sl:
                                 e.scalar_tensor_tensor(
                                     xo_sb[:, sl * 512:(sl + 1) * 512],
                                     po_sb[:, sl * 512:(sl + 1) * 512],
                                     bo_sb[:, ot:ot + 1],
                                     xr_sb[:, sl * 512:(sl + 1) * 512],
                                     ALU.add, ALU.add), after=[t_x, t_p, t_bo])
                else:
                    t_r = p.comp("vector", lambda e, sl=sl: e.tensor_add(
                        xo_sb[:, sl * 512:(sl + 1) * 512],
                        po_sb[:, sl * 512:(sl + 1) * 512],
                        xr_sb[:, sl * 512:(sl + 1) * 512]), after=[t_x, t_p])
                t_o = p.dma("scalar", lambda e, ot=ot, half=half, sl=sl:
                            e.dma_start(
                                out=x1_d[ot * 128:(ot + 1) * 128,
                                         half * 512:(half + 1) * 512],
                                in_=xo_sb[:, sl * 512:(sl + 1) * 512]),
                            after=[t_r])
                res_toks.append(t_o)

    return res_toks[-1]




def prep_attn_maps(maps, inputs, layer, pre):
    if layer == 0:
        wqkvT = np.ascontiguousarray(np.asarray(inputs["L0_attn_w_qkv"]).T)
        bqkv = np.asarray(inputs["L0_attn_b_qkv"])
        wdT = np.asarray(inputs["L0_attn_w_dense"]).T
        boT = np.ascontiguousarray(
            np.asarray(inputs["L0_attn_b_dense"]).reshape(32, 128).T)
        cos, sinS = rope_tables()
    else:
        wq = np.asarray(inputs["L1_attn_wq"])
        wk = np.asarray(inputs["L1_attn_wk"])
        wv = np.asarray(inputs["L1_attn_wv"])
        wo1 = np.asarray(inputs["L1_attn_wo"])
        wkv = np.ascontiguousarray(np.concatenate([wk, wv], axis=1))
    _plan, masks = build_tile_plan(layer)
    for c in range(NCORES):
        m = maps[c]
        wsl = wdT if layer == 0 else wo1
        m[pre + "wo"] = np.ascontiguousarray(wsl[512 * c:512 * (c + 1), :])
        m[pre + "masks"] = masks
        if layer == 0:
            m[pre + "wq"] = np.ascontiguousarray(wqkvT[:, 768 * c:768 * c + 512])
            m[pre + "wkv"] = np.ascontiguousarray(
                wqkvT[:, 768 * c + 512:768 * (c + 1)])
            m[pre + "bq"] = np.ascontiguousarray(
                bqkv[768 * c:768 * (c + 1)].reshape(6, 128).T)
            m[pre + "bo"] = boT
            m["cosT"] = cos
            m["sinSW"] = np.ascontiguousarray(np.roll(sinS, 64, axis=0))
        else:
            m[pre + "wq"] = np.ascontiguousarray(wq[:, 512 * c:512 * (c + 1)])
            m[pre + "wkv"] = wkv




def emit_ln(p, nc, x_d, xt_sb, sq_sb, stat_sb, bc_sb, onesf_sb, pm, pb,
            t_ones, write_fn):
    TPCl = TPC
    mm_tok = None
    slot_free = [None, None]
    for ic in range(NDT):
        sl = ic % 2
        t_ld = p.dma("sync", lambda e, ic=ic, sl=sl: e.dma_start(
            out=xt_sb[:, sl * TPCl:(sl + 1) * TPCl],
            in_=x_d[ic * 128:(ic + 1) * 128, :]), after=[slot_free[sl]])
        t_sq = p.comp("scalar", lambda e, sl=sl: e.activation(
            sq_sb[:, 0:TPCl], xt_sb[:, sl * TPCl:(sl + 1) * TPCl], AF.Square),
            after=[t_ld, mm_tok])
        for h in range(2):
            p.comp("tensor", lambda e, sl=sl, h=h, ic=ic: e.matmul(
                pm[h][0:1, :], onesf_sb[:, 0:1],
                xt_sb[:, sl * TPCl + h * 512:sl * TPCl + (h + 1) * 512],
                start=(ic == 0), stop=(ic == NDT - 1)),
                after=[t_ld, t_ones])
            mm_tok = p.comp("tensor", lambda e, h=h, ic=ic: e.matmul(
                pm[2 + h][0:1, :], onesf_sb[:, 0:1],
                sq_sb[:, h * 512:(h + 1) * 512],
                start=(ic == 0), stop=(ic == NDT - 1)),
                after=[t_sq])
        slot_free[sl] = mm_tok
    m_ = stat_sb[0:1, 0:TPCl]
    msq = stat_sb[0:1, TPCl:2 * TPCl]
    m2 = stat_sb[0:1, 2 * TPCl:3 * TPCl]
    var = stat_sb[0:1, 3 * TPCl:4 * TPCl]
    r_ = stat_sb[0:1, 3 * TPCl:4 * TPCl]
    mr = stat_sb[0:1, 2 * TPCl:3 * TPCl]
    t1 = p.comp("scalar", lambda e: e.activation(
        m_[:, 0:512], pm[0][0:1, :], AF.Copy, scale=1.0 / D), after=[mm_tok])
    t1 = p.comp("scalar", lambda e: e.activation(
        m_[:, 512:1024], pm[1][0:1, :], AF.Copy, scale=1.0 / D), after=[t1])
    t2 = p.comp("scalar", lambda e: e.activation(
        msq[:, 0:512], pm[2][0:1, :], AF.Copy, scale=1.0 / D), after=[t1])
    t2 = p.comp("scalar", lambda e: e.activation(
        msq[:, 512:1024], pm[3][0:1, :], AF.Copy, scale=1.0 / D), after=[t2])
    t3 = p.comp("scalar", lambda e: e.activation(m2[:], m_[:], AF.Square),
                after=[t2])
    t4 = p.comp("vector", lambda e: e.tensor_sub(var[:], msq[:], m2[:]),
                after=[t3])
    t4b = p.comp("vector", lambda e: e.tensor_scalar_add(var[:], var[:], 1e-5),
                 after=[t4])
    t5 = p.comp("scalar", lambda e: e.activation(var[:], var[:], AF.Sqrt),
                after=[t4b])
    t6 = p.comp("vector", lambda e: e.reciprocal(r_[:], var[:]), after=[t5])
    t7 = p.comp("vector", lambda e: e.tensor_mul(mr[:], m_[:], r_[:]),
                after=[t6])
    bts = [p.comp("tensor", lambda e, h=h: e.matmul(
        pb[:, h * 512:(h + 1) * 512], onesf_sb[0:1, :],
        r_[:, h * 512:(h + 1) * 512], start=True, stop=True),
        after=[t6]) for h in range(2)]
    t8 = p.comp("scalar", lambda e: e.activation(bc_sb[:, 0:TPCl], pb[:],
                                                 AF.Copy), after=bts)
    bts2 = [p.comp("tensor", lambda e, h=h: e.matmul(
        pb[:, h * 512:(h + 1) * 512], onesf_sb[0:1, :],
        mr[:, h * 512:(h + 1) * 512], start=True, stop=True),
        after=[t7, t8]) for h in range(2)]
    t9 = p.comp("scalar", lambda e: e.activation(bc_sb[:, TPCl:2 * TPCl], pb[:],
                                                 AF.Copy), after=bts2)
    rb = bc_sb[:, 0:TPCl]
    mrb = bc_sb[:, TPCl:2 * TPCl]
    toks = []
    slot_free = [None, None]
    for ic in range(NDT):
        sl = ic % 2
        t_ld = p.dma("sync", lambda e, ic=ic, sl=sl: e.dma_start(
            out=xt_sb[:, sl * TPCl:(sl + 1) * TPCl],
            in_=x_d[ic * 128:(ic + 1) * 128, :]), after=[t8, slot_free[sl]])
        tm = p.comp("vector", lambda e, sl=sl: e.tensor_mul(
            xt_sb[:, sl * TPCl:(sl + 1) * TPCl],
            xt_sb[:, sl * TPCl:(sl + 1) * TPCl], rb), after=[t_ld, t8])
        tok = write_fn(ic, sl, tm, t9, mrb)
        toks.append(tok)
        slot_free[sl] = tok
    return toks, bts2[-1]




def emit_l1_ffn(p, io, pre):
    nc = p.nc
    x_d = io["x"]
    bw_d = io["bw"]
    bb_d = io["bb"]
    at_d = io["at"]
    bs_d = io["bs"]
    rw_d = io["rw"]
    eyeb_d = io["eyeb"]
    x1_d = io["xout"]

    ctx = contextlib.ExitStack()
    with ctx:
        sb = lambda name, shape, dt: ctx.enter_context(
            nc.sbuf_tensor(pre + name, shape, dt))
        xln_sb = sb("xln_sb", [128, NDT * TPC], BF16)
        bwb_sb = sb("bwb_sb", [128, 2 * NDT * 128], BF16)
        bsb_sb = sb("bsb_sb", [128, 2 * 8 * 128], BF16)
        atb_sb = sb("atb_sb", [128, NDT * 64], BF16)
        rwb_sb = sb("rwb_sb", [128, NDT * 16], BF16)
        eyeb_sb = sb("eyeb_sb", [128, 128], BF16)
        onesf_sb = sb("onesf_sb", [128, 128], F32)
        xt_sb = sb("xt_sb", [128, 2 * TPC], F32)
        sq_sb = sb("sq_sb", [128, TPC], F32)
        stat_sb = sb("stat_sb", [1, 4 * TPC], F32)
        bc_sb = sb("bc_sb", [128, 2 * TPC], F32)
        bb_sb = sb("bb_sb", [128, 32], F32)
        pr_sb = sb("pr_sb", [128, 16], F32)
        sc_sb = sb("sc_sb", [128, 8], F32)
        comba_sb = sb("comba_sb", [128, 8 * 16], BF16)
        rowE_sb = sb("rowE_sb", [1, TPC], F32)
        rhT_sb = sb("rhT_sb", [64, TPC], F32)
        RH_sb = sb("RH_sb", [128, 8 * TPC], BF16)
        xr_sb = sb("xr_sb", [128, 2 * 512], F32)
        xo_sb = sb("xo_sb", [128, 2 * 512], F32)

        psb = lambda name, shape, dt: ctx.enter_context(
            nc.psum_tensor(pre + name, shape, dt))
        pb = psb("pb", [128, TPC], F32)
        pA = [psb(f"pA{i}", [128, 512], F32) for i in range(4)]
        ptr = psb("ptr", [128, 512], BF16)

        t_eye = p.dma("sync", lambda e: e.dma_start(out=eyeb_sb[:], in_=eyeb_d))
        t_bb = p.dma("sync", lambda e: e.dma_start(out=bb_sb[:], in_=bb_d))
        t_ones = p.comp("vector", lambda e: e.memset(onesf_sb[:], 1.0))
        t_at = p.dma("gpsimd", lambda e: e.dma_start(
            out=atb_sb[:].rearrange("p (a o) -> p a o", o=64),
            in_=at_d.rearrange("(a p) o -> p a o", p=128)))
        t_rw = p.dma("gpsimd", lambda e: e.dma_start(
            out=rwb_sb[:].rearrange("p (a o) -> p a o", o=16),
            in_=rw_d.rearrange("(a p) o -> p a o", p=128)))

        def xw(ic, sl, tm, t9, mrb):
            return p.comp("vector", lambda e, ic=ic, sl=sl: e.tensor_sub(
                xln_sb[:, ic * TPC:(ic + 1) * TPC],
                xt_sb[:, sl * TPC:(sl + 1) * TPC], mrb), after=[tm, t9])

        xln_toks, t_lnmm = emit_ln(p, nc, x_d, xt_sb, sq_sb, stat_sb, bc_sb,
                                   onesf_sb, pA, pb, t_ones, xw)
        xln_done = xln_toks[-1]

        def xln(ic):
            return xln_sb[:, ic * TPC:(ic + 1) * TPC]

        mrh = None
        for h in range(2):
            for ic in range(NDT):
                mrh = p.comp("tensor", lambda e, ic=ic, h=h: e.matmul(
                    pb[0:64, h * 512:(h + 1) * 512],
                    atb_sb[:, ic * 64:(ic + 1) * 64],
                    xln(ic)[:, h * 512:(h + 1) * 512],
                    start=(ic == 0), stop=(ic == NDT - 1)),
                    after=[t_at, xln_done, t_lnmm] if (ic == 0 and h == 0)
                    else [])
        t_rh = p.comp("scalar", lambda e: e.activation(rhT_sb[:], pb[0:64, :],
                                                       AF.Copy), after=[mrh])

        tr_toks = []
        for tt in range(8):
            mro = None
            for ic in range(NDT):
                mro = p.comp("tensor", lambda e, ic=ic, tt=tt: e.matmul(
                    pA[0][:, 0:16],
                    xln_sb[:, ic * TPC + tt * 128:ic * TPC + (tt + 1) * 128],
                    rwb_sb[:, ic * 16:(ic + 1) * 16],
                    start=(ic == 0), stop=(ic == NDT - 1)),
                    after=[t_rw, xln_done, t_rh,
                           tr_toks[-1] if tr_toks else None]
                    if ic == 0 else [])
            c0 = p.comp("scalar", lambda e: e.activation(
                pr_sb[:], pA[0][:, 0:16], AF.Copy),
                after=[mro, tr_toks[-1] if tr_toks else None])
            c1 = p.comp("vector", lambda e: e.tensor_reduce(
                sc_sb[:, 0:1], pr_sb[:], AX.X, ALU.max), after=[c0])
            c2 = p.comp("vector", lambda e: e.tensor_scalar_mul(
                sc_sb[:, 1:2], sc_sb[:, 0:1], -1.0), after=[c1])
            c3 = p.comp("scalar", lambda e: e.activation(
                pr_sb[:], pr_sb[:], AF.Exp, bias=sc_sb[:, 1:2]), after=[c2])
            c4 = p.comp("vector", lambda e: e.tensor_reduce(
                sc_sb[:, 2:3], pr_sb[:], AX.X, ALU.add), after=[c3])
            c5 = p.comp("vector", lambda e: e.reciprocal(
                sc_sb[:, 3:4], sc_sb[:, 2:3]), after=[c4])
            c6 = p.comp("vector", lambda e: e.tensor_scalar_mul(
                pr_sb[:], pr_sb[:], sc_sb[:, 3:4]), after=[c5])
            c7 = p.comp("vector", lambda e: e.tensor_reduce(
                sc_sb[:, 0:1], pr_sb[:], AX.X, ALU.max), after=[c6])
            c8 = p.comp("vector", lambda e: e.tensor_scalar(
                sq_sb[:, 0:16], pr_sb[:], sc_sb[:, 0:1], None, ALU.is_equal),
                after=[c7])
            c9 = p.comp("vector", lambda e: e.scalar_tensor_tensor(
                sq_sb[:, 16:32], sq_sb[:, 0:16], -2.0, pr_sb[:],
                ALU.mult, ALU.add), after=[c8])
            c10 = p.comp("vector", lambda e: e.tensor_reduce(
                sc_sb[:, 1:2], sq_sb[:, 16:32], AX.X, ALU.max), after=[c9])
            c11 = p.comp("vector", lambda e: e.tensor_scalar(
                sq_sb[:, 32:48], pr_sb[:], sc_sb[:, 1:2], None, ALU.is_equal),
                after=[c10])
            c12 = p.comp("vector", lambda e: e.tensor_add(
                sq_sb[:, 0:16], sq_sb[:, 0:16], sq_sb[:, 32:48]), after=[c11])
            c13 = p.comp("vector", lambda e: e.tensor_add(
                sc_sb[:, 4:5], sc_sb[:, 0:1], sc_sb[:, 1:2]), after=[c10])
            c14 = p.comp("vector", lambda e: e.tensor_scalar_add(
                sc_sb[:, 4:5], sc_sb[:, 4:5], 1e-6), after=[c13])
            c15 = p.comp("vector", lambda e: e.reciprocal(
                sc_sb[:, 5:6], sc_sb[:, 4:5]), after=[c14])
            c16 = p.comp("vector", lambda e: e.tensor_scalar_mul(
                sc_sb[:, 5:6], sc_sb[:, 5:6], 0.5), after=[c15])
            c17 = p.comp("vector", lambda e: e.tensor_mul(
                sq_sb[:, 0:16], sq_sb[:, 0:16], pr_sb[:]), after=[c12, c6])
            c18 = p.comp("vector", lambda e, tt=tt: e.tensor_scalar_mul(
                comba_sb[:, tt * 16:(tt + 1) * 16], sq_sb[:, 0:16],
                sc_sb[:, 5:6]), after=[c16, c17])
            tr_toks.append(c18)

        rh_toks = []
        prev_tr = None
        prev_row = None
        for e_ in range(16):
            row_toks = []
            for tt in range(8):
                ta = p.comp("tensor", lambda e, e_=e_, tt=tt: e.transpose(
                    ptr[0:1, 0:128],
                    comba_sb[:, tt * 16 + e_:tt * 16 + e_ + 1], eyeb_sb[:]),
                    after=[tr_toks[tt], t_eye, prev_tr,
                           rh_toks[-1] if rh_toks else None])
                prev_tr = p.comp("scalar", lambda e, tt=tt: e.activation(
                    rowE_sb[0:1, tt * 128:(tt + 1) * 128], ptr[0:1, 0:128],
                    AF.Copy), after=[ta, prev_row])
                row_toks.append(prev_tr)
            bc1 = p.comp("tensor", lambda e: e.matmul(
                pA[1][0:64, :], onesf_sb[0:1, 0:64], rowE_sb[0:1, 0:512],
                start=True, stop=True), after=row_toks + [prev_row])
            bc2 = p.comp("tensor", lambda e: e.matmul(
                pA[2][0:64, :], onesf_sb[0:1, 0:64], rowE_sb[0:1, 512:1024],
                start=True, stop=True), after=row_toks)
            cc = e_ // 2
            half = (e_ % 2) * 64
            m1 = p.comp("vector", lambda e, cc=cc, half=half: e.tensor_mul(
                RH_sb[half:half + 64, cc * TPC:cc * TPC + 512],
                pA[1][0:64, :], rhT_sb[0:64, 0:512]), after=[bc1, t_rh])
            m2 = p.comp("vector", lambda e, cc=cc, half=half: e.tensor_mul(
                RH_sb[half:half + 64, cc * TPC + 512:(cc + 1) * TPC],
                pA[2][0:64, :], rhT_sb[0:64, 512:1024]), after=[bc2, t_rh])
            prev_row = m2
            rh_toks += [m1, m2]

        res_toks = []
        wslot = [None, None]
        bslot = [None, None]
        for ot in range(NDT):
            sl = ot % 2
            t_wl = p.dma("gpsimd", lambda e, ot=ot, sl=sl: e.dma_start(
                out=bwb_sb[:, sl * NDT * 128:(sl + 1) * NDT * 128].rearrange(
                    "p (a o) -> p a o", o=128),
                in_=bw_d[:, ot * 128:(ot + 1) * 128].rearrange(
                    "(a p) o -> p a o", p=128)), after=[wslot[sl]])
            t_bs = p.dma("gpsimd", lambda e, ot=ot, sl=sl: e.dma_start(
                out=bsb_sb[:, sl * 8 * 128:(sl + 1) * 8 * 128].rearrange(
                    "p (a o) -> p a o", o=128),
                in_=bs_d[:, ot * 128:(ot + 1) * 128].rearrange(
                    "(a p) o -> p a o", p=128)), after=[bslot[sl]])
            tok = None
            for h in range(2):
                for ic in range(NDT):
                    tok = p.comp("tensor", lambda e, sl=sl, ic=ic, h=h: e.matmul(
                        pb[:, h * 512:(h + 1) * 512],
                        bwb_sb[:, sl * NDT * 128 + ic * 128:
                               sl * NDT * 128 + (ic + 1) * 128],
                        xln(ic)[:, h * 512:(h + 1) * 512],
                        start=(ic == 0), stop=False),
                        after=[t_wl, xln_done,
                               res_toks[-2] if len(res_toks) >= 2 else None,
                               rh_toks[-1]]
                        if (ic == 0 and h == 0) else [])
                for cc in range(8):
                    tok = p.comp("tensor", lambda e, sl=sl, cc=cc, h=h: e.matmul(
                        pb[:, h * 512:(h + 1) * 512],
                        bsb_sb[:, sl * 8 * 128 + cc * 128:
                               sl * 8 * 128 + (cc + 1) * 128],
                        RH_sb[:, cc * TPC + h * 512:cc * TPC + (h + 1) * 512],
                        start=False, stop=(cc == 7)),
                        after=[t_bs, rh_toks[-1]] if (cc == 0 and h == 0)
                        else [])
            wslot[sl] = tok
            bslot[sl] = tok
            t_r_prev = None
            for half in range(2):
                t_x = p.dma("sync", lambda e, ot=ot, half=half, sl=sl:
                            e.dma_start(
                                out=xr_sb[:, sl * 512:(sl + 1) * 512],
                                in_=x_d[ot * 128:(ot + 1) * 128,
                                        half * 512:(half + 1) * 512]),
                            after=[t_r_prev,
                                   res_toks[-2] if len(res_toks) >= 2
                                   else None])
                t_r = p.comp("vector", lambda e, ot=ot, half=half, sl=sl:
                             e.scalar_tensor_tensor(
                                 xo_sb[:, sl * 512:(sl + 1) * 512],
                                 pb[:, half * 512:(half + 1) * 512],
                                 bb_sb[:, ot:ot + 1],
                                 xr_sb[:, sl * 512:(sl + 1) * 512],
                                 ALU.add, ALU.add),
                             after=[tok, t_x, t_bb,
                                    res_toks[-1] if res_toks else None])
                t_r_prev = t_r
                t_o = p.dma("gpsimd", lambda e, ot=ot, half=half, sl=sl:
                            e.dma_start(
                                out=x1_d[ot * 128:(ot + 1) * 128,
                                         half * 512:(half + 1) * 512],
                                in_=xo_sb[:, sl * 512:(sl + 1) * 512]),
                            after=[t_r])
                res_toks.append(t_o)
    return res_toks[-1]


def prep_l1_ffn_maps(maps, inputs, pre):
    bwT = np.ascontiguousarray(np.asarray(inputs["L1_ffn_base_w"]).T)
    bbT = np.ascontiguousarray(np.asarray(inputs["L1_ffn_base_b"]).reshape(32, 128).T)
    AT = np.ascontiguousarray(np.asarray(inputs["L1_ffn_A"]).T)
    BsT = np.ascontiguousarray(
        np.asarray(inputs["L1_ffn_B"]).transpose(0, 2, 1).reshape(1024, D))
    rwT = np.ascontiguousarray(np.asarray(inputs["L1_ffn_router_w"]).T)
    for c in range(NCORES):
        maps[c].update({pre + "bw": bwT, pre + "bb": bbT, pre + "at": AT,
                        pre + "bs": BsT, pre + "rw": rwT})



NCHUNK = 11
NREAL_IB = 86


NPAR = 3


def emit_l0_ffn(p, io, pre, n_chunks=NCHUNK):
    nc = p.nc
    x_d = io["x"]
    cb_d = io["cb"]
    iw_d = io["iw"]
    idm_d = io["idm"]
    x1_d = io["xout"]
    cbp_d = io["cbp"]
    wgst_d = io["wgst"]
    wust_d = io["wust"]
    wdst_d = io["wdst"]
    wgag_d = io["wgag"]
    wuag_d = io["wuag"]
    wdag_d = io["wdag"]

    ctx = contextlib.ExitStack()
    with ctx:
        sb = lambda name, shape, dt: ctx.enter_context(
            nc.sbuf_tensor(pre + name, shape, dt))
        xln_sb = sb("xln_sb", [128, NDT * TPC], BF16)
        ggb_sb = sb("ggb_sb", [128, 4096], BF16)
        gub_sb = sb("gub_sb", [128, 4096], BF16)
        dstf_sb = sb("dstf_sb", [128, 4, 32, 128], BF16)
        i16w_sb = sb("i16w_sb", [128, 4096], I16)
        i32w_sb = sb("i32w_sb", [128, 2048], I32)
        i32d_sb = sb("i32d_sb", [128, 512], I32)
        mk_sb = sb("mk_sb", [128, 3 * 512], F32)
        tmpf_sb = sb("tmpf_sb", [128, 32, 8], F32)
        wdT_sb = sb("wdT_sb", [128, 2 * 384], BF16)
        h_sb = sb("h_sb", [128, 8 * TPC], BF16)
        hg_sb = sb("hg_sb", [128, TPC], BF16)
        onesf_sb = sb("onesf_sb", [128, 128], F32)
        xt_sb = sb("xt_sb", [128, 2 * TPC], F32)
        lnctx = contextlib.ExitStack()
        sq_sb = lnctx.enter_context(nc.sbuf_tensor(pre + "sq_sb", [128, TPC], F32))
        stat_sb = lnctx.enter_context(
            nc.sbuf_tensor(pre + "stat_sb", [1, 4 * TPC], F32))
        bc_sb = lnctx.enter_context(
            nc.sbuf_tensor(pre + "bc_sb", [128, 2 * TPC], F32))
        lnctx.close()
        wgT_sb = sb("wgT_sb", [128, NDT * 256], BF16)
        wuT_sb = sb("wuT_sb", [128, NDT * 256], BF16)
        xr_sb = xt_sb[:, 0:TPC]
        xo_sb = xt_sb[:, TPC:2 * TPC]

        psb = lambda name, shape, dt: ctx.enter_context(
            nc.psum_tensor(pre + name, shape, dt))
        pb = psb("pb", [128, TPC], F32)
        pA = [psb(f"pA{i}", [128, 512], F32) for i in range(6)]

        rg = [list(range(NCORES))]

        t_ones = p.comp("vector", lambda e: e.memset(onesf_sb[:], 1.0))

        t_init = p.dma("sync", lambda e: e.dma_start(out=x1_d, in_=x_d))

        t_cbp = None
        for r0 in range(0, 32768, 4096):
            t_cbp = p.dma("gpsimd", lambda e, r0=r0: e.dma_start(
                out=cbp_d[r0:r0 + 4096, 0:16],
                in_=cb_d[2 * r0:2 * r0 + 8192, :].rearrange(
                    "(r t) e -> r (t e)", t=2)), after=[t_cbp])

        def xw(ic, sl, tm, t9, mrb):
            return p.comp("vector", lambda e, ic=ic, sl=sl: e.tensor_sub(
                xln_sb[:, ic * TPC:(ic + 1) * TPC],
                xt_sb[:, sl * TPC:(sl + 1) * TPC], mrb), after=[tm, t9])

        xln_toks, t_lnmm = emit_ln(p, nc, x_d, xt_sb, sq_sb, stat_sb, bc_sb,
                                   onesf_sb, pA, pb, t_ones, xw)
        xln_done = xln_toks[-1]

        def xln(ic):
            return xln_sb[:, ic * TPC:(ic + 1) * TPC]

        ag_toks = []
        xacc_tok = [t_init] * 64
        xacc_ld = [None] * 64
        compute_done = []

        NPC = 16
        regcell = {}

        def setup_reg(e):
            regcell["r"] = e.alloc_register(pre + "nidx")
            e.reg_mov(regcell["r"], 4096)

        p.add("gpsimd", setup_reg)
        t_i16_prev = None
        t_mb_prev = None
        sel_last = [None, None, None]
        stage_prev = {0: None, 1: None, 2: None}
        last_gq = [None] * 4
        sel_q = [None] * 4
        sel_prev = None
        for c in range(n_chunks):
            par = c % 3
            stage_toks = []
            for m in range(3):
                tgt_sb = gub_sb if m == 1 else ggb_sb
                t_i16 = None
                for hf in range(2):
                    t_lw = p.dma("scalar", lambda e, m=m, c=c, hf=hf:
                                 e.dma_start(
                                     out=i32w_sb[:],
                                     in_=iw_d[m, c, :,
                                              hf * 2048:(hf + 1) * 2048]),
                                 after=[t_i16_prev, t_i16])
                    t_sh = p.comp("vector", lambda e: e.tensor_scalar(
                        i32w_sb[:], i32w_sb[:], 1, None,
                        ALU.logical_shift_right), after=[t_lw])
                    t_i16 = p.comp("vector", lambda e, hf=hf: e.tensor_copy(
                        i16w_sb[:, hf * 2048:(hf + 1) * 2048], i32w_sb[:]),
                        after=[t_sh] + (list(last_gq) if hf == 0 else []))
                t_i16_prev = t_i16
                t_lm = p.dma("scalar", lambda e, m=m, c=c: e.dma_start(
                    out=i32d_sb[:], in_=idm_d[m, c]), after=[t_mb_prev])
                t_and = p.comp("vector", lambda e: e.tensor_scalar(
                    i32d_sb[:], i32d_sb[:], 1, None, ALU.bitwise_and),
                    after=[t_lm, t_mb_prev])
                t_mb = p.comp("vector", lambda e, m=m: e.tensor_copy(
                    mk_sb[:, m * 512:(m + 1) * 512], i32d_sb[:]),
                    after=[t_and, sel_last[m]])
                t_mb_prev = t_mb
                war = stage_prev[0 if m == 2 else (2 if m == 0 else 1)]
                for pi in range(NPC):
                    q = pi % 4
                    t_g = p.add("gpsimd", lambda e, pi=pi, q=q: e.dma_gather(
                        dstf_sb[:, q], cbp_d,
                        i16w_sb[:, pi * 256:(pi + 1) * 256],
                        4096, regcell["r"], 128, elem_step=128,
                        single_packet=False, queue_num=q),
                        after=([t_i16, t_cbp] if pi < 4 else []) + [sel_q[q]],
                        inc=f"gq{q}", k=16, serial=False)
                    last_gq[q] = t_g
                    s1 = p.comp("vector", lambda e, q=q: e.tensor_sub(
                        tmpf_sb[:], dstf_sb[:, q, :, 8:16],
                        dstf_sb[:, q, :, 0:8]), after=[t_g, sel_prev])
                    m8 = mk_sb[:, m * 512 + pi * 32:m * 512 + (pi + 1) * 32]
                    s2 = p.comp("vector", lambda e, m8=m8: e.tensor_mul(
                        tmpf_sb[:], tmpf_sb[:],
                        m8.rearrange("p (j o) -> p j o", o=1).broadcast_to(
                            [128, 32, 8])), after=[s1, t_mb])
                    sel_prev = p.comp(
                        "vector", lambda e, pi=pi, q=q, tgt_sb=tgt_sb:
                        e.tensor_add(
                            tgt_sb[:, pi * 256:(pi + 1) * 256]
                            .rearrange("p (j e) -> p j e", e=8),
                            tmpf_sb[:], dstf_sb[:, q, :, 0:8]),
                        after=[s2, war if pi == 0 else None])
                    sel_q[q] = sel_prev
                sel_last[m] = sel_prev
                agg = ag_toks[-3] if len(ag_toks) >= 3 else None
                if m == 0:
                    st = p.dma("gpsimd", lambda e, par=par: e.dma_start(
                        out=wgst_d[par], in_=ggb_sb[:]),
                        after=[sel_prev, agg])
                elif m == 1:
                    st = p.dma("gpsimd", lambda e, par=par: e.dma_start(
                        out=wust_d[par], in_=gub_sb[:]),
                        after=[sel_prev, agg])
                else:
                    st = p.dma("gpsimd", lambda e, par=par: e.dma_start(
                        out=wdst_d[par].rearrange("(s p) g -> p s g", p=128),
                        in_=ggb_sb[:].rearrange("p (s g) -> p s g", g=1024)),
                        after=[sel_prev, agg])
                stage_prev[m] = st
                stage_toks.append(st)
            t_sg, t_su, t_sd = stage_toks
            cgate = compute_done[c - 3] if c >= 3 else None
            a1 = p.add("gpsimd", lambda e, par=par: e.collective_compute(
                "AllGather", ALU.bypass, rg, [wgst_d[par]], [wgag_d[par]]),
                after=[t_sg, cgate], inc="ag", k=1)
            a2 = p.add("gpsimd", lambda e, par=par: e.collective_compute(
                "AllGather", ALU.bypass, rg, [wust_d[par]], [wuag_d[par]]),
                after=[t_su], inc="ag", k=1)
            a3 = p.add("gpsimd", lambda e, par=par: e.collective_compute(
                "AllGather", ALU.bypass, rg, [wdst_d[par]], [wdag_d[par]]),
                after=[t_sd], inc="ag", k=1)
            ag_toks.append(a3)

            n_ib = min(8, NREAL_IB - c * 8)
            h_toks = []
            prev_pair_mm = compute_done[-1] if compute_done else None
            up_last = None
            for pp in range((n_ib + 1) // 2):
                tw = []
                for ic in range(NDT):
                    tw.append(p.dma("sync", lambda e, c=c, par=par, pp=pp,
                                    ic=ic: e.dma_start(
                                        out=wgT_sb[:, ic * 256:(ic + 1) * 256],
                                        in_=wgag_d[par][pp * 256:(pp + 1) * 256,
                                                        ic * 128:(ic + 1) * 128],
                                        transpose=True),
                                    after=[a1, prev_pair_mm, xln_done,
                                           t_lnmm] if ic == 0
                                    else []))
                    tw.append(p.dma("sync", lambda e, c=c, par=par, pp=pp,
                                    ic=ic: e.dma_start(
                                        out=wuT_sb[:, ic * 256:(ic + 1) * 256],
                                        in_=wuag_d[par][pp * 256:(pp + 1) * 256,
                                                        ic * 128:(ic + 1) * 128],
                                        transpose=True),
                                    after=[a2, xln_done, t_lnmm]
                                    if ic == 0 else []))
                for jj in range(2):
                    ib = pp * 2 + jj
                    if ib >= n_ib:
                        continue
                    mg = mu = None
                    for h in range(2):
                        for ic in range(NDT):
                            mg = p.comp("tensor", lambda e, ic=ic, jj=jj, h=h:
                                        e.matmul(
                                            pb[:, h * 512:(h + 1) * 512],
                                            wgT_sb[:, ic * 256 + jj * 128:
                                                   ic * 256 + (jj + 1) * 128],
                                            xln(ic)[:, h * 512:(h + 1) * 512],
                                            start=(ic == 0),
                                            stop=(ic == NDT - 1)),
                                        after=([tw[2 * ic], xln_done, t_lnmm,
                                                up_last]
                                               if (h == 0 and ic == 0) else
                                               [tw[2 * ic]] if jj == 0
                                               else []))
                    for h in range(2):
                        for ic in range(NDT):
                            mu = p.comp("tensor", lambda e, ic=ic, jj=jj, h=h:
                                        e.matmul(
                                            pA[h][:],
                                            wuT_sb[:, ic * 256 + jj * 128:
                                                   ic * 256 + (jj + 1) * 128],
                                            xln(ic)[:, h * 512:(h + 1) * 512],
                                            start=(ic == 0),
                                            stop=(ic == NDT - 1)),
                                        after=([tw[2 * ic + 1], mg, up_last]
                                               if (h == 0 and ic == 0) else
                                               [tw[2 * ic + 1]] if jj == 0
                                               else [mg] if ic == 0 else []))
                    ts1 = p.comp("scalar", lambda e: e.activation(
                        hg_sb[:, 0:512], pb[:, 0:512], AF.Silu), after=[mg, mu])
                    ts2 = p.comp("scalar", lambda e: e.activation(
                        hg_sb[:, 512:1024], pb[:, 512:1024], AF.Silu),
                        after=[ts1])
                    ts3 = p.comp("scalar", lambda e, ib=ib: e.activation(
                        h_sb[:, ib * TPC:ib * TPC + 512], pA[0][:], AF.Copy),
                        after=[ts2, h_toks[-1] if h_toks else None])
                    ts4 = p.comp("scalar", lambda e, ib=ib: e.activation(
                        h_sb[:, ib * TPC + 512:(ib + 1) * TPC], pA[1][:],
                        AF.Copy), after=[ts3])
                    tm = p.comp("vector", lambda e, ib=ib: e.tensor_mul(
                        h_sb[:, ib * TPC:(ib + 1) * TPC],
                        h_sb[:, ib * TPC:(ib + 1) * TPC], hg_sb[:]),
                        after=[ts4])
                    h_toks.append(tm)
                    up_last = tm
                prev_pair_mm = up_last
            dn_last = None
            ev_last = None
            for hg0 in range(0, NDT, 3):
                ntile = min(3, NDT - hg0)
                slot_mm = [None, None]
                twd = []
                for ib in range(n_ib):
                    t_wd = p.dma("sync", lambda e, c=c, par=par, hg0=hg0,
                                 ib=ib, ntile=ntile: e.dma_start(
                                     out=wdT_sb[:, (ib % 2) * 384:
                                                (ib % 2) * 384 + ntile * 128],
                                     in_=wdag_d[par][hg0 * 128:
                                                     hg0 * 128 + ntile * 128,
                                                     ib * 128:(ib + 1) * 128],
                                     transpose=True),
                                 after=[a3, slot_mm[ib % 2],
                                        h_toks[-1] if ib == 0 else None,
                                        ev_last if ib == 0 else None])
                    twd.append(t_wd)
                    for t in range(ntile):
                        for h in range(2):
                            dn_last = p.comp(
                                "tensor",
                                lambda e, ib=ib, t=t, h=h, n_ib=n_ib:
                                e.matmul(
                                    pA[t * 2 + h][:],
                                    wdT_sb[:, (ib % 2) * 384 + t * 128:
                                           (ib % 2) * 384 + (t + 1) * 128],
                                    h_sb[:, ib * TPC + h * 512:
                                         ib * TPC + (h + 1) * 512],
                                    start=(ib == 0), stop=(ib == n_ib - 1)),
                                after=[t_wd, h_toks[ib]] +
                                ([ev_last] if ib == 0 else [])
                                if (t == 0 and h == 0) else [])
                    slot_mm[ib % 2] = dn_last
                for t in range(ntile):
                    ot = hg0 + t
                    for h in range(2):
                        sl = (2 * t + h) % 2
                        ti = ot * 2 + h
                        t_x = p.dma("sync", lambda e, ot=ot, h=h, sl=sl:
                                    e.dma_start(
                                        out=xr_sb[:, sl * 512:(sl + 1) * 512],
                                        in_=x1_d[ot * 128:(ot + 1) * 128,
                                                 h * 512:(h + 1) * 512]),
                                    after=[xacc_tok[ti], dn_last,
                                           xacc_ld[ti]])
                        t_r = p.comp("vector", lambda e, t=t, h=h, sl=sl:
                                     e.tensor_add(
                                         xo_sb[:, sl * 512:(sl + 1) * 512],
                                         pA[t * 2 + h][:],
                                         xr_sb[:, sl * 512:(sl + 1) * 512]),
                                     after=[t_x, dn_last])
                        t_o = p.dma("sync", lambda e, ot=ot, h=h, sl=sl:
                                    e.dma_start(
                                        out=x1_d[ot * 128:(ot + 1) * 128,
                                                 h * 512:(h + 1) * 512],
                                        in_=xo_sb[:, sl * 512:(sl + 1) * 512]),
                                    after=[t_r])
                        xacc_tok[ti] = t_o
                        xacc_ld[ti] = t_r
                        ev_last = t_r
            compute_done.append(ev_last)

    return xacc_tok[-1]


def _wrap16(mat):
    lin = np.ascontiguousarray(mat.T).reshape(-1)
    w16 = np.ascontiguousarray(lin.reshape(-1, 16).T)
    return np.tile(w16, (8, 1))


def prep_l0_ffn_maps(maps, inputs, pre):
    cb = np.ascontiguousarray(np.asarray(inputs["L0_ffn_codebook"], np.float32))
    ig = np.asarray(inputs["L0_ffn_idx_gate"])
    iu = np.asarray(inputs["L0_ffn_idx_up"])
    idn = np.asarray(inputs["L0_ffn_idx_down"])
    for c in range(NCORES):
        iw = np.zeros((3, NCHUNK, 128, 4096), np.int32)
        idm = np.zeros((3, NCHUNK, 128, 512), np.int32)
        for ch in range(NCHUNK):
            tile = ch * 8 + c
            if tile < NREAL_IB:
                rows = slice(tile * 128, (tile + 1) * 128)
                idm[0, ch] = ig[rows]
                iw[0, ch] = _wrap16(ig[rows])
                idm[1, ch] = iu[rows]
                iw[1, ch] = _wrap16(iu[rows])
            gd0 = ch * 128
            gd1 = min(gd0 + 128, idn.shape[1])
            dtile = np.zeros((4, 128, 128), np.int32)
            for s_ in range(4):
                hr = slice(512 * c + s_ * 128, 512 * c + (s_ + 1) * 128)
                dtile[s_, :, 0:gd1 - gd0] = idn[hr, gd0:gd1]
            dmat = np.ascontiguousarray(
                dtile.transpose(1, 0, 2).reshape(128, 512))
            idm[2, ch] = dmat
            iw[2, ch] = _wrap16(dmat)
        maps[c].update({"cb": cb, pre + "iw": iw, pre + "idm": idm})




def build_mega():
    nc = bass.Bass("TRN2", target_bir_lowering=False, num_swdge_queues=4)

    def dt(name, shape, dtyp, kind):
        return nc.dram_tensor(name, shape, dtyp, kind=kind).ap()

    x_d = dt("x", [D, TPC], F32, "ExternalInput")
    eyeb_d = dt("eyeb", [128, 128], BF16, "ExternalInput")
    nm0 = build_tile_plan(0)[1].shape[0]
    nm1 = build_tile_plan(1)[1].shape[0]

    a0 = dict(
        x=x_d, eyeb=eyeb_d,
        wq=dt("a0_wq", [D, 512], F32, "ExternalInput"),
        wkv=dt("a0_wkv", [D, 256], F32, "ExternalInput"),
        wo=dt("a0_wo", [512, D], F32, "ExternalInput"),
        masks=dt("a0_masks", [nm0, 128, 512], BF16, "ExternalInput"),
        bq=dt("a0_bq", [128, 6], F32, "ExternalInput"),
        bo=dt("a0_bo", [128, 32], F32, "ExternalInput"),
        cos=dt("cosT", [128, S], F32, "ExternalInput"),
        sin=dt("sinSW", [128, S], F32, "ExternalInput"),
        xout=dt("xa", [D, TPC], F32, "Internal"),
        xstage=dt("a0_xstage", [D, TPC], BF16, "Internal"),
        xag=dt("a0_xag", [NCORES * D, TPC], BF16, "Internal"),
        rsi=dt("a0_rsi", [NCORES * D, TPC], F32, "Internal"),
        rso=dt("a0_rso", [D, TPC], F32, "Internal"),
    )
    f0 = dict(
        x=a0["xout"],
        cb=dt("cb", [65536, 8], F32, "ExternalInput"),
        iw=dt("f0_iw", [3, NCHUNK, 128, 4096], I32, "ExternalInput"),
        idm=dt("f0_idm", [3, NCHUNK, 128, 512], I32, "ExternalInput"),
        cbp=dt("f0_cbp", [32768, 128], BF16, "Internal"),
        xout=dt("xb", [D, TPC], F32, "Internal"),
        wgst=[dt(f"f0_wgst{i}", [128, 4096], BF16, "Internal")
              for i in range(NPAR)],
        wust=[dt(f"f0_wust{i}", [128, 4096], BF16, "Internal")
              for i in range(NPAR)],
        wdst=[dt(f"f0_wdst{i}", [512, 1024], BF16, "Internal")
              for i in range(NPAR)],
        wgag=[dt(f"f0_wgag{i}", [1024, 4096], BF16, "Internal")
              for i in range(NPAR)],
        wuag=[dt(f"f0_wuag{i}", [1024, 4096], BF16, "Internal")
              for i in range(NPAR)],
        wdag=[dt(f"f0_wdag{i}", [4096, 1024], BF16, "Internal")
              for i in range(NPAR)],
    )
    a1 = dict(
        x=f0["xout"], eyeb=eyeb_d,
        wq=dt("a1_wq", [D, 512], F32, "ExternalInput"),
        wkv=dt("a1_wkv", [D, 256], F32, "ExternalInput"),
        wo=dt("a1_wo", [512, D], F32, "ExternalInput"),
        masks=dt("a1_masks", [nm1, 128, 512], BF16, "ExternalInput"),
        xout=dt("xc", [D, TPC], F32, "Internal"),
        xstage=dt("a1_xstage", [D, TPC], BF16, "Internal"),
        xag=dt("a1_xag", [NCORES * D, TPC], BF16, "Internal"),
        rsi=dt("a1_rsi", [NCORES * D, TPC], F32, "Internal"),
        rso=dt("a1_rso", [D, TPC], F32, "Internal"),
        kvs=dt("a1_kvs", [256, TPC], BF16, "Internal"),
        kvag=dt("a1_kvag", [NCORES * 256, TPC], BF16, "Internal"),
    )
    f1 = dict(
        x=a1["xout"], eyeb=eyeb_d,
        bw=dt("f1_bw", [D, D], F32, "ExternalInput"),
        bb=dt("f1_bb", [128, 32], F32, "ExternalInput"),
        at=dt("f1_at", [D, 64], F32, "ExternalInput"),
        bs=dt("f1_bs", [1024, D], F32, "ExternalInput"),
        rw=dt("f1_rw", [D, 16], F32, "ExternalInput"),
        xout=dt("x1", [D, TPC], F32, "ExternalOutput"),
    )

    ctx = contextlib.ExitStack()
    with ctx:
        p = Prog(nc, ctx)
        blk = ctx.enter_context(nc.Block())
        p.add("gpsimd", lambda e: e.load_library(mlp))
        emit_attn(p, a0, 0, "a0")
        p.stage_gate()
        emit_l0_ffn(p, f0, "f0")
        p.stage_gate()
        emit_attn(p, a1, 1, "a1")
        p.stage_gate()
        last = emit_l1_ffn(p, f1, "f1")
        p.add("gpsimd", lambda e: e.engine_nop(), after=[last])
        p.emit(blk)
    mybir.codegen_inst_isa_subclasses(nc)
    return nc




def kernel(**inputs):
    x0 = np.asarray(inputs["L0_attn_hidden_states"], np.float32).reshape(S, D)
    eyeb = np.eye(128, dtype=np.float32).astype(BF)
    maps = [{"x": np.ascontiguousarray(x0[core_token_ids(c)].T), "eyeb": eyeb}
            for c in range(NCORES)]
    prep_attn_maps(maps, inputs, 0, "a0_")
    prep_l0_ffn_maps(maps, inputs, "f0_")
    prep_attn_maps(maps, inputs, 1, "a1_")
    prep_l1_ffn_maps(maps, inputs, "f1_")
    nc = build_mega()
    res = run_bass_kernel_spmd(nc, maps, core_ids=list(range(NCORES)))
    out = np.zeros((S, D), dtype=np.float32)
    for c in range(NCORES):
        out[core_token_ids(c)] = res.results[c]["x1"].T
    return out.reshape(1, S, D)


# version 07
# speedup vs baseline: 1.2738x; 1.0011x over previous
import contextlib
import numpy as np
import ml_dtypes
import concourse.bass as bass
import concourse.mybir as mybir
from concourse.bass_utils import run_bass_kernel_spmd
from concourse.library_config import mlp

F32 = mybir.dt.float32
BF16 = mybir.dt.bfloat16
I32 = mybir.dt.int32
I16 = mybir.dt.int16
AF = mybir.ActivationFunctionType
ALU = mybir.AluOpType
AX = mybir.AxisListType

NCORES = 8
S = 8192
D = 4096
TPC = 1024
NDT = D // 128
BF = ml_dtypes.bfloat16
NQW = S // 512


def core_token_ids(c):
    return np.arange(TPC * c, TPC * (c + 1))




def l0_tile_mask(q0, k0):
    k = np.arange(k0, k0 + 128)[:, None]
    q = np.arange(q0, q0 + 512)[None, :]
    qb = q // 64
    kb = k // 64
    act = (kb <= qb) & ((qb - kb < 16) | ((kb + 1) % 8 == 0))
    return act & (k <= q)


def l1_tile_mask(q0, k0):
    k = np.arange(k0, k0 + 128)[:, None]
    q = np.arange(q0, q0 + 512)[None, :]
    return k <= q


def build_tile_plan(layer):
    plan = []
    masks = {}
    order = []
    for qw in range(NQW):
        q0 = 512 * qw
        tiles = []
        for k0 in range(0, q0 + 512, 128):
            m = l0_tile_mask(q0, k0) if layer == 0 else l1_tile_mask(q0, k0)
            if not m.any():
                continue
            if m.all():
                tiles.append((k0, None))
            else:
                key = m.tobytes()
                if key not in masks:
                    masks[key] = len(order)
                    order.append(m)
                tiles.append((k0, masks[key]))
        plan.append(tiles)
    marr = np.stack(order).astype(BF) if order else np.zeros((1, 128, 512), BF)
    return plan, marr


def rope_tables():
    hd = 128
    inv = 1.0 / (1000000.0 ** (np.arange(0, hd, 2, dtype=np.float64) / hd))
    t = np.arange(S, dtype=np.float64)
    fr = np.outer(t, inv)
    emb = np.concatenate([fr, fr], axis=1)
    cos = np.cos(emb).T.astype(np.float32)
    sin = np.sin(emb).T.astype(np.float32)
    sinS = sin.copy()
    sinS[:64] *= -1.0
    return np.ascontiguousarray(cos), np.ascontiguousarray(sinS)




class Sem:
    def __init__(self, stack, nc, name):
        self.h = stack.enter_context(nc.semaphore(name))
        self.n = 0


class Prog:
    ENGS = ("sync", "tensor", "vector", "scalar", "gpsimd")

    def __init__(self, nc, stack):
        self.nc = nc
        self.stack = stack
        self.ops = {e: [] for e in self.ENGS}
        self.sems = {}
        self.gate = ()
        self.gated = set()

    def sem(self, name):
        if name not in self.sems:
            self.sems[name] = Sem(self.stack, self.nc, name)
        return self.sems[name]

    def stage_gate(self):
        self.gate = tuple((nm, s.n) for nm, s in self.sems.items() if s.n > 0)
        self.gated = set()

    def add(self, eng, fn, after=(), inc=None, k=1, serial=True):
        if self.gate and eng not in self.gated:
            self.gated.add(eng)
            after = tuple(t for t in after if t is not None) + self.gate
        s = self.sem(inc) if inc else None
        tok = None
        extra = ()
        if s is not None:
            if serial and s.n > 0:
                extra = ((inc, s.n),)
            s.n += k
            tok = (inc, s.n)
        after = tuple(t for t in after if t is not None) + extra
        self.ops[eng].append((after, fn, s, k))
        return tok

    def dma(self, eng, fn, after=()):
        return self.add(eng, fn, after, inc="dma_" + eng, k=16)

    def comp(self, eng, fn, after=()):
        return self.add(eng, fn, after, inc="c_" + eng, k=1)

    def barrier_tok(self, eng_sem):
        return (eng_sem, self.sem(eng_sem).n)

    def emit(self, blk):
        def make_body(name):
            def body(eng):
                wm = {}
                own = "c_" + name
                for (waits, fn, s, k) in self.ops[name]:
                    for (snm, val) in waits:
                        if name == "tensor" and snm == own:
                            continue
                        if wm.get(snm, 0) >= val:
                            continue
                        eng.wait_ge(self.sems[snm].h, val)
                        wm[snm] = val
                    ins = fn(eng)
                    if s is not None:
                        assert ins is not None, f"op on {name} returned None"
                        ins.then_inc(s.h, k)
            return body

        for name in self.ENGS:
            if self.ops[name]:
                getattr(blk, name)(make_body(name))




def emit_attn(p, io, layer, pre):
    plan, masks_np = build_tile_plan(layer)
    nm = masks_np.shape[0]
    sm_scale = (1.0 / 128.0) if layer == 0 else float(1.0 / np.sqrt(128.0))

    nc = p.nc
    x_d = io["x"]
    wq_d = io["wq"]
    wkv_d = io["wkv"]
    wo_d = io["wo"]
    masks_d = io["masks"]
    eyeb_d = io["eyeb"]
    x1_d = io["xout"]
    xstage_d = io["xstage"]
    xag_d = io["xag"]
    rsi_d = io["rsi"]
    rso_d = io["rso"]
    if layer == 0:
        bq_d = io["bq"]
        bo_d = io["bo"]
        cos_d = io["cos"]
        sin_d = io["sin"]
    else:
        kvstage_d = io["kvs"]
        kvag_d = io["kvag"]

    ctx = contextlib.ExitStack()
    with ctx:
        sb = lambda name, shape, dt: ctx.enter_context(
            nc.sbuf_tensor(pre + name, shape, dt))
        xw_sb = sb("xw_sb", [128, NDT * 512], BF16)
        K_sb = sb("K_sb", [128, S], BF16)
        V_sb = sb("V_sb", [128, S], BF16)
        wqb_sb = sb("wqb_sb", [128, NDT * 512], BF16)
        wkvb_sb = sb("wkvb_sb", [128, NDT * 256], BF16)
        masks_sb = sb("masks_sb", [128, nm * 512], BF16)
        eyeb_sb = sb("eyeb_sb", [128, 128], BF16)
        onesf_sb = sb("onesf_sb", [128, 128], F32)
        onesb_sb = sb("onesb_sb", [128, 4], BF16)
        xt_sb = sb("xt_sb", [128, 2 * TPC], F32)
        xb_sb = sb("xb_sb", [128, 2 * TPC], BF16)
        sq_sb = sb("sq_sb", [128, TPC], F32)
        stat_sb = sb("stat_sb", [1, 4 * TPC], F32)
        bc_sb = sb("bc_sb", [128, 2 * TPC], F32)
        if layer == 0:
            bq_sb = sb("bq_sb", [128, 6], F32)
            bo_sb = sb("bo_sb", [128, 32], F32)
            cosw_sb = sb("cosw_sb", [128, 512], F32)
            sinw_sb = sb("sinw_sb", [128, 512], F32)
        wot_sb = sb("wot_sb", [128, 2 * 512], BF16)
        wotf_sb = sb("wotf_sb", [128, 2 * 512], F32)
        po_sb = sb("po_sb", [128, 2 * 512], F32)
        ctxh_sb = sb("ctxh_sb", [128, 4 * 512], BF16)
        kvf_sb = sb("kvf_sb", [128, 2 * 512], F32)
        vb_sb = sb("vb_sb", [128, 512], BF16)
        q_sb = sb("q_sb", [128, 4 * 512], BF16)
        qf_sb = sb("qf_sb", [128, 512], F32)
        kn_sb = sb("kn_sb", [1, 64], F32)
        nrm_sb = sb("nrm_sb", [1, 16], F32)
        cc_sb = sb("cc_sb", [128, 1], F32)
        pt_sb = sb("pt_sb", [128, 2 * 512], BF16)
        den_sb = sb("den_sb", [1, 512], F32)
        ctxf_sb = sb("ctxf_sb", [128, 512], F32)
        xr_sb = sb("xr_sb", [128, 2 * 512], F32)
        xo_sb = sb("xo_sb", [128, 2 * 512], F32)

        psb = lambda name, shape, dt: ctx.enter_context(
            nc.psum_tensor(pre + name, shape, dt))
        pb = psb("pb", [128, TPC], F32)
        pA = [psb(f"pA{i}", [128, 512], F32) for i in range(5)]
        ptr = psb("ptr", [128, 512], BF16)

        rg = [list(range(NCORES))]

        for n in range(nm):
            p.dma("sync", lambda e, n=n: e.dma_start(
                out=masks_sb[:, n * 512:(n + 1) * 512], in_=masks_d[n]))
        t_eye = p.dma("sync", lambda e: e.dma_start(out=eyeb_sb[:], in_=eyeb_d))
        if layer == 0:
            p.dma("sync", lambda e: e.dma_start(out=bq_sb[:], in_=bq_d))
            t_bo = p.dma("sync", lambda e: e.dma_start(out=bo_sb[:], in_=bo_d))
        else:
            t_bo = None
        t_const = p.barrier_tok("dma_sync")
        t_ones = p.comp("vector", lambda e: e.memset(onesf_sb[:], 1.0))
        t_onesb = p.comp("vector", lambda e: e.memset(onesb_sb[:], 1.0))

        pm0, pm1, pv0, pv1 = pA[0], pA[1], pA[2], pA[3]
        mm_tok = None
        slot_free = [None, None]
        for ic in range(NDT):
            sl = ic % 2
            t_ld = p.dma("sync", lambda e, ic=ic, sl=sl: e.dma_start(
                out=xt_sb[:, sl * TPC:(sl + 1) * TPC],
                in_=x_d[ic * 128:(ic + 1) * 128, :]), after=[slot_free[sl]])
            t_sq = p.comp("scalar", lambda e, sl=sl: e.activation(
                sq_sb[:], xt_sb[:, sl * TPC:(sl + 1) * TPC], AF.Square),
                after=[t_ld, mm_tok])
            for h in range(2):
                p.comp("tensor", lambda e, sl=sl, h=h, ic=ic: e.matmul(
                    (pm0 if h == 0 else pm1)[0:1, :], onesf_sb[:, 0:1],
                    xt_sb[:, sl * TPC + h * 512:sl * TPC + (h + 1) * 512],
                    start=(ic == 0), stop=(ic == NDT - 1)),
                    after=[t_ld, t_ones])
                mm_tok = p.comp("tensor", lambda e, h=h, ic=ic: e.matmul(
                    (pv0 if h == 0 else pv1)[0:1, :], onesf_sb[:, 0:1],
                    sq_sb[:, h * 512:(h + 1) * 512],
                    start=(ic == 0), stop=(ic == NDT - 1)),
                    after=[t_sq])
            slot_free[sl] = mm_tok
        m_ = stat_sb[0:1, 0:TPC]
        msq = stat_sb[0:1, TPC:2 * TPC]
        m2 = stat_sb[0:1, 2 * TPC:3 * TPC]
        var = stat_sb[0:1, 3 * TPC:4 * TPC]
        r_ = stat_sb[0:1, 3 * TPC:4 * TPC]
        mr = stat_sb[0:1, 2 * TPC:3 * TPC]
        t1 = p.comp("scalar", lambda e: e.activation(
            m_[:, 0:512], pm0[0:1, :], AF.Copy, scale=1.0 / D), after=[mm_tok])
        t1 = p.comp("scalar", lambda e: e.activation(
            m_[:, 512:1024], pm1[0:1, :], AF.Copy, scale=1.0 / D), after=[t1])
        t2 = p.comp("scalar", lambda e: e.activation(
            msq[:, 0:512], pv0[0:1, :], AF.Copy, scale=1.0 / D), after=[t1])
        t2 = p.comp("scalar", lambda e: e.activation(
            msq[:, 512:1024], pv1[0:1, :], AF.Copy, scale=1.0 / D), after=[t2])
        t3 = p.comp("scalar", lambda e: e.activation(m2[:], m_[:], AF.Square),
                    after=[t2])
        t4 = p.comp("vector", lambda e: e.tensor_sub(var[:], msq[:], m2[:]),
                    after=[t3])
        t4b = p.comp("vector", lambda e: e.tensor_scalar_add(var[:], var[:],
                                                             1e-5), after=[t4])
        t5 = p.comp("scalar", lambda e: e.activation(var[:], var[:], AF.Sqrt),
                    after=[t4b])
        t6 = p.comp("vector", lambda e: e.reciprocal(r_[:], var[:]), after=[t5])
        t7 = p.comp("vector", lambda e: e.tensor_mul(mr[:], m_[:], r_[:]),
                    after=[t6])
        bts = [p.comp("tensor", lambda e, h=h: e.matmul(
            pb[:, h * 512:(h + 1) * 512], onesf_sb[0:1, :],
            r_[:, h * 512:(h + 1) * 512], start=True, stop=True),
            after=[t6]) for h in range(2)]
        t8 = p.comp("scalar", lambda e: e.activation(bc_sb[:, 0:TPC], pb[:],
                                                     AF.Copy), after=bts)
        bts2 = [p.comp("tensor", lambda e, h=h: e.matmul(
            pb[:, h * 512:(h + 1) * 512], onesf_sb[0:1, :],
            mr[:, h * 512:(h + 1) * 512], start=True, stop=True),
            after=[t7, t8]) for h in range(2)]
        t9 = p.comp("scalar", lambda e: e.activation(bc_sb[:, TPC:2 * TPC], pb[:],
                                                     AF.Copy), after=bts2)
        rb = bc_sb[:, 0:TPC]
        mrb = bc_sb[:, TPC:2 * TPC]
        xst_toks = []
        slot_free = [None, None]
        for ic in range(NDT):
            sl = ic % 2
            t_ld = p.dma("sync", lambda e, ic=ic, sl=sl: e.dma_start(
                out=xt_sb[:, sl * TPC:(sl + 1) * TPC],
                in_=x_d[ic * 128:(ic + 1) * 128, :]), after=[t8, slot_free[sl]])
            tm = p.comp("vector", lambda e, sl=sl: e.tensor_mul(
                xt_sb[:, sl * TPC:(sl + 1) * TPC],
                xt_sb[:, sl * TPC:(sl + 1) * TPC], rb), after=[t_ld, t8])
            tsb = p.comp("vector", lambda e, sl=sl: e.tensor_sub(
                xb_sb[:, sl * TPC:(sl + 1) * TPC],
                xt_sb[:, sl * TPC:(sl + 1) * TPC], mrb), after=[tm, t9])
            tst = p.dma("scalar", lambda e, ic=ic, sl=sl: e.dma_start(
                out=xstage_d[ic * 128:(ic + 1) * 128, :],
                in_=xb_sb[:, sl * TPC:(sl + 1) * TPC]), after=[tsb])
            xst_toks.append(tst)
            slot_free[sl] = tst
        t_agx = p.add("gpsimd", lambda e: e.collective_compute(
            "AllGather", ALU.bypass, rg, [xstage_d], [xag_d]),
            after=[xst_toks[-1]], inc="ag", k=1)
        t_ln_done = t9

        def stream_slab_to(w_dram, col0, ncols, dst_sb, dst_off, after=()):
            return p.dma("gpsimd", lambda e: e.dma_start(
                out=dst_sb[:, dst_off:dst_off + NDT * ncols].rearrange(
                    "p (a o) -> p a o", o=ncols),
                in_=w_dram[:, col0:col0 + ncols].rearrange(
                    "(a p) o -> p a o", p=128)), after=after)

        wq_toks = []
        prev = None
        for j in range(4):
            prev = stream_slab_to(wq_d, j * 128, 128, wqb_sb, j * NDT * 128,
                                  after=[prev])
            wq_toks.append(prev)
        t_wkv = stream_slab_to(wkv_d, 0, 256, wkvb_sb, 0, after=[prev])

        def wq_tile(j, ic):
            return wqb_sb[:, j * NDT * 128 + ic * 128:j * NDT * 128 + (ic + 1) * 128]

        def wkv_tile(which, ic):
            o = ic * 256 + which * 128
            return wkvb_sb[:, o:o + 128]

        kv_ready = None
        kn_done = {}
        xw_init_gate = None
        if layer == 1:
            own_done = []
            ldt_last = None
            for h in range(2):
                ldt = []
                for ic in range(NDT):
                    ldt.append(p.dma("sync", lambda e, ic=ic, h=h: e.dma_start(
                        out=xw_sb[:, ic * 512:(ic + 1) * 512],
                        in_=xstage_d[ic * 128:(ic + 1) * 128,
                                     h * 512:(h + 1) * 512]),
                        after=[xst_toks[ic],
                               own_done[-1] if own_done else None]))
                mk = mv = None
                for ic in range(NDT):
                    mk = p.comp("tensor", lambda e, ic=ic: e.matmul(
                        pA[0][:], wkv_tile(0, ic),
                        xw_sb[:, ic * 512:(ic + 1) * 512],
                        start=(ic == 0), stop=(ic == NDT - 1)),
                        after=[t_wkv, t_ln_done, ldt[ic]] if ic == 0
                        else [ldt[ic]])
                for ic in range(NDT):
                    mv = p.comp("tensor", lambda e, ic=ic: e.matmul(
                        pA[1][:], wkv_tile(1, ic),
                        xw_sb[:, ic * 512:(ic + 1) * 512],
                        start=(ic == 0), stop=(ic == NDT - 1)), after=[])
                tk = p.comp("scalar", lambda e, h=h: e.activation(
                    xb_sb[:, h * 512:(h + 1) * 512], pA[0][:], AF.Copy),
                    after=[mk, mv])
                tv = p.comp("scalar", lambda e, h=h: e.activation(
                    xb_sb[:, TPC + h * 512:TPC + (h + 1) * 512], pA[1][:],
                    AF.Copy), after=[tk])
                own_done.append(tv)
                ldt_last = ldt[-1]
            tsk = p.dma("scalar", lambda e: e.dma_start(
                out=kvstage_d[0:128, :], in_=xb_sb[:, 0:TPC]), after=own_done)
            tsv = p.dma("scalar", lambda e: e.dma_start(
                out=kvstage_d[128:256, :], in_=xb_sb[:, TPC:2 * TPC]),
                after=own_done)
            t_agkv = p.add("gpsimd", lambda e: e.collective_compute(
                "AllGather", ALU.bypass, rg, [kvstage_d], [kvag_d]),
                after=[tsk, tsv], inc="ag", k=1)
            kl = []
            for r in range(NCORES):
                kl.append(p.dma("sync", lambda e, r=r: e.dma_start(
                    out=K_sb[:, r * TPC:(r + 1) * TPC],
                    in_=kvag_d[r * 256:r * 256 + 128, :]), after=[t_agkv]))
                kl.append(p.dma("sync", lambda e, r=r: e.dma_start(
                    out=V_sb[:, r * TPC:(r + 1) * TPC],
                    in_=kvag_d[r * 256 + 128:r * 256 + 256, :]),
                    after=[t_agkv]))
            tprev = None
            for kt in range(64):
                ta = p.comp("tensor", lambda e, kt=kt: e.transpose(
                    ptr[:, 0:128], V_sb[:, kt * 128:(kt + 1) * 128], eyeb_sb[:]),
                    after=[kl[2 * (kt // 8) + 1], t_eye, tprev])
                tprev = p.comp("vector", lambda e, kt=kt: e.tensor_copy(
                    V_sb[:, kt * 128:(kt + 1) * 128], ptr[:, 0:128]), after=[ta])
            kv_ready = tprev
            knt = None
            for kt in range(64):
                tsq2 = p.comp("scalar", lambda e, kt=kt: e.activation(
                    sq_sb[:, 0:128], K_sb[:, kt * 128:(kt + 1) * 128],
                    AF.Square), after=[kl[2 * (kt // 8)], knt])
                tm1 = p.comp("tensor", lambda e: e.matmul(
                    pA[4][0:1, 0:128], onesf_sb[:, 0:1], sq_sb[:, 0:128],
                    start=True, stop=True), after=[tsq2])
                knt = p.comp("vector", lambda e, kt=kt: e.tensor_reduce(
                    kn_sb[0:1, kt:kt + 1], pA[4][0:1, 0:128], AX.X, ALU.max),
                    after=[tm1])
                kn_done[kt] = knt
            xw_init_gate = ldt_last
            xw_init_gate = tsv

        NH = 4
        kwin_done = {}
        if layer == 1:
            for w in range(NQW):
                kwin_done[w] = kv_ready
        xw_free = xw_init_gate
        prev_qnorm = None
        prev_qproj = None
        qevac_prev = [None, None]
        prev_cx0 = kn_done.get(63) if layer == 1 else None
        prev_ctx_st = None
        prev_cf = None
        prev_cx = kn_done.get(63) if layer == 1 else None
        prev_d1 = None
        prev_d3 = None
        for qw in range(NQW):
            r = qw // 2
            hh = qw % 2
            t_ldw = p.dma("sync", lambda e, r=r, hh=hh: e.dma_start(
                out=xw_sb[:].rearrange("p (a t) -> p a t", t=512),
                in_=xag_d[r * D:(r + 1) * D,
                          hh * 512:(hh + 1) * 512].rearrange(
                    "(a p) t -> p a t", p=128)),
                after=[t_agx, xw_free])
            ldw = [t_ldw] * NDT
            if layer == 0:
                t_cs = p.dma("sync", lambda e, qw=qw: e.dma_start(
                    out=cosw_sb[:], in_=cos_d[:, qw * 512:(qw + 1) * 512]),
                    after=[prev_qproj, t_const])
                t_sn = p.dma("sync", lambda e, qw=qw: e.dma_start(
                    out=sinw_sb[:], in_=sin_d[:, qw * 512:(qw + 1) * 512]),
                    after=[prev_qproj])
                mk = mv = None
                for ic in range(NDT):
                    mk = p.comp("tensor", lambda e, ic=ic: e.matmul(
                        pA[0][:], wkv_tile(0, ic),
                        xw_sb[:, ic * 512:(ic + 1) * 512],
                        start=(ic == 0), stop=(ic == NDT - 1)),
                        after=[t_wkv, t_ln_done, ldw[ic], prev_ctx_st]
                        if ic == 0 else [ldw[ic]])
                for ic in range(NDT):
                    mv = p.comp("tensor", lambda e, ic=ic: e.matmul(
                        pA[1][:], wkv_tile(1, ic),
                        xw_sb[:, ic * 512:(ic + 1) * 512],
                        start=(ic == 0), stop=(ic == NDT - 1)), after=[])
                tke = p.comp("scalar", lambda e: e.activation(
                    kvf_sb[:, 0:512], pA[0][:], AF.Identity, bias=bq_sb[:, 4:5]),
                    after=[mk, mv])
                tr1 = p.comp("vector", lambda e: e.tensor_mul(
                    kvf_sb[:, 512:1024], kvf_sb[:, 0:512], cosw_sb[:]),
                    after=[tke, t_cs])
                tr2 = p.comp("vector", lambda e: e.tensor_mul(
                    qf_sb[0:64, :], kvf_sb[64:128, 0:512], sinw_sb[64:128, :]),
                    after=[tke, t_sn])
                tr3 = p.comp("vector", lambda e: e.tensor_mul(
                    qf_sb[64:128, :], kvf_sb[0:64, 0:512], sinw_sb[0:64, :]),
                    after=[tke])
                tk_bf = p.comp("vector", lambda e, qw=qw: e.tensor_add(
                    K_sb[:, qw * 512:(qw + 1) * 512], kvf_sb[:, 512:1024],
                    qf_sb[:, 0:512]), after=[tr1, tr2, tr3])
                tve = p.comp("scalar", lambda e: e.activation(
                    vb_sb[:], pA[1][:], AF.Identity, bias=bq_sb[:, 5:6]),
                    after=[tke])
                tprev = None
                for j in range(4):
                    ta = p.comp("tensor", lambda e, j=j: e.transpose(
                        ptr[:, 0:128], vb_sb[:, j * 128:(j + 1) * 128],
                        eyeb_sb[:]), after=[tve, t_eye, tprev])
                    tprev = p.comp("vector", lambda e, j=j, qw=qw: e.tensor_copy(
                        V_sb[:, qw * 512 + j * 128:qw * 512 + (j + 1) * 128],
                        ptr[:, 0:128]), after=[ta])
                for j in range(4):
                    kt = qw * 4 + j
                    tsq2 = p.comp("scalar", lambda e, qw=qw, j=j: e.activation(
                        sq_sb[:, 0:128],
                        K_sb[:, qw * 512 + j * 128:qw * 512 + (j + 1) * 128],
                        AF.Square),
                        after=[tk_bf, kn_done.get(kt - 1), prev_qnorm])
                    tm1 = p.comp("tensor", lambda e: e.matmul(
                        pA[4][0:1, 0:128], onesf_sb[:, 0:1], sq_sb[:, 0:128],
                        start=True, stop=True), after=[tsq2, prev_cx])
                    kn_done[kt] = p.comp(
                        "vector", lambda e, kt=kt: e.tensor_reduce(
                            kn_sb[0:1, kt:kt + 1], pA[4][0:1, 0:128], AX.X,
                            ALU.max), after=[tm1])
                kwin_done[qw] = tprev

            q_toks = []
            qn_toks = []
            for j in range(NH):
                qh = j % 2
                mq = None
                for ic in range(NDT):
                    mq = p.comp("tensor", lambda e, j=j, ic=ic, qh=qh: e.matmul(
                        pb[:, qh * 512:(qh + 1) * 512], wq_tile(j, ic),
                        xw_sb[:, ic * 512:(ic + 1) * 512],
                        start=(ic == 0), stop=(ic == NDT - 1)),
                        after=[wq_toks[j], t_ln_done, ldw[ic], prev_qproj,
                               qevac_prev[qh]]
                        if ic == 0 else [ldw[ic]])
                if layer == 0:
                    tqe = p.comp("scalar", lambda e, j=j, qh=qh: e.activation(
                        qf_sb[:], pb[:, qh * 512:(qh + 1) * 512], AF.Identity,
                        bias=bq_sb[:, j:j + 1]),
                        after=[mq])
                    qevac_prev[qh] = tqe
                    r1 = p.comp("vector", lambda e: e.tensor_mul(
                        kvf_sb[:, 0:512], qf_sb[:], cosw_sb[:]), after=[tqe])
                    r2 = p.comp("vector", lambda e: e.tensor_mul(
                        kvf_sb[0:64, 512:1024], qf_sb[64:128, :],
                        sinw_sb[64:128, :]), after=[tqe])
                    r3 = p.comp("vector", lambda e: e.tensor_mul(
                        kvf_sb[64:128, 512:1024], qf_sb[0:64, :],
                        sinw_sb[0:64, :]), after=[tqe])
                    tq = p.comp("vector", lambda e, j=j: e.tensor_add(
                        q_sb[:, j * 512:(j + 1) * 512], kvf_sb[:, 0:512],
                        kvf_sb[:, 512:1024]), after=[r1, r2, r3])
                else:
                    tq = p.comp("scalar", lambda e, j=j, qh=qh: e.activation(
                        q_sb[:, j * 512:(j + 1) * 512],
                        pb[:, qh * 512:(qh + 1) * 512], AF.Copy),
                        after=[mq])
                    qevac_prev[qh] = tq
                q_toks.append(tq)
                tsq3 = p.comp("scalar", lambda e, j=j: e.activation(
                    sq_sb[:, 128:640], q_sb[:, j * 512:(j + 1) * 512],
                    AF.Square), after=[tq, qn_toks[-1] if qn_toks else
                                       prev_qnorm])
                tm1 = p.comp("tensor", lambda e: e.matmul(
                    pA[4][0:1, 0:512], onesf_sb[:, 0:1], sq_sb[:, 128:640],
                    start=True, stop=True), after=[tsq3, prev_cx])
                qn_toks.append(p.comp("vector", lambda e, j=j: e.tensor_reduce(
                    nrm_sb[0:1, j:j + 1], pA[4][0:1, 0:512], AX.X, ALU.max),
                    after=[tm1]))
            prev_qproj = q_toks[-1]
            prev_qnorm = qn_toks[-1]

            tiles = plan[qw]
            last_kt = max(k0 for (k0, _m) in tiles) // 128
            tqm = p.comp("vector", lambda e: e.tensor_reduce(
                nrm_sb[0:1, 8:9], nrm_sb[0:1, 0:4], AX.X, ALU.max),
                after=qn_toks)
            tkm = p.comp("vector", lambda e, last_kt=last_kt: e.tensor_reduce(
                nrm_sb[0:1, 9:10], kn_sb[0:1, 0:last_kt + 1], AX.X, ALU.max),
                after=[kn_done[last_kt]])
            tc1 = p.comp("vector", lambda e: e.tensor_mul(
                nrm_sb[0:1, 10:11], nrm_sb[0:1, 8:9], nrm_sb[0:1, 9:10]),
                after=[tqm, tkm])
            tc2 = p.comp("scalar", lambda e: e.activation(
                nrm_sb[0:1, 11:12], nrm_sb[0:1, 10:11], AF.Sqrt), after=[tc1])
            tc3 = p.comp("scalar", lambda e: e.mul(
                nrm_sb[0:1, 12:13], nrm_sb[0:1, 11:12], -1.001 * sm_scale),
                after=[tc2])
            tc4 = p.comp("tensor", lambda e: e.matmul(
                pA[4][:, 0:1], onesf_sb[0:1, :], nrm_sb[0:1, 12:13],
                start=True, stop=True), after=[tc3, prev_cx])
            tc5 = p.comp("scalar", lambda e: e.activation(
                cc_sb[:], pA[4][:, 0:1], AF.Copy), after=[tc4])

            ctx_head_toks = []
            for j in range(NH):
                n_t = len(tiles)
                exp_prev = [None, None]
                pt_readers = [None, None]
                pv_prev = None
                den_prev = None
                kreq = kwin_done[qw]
                for i, (k0, mid) in enumerate(tiles):
                    bk = i % 2
                    t_s = p.comp("tensor", lambda e, k0=k0, bk=bk, j=j: e.matmul(
                        pA[bk][:], K_sb[:, k0:k0 + 128],
                        q_sb[:, j * 512:(j + 1) * 512], start=True, stop=True),
                        after=[exp_prev[bk], q_toks[j], kreq] if i < 2
                        else [exp_prev[bk]])
                    t_e = p.comp("scalar", lambda e, bk=bk: e.activation(
                        pt_sb[:, bk * 512:(bk + 1) * 512], pA[bk][:], AF.Exp,
                        bias=cc_sb[:, 0:1], scale=sm_scale),
                        after=[t_s, tc5, pt_readers[bk]])
                    exp_prev[bk] = t_e
                    t_m = t_e
                    if mid is not None:
                        t_m = p.comp("vector", lambda e, bk=bk, mid=mid:
                                     e.tensor_mul(
                                         pt_sb[:, bk * 512:(bk + 1) * 512],
                                         pt_sb[:, bk * 512:(bk + 1) * 512],
                                         masks_sb[:, mid * 512:(mid + 1) * 512]),
                                     after=[t_e, t_const])
                    t_pv = p.comp("tensor", lambda e, k0=k0, bk=bk, i=i, n_t=n_t:
                                  e.matmul(pA[2][:], V_sb[:, k0:k0 + 128],
                                           pt_sb[:, bk * 512:(bk + 1) * 512],
                                           start=(i == 0), stop=(i == n_t - 1)),
                                  after=[t_m, prev_cf] if i == 0 else [t_m])
                    pv_prev = t_pv
                    den_prev = p.comp("tensor", lambda e, bk=bk, i=i, n_t=n_t:
                                      e.matmul(pA[3][0:1, :], onesb_sb[:, 0:1],
                                               pt_sb[:, bk * 512:(bk + 1) * 512],
                                               start=(i == 0), stop=(i == n_t - 1)),
                                      after=[t_m, t_onesb, prev_d1] if i == 0
                                      else [])
                    pt_readers[bk] = den_prev
                t_d1 = p.comp("scalar", lambda e: e.activation(
                    den_sb[:], pA[3][0:1, :], AF.Copy), after=[den_prev])
                prev_d1 = t_d1
                t_d2 = p.comp("vector", lambda e: e.reciprocal(
                    den_sb[:], den_sb[:]), after=[t_d1])
                t_d3 = p.comp("tensor", lambda e: e.matmul(
                    pA[4][:], onesf_sb[0:1, :], den_sb[:], start=True,
                    stop=True), after=[t_d2, prev_cx])
                prev_d3 = t_d3
                t_cf = p.comp("scalar", lambda e: e.activation(
                    ctxf_sb[:], pA[2][:], AF.Copy), after=[pv_prev, t_d1,
                                                           prev_cx])
                prev_cf = t_cf
                t_cx = p.comp("vector", lambda e, j=j: e.tensor_mul(
                    ctxh_sb[:, j * 512:(j + 1) * 512], ctxf_sb[:], pA[4][:]),
                    after=[t_cf, t_d3, prev_ctx_st])
                prev_cx = t_cx
                ctx_head_toks.append(t_cx)
            st_toks = []
            wslot = [None, None]
            wcv = [None, None]
            pe_prev = [None, None]
            for ot in range(NDT):
                sl = ot % 2
                t_wf = p.dma("sync", lambda e, ot=ot, sl=sl: e.dma_start(
                    out=wotf_sb[:, sl * 512:(sl + 1) * 512].rearrange(
                        "p (j o) -> p j o", o=128),
                    in_=wo_d[:, ot * 128:(ot + 1) * 128].rearrange(
                        "(j p) o -> p j o", p=128)),
                    after=[wcv[sl] if True else None])
                t_wl = p.comp("vector", lambda e, sl=sl: e.tensor_copy(
                    wot_sb[:, sl * 512:(sl + 1) * 512],
                    wotf_sb[:, sl * 512:(sl + 1) * 512]),
                    after=[t_wf, wslot[sl]])
                wcv[sl] = t_wl
                mo = None
                for j in range(4):
                    mo = p.comp("tensor", lambda e, sl=sl, j=j: e.matmul(
                        pb[:, sl * 512:(sl + 1) * 512],
                        wot_sb[:, sl * 512 + j * 128:sl * 512 + (j + 1) * 128],
                        ctxh_sb[:, j * 512:(j + 1) * 512],
                        start=(j == 0), stop=(j == 3)),
                        after=[t_wl, pe_prev[sl]] + ctx_head_toks
                        if j == 0 else [])
                wslot[sl] = mo
                t_pe = p.comp("scalar", lambda e, sl=sl: e.activation(
                    po_sb[:, sl * 512:(sl + 1) * 512],
                    pb[:, sl * 512:(sl + 1) * 512], AF.Copy),
                    after=[mo, st_toks[-2] if len(st_toks) >= 2 else None])
                pe_prev[sl] = t_pe
                t_st = p.dma("scalar", lambda e, qw=qw, ot=ot, sl=sl: e.dma_start(
                    out=rsi_d[(qw // 2) * D + ot * 128:
                              (qw // 2) * D + (ot + 1) * 128,
                              (qw % 2) * 512:(qw % 2 + 1) * 512],
                    in_=po_sb[:, sl * 512:(sl + 1) * 512]), after=[t_pe])
                st_toks.append(t_st)
            prev_ctx_st = st_toks[-1]
            xw_free = prev_ctx_st

        t_rs = p.add("gpsimd", lambda e: e.collective_compute(
            "ReduceScatter", ALU.add, rg, [rsi_d], [rso_d]),
            after=[prev_ctx_st], inc="ag", k=1)

        res_toks = []
        for half in range(2):
            for ot in range(NDT):
                sl = ot % 2
                t_x = p.dma("sync", lambda e, ot=ot, half=half, sl=sl:
                            e.dma_start(
                                out=xr_sb[:, sl * 512:(sl + 1) * 512],
                                in_=x_d[ot * 128:(ot + 1) * 128,
                                        half * 512:(half + 1) * 512]),
                            after=[res_toks[-2] if len(res_toks) >= 2 else None])
                t_p = p.dma("sync", lambda e, ot=ot, half=half, sl=sl:
                            e.dma_start(
                                out=po_sb[:, sl * 512:(sl + 1) * 512],
                                in_=rso_d[ot * 128:(ot + 1) * 128,
                                          half * 512:(half + 1) * 512]),
                            after=[t_rs,
                                   res_toks[-2] if len(res_toks) >= 2 else None])
                if layer == 0:
                    t_r = p.comp("vector", lambda e, ot=ot, sl=sl:
                                 e.scalar_tensor_tensor(
                                     xo_sb[:, sl * 512:(sl + 1) * 512],
                                     po_sb[:, sl * 512:(sl + 1) * 512],
                                     bo_sb[:, ot:ot + 1],
                                     xr_sb[:, sl * 512:(sl + 1) * 512],
                                     ALU.add, ALU.add), after=[t_x, t_p, t_bo])
                else:
                    t_r = p.comp("vector", lambda e, sl=sl: e.tensor_add(
                        xo_sb[:, sl * 512:(sl + 1) * 512],
                        po_sb[:, sl * 512:(sl + 1) * 512],
                        xr_sb[:, sl * 512:(sl + 1) * 512]), after=[t_x, t_p])
                t_o = p.dma("scalar", lambda e, ot=ot, half=half, sl=sl:
                            e.dma_start(
                                out=x1_d[ot * 128:(ot + 1) * 128,
                                         half * 512:(half + 1) * 512],
                                in_=xo_sb[:, sl * 512:(sl + 1) * 512]),
                            after=[t_r])
                res_toks.append(t_o)

    return res_toks[-1]




def prep_attn_maps(maps, inputs, layer, pre):
    if layer == 0:
        wqkvT = np.ascontiguousarray(np.asarray(inputs["L0_attn_w_qkv"]).T)
        bqkv = np.asarray(inputs["L0_attn_b_qkv"])
        wdT = np.asarray(inputs["L0_attn_w_dense"]).T
        boT = np.ascontiguousarray(
            np.asarray(inputs["L0_attn_b_dense"]).reshape(32, 128).T)
        cos, sinS = rope_tables()
    else:
        wq = np.asarray(inputs["L1_attn_wq"])
        wk = np.asarray(inputs["L1_attn_wk"])
        wv = np.asarray(inputs["L1_attn_wv"])
        wo1 = np.asarray(inputs["L1_attn_wo"])
        wkv = np.ascontiguousarray(np.concatenate([wk, wv], axis=1))
    _plan, masks = build_tile_plan(layer)
    for c in range(NCORES):
        m = maps[c]
        wsl = wdT if layer == 0 else wo1
        m[pre + "wo"] = np.ascontiguousarray(wsl[512 * c:512 * (c + 1), :])
        m[pre + "masks"] = masks
        if layer == 0:
            m[pre + "wq"] = np.ascontiguousarray(wqkvT[:, 768 * c:768 * c + 512])
            m[pre + "wkv"] = np.ascontiguousarray(
                wqkvT[:, 768 * c + 512:768 * (c + 1)])
            m[pre + "bq"] = np.ascontiguousarray(
                bqkv[768 * c:768 * (c + 1)].reshape(6, 128).T)
            m[pre + "bo"] = boT
            m["cosT"] = cos
            m["sinSW"] = np.ascontiguousarray(np.roll(sinS, 64, axis=0))
        else:
            m[pre + "wq"] = np.ascontiguousarray(wq[:, 512 * c:512 * (c + 1)])
            m[pre + "wkv"] = wkv




def emit_ln(p, nc, x_d, xt_sb, sq_sb, stat_sb, bc_sb, onesf_sb, pm, pb,
            t_ones, write_fn):
    TPCl = TPC
    mm_tok = None
    slot_free = [None, None]
    for ic in range(NDT):
        sl = ic % 2
        t_ld = p.dma("sync", lambda e, ic=ic, sl=sl: e.dma_start(
            out=xt_sb[:, sl * TPCl:(sl + 1) * TPCl],
            in_=x_d[ic * 128:(ic + 1) * 128, :]), after=[slot_free[sl]])
        t_sq = p.comp("scalar", lambda e, sl=sl: e.activation(
            sq_sb[:, 0:TPCl], xt_sb[:, sl * TPCl:(sl + 1) * TPCl], AF.Square),
            after=[t_ld, mm_tok])
        for h in range(2):
            p.comp("tensor", lambda e, sl=sl, h=h, ic=ic: e.matmul(
                pm[h][0:1, :], onesf_sb[:, 0:1],
                xt_sb[:, sl * TPCl + h * 512:sl * TPCl + (h + 1) * 512],
                start=(ic == 0), stop=(ic == NDT - 1)),
                after=[t_ld, t_ones])
            mm_tok = p.comp("tensor", lambda e, h=h, ic=ic: e.matmul(
                pm[2 + h][0:1, :], onesf_sb[:, 0:1],
                sq_sb[:, h * 512:(h + 1) * 512],
                start=(ic == 0), stop=(ic == NDT - 1)),
                after=[t_sq])
        slot_free[sl] = mm_tok
    m_ = stat_sb[0:1, 0:TPCl]
    msq = stat_sb[0:1, TPCl:2 * TPCl]
    m2 = stat_sb[0:1, 2 * TPCl:3 * TPCl]
    var = stat_sb[0:1, 3 * TPCl:4 * TPCl]
    r_ = stat_sb[0:1, 3 * TPCl:4 * TPCl]
    mr = stat_sb[0:1, 2 * TPCl:3 * TPCl]
    t1 = p.comp("scalar", lambda e: e.activation(
        m_[:, 0:512], pm[0][0:1, :], AF.Copy, scale=1.0 / D), after=[mm_tok])
    t1 = p.comp("scalar", lambda e: e.activation(
        m_[:, 512:1024], pm[1][0:1, :], AF.Copy, scale=1.0 / D), after=[t1])
    t2 = p.comp("scalar", lambda e: e.activation(
        msq[:, 0:512], pm[2][0:1, :], AF.Copy, scale=1.0 / D), after=[t1])
    t2 = p.comp("scalar", lambda e: e.activation(
        msq[:, 512:1024], pm[3][0:1, :], AF.Copy, scale=1.0 / D), after=[t2])
    t3 = p.comp("scalar", lambda e: e.activation(m2[:], m_[:], AF.Square),
                after=[t2])
    t4 = p.comp("vector", lambda e: e.tensor_sub(var[:], msq[:], m2[:]),
                after=[t3])
    t4b = p.comp("vector", lambda e: e.tensor_scalar_add(var[:], var[:], 1e-5),
                 after=[t4])
    t5 = p.comp("scalar", lambda e: e.activation(var[:], var[:], AF.Sqrt),
                after=[t4b])
    t6 = p.comp("vector", lambda e: e.reciprocal(r_[:], var[:]), after=[t5])
    t7 = p.comp("vector", lambda e: e.tensor_mul(mr[:], m_[:], r_[:]),
                after=[t6])
    bts = [p.comp("tensor", lambda e, h=h: e.matmul(
        pb[:, h * 512:(h + 1) * 512], onesf_sb[0:1, :],
        r_[:, h * 512:(h + 1) * 512], start=True, stop=True),
        after=[t6]) for h in range(2)]
    t8 = p.comp("scalar", lambda e: e.activation(bc_sb[:, 0:TPCl], pb[:],
                                                 AF.Copy), after=bts)
    bts2 = [p.comp("tensor", lambda e, h=h: e.matmul(
        pb[:, h * 512:(h + 1) * 512], onesf_sb[0:1, :],
        mr[:, h * 512:(h + 1) * 512], start=True, stop=True),
        after=[t7, t8]) for h in range(2)]
    t9 = p.comp("scalar", lambda e: e.activation(bc_sb[:, TPCl:2 * TPCl], pb[:],
                                                 AF.Copy), after=bts2)
    rb = bc_sb[:, 0:TPCl]
    mrb = bc_sb[:, TPCl:2 * TPCl]
    toks = []
    slot_free = [None, None]
    for ic in range(NDT):
        sl = ic % 2
        t_ld = p.dma("sync", lambda e, ic=ic, sl=sl: e.dma_start(
            out=xt_sb[:, sl * TPCl:(sl + 1) * TPCl],
            in_=x_d[ic * 128:(ic + 1) * 128, :]), after=[t8, slot_free[sl]])
        tm = p.comp("vector", lambda e, sl=sl: e.tensor_mul(
            xt_sb[:, sl * TPCl:(sl + 1) * TPCl],
            xt_sb[:, sl * TPCl:(sl + 1) * TPCl], rb), after=[t_ld, t8])
        tok = write_fn(ic, sl, tm, t9, mrb)
        toks.append(tok)
        slot_free[sl] = tok
    return toks, bts2[-1]




def emit_l1_ffn(p, io, pre):
    nc = p.nc
    x_d = io["x"]
    bw_d = io["bw"]
    bb_d = io["bb"]
    at_d = io["at"]
    bs_d = io["bs"]
    rw_d = io["rw"]
    eyeb_d = io["eyeb"]
    x1_d = io["xout"]

    ctx = contextlib.ExitStack()
    with ctx:
        sb = lambda name, shape, dt: ctx.enter_context(
            nc.sbuf_tensor(pre + name, shape, dt))
        xln_sb = sb("xln_sb", [128, NDT * TPC], BF16)
        bwb_sb = sb("bwb_sb", [128, 2 * NDT * 128], BF16)
        bsb_sb = sb("bsb_sb", [128, 2 * 8 * 128], BF16)
        atb_sb = sb("atb_sb", [128, NDT * 64], BF16)
        rwb_sb = sb("rwb_sb", [128, NDT * 16], BF16)
        eyeb_sb = sb("eyeb_sb", [128, 128], BF16)
        onesf_sb = sb("onesf_sb", [128, 128], F32)
        xt_sb = sb("xt_sb", [128, 2 * TPC], F32)
        sq_sb = sb("sq_sb", [128, TPC], F32)
        stat_sb = sb("stat_sb", [1, 4 * TPC], F32)
        bc_sb = sb("bc_sb", [128, 2 * TPC], F32)
        bb_sb = sb("bb_sb", [128, 32], F32)
        pr_sb = sb("pr_sb", [128, 16], F32)
        sc_sb = sb("sc_sb", [128, 8], F32)
        comba_sb = sb("comba_sb", [128, 8 * 16], BF16)
        rowE_sb = sb("rowE_sb", [1, TPC], F32)
        rhT_sb = sb("rhT_sb", [64, TPC], F32)
        RH_sb = sb("RH_sb", [128, 8 * TPC], BF16)
        xr_sb = sb("xr_sb", [128, 2 * 512], F32)
        xo_sb = sb("xo_sb", [128, 2 * 512], F32)

        psb = lambda name, shape, dt: ctx.enter_context(
            nc.psum_tensor(pre + name, shape, dt))
        pb = psb("pb", [128, TPC], F32)
        pA = [psb(f"pA{i}", [128, 512], F32) for i in range(4)]
        ptr = psb("ptr", [128, 512], BF16)

        t_eye = p.dma("sync", lambda e: e.dma_start(out=eyeb_sb[:], in_=eyeb_d))
        t_bb = p.dma("sync", lambda e: e.dma_start(out=bb_sb[:], in_=bb_d))
        t_ones = p.comp("vector", lambda e: e.memset(onesf_sb[:], 1.0))
        t_at = p.dma("gpsimd", lambda e: e.dma_start(
            out=atb_sb[:].rearrange("p (a o) -> p a o", o=64),
            in_=at_d.rearrange("(a p) o -> p a o", p=128)))
        t_rw = p.dma("gpsimd", lambda e: e.dma_start(
            out=rwb_sb[:].rearrange("p (a o) -> p a o", o=16),
            in_=rw_d.rearrange("(a p) o -> p a o", p=128)))

        def xw(ic, sl, tm, t9, mrb):
            return p.comp("vector", lambda e, ic=ic, sl=sl: e.tensor_sub(
                xln_sb[:, ic * TPC:(ic + 1) * TPC],
                xt_sb[:, sl * TPC:(sl + 1) * TPC], mrb), after=[tm, t9])

        xln_toks, t_lnmm = emit_ln(p, nc, x_d, xt_sb, sq_sb, stat_sb, bc_sb,
                                   onesf_sb, pA, pb, t_ones, xw)
        xln_done = xln_toks[-1]

        def xln(ic):
            return xln_sb[:, ic * TPC:(ic + 1) * TPC]

        mrh = None
        for h in range(2):
            for ic in range(NDT):
                mrh = p.comp("tensor", lambda e, ic=ic, h=h: e.matmul(
                    pb[0:64, h * 512:(h + 1) * 512],
                    atb_sb[:, ic * 64:(ic + 1) * 64],
                    xln(ic)[:, h * 512:(h + 1) * 512],
                    start=(ic == 0), stop=(ic == NDT - 1)),
                    after=[t_at, xln_done, t_lnmm] if (ic == 0 and h == 0)
                    else [])
        t_rh = p.comp("scalar", lambda e: e.activation(rhT_sb[:], pb[0:64, :],
                                                       AF.Copy), after=[mrh])

        tr_toks = []
        for tt in range(8):
            mro = None
            for ic in range(NDT):
                mro = p.comp("tensor", lambda e, ic=ic, tt=tt: e.matmul(
                    pA[0][:, 0:16],
                    xln_sb[:, ic * TPC + tt * 128:ic * TPC + (tt + 1) * 128],
                    rwb_sb[:, ic * 16:(ic + 1) * 16],
                    start=(ic == 0), stop=(ic == NDT - 1)),
                    after=[t_rw, xln_done, t_rh,
                           tr_toks[-1] if tr_toks else None]
                    if ic == 0 else [])
            c0 = p.comp("scalar", lambda e: e.activation(
                pr_sb[:], pA[0][:, 0:16], AF.Copy),
                after=[mro, tr_toks[-1] if tr_toks else None])
            c1 = p.comp("vector", lambda e: e.tensor_reduce(
                sc_sb[:, 0:1], pr_sb[:], AX.X, ALU.max), after=[c0])
            c2 = p.comp("vector", lambda e: e.tensor_scalar_mul(
                sc_sb[:, 1:2], sc_sb[:, 0:1], -1.0), after=[c1])
            c3 = p.comp("scalar", lambda e: e.activation(
                pr_sb[:], pr_sb[:], AF.Exp, bias=sc_sb[:, 1:2]), after=[c2])
            c4 = p.comp("vector", lambda e: e.tensor_reduce(
                sc_sb[:, 2:3], pr_sb[:], AX.X, ALU.add), after=[c3])
            c5 = p.comp("vector", lambda e: e.reciprocal(
                sc_sb[:, 3:4], sc_sb[:, 2:3]), after=[c4])
            c6 = p.comp("vector", lambda e: e.tensor_scalar_mul(
                pr_sb[:], pr_sb[:], sc_sb[:, 3:4]), after=[c5])
            c7 = p.comp("vector", lambda e: e.tensor_reduce(
                sc_sb[:, 0:1], pr_sb[:], AX.X, ALU.max), after=[c6])
            c8 = p.comp("vector", lambda e: e.tensor_scalar(
                sq_sb[:, 0:16], pr_sb[:], sc_sb[:, 0:1], None, ALU.is_equal),
                after=[c7])
            c9 = p.comp("vector", lambda e: e.scalar_tensor_tensor(
                sq_sb[:, 16:32], sq_sb[:, 0:16], -2.0, pr_sb[:],
                ALU.mult, ALU.add), after=[c8])
            c10 = p.comp("vector", lambda e: e.tensor_reduce(
                sc_sb[:, 1:2], sq_sb[:, 16:32], AX.X, ALU.max), after=[c9])
            c11 = p.comp("vector", lambda e: e.tensor_scalar(
                sq_sb[:, 32:48], pr_sb[:], sc_sb[:, 1:2], None, ALU.is_equal),
                after=[c10])
            c12 = p.comp("vector", lambda e: e.tensor_add(
                sq_sb[:, 0:16], sq_sb[:, 0:16], sq_sb[:, 32:48]), after=[c11])
            c13 = p.comp("vector", lambda e: e.tensor_add(
                sc_sb[:, 4:5], sc_sb[:, 0:1], sc_sb[:, 1:2]), after=[c10])
            c14 = p.comp("vector", lambda e: e.tensor_scalar_add(
                sc_sb[:, 4:5], sc_sb[:, 4:5], 1e-6), after=[c13])
            c15 = p.comp("vector", lambda e: e.reciprocal(
                sc_sb[:, 5:6], sc_sb[:, 4:5]), after=[c14])
            c16 = p.comp("vector", lambda e: e.tensor_scalar_mul(
                sc_sb[:, 5:6], sc_sb[:, 5:6], 0.5), after=[c15])
            c17 = p.comp("vector", lambda e: e.tensor_mul(
                sq_sb[:, 0:16], sq_sb[:, 0:16], pr_sb[:]), after=[c12, c6])
            c18 = p.comp("vector", lambda e, tt=tt: e.tensor_scalar_mul(
                comba_sb[:, tt * 16:(tt + 1) * 16], sq_sb[:, 0:16],
                sc_sb[:, 5:6]), after=[c16, c17])
            tr_toks.append(c18)

        rh_toks = []
        prev_tr = None
        prev_row = None
        for e_ in range(16):
            row_toks = []
            for tt in range(8):
                ta = p.comp("tensor", lambda e, e_=e_, tt=tt: e.transpose(
                    ptr[0:1, 0:128],
                    comba_sb[:, tt * 16 + e_:tt * 16 + e_ + 1], eyeb_sb[:]),
                    after=[tr_toks[tt], t_eye, prev_tr,
                           rh_toks[-1] if rh_toks else None])
                prev_tr = p.comp("scalar", lambda e, tt=tt: e.activation(
                    rowE_sb[0:1, tt * 128:(tt + 1) * 128], ptr[0:1, 0:128],
                    AF.Copy), after=[ta, prev_row])
                row_toks.append(prev_tr)
            bc1 = p.comp("tensor", lambda e: e.matmul(
                pA[1][0:64, :], onesf_sb[0:1, 0:64], rowE_sb[0:1, 0:512],
                start=True, stop=True), after=row_toks + [prev_row])
            bc2 = p.comp("tensor", lambda e: e.matmul(
                pA[2][0:64, :], onesf_sb[0:1, 0:64], rowE_sb[0:1, 512:1024],
                start=True, stop=True), after=row_toks)
            cc = e_ // 2
            half = (e_ % 2) * 64
            m1 = p.comp("vector", lambda e, cc=cc, half=half: e.tensor_mul(
                RH_sb[half:half + 64, cc * TPC:cc * TPC + 512],
                pA[1][0:64, :], rhT_sb[0:64, 0:512]), after=[bc1, t_rh])
            m2 = p.comp("vector", lambda e, cc=cc, half=half: e.tensor_mul(
                RH_sb[half:half + 64, cc * TPC + 512:(cc + 1) * TPC],
                pA[2][0:64, :], rhT_sb[0:64, 512:1024]), after=[bc2, t_rh])
            prev_row = m2
            rh_toks += [m1, m2]

        res_toks = []
        wslot = [None, None]
        bslot = [None, None]
        for ot in range(NDT):
            sl = ot % 2
            t_wl = p.dma("gpsimd", lambda e, ot=ot, sl=sl: e.dma_start(
                out=bwb_sb[:, sl * NDT * 128:(sl + 1) * NDT * 128].rearrange(
                    "p (a o) -> p a o", o=128),
                in_=bw_d[:, ot * 128:(ot + 1) * 128].rearrange(
                    "(a p) o -> p a o", p=128)), after=[wslot[sl]])
            t_bs = p.dma("gpsimd", lambda e, ot=ot, sl=sl: e.dma_start(
                out=bsb_sb[:, sl * 8 * 128:(sl + 1) * 8 * 128].rearrange(
                    "p (a o) -> p a o", o=128),
                in_=bs_d[:, ot * 128:(ot + 1) * 128].rearrange(
                    "(a p) o -> p a o", p=128)), after=[bslot[sl]])
            tok = None
            for h in range(2):
                for ic in range(NDT):
                    tok = p.comp("tensor", lambda e, sl=sl, ic=ic, h=h: e.matmul(
                        pb[:, h * 512:(h + 1) * 512],
                        bwb_sb[:, sl * NDT * 128 + ic * 128:
                               sl * NDT * 128 + (ic + 1) * 128],
                        xln(ic)[:, h * 512:(h + 1) * 512],
                        start=(ic == 0), stop=False),
                        after=[t_wl, xln_done,
                               res_toks[-2] if len(res_toks) >= 2 else None,
                               rh_toks[-1]]
                        if (ic == 0 and h == 0) else [])
                for cc in range(8):
                    tok = p.comp("tensor", lambda e, sl=sl, cc=cc, h=h: e.matmul(
                        pb[:, h * 512:(h + 1) * 512],
                        bsb_sb[:, sl * 8 * 128 + cc * 128:
                               sl * 8 * 128 + (cc + 1) * 128],
                        RH_sb[:, cc * TPC + h * 512:cc * TPC + (h + 1) * 512],
                        start=False, stop=(cc == 7)),
                        after=[t_bs, rh_toks[-1]] if (cc == 0 and h == 0)
                        else [])
            wslot[sl] = tok
            bslot[sl] = tok
            t_r_prev = None
            for half in range(2):
                t_x = p.dma("sync", lambda e, ot=ot, half=half, sl=sl:
                            e.dma_start(
                                out=xr_sb[:, sl * 512:(sl + 1) * 512],
                                in_=x_d[ot * 128:(ot + 1) * 128,
                                        half * 512:(half + 1) * 512]),
                            after=[t_r_prev,
                                   res_toks[-2] if len(res_toks) >= 2
                                   else None])
                t_r = p.comp("vector", lambda e, ot=ot, half=half, sl=sl:
                             e.scalar_tensor_tensor(
                                 xo_sb[:, sl * 512:(sl + 1) * 512],
                                 pb[:, half * 512:(half + 1) * 512],
                                 bb_sb[:, ot:ot + 1],
                                 xr_sb[:, sl * 512:(sl + 1) * 512],
                                 ALU.add, ALU.add),
                             after=[tok, t_x, t_bb,
                                    res_toks[-1] if res_toks else None])
                t_r_prev = t_r
                t_o = p.dma("gpsimd", lambda e, ot=ot, half=half, sl=sl:
                            e.dma_start(
                                out=x1_d[ot * 128:(ot + 1) * 128,
                                         half * 512:(half + 1) * 512],
                                in_=xo_sb[:, sl * 512:(sl + 1) * 512]),
                            after=[t_r])
                res_toks.append(t_o)
    return res_toks[-1]


def prep_l1_ffn_maps(maps, inputs, pre):
    bwT = np.ascontiguousarray(np.asarray(inputs["L1_ffn_base_w"]).T)
    bbT = np.ascontiguousarray(np.asarray(inputs["L1_ffn_base_b"]).reshape(32, 128).T)
    AT = np.ascontiguousarray(np.asarray(inputs["L1_ffn_A"]).T)
    BsT = np.ascontiguousarray(
        np.asarray(inputs["L1_ffn_B"]).transpose(0, 2, 1).reshape(1024, D))
    rwT = np.ascontiguousarray(np.asarray(inputs["L1_ffn_router_w"]).T)
    for c in range(NCORES):
        maps[c].update({pre + "bw": bwT, pre + "bb": bbT, pre + "at": AT,
                        pre + "bs": BsT, pre + "rw": rwT})



NCHUNK = 11
NREAL_IB = 86


NPAR = 3


def emit_l0_ffn(p, io, pre, n_chunks=NCHUNK):
    nc = p.nc
    x_d = io["x"]
    cb_d = io["cb"]
    iw_d = io["iw"]
    idm_d = io["idm"]
    x1_d = io["xout"]
    cbp_d = io["cbp"]
    wgst_d = io["wgst"]
    wust_d = io["wust"]
    wdst_d = io["wdst"]
    wgag_d = io["wgag"]
    wuag_d = io["wuag"]
    wdag_d = io["wdag"]

    ctx = contextlib.ExitStack()
    with ctx:
        sb = lambda name, shape, dt: ctx.enter_context(
            nc.sbuf_tensor(pre + name, shape, dt))
        xln_sb = sb("xln_sb", [128, NDT * TPC], BF16)
        ggb_sb = sb("ggb_sb", [128, 4096], BF16)
        gub_sb = sb("gub_sb", [128, 4096], BF16)
        dstf_sb = sb("dstf_sb", [128, 4, 32, 128], BF16)
        i16w_sb = sb("i16w_sb", [128, 4096], I16)
        i32w_sb = sb("i32w_sb", [128, 2048], I32)
        i32d_sb = sb("i32d_sb", [128, 512], I32)
        mk_sb = sb("mk_sb", [128, 3 * 512], F32)
        tmpf_sb = sb("tmpf_sb", [128, 32, 8], F32)
        wdT_sb = sb("wdT_sb", [128, 2 * 384], BF16)
        h_sb = sb("h_sb", [128, 8 * TPC], BF16)
        hg_sb = sb("hg_sb", [128, TPC], BF16)
        onesf_sb = sb("onesf_sb", [128, 128], F32)
        xt_sb = sb("xt_sb", [128, 2 * TPC], F32)
        lnctx = contextlib.ExitStack()
        sq_sb = lnctx.enter_context(nc.sbuf_tensor(pre + "sq_sb", [128, TPC], F32))
        stat_sb = lnctx.enter_context(
            nc.sbuf_tensor(pre + "stat_sb", [1, 4 * TPC], F32))
        bc_sb = lnctx.enter_context(
            nc.sbuf_tensor(pre + "bc_sb", [128, 2 * TPC], F32))
        lnctx.close()
        wgT_sb = sb("wgT_sb", [128, NDT * 256], BF16)
        wuT_sb = sb("wuT_sb", [128, NDT * 256], BF16)
        xr_sb = xt_sb[:, 0:TPC]
        xo_sb = xt_sb[:, TPC:2 * TPC]

        psb = lambda name, shape, dt: ctx.enter_context(
            nc.psum_tensor(pre + name, shape, dt))
        pb = psb("pb", [128, TPC], F32)
        pA = [psb(f"pA{i}", [128, 512], F32) for i in range(6)]

        rg = [list(range(NCORES))]

        t_ones = p.comp("vector", lambda e: e.memset(onesf_sb[:], 1.0))

        t_init = p.dma("sync", lambda e: e.dma_start(out=x1_d, in_=x_d))

        t_cbp = None
        for r0 in range(0, 32768, 4096):
            t_cbp = p.dma("gpsimd", lambda e, r0=r0: e.dma_start(
                out=cbp_d[r0:r0 + 4096, 0:16],
                in_=cb_d[2 * r0:2 * r0 + 8192, :].rearrange(
                    "(r t) e -> r (t e)", t=2)), after=[t_cbp])

        def xw(ic, sl, tm, t9, mrb):
            return p.comp("vector", lambda e, ic=ic, sl=sl: e.tensor_sub(
                xln_sb[:, ic * TPC:(ic + 1) * TPC],
                xt_sb[:, sl * TPC:(sl + 1) * TPC], mrb), after=[tm, t9])

        xln_toks, t_lnmm = emit_ln(p, nc, x_d, xt_sb, sq_sb, stat_sb, bc_sb,
                                   onesf_sb, pA, pb, t_ones, xw)
        xln_done = xln_toks[-1]

        def xln(ic):
            return xln_sb[:, ic * TPC:(ic + 1) * TPC]

        ag_toks = []
        xacc_tok = [t_init] * 64
        xacc_ld = [None] * 64
        compute_done = []

        NPC = 16
        regcell = {}

        def setup_reg(e):
            regcell["r"] = e.alloc_register(pre + "nidx")
            e.reg_mov(regcell["r"], 4096)

        p.add("gpsimd", setup_reg)
        t_i16_prev = None
        t_mb_prev = None
        sel_last = [None, None, None]
        stage_prev = {0: None, 1: None, 2: None}
        last_gq = [None] * 4
        sel_q = [None] * 4
        sel_prev = None
        for c in range(n_chunks):
            par = c % 3
            stage_toks = []
            for m in range(3):
                tgt_sb = gub_sb if m == 1 else ggb_sb
                t_i16 = None
                for hf in range(2):
                    t_lw = p.dma("scalar", lambda e, m=m, c=c, hf=hf:
                                 e.dma_start(
                                     out=i32w_sb[:],
                                     in_=iw_d[m, c, :,
                                              hf * 2048:(hf + 1) * 2048]),
                                 after=[t_i16_prev, t_i16])
                    t_sh = p.comp("vector", lambda e: e.tensor_scalar(
                        i32w_sb[:], i32w_sb[:], 1, None,
                        ALU.logical_shift_right), after=[t_lw])
                    t_i16 = p.comp("vector", lambda e, hf=hf: e.tensor_copy(
                        i16w_sb[:, hf * 2048:(hf + 1) * 2048], i32w_sb[:]),
                        after=[t_sh] + (list(last_gq) if hf == 0 else []))
                t_i16_prev = t_i16
                t_lm = p.dma("scalar", lambda e, m=m, c=c: e.dma_start(
                    out=i32d_sb[:], in_=idm_d[m, c]), after=[t_mb_prev])
                t_and = p.comp("vector", lambda e: e.tensor_scalar(
                    i32d_sb[:], i32d_sb[:], 1, None, ALU.bitwise_and),
                    after=[t_lm, t_mb_prev])
                t_mb = p.comp("vector", lambda e, m=m: e.tensor_copy(
                    mk_sb[:, m * 512:(m + 1) * 512], i32d_sb[:]),
                    after=[t_and, sel_last[m]])
                t_mb_prev = t_mb
                war = stage_prev[0 if m == 2 else (2 if m == 0 else 1)]
                for pi in range(NPC):
                    q = pi % 4
                    t_g = p.add("gpsimd", lambda e, pi=pi, q=q: e.dma_gather(
                        dstf_sb[:, q], cbp_d,
                        i16w_sb[:, pi * 256:(pi + 1) * 256],
                        4096, regcell["r"], 128, elem_step=128,
                        single_packet=False, queue_num=q),
                        after=([t_i16, t_cbp] if pi < 4 else []) + [sel_q[q]],
                        inc=f"gq{q}", k=16, serial=False)
                    last_gq[q] = t_g
                    s1 = p.comp("vector", lambda e, q=q: e.tensor_sub(
                        tmpf_sb[:], dstf_sb[:, q, :, 8:16],
                        dstf_sb[:, q, :, 0:8]), after=[t_g, sel_prev])
                    m8 = mk_sb[:, m * 512 + pi * 32:m * 512 + (pi + 1) * 32]
                    s2 = p.comp("vector", lambda e, m8=m8: e.tensor_mul(
                        tmpf_sb[:], tmpf_sb[:],
                        m8.rearrange("p (j o) -> p j o", o=1).broadcast_to(
                            [128, 32, 8])), after=[s1, t_mb])
                    sel_prev = p.comp(
                        "vector", lambda e, pi=pi, q=q, tgt_sb=tgt_sb:
                        e.tensor_add(
                            tgt_sb[:, pi * 256:(pi + 1) * 256]
                            .rearrange("p (j e) -> p j e", e=8),
                            tmpf_sb[:], dstf_sb[:, q, :, 0:8]),
                        after=[s2, war if pi == 0 else None])
                    sel_q[q] = sel_prev
                sel_last[m] = sel_prev
                agg = ag_toks[-3] if len(ag_toks) >= 3 else None
                if m == 0:
                    st = p.dma("gpsimd", lambda e, par=par: e.dma_start(
                        out=wgst_d[par], in_=ggb_sb[:]),
                        after=[sel_prev, agg])
                elif m == 1:
                    st = p.dma("gpsimd", lambda e, par=par: e.dma_start(
                        out=wust_d[par], in_=gub_sb[:]),
                        after=[sel_prev, agg])
                else:
                    st = p.dma("gpsimd", lambda e, par=par: e.dma_start(
                        out=wdst_d[par].rearrange("(s p) g -> p s g", p=128),
                        in_=ggb_sb[:].rearrange("p (s g) -> p s g", g=1024)),
                        after=[sel_prev, agg])
                stage_prev[m] = st
                stage_toks.append(st)
            t_sg, t_su, t_sd = stage_toks
            cgate = compute_done[c - 3] if c >= 3 else None
            a1 = p.add("gpsimd", lambda e, par=par: e.collective_compute(
                "AllGather", ALU.bypass, rg, [wgst_d[par]], [wgag_d[par]]),
                after=[t_sg, cgate], inc="ag", k=1, serial=False)
            a2 = p.add("gpsimd", lambda e, par=par: e.collective_compute(
                "AllGather", ALU.bypass, rg, [wust_d[par]], [wuag_d[par]]),
                after=[t_su], inc="ag", k=1, serial=False)
            a3 = p.add("gpsimd", lambda e, par=par: e.collective_compute(
                "AllGather", ALU.bypass, rg, [wdst_d[par]], [wdag_d[par]]),
                after=[t_sd], inc="ag", k=1, serial=False)
            ag_toks.append(a3)

            n_ib = min(8, NREAL_IB - c * 8)
            h_toks = []
            prev_pair_mm = compute_done[-1] if compute_done else None
            up_last = None
            for pp in range((n_ib + 1) // 2):
                tw = []
                for ic in range(NDT):
                    tw.append(p.dma("sync", lambda e, c=c, par=par, pp=pp,
                                    ic=ic: e.dma_start(
                                        out=wgT_sb[:, ic * 256:(ic + 1) * 256],
                                        in_=wgag_d[par][pp * 256:(pp + 1) * 256,
                                                        ic * 128:(ic + 1) * 128],
                                        transpose=True),
                                    after=[a1, prev_pair_mm, xln_done,
                                           t_lnmm] if ic == 0
                                    else []))
                    tw.append(p.dma("sync", lambda e, c=c, par=par, pp=pp,
                                    ic=ic: e.dma_start(
                                        out=wuT_sb[:, ic * 256:(ic + 1) * 256],
                                        in_=wuag_d[par][pp * 256:(pp + 1) * 256,
                                                        ic * 128:(ic + 1) * 128],
                                        transpose=True),
                                    after=[a2, xln_done, t_lnmm]
                                    if ic == 0 else []))
                for jj in range(2):
                    ib = pp * 2 + jj
                    if ib >= n_ib:
                        continue
                    mg = mu = None
                    for h in range(2):
                        for ic in range(NDT):
                            mg = p.comp("tensor", lambda e, ic=ic, jj=jj, h=h:
                                        e.matmul(
                                            pb[:, h * 512:(h + 1) * 512],
                                            wgT_sb[:, ic * 256 + jj * 128:
                                                   ic * 256 + (jj + 1) * 128],
                                            xln(ic)[:, h * 512:(h + 1) * 512],
                                            start=(ic == 0),
                                            stop=(ic == NDT - 1)),
                                        after=([tw[2 * ic], xln_done, t_lnmm,
                                                up_last]
                                               if (h == 0 and ic == 0) else
                                               [tw[2 * ic]] if jj == 0
                                               else []))
                    for h in range(2):
                        for ic in range(NDT):
                            mu = p.comp("tensor", lambda e, ic=ic, jj=jj, h=h:
                                        e.matmul(
                                            pA[h][:],
                                            wuT_sb[:, ic * 256 + jj * 128:
                                                   ic * 256 + (jj + 1) * 128],
                                            xln(ic)[:, h * 512:(h + 1) * 512],
                                            start=(ic == 0),
                                            stop=(ic == NDT - 1)),
                                        after=([tw[2 * ic + 1], mg, up_last]
                                               if (h == 0 and ic == 0) else
                                               [tw[2 * ic + 1]] if jj == 0
                                               else [mg] if ic == 0 else []))
                    ts1 = p.comp("scalar", lambda e: e.activation(
                        hg_sb[:, 0:512], pb[:, 0:512], AF.Silu), after=[mg, mu])
                    ts2 = p.comp("scalar", lambda e: e.activation(
                        hg_sb[:, 512:1024], pb[:, 512:1024], AF.Silu),
                        after=[ts1])
                    ts3 = p.comp("scalar", lambda e, ib=ib: e.activation(
                        h_sb[:, ib * TPC:ib * TPC + 512], pA[0][:], AF.Copy),
                        after=[ts2, h_toks[-1] if h_toks else None])
                    ts4 = p.comp("scalar", lambda e, ib=ib: e.activation(
                        h_sb[:, ib * TPC + 512:(ib + 1) * TPC], pA[1][:],
                        AF.Copy), after=[ts3])
                    tm = p.comp("vector", lambda e, ib=ib: e.tensor_mul(
                        h_sb[:, ib * TPC:(ib + 1) * TPC],
                        h_sb[:, ib * TPC:(ib + 1) * TPC], hg_sb[:]),
                        after=[ts4])
                    h_toks.append(tm)
                    up_last = tm
                prev_pair_mm = up_last
            dn_last = None
            ev_last = None
            for hg0 in range(0, NDT, 3):
                ntile = min(3, NDT - hg0)
                slot_mm = [None, None]
                twd = []
                for ib in range(n_ib):
                    t_wd = p.dma("sync", lambda e, c=c, par=par, hg0=hg0,
                                 ib=ib, ntile=ntile: e.dma_start(
                                     out=wdT_sb[:, (ib % 2) * 384:
                                                (ib % 2) * 384 + ntile * 128],
                                     in_=wdag_d[par][hg0 * 128:
                                                     hg0 * 128 + ntile * 128,
                                                     ib * 128:(ib + 1) * 128],
                                     transpose=True),
                                 after=[a3, slot_mm[ib % 2],
                                        h_toks[-1] if ib == 0 else None,
                                        ev_last if ib == 0 else None])
                    twd.append(t_wd)
                    for t in range(ntile):
                        for h in range(2):
                            dn_last = p.comp(
                                "tensor",
                                lambda e, ib=ib, t=t, h=h, n_ib=n_ib:
                                e.matmul(
                                    pA[t * 2 + h][:],
                                    wdT_sb[:, (ib % 2) * 384 + t * 128:
                                           (ib % 2) * 384 + (t + 1) * 128],
                                    h_sb[:, ib * TPC + h * 512:
                                         ib * TPC + (h + 1) * 512],
                                    start=(ib == 0), stop=(ib == n_ib - 1)),
                                after=[t_wd, h_toks[ib]] +
                                ([ev_last] if ib == 0 else [])
                                if (t == 0 and h == 0) else [])
                    slot_mm[ib % 2] = dn_last
                for t in range(ntile):
                    ot = hg0 + t
                    for h in range(2):
                        sl = (2 * t + h) % 2
                        ti = ot * 2 + h
                        t_x = p.dma("sync", lambda e, ot=ot, h=h, sl=sl:
                                    e.dma_start(
                                        out=xr_sb[:, sl * 512:(sl + 1) * 512],
                                        in_=x1_d[ot * 128:(ot + 1) * 128,
                                                 h * 512:(h + 1) * 512]),
                                    after=[xacc_tok[ti], dn_last,
                                           xacc_ld[ti]])
                        t_r = p.comp("vector", lambda e, t=t, h=h, sl=sl:
                                     e.tensor_add(
                                         xo_sb[:, sl * 512:(sl + 1) * 512],
                                         pA[t * 2 + h][:],
                                         xr_sb[:, sl * 512:(sl + 1) * 512]),
                                     after=[t_x, dn_last])
                        t_o = p.dma("sync", lambda e, ot=ot, h=h, sl=sl:
                                    e.dma_start(
                                        out=x1_d[ot * 128:(ot + 1) * 128,
                                                 h * 512:(h + 1) * 512],
                                        in_=xo_sb[:, sl * 512:(sl + 1) * 512]),
                                    after=[t_r])
                        xacc_tok[ti] = t_o
                        xacc_ld[ti] = t_r
                        ev_last = t_r
            compute_done.append(ev_last)

    return xacc_tok[-1]


def _wrap16(mat):
    lin = np.ascontiguousarray(mat.T).reshape(-1)
    w16 = np.ascontiguousarray(lin.reshape(-1, 16).T)
    return np.tile(w16, (8, 1))


def prep_l0_ffn_maps(maps, inputs, pre):
    cb = np.ascontiguousarray(np.asarray(inputs["L0_ffn_codebook"], np.float32))
    ig = np.asarray(inputs["L0_ffn_idx_gate"])
    iu = np.asarray(inputs["L0_ffn_idx_up"])
    idn = np.asarray(inputs["L0_ffn_idx_down"])
    for c in range(NCORES):
        iw = np.zeros((3, NCHUNK, 128, 4096), np.int32)
        idm = np.zeros((3, NCHUNK, 128, 512), np.int32)
        for ch in range(NCHUNK):
            tile = ch * 8 + c
            if tile < NREAL_IB:
                rows = slice(tile * 128, (tile + 1) * 128)
                idm[0, ch] = ig[rows]
                iw[0, ch] = _wrap16(ig[rows])
                idm[1, ch] = iu[rows]
                iw[1, ch] = _wrap16(iu[rows])
            gd0 = ch * 128
            gd1 = min(gd0 + 128, idn.shape[1])
            dtile = np.zeros((4, 128, 128), np.int32)
            for s_ in range(4):
                hr = slice(512 * c + s_ * 128, 512 * c + (s_ + 1) * 128)
                dtile[s_, :, 0:gd1 - gd0] = idn[hr, gd0:gd1]
            dmat = np.ascontiguousarray(
                dtile.transpose(1, 0, 2).reshape(128, 512))
            idm[2, ch] = dmat
            iw[2, ch] = _wrap16(dmat)
        maps[c].update({"cb": cb, pre + "iw": iw, pre + "idm": idm})




def build_mega():
    nc = bass.Bass("TRN2", target_bir_lowering=False, num_swdge_queues=4)

    def dt(name, shape, dtyp, kind):
        return nc.dram_tensor(name, shape, dtyp, kind=kind).ap()

    x_d = dt("x", [D, TPC], F32, "ExternalInput")
    eyeb_d = dt("eyeb", [128, 128], BF16, "ExternalInput")
    nm0 = build_tile_plan(0)[1].shape[0]
    nm1 = build_tile_plan(1)[1].shape[0]

    a0 = dict(
        x=x_d, eyeb=eyeb_d,
        wq=dt("a0_wq", [D, 512], F32, "ExternalInput"),
        wkv=dt("a0_wkv", [D, 256], F32, "ExternalInput"),
        wo=dt("a0_wo", [512, D], F32, "ExternalInput"),
        masks=dt("a0_masks", [nm0, 128, 512], BF16, "ExternalInput"),
        bq=dt("a0_bq", [128, 6], F32, "ExternalInput"),
        bo=dt("a0_bo", [128, 32], F32, "ExternalInput"),
        cos=dt("cosT", [128, S], F32, "ExternalInput"),
        sin=dt("sinSW", [128, S], F32, "ExternalInput"),
        xout=dt("xa", [D, TPC], F32, "Internal"),
        xstage=dt("a0_xstage", [D, TPC], BF16, "Internal"),
        xag=dt("a0_xag", [NCORES * D, TPC], BF16, "Internal"),
        rsi=dt("a0_rsi", [NCORES * D, TPC], F32, "Internal"),
        rso=dt("a0_rso", [D, TPC], F32, "Internal"),
    )
    f0 = dict(
        x=a0["xout"],
        cb=dt("cb", [65536, 8], F32, "ExternalInput"),
        iw=dt("f0_iw", [3, NCHUNK, 128, 4096], I32, "ExternalInput"),
        idm=dt("f0_idm", [3, NCHUNK, 128, 512], I32, "ExternalInput"),
        cbp=dt("f0_cbp", [32768, 128], BF16, "Internal"),
        xout=dt("xb", [D, TPC], F32, "Internal"),
        wgst=[dt(f"f0_wgst{i}", [128, 4096], BF16, "Internal")
              for i in range(NPAR)],
        wust=[dt(f"f0_wust{i}", [128, 4096], BF16, "Internal")
              for i in range(NPAR)],
        wdst=[dt(f"f0_wdst{i}", [512, 1024], BF16, "Internal")
              for i in range(NPAR)],
        wgag=[dt(f"f0_wgag{i}", [1024, 4096], BF16, "Internal")
              for i in range(NPAR)],
        wuag=[dt(f"f0_wuag{i}", [1024, 4096], BF16, "Internal")
              for i in range(NPAR)],
        wdag=[dt(f"f0_wdag{i}", [4096, 1024], BF16, "Internal")
              for i in range(NPAR)],
    )
    a1 = dict(
        x=f0["xout"], eyeb=eyeb_d,
        wq=dt("a1_wq", [D, 512], F32, "ExternalInput"),
        wkv=dt("a1_wkv", [D, 256], F32, "ExternalInput"),
        wo=dt("a1_wo", [512, D], F32, "ExternalInput"),
        masks=dt("a1_masks", [nm1, 128, 512], BF16, "ExternalInput"),
        xout=dt("xc", [D, TPC], F32, "Internal"),
        xstage=dt("a1_xstage", [D, TPC], BF16, "Internal"),
        xag=dt("a1_xag", [NCORES * D, TPC], BF16, "Internal"),
        rsi=dt("a1_rsi", [NCORES * D, TPC], F32, "Internal"),
        rso=dt("a1_rso", [D, TPC], F32, "Internal"),
        kvs=dt("a1_kvs", [256, TPC], BF16, "Internal"),
        kvag=dt("a1_kvag", [NCORES * 256, TPC], BF16, "Internal"),
    )
    f1 = dict(
        x=a1["xout"], eyeb=eyeb_d,
        bw=dt("f1_bw", [D, D], F32, "ExternalInput"),
        bb=dt("f1_bb", [128, 32], F32, "ExternalInput"),
        at=dt("f1_at", [D, 64], F32, "ExternalInput"),
        bs=dt("f1_bs", [1024, D], F32, "ExternalInput"),
        rw=dt("f1_rw", [D, 16], F32, "ExternalInput"),
        xout=dt("x1", [D, TPC], F32, "ExternalOutput"),
    )

    ctx = contextlib.ExitStack()
    with ctx:
        p = Prog(nc, ctx)
        blk = ctx.enter_context(nc.Block())
        p.add("gpsimd", lambda e: e.load_library(mlp))
        emit_attn(p, a0, 0, "a0")
        p.stage_gate()
        emit_l0_ffn(p, f0, "f0")
        p.stage_gate()
        emit_attn(p, a1, 1, "a1")
        p.stage_gate()
        last = emit_l1_ffn(p, f1, "f1")
        p.add("gpsimd", lambda e: e.engine_nop(), after=[last])
        p.emit(blk)
    mybir.codegen_inst_isa_subclasses(nc)
    return nc




def kernel(**inputs):
    x0 = np.asarray(inputs["L0_attn_hidden_states"], np.float32).reshape(S, D)
    eyeb = np.eye(128, dtype=np.float32).astype(BF)
    maps = [{"x": np.ascontiguousarray(x0[core_token_ids(c)].T), "eyeb": eyeb}
            for c in range(NCORES)]
    prep_attn_maps(maps, inputs, 0, "a0_")
    prep_l0_ffn_maps(maps, inputs, "f0_")
    prep_attn_maps(maps, inputs, 1, "a1_")
    prep_l1_ffn_maps(maps, inputs, "f1_")
    nc = build_mega()
    res = run_bass_kernel_spmd(nc, maps, core_ids=list(range(NCORES)))
    out = np.zeros((S, D), dtype=np.float32)
    for c in range(NCORES):
        out[core_token_ids(c)] = res.results[c]["x1"].T
    return out.reshape(1, S, D)


# version 08
# speedup vs baseline: 1.2792x; 1.0042x over previous
import contextlib
import numpy as np
import ml_dtypes
import concourse.bass as bass
import concourse.mybir as mybir
from concourse.bass_utils import run_bass_kernel_spmd
from concourse.library_config import mlp

F32 = mybir.dt.float32
BF16 = mybir.dt.bfloat16
I32 = mybir.dt.int32
I16 = mybir.dt.int16
AF = mybir.ActivationFunctionType
ALU = mybir.AluOpType
AX = mybir.AxisListType

NCORES = 8
S = 8192
D = 4096
TPC = 1024
NDT = D // 128
BF = ml_dtypes.bfloat16
NQW = S // 512


def core_token_ids(c):
    return np.arange(TPC * c, TPC * (c + 1))




def l0_tile_mask(q0, k0):
    k = np.arange(k0, k0 + 128)[:, None]
    q = np.arange(q0, q0 + 512)[None, :]
    qb = q // 64
    kb = k // 64
    act = (kb <= qb) & ((qb - kb < 16) | ((kb + 1) % 8 == 0))
    return act & (k <= q)


def l1_tile_mask(q0, k0):
    k = np.arange(k0, k0 + 128)[:, None]
    q = np.arange(q0, q0 + 512)[None, :]
    return k <= q


def build_tile_plan(layer):
    plan = []
    masks = {}
    order = []
    for qw in range(NQW):
        q0 = 512 * qw
        tiles = []
        for k0 in range(0, q0 + 512, 128):
            m = l0_tile_mask(q0, k0) if layer == 0 else l1_tile_mask(q0, k0)
            if not m.any():
                continue
            if m.all():
                tiles.append((k0, None))
            else:
                key = m.tobytes()
                if key not in masks:
                    masks[key] = len(order)
                    order.append(m)
                tiles.append((k0, masks[key]))
        plan.append(tiles)
    marr = np.stack(order).astype(BF) if order else np.zeros((1, 128, 512), BF)
    return plan, marr


def rope_tables():
    hd = 128
    inv = 1.0 / (1000000.0 ** (np.arange(0, hd, 2, dtype=np.float64) / hd))
    t = np.arange(S, dtype=np.float64)
    fr = np.outer(t, inv)
    emb = np.concatenate([fr, fr], axis=1)
    cos = np.cos(emb).T.astype(np.float32)
    sin = np.sin(emb).T.astype(np.float32)
    sinS = sin.copy()
    sinS[:64] *= -1.0
    return np.ascontiguousarray(cos), np.ascontiguousarray(sinS)




class Sem:
    def __init__(self, stack, nc, name):
        self.h = stack.enter_context(nc.semaphore(name))
        self.n = 0


class Prog:
    ENGS = ("sync", "tensor", "vector", "scalar", "gpsimd")

    def __init__(self, nc, stack):
        self.nc = nc
        self.stack = stack
        self.ops = {e: [] for e in self.ENGS}
        self.sems = {}
        self.gate = ()
        self.gated = set()

    def sem(self, name):
        if name not in self.sems:
            self.sems[name] = Sem(self.stack, self.nc, name)
        return self.sems[name]

    def stage_gate(self):
        self.gate = tuple((nm, s.n) for nm, s in self.sems.items() if s.n > 0)
        self.gated = set()

    def add(self, eng, fn, after=(), inc=None, k=1, serial=True):
        if self.gate and eng not in self.gated:
            self.gated.add(eng)
            after = tuple(t for t in after if t is not None) + self.gate
        s = self.sem(inc) if inc else None
        tok = None
        extra = ()
        if s is not None:
            if serial and s.n > 0:
                extra = ((inc, s.n),)
            s.n += k
            tok = (inc, s.n)
        after = tuple(t for t in after if t is not None) + extra
        self.ops[eng].append((after, fn, s, k))
        return tok

    def dma(self, eng, fn, after=()):
        return self.add(eng, fn, after, inc="dma_" + eng, k=16)

    def comp(self, eng, fn, after=()):
        return self.add(eng, fn, after, inc="c_" + eng, k=1)

    def barrier_tok(self, eng_sem):
        return (eng_sem, self.sem(eng_sem).n)

    def emit(self, blk):
        def make_body(name):
            def body(eng):
                wm = {}
                own = "c_" + name
                for (waits, fn, s, k) in self.ops[name]:
                    for (snm, val) in waits:
                        if name == "tensor" and snm == own:
                            continue
                        if wm.get(snm, 0) >= val:
                            continue
                        eng.wait_ge(self.sems[snm].h, val)
                        wm[snm] = val
                    ins = fn(eng)
                    if s is not None:
                        assert ins is not None, f"op on {name} returned None"
                        ins.then_inc(s.h, k)
            return body

        for name in self.ENGS:
            if self.ops[name]:
                getattr(blk, name)(make_body(name))




def emit_attn(p, io, layer, pre):
    plan, masks_np = build_tile_plan(layer)
    nm = masks_np.shape[0]
    sm_scale = (1.0 / 128.0) if layer == 0 else float(1.0 / np.sqrt(128.0))

    nc = p.nc
    x_d = io["x"]
    wq_d = io["wq"]
    wkv_d = io["wkv"]
    wo_d = io["wo"]
    masks_d = io["masks"]
    eyeb_d = io["eyeb"]
    x1_d = io["xout"]
    xstage_d = io["xstage"]
    xag_d = io["xag"]
    rsi_d = io["rsi"]
    rso_d = io["rso"]
    if layer == 0:
        bq_d = io["bq"]
        bo_d = io["bo"]
        cos_d = io["cos"]
        sin_d = io["sin"]
    else:
        kvstage_d = io["kvs"]
        kvag_d = io["kvag"]

    ctx = contextlib.ExitStack()
    with ctx:
        sb = lambda name, shape, dt: ctx.enter_context(
            nc.sbuf_tensor(pre + name, shape, dt))
        xw_sb = sb("xw_sb", [128, NDT * 512], BF16)
        K_sb = sb("K_sb", [128, S], BF16)
        V_sb = sb("V_sb", [128, S], BF16)
        wqb_sb = sb("wqb_sb", [128, NDT * 512], BF16)
        wkvb_sb = sb("wkvb_sb", [128, NDT * 256], BF16)
        masks_sb = sb("masks_sb", [128, nm * 512], BF16)
        eyeb_sb = sb("eyeb_sb", [128, 128], BF16)
        onesf_sb = sb("onesf_sb", [128, 128], F32)
        onesb_sb = sb("onesb_sb", [128, 4], BF16)
        xt_sb = sb("xt_sb", [128, 2 * TPC], F32)
        xb_sb = sb("xb_sb", [128, 2 * TPC], BF16)
        sq_sb = sb("sq_sb", [128, TPC], F32)
        stat_sb = sb("stat_sb", [1, 4 * TPC], F32)
        bc_sb = sb("bc_sb", [128, 2 * TPC], F32)
        if layer == 0:
            bq_sb = sb("bq_sb", [128, 6], F32)
            bo_sb = sb("bo_sb", [128, 32], F32)
            cosw_sb = sb("cosw_sb", [128, 512], F32)
            sinw_sb = sb("sinw_sb", [128, 512], F32)
        wot_sb = sb("wot_sb", [128, 2 * 512], BF16)
        wotf_sb = sb("wotf_sb", [128, 2 * 512], F32)
        po_sb = sb("po_sb", [128, 2 * 512], F32)
        ctxh_sb = sb("ctxh_sb", [128, 4 * 512], BF16)
        kvf_sb = sb("kvf_sb", [128, 2 * 512], F32)
        vb_sb = sb("vb_sb", [128, 512], BF16)
        q_sb = sb("q_sb", [128, 4 * 512], BF16)
        qf_sb = sb("qf_sb", [128, 512], F32)
        kn_sb = sb("kn_sb", [1, 64], F32)
        nrm_sb = sb("nrm_sb", [1, 16], F32)
        cc_sb = sb("cc_sb", [128, 1], F32)
        pt_sb = sb("pt_sb", [128, 2 * 512], BF16)
        den_sb = sb("den_sb", [1, 512], F32)
        ctxf_sb = sb("ctxf_sb", [128, 512], F32)
        xr_sb = sb("xr_sb", [128, 2 * 512], F32)
        xo_sb = sb("xo_sb", [128, 2 * 512], F32)

        psb = lambda name, shape, dt: ctx.enter_context(
            nc.psum_tensor(pre + name, shape, dt))
        pb = psb("pb", [128, TPC], F32)
        pA = [psb(f"pA{i}", [128, 512], F32) for i in range(5)]
        ptr = psb("ptr", [128, 512], BF16)

        rg = [list(range(NCORES))]

        for n in range(nm):
            p.dma("sync", lambda e, n=n: e.dma_start(
                out=masks_sb[:, n * 512:(n + 1) * 512], in_=masks_d[n]))
        t_eye = p.dma("sync", lambda e: e.dma_start(out=eyeb_sb[:], in_=eyeb_d))
        if layer == 0:
            p.dma("sync", lambda e: e.dma_start(out=bq_sb[:], in_=bq_d))
            t_bo = p.dma("sync", lambda e: e.dma_start(out=bo_sb[:], in_=bo_d))
        else:
            t_bo = None
        t_const = p.barrier_tok("dma_sync")
        t_ones = p.comp("vector", lambda e: e.memset(onesf_sb[:], 1.0))
        t_onesb = p.comp("vector", lambda e: e.memset(onesb_sb[:], 1.0))

        pm0, pm1, pv0, pv1 = pA[0], pA[1], pA[2], pA[3]
        mm_tok = None
        slot_free = [None, None]
        for ic in range(NDT):
            sl = ic % 2
            t_ld = p.dma("sync", lambda e, ic=ic, sl=sl: e.dma_start(
                out=xt_sb[:, sl * TPC:(sl + 1) * TPC],
                in_=x_d[ic * 128:(ic + 1) * 128, :]), after=[slot_free[sl]])
            t_sq = p.comp("scalar", lambda e, sl=sl: e.activation(
                sq_sb[:], xt_sb[:, sl * TPC:(sl + 1) * TPC], AF.Square),
                after=[t_ld, mm_tok])
            for h in range(2):
                p.comp("tensor", lambda e, sl=sl, h=h, ic=ic: e.matmul(
                    (pm0 if h == 0 else pm1)[0:1, :], onesf_sb[:, 0:1],
                    xt_sb[:, sl * TPC + h * 512:sl * TPC + (h + 1) * 512],
                    start=(ic == 0), stop=(ic == NDT - 1)),
                    after=[t_ld, t_ones])
                mm_tok = p.comp("tensor", lambda e, h=h, ic=ic: e.matmul(
                    (pv0 if h == 0 else pv1)[0:1, :], onesf_sb[:, 0:1],
                    sq_sb[:, h * 512:(h + 1) * 512],
                    start=(ic == 0), stop=(ic == NDT - 1)),
                    after=[t_sq])
            slot_free[sl] = mm_tok
        m_ = stat_sb[0:1, 0:TPC]
        msq = stat_sb[0:1, TPC:2 * TPC]
        m2 = stat_sb[0:1, 2 * TPC:3 * TPC]
        var = stat_sb[0:1, 3 * TPC:4 * TPC]
        r_ = stat_sb[0:1, 3 * TPC:4 * TPC]
        mr = stat_sb[0:1, 2 * TPC:3 * TPC]
        t1 = p.comp("scalar", lambda e: e.activation(
            m_[:, 0:512], pm0[0:1, :], AF.Copy, scale=1.0 / D), after=[mm_tok])
        t1 = p.comp("scalar", lambda e: e.activation(
            m_[:, 512:1024], pm1[0:1, :], AF.Copy, scale=1.0 / D), after=[t1])
        t2 = p.comp("scalar", lambda e: e.activation(
            msq[:, 0:512], pv0[0:1, :], AF.Copy, scale=1.0 / D), after=[t1])
        t2 = p.comp("scalar", lambda e: e.activation(
            msq[:, 512:1024], pv1[0:1, :], AF.Copy, scale=1.0 / D), after=[t2])
        t3 = p.comp("scalar", lambda e: e.activation(m2[:], m_[:], AF.Square),
                    after=[t2])
        t4 = p.comp("vector", lambda e: e.tensor_sub(var[:], msq[:], m2[:]),
                    after=[t3])
        t4b = p.comp("vector", lambda e: e.tensor_scalar_add(var[:], var[:],
                                                             1e-5), after=[t4])
        t5 = p.comp("scalar", lambda e: e.activation(var[:], var[:], AF.Sqrt),
                    after=[t4b])
        t6 = p.comp("vector", lambda e: e.reciprocal(r_[:], var[:]), after=[t5])
        t7 = p.comp("vector", lambda e: e.tensor_mul(mr[:], m_[:], r_[:]),
                    after=[t6])
        bts = [p.comp("tensor", lambda e, h=h: e.matmul(
            pb[:, h * 512:(h + 1) * 512], onesf_sb[0:1, :],
            r_[:, h * 512:(h + 1) * 512], start=True, stop=True),
            after=[t6]) for h in range(2)]
        t8 = p.comp("scalar", lambda e: e.activation(bc_sb[:, 0:TPC], pb[:],
                                                     AF.Copy), after=bts)
        bts2 = [p.comp("tensor", lambda e, h=h: e.matmul(
            pb[:, h * 512:(h + 1) * 512], onesf_sb[0:1, :],
            mr[:, h * 512:(h + 1) * 512], start=True, stop=True),
            after=[t7, t8]) for h in range(2)]
        t9 = p.comp("scalar", lambda e: e.activation(bc_sb[:, TPC:2 * TPC], pb[:],
                                                     AF.Copy), after=bts2)
        rb = bc_sb[:, 0:TPC]
        mrb = bc_sb[:, TPC:2 * TPC]
        xst_toks = []
        slot_free = [None, None]
        for ic in range(NDT):
            sl = ic % 2
            t_ld = p.dma("sync", lambda e, ic=ic, sl=sl: e.dma_start(
                out=xt_sb[:, sl * TPC:(sl + 1) * TPC],
                in_=x_d[ic * 128:(ic + 1) * 128, :]), after=[t8, slot_free[sl]])
            tm = p.comp("vector", lambda e, sl=sl: e.tensor_mul(
                xt_sb[:, sl * TPC:(sl + 1) * TPC],
                xt_sb[:, sl * TPC:(sl + 1) * TPC], rb), after=[t_ld, t8])
            tsb = p.comp("vector", lambda e, sl=sl: e.tensor_sub(
                xb_sb[:, sl * TPC:(sl + 1) * TPC],
                xt_sb[:, sl * TPC:(sl + 1) * TPC], mrb), after=[tm, t9])
            tst = p.dma("scalar", lambda e, ic=ic, sl=sl: e.dma_start(
                out=xstage_d[ic * 128:(ic + 1) * 128, :],
                in_=xb_sb[:, sl * TPC:(sl + 1) * TPC]), after=[tsb])
            xst_toks.append(tst)
            slot_free[sl] = tst
        t_agx = p.add("gpsimd", lambda e: e.collective_compute(
            "AllGather", ALU.bypass, rg, [xstage_d], [xag_d]),
            after=[xst_toks[-1]], inc="ag", k=1)
        t_ln_done = t9

        def stream_slab_to(w_dram, col0, ncols, dst_sb, dst_off, after=()):
            return p.dma("gpsimd", lambda e: e.dma_start(
                out=dst_sb[:, dst_off:dst_off + NDT * ncols].rearrange(
                    "p (a o) -> p a o", o=ncols),
                in_=w_dram[:, col0:col0 + ncols].rearrange(
                    "(a p) o -> p a o", p=128)), after=after)

        wq_toks = []
        prev = None
        for j in range(4):
            prev = stream_slab_to(wq_d, j * 128, 128, wqb_sb, j * NDT * 128,
                                  after=[prev])
            wq_toks.append(prev)
        t_wkv = stream_slab_to(wkv_d, 0, 256, wkvb_sb, 0, after=[prev])

        def wq_tile(j, ic):
            return wqb_sb[:, j * NDT * 128 + ic * 128:j * NDT * 128 + (ic + 1) * 128]

        def wkv_tile(which, ic):
            o = ic * 256 + which * 128
            return wkvb_sb[:, o:o + 128]

        kv_ready = None
        kn_done = {}
        xw_init_gate = None
        if layer == 1:
            own_done = []
            ldt_last = None
            for h in range(2):
                ldt = []
                for ic in range(NDT):
                    ldt.append(p.dma("sync", lambda e, ic=ic, h=h: e.dma_start(
                        out=xw_sb[:, ic * 512:(ic + 1) * 512],
                        in_=xstage_d[ic * 128:(ic + 1) * 128,
                                     h * 512:(h + 1) * 512]),
                        after=[xst_toks[ic],
                               own_done[-1] if own_done else None]))
                mk = mv = None
                for ic in range(NDT):
                    mk = p.comp("tensor", lambda e, ic=ic: e.matmul(
                        pA[0][:], wkv_tile(0, ic),
                        xw_sb[:, ic * 512:(ic + 1) * 512],
                        start=(ic == 0), stop=(ic == NDT - 1)),
                        after=[t_wkv, t_ln_done, ldt[ic]] if ic == 0
                        else [ldt[ic]])
                for ic in range(NDT):
                    mv = p.comp("tensor", lambda e, ic=ic: e.matmul(
                        pA[1][:], wkv_tile(1, ic),
                        xw_sb[:, ic * 512:(ic + 1) * 512],
                        start=(ic == 0), stop=(ic == NDT - 1)), after=[])
                tk = p.comp("scalar", lambda e, h=h: e.activation(
                    xb_sb[:, h * 512:(h + 1) * 512], pA[0][:], AF.Copy),
                    after=[mk, mv])
                tv = p.comp("scalar", lambda e, h=h: e.activation(
                    xb_sb[:, TPC + h * 512:TPC + (h + 1) * 512], pA[1][:],
                    AF.Copy), after=[tk])
                own_done.append(tv)
                ldt_last = ldt[-1]
            tsk = p.dma("scalar", lambda e: e.dma_start(
                out=kvstage_d[0:128, :], in_=xb_sb[:, 0:TPC]), after=own_done)
            tsv = p.dma("scalar", lambda e: e.dma_start(
                out=kvstage_d[128:256, :], in_=xb_sb[:, TPC:2 * TPC]),
                after=own_done)
            t_agkv = p.add("gpsimd", lambda e: e.collective_compute(
                "AllGather", ALU.bypass, rg, [kvstage_d], [kvag_d]),
                after=[tsk, tsv], inc="ag", k=1)
            kl = []
            for r in range(NCORES):
                kl.append(p.dma("sync", lambda e, r=r: e.dma_start(
                    out=K_sb[:, r * TPC:(r + 1) * TPC],
                    in_=kvag_d[r * 256:r * 256 + 128, :]), after=[t_agkv]))
                kl.append(p.dma("sync", lambda e, r=r: e.dma_start(
                    out=V_sb[:, r * TPC:(r + 1) * TPC],
                    in_=kvag_d[r * 256 + 128:r * 256 + 256, :]),
                    after=[t_agkv]))
            tprev = None
            for kt in range(64):
                ta = p.comp("tensor", lambda e, kt=kt: e.transpose(
                    ptr[:, 0:128], V_sb[:, kt * 128:(kt + 1) * 128], eyeb_sb[:]),
                    after=[kl[2 * (kt // 8) + 1], t_eye, tprev])
                tprev = p.comp("vector", lambda e, kt=kt: e.tensor_copy(
                    V_sb[:, kt * 128:(kt + 1) * 128], ptr[:, 0:128]), after=[ta])
            kv_ready = tprev
            knt = None
            for kt in range(64):
                tsq2 = p.comp("scalar", lambda e, kt=kt: e.activation(
                    sq_sb[:, 0:128], K_sb[:, kt * 128:(kt + 1) * 128],
                    AF.Square), after=[kl[2 * (kt // 8)], knt])
                tm1 = p.comp("tensor", lambda e: e.matmul(
                    pA[4][0:1, 0:128], onesf_sb[:, 0:1], sq_sb[:, 0:128],
                    start=True, stop=True), after=[tsq2])
                knt = p.comp("vector", lambda e, kt=kt: e.tensor_reduce(
                    kn_sb[0:1, kt:kt + 1], pA[4][0:1, 0:128], AX.X, ALU.max),
                    after=[tm1])
                kn_done[kt] = knt
            xw_init_gate = ldt_last
            xw_init_gate = tsv

        NH = 4
        kwin_done = {}
        if layer == 1:
            for w in range(NQW):
                kwin_done[w] = kv_ready
        xw_free = xw_init_gate
        prev_qnorm = None
        prev_qproj = None
        qevac_prev = [None, None]
        prev_cx0 = kn_done.get(63) if layer == 1 else None
        prev_ctx_st = None
        prev_cf = None
        prev_cx = kn_done.get(63) if layer == 1 else None
        prev_d1 = None
        prev_d3 = None
        for qw in range(NQW):
            r = qw // 2
            hh = qw % 2
            t_ldw = p.dma("sync", lambda e, r=r, hh=hh: e.dma_start(
                out=xw_sb[:].rearrange("p (a t) -> p a t", t=512),
                in_=xag_d[r * D:(r + 1) * D,
                          hh * 512:(hh + 1) * 512].rearrange(
                    "(a p) t -> p a t", p=128)),
                after=[t_agx, xw_free])
            ldw = [t_ldw] * NDT
            if layer == 0:
                t_cs = p.dma("sync", lambda e, qw=qw: e.dma_start(
                    out=cosw_sb[:], in_=cos_d[:, qw * 512:(qw + 1) * 512]),
                    after=[prev_qproj, t_const])
                t_sn = p.dma("sync", lambda e, qw=qw: e.dma_start(
                    out=sinw_sb[:], in_=sin_d[:, qw * 512:(qw + 1) * 512]),
                    after=[prev_qproj])
                mk = mv = None
                for ic in range(NDT):
                    mk = p.comp("tensor", lambda e, ic=ic: e.matmul(
                        pA[0][:], wkv_tile(0, ic),
                        xw_sb[:, ic * 512:(ic + 1) * 512],
                        start=(ic == 0), stop=(ic == NDT - 1)),
                        after=[t_wkv, t_ln_done, ldw[ic], prev_ctx_st]
                        if ic == 0 else [ldw[ic]])
                for ic in range(NDT):
                    mv = p.comp("tensor", lambda e, ic=ic: e.matmul(
                        pA[1][:], wkv_tile(1, ic),
                        xw_sb[:, ic * 512:(ic + 1) * 512],
                        start=(ic == 0), stop=(ic == NDT - 1)), after=[])
                tke = p.comp("scalar", lambda e: e.activation(
                    kvf_sb[:, 0:512], pA[0][:], AF.Identity, bias=bq_sb[:, 4:5]),
                    after=[mk, mv])
                tr1 = p.comp("vector", lambda e: e.tensor_mul(
                    kvf_sb[:, 512:1024], kvf_sb[:, 0:512], cosw_sb[:]),
                    after=[tke, t_cs])
                tr2 = p.comp("vector", lambda e: e.tensor_mul(
                    qf_sb[0:64, :], kvf_sb[64:128, 0:512], sinw_sb[64:128, :]),
                    after=[tke, t_sn])
                tr3 = p.comp("vector", lambda e: e.tensor_mul(
                    qf_sb[64:128, :], kvf_sb[0:64, 0:512], sinw_sb[0:64, :]),
                    after=[tke])
                tk_bf = p.comp("vector", lambda e, qw=qw: e.tensor_add(
                    K_sb[:, qw * 512:(qw + 1) * 512], kvf_sb[:, 512:1024],
                    qf_sb[:, 0:512]), after=[tr1, tr2, tr3])
                tve = p.comp("scalar", lambda e: e.activation(
                    vb_sb[:], pA[1][:], AF.Identity, bias=bq_sb[:, 5:6]),
                    after=[tke])
                tprev = None
                for j in range(4):
                    ta = p.comp("tensor", lambda e, j=j: e.transpose(
                        ptr[:, 0:128], vb_sb[:, j * 128:(j + 1) * 128],
                        eyeb_sb[:]), after=[tve, t_eye, tprev])
                    tprev = p.comp("vector", lambda e, j=j, qw=qw: e.tensor_copy(
                        V_sb[:, qw * 512 + j * 128:qw * 512 + (j + 1) * 128],
                        ptr[:, 0:128]), after=[ta])
                for j in range(4):
                    kt = qw * 4 + j
                    tsq2 = p.comp("scalar", lambda e, qw=qw, j=j: e.activation(
                        sq_sb[:, 0:128],
                        K_sb[:, qw * 512 + j * 128:qw * 512 + (j + 1) * 128],
                        AF.Square),
                        after=[tk_bf, kn_done.get(kt - 1), prev_qnorm])
                    tm1 = p.comp("tensor", lambda e: e.matmul(
                        pA[4][0:1, 0:128], onesf_sb[:, 0:1], sq_sb[:, 0:128],
                        start=True, stop=True), after=[tsq2, prev_cx])
                    kn_done[kt] = p.comp(
                        "vector", lambda e, kt=kt: e.tensor_reduce(
                            kn_sb[0:1, kt:kt + 1], pA[4][0:1, 0:128], AX.X,
                            ALU.max), after=[tm1])
                kwin_done[qw] = tprev

            q_toks = []
            qn_toks = []
            for j in range(NH):
                qh = j % 2
                mq = None
                for ic in range(NDT):
                    mq = p.comp("tensor", lambda e, j=j, ic=ic, qh=qh: e.matmul(
                        pb[:, qh * 512:(qh + 1) * 512], wq_tile(j, ic),
                        xw_sb[:, ic * 512:(ic + 1) * 512],
                        start=(ic == 0), stop=(ic == NDT - 1)),
                        after=[wq_toks[j], t_ln_done, ldw[ic], prev_qproj,
                               qevac_prev[qh]]
                        if ic == 0 else [ldw[ic]])
                if layer == 0:
                    tqe = p.comp("scalar", lambda e, j=j, qh=qh: e.activation(
                        qf_sb[:], pb[:, qh * 512:(qh + 1) * 512], AF.Identity,
                        bias=bq_sb[:, j:j + 1]),
                        after=[mq])
                    qevac_prev[qh] = tqe
                    r1 = p.comp("vector", lambda e: e.tensor_mul(
                        kvf_sb[:, 0:512], qf_sb[:], cosw_sb[:]), after=[tqe])
                    r2 = p.comp("vector", lambda e: e.tensor_mul(
                        kvf_sb[0:64, 512:1024], qf_sb[64:128, :],
                        sinw_sb[64:128, :]), after=[tqe])
                    r3 = p.comp("vector", lambda e: e.tensor_mul(
                        kvf_sb[64:128, 512:1024], qf_sb[0:64, :],
                        sinw_sb[0:64, :]), after=[tqe])
                    tq = p.comp("vector", lambda e, j=j: e.tensor_add(
                        q_sb[:, j * 512:(j + 1) * 512], kvf_sb[:, 0:512],
                        kvf_sb[:, 512:1024]), after=[r1, r2, r3])
                else:
                    tq = p.comp("scalar", lambda e, j=j, qh=qh: e.activation(
                        q_sb[:, j * 512:(j + 1) * 512],
                        pb[:, qh * 512:(qh + 1) * 512], AF.Copy),
                        after=[mq])
                    qevac_prev[qh] = tq
                q_toks.append(tq)
                tsq3 = p.comp("scalar", lambda e, j=j: e.activation(
                    sq_sb[:, 128:640], q_sb[:, j * 512:(j + 1) * 512],
                    AF.Square), after=[tq, qn_toks[-1] if qn_toks else
                                       prev_qnorm])
                tm1 = p.comp("tensor", lambda e: e.matmul(
                    pA[4][0:1, 0:512], onesf_sb[:, 0:1], sq_sb[:, 128:640],
                    start=True, stop=True), after=[tsq3, prev_cx])
                qn_toks.append(p.comp("vector", lambda e, j=j: e.tensor_reduce(
                    nrm_sb[0:1, j:j + 1], pA[4][0:1, 0:512], AX.X, ALU.max),
                    after=[tm1]))
            prev_qproj = q_toks[-1]
            prev_qnorm = qn_toks[-1]

            tiles = plan[qw]
            last_kt = max(k0 for (k0, _m) in tiles) // 128
            tqm = p.comp("vector", lambda e: e.tensor_reduce(
                nrm_sb[0:1, 8:9], nrm_sb[0:1, 0:4], AX.X, ALU.max),
                after=qn_toks)
            tkm = p.comp("vector", lambda e, last_kt=last_kt: e.tensor_reduce(
                nrm_sb[0:1, 9:10], kn_sb[0:1, 0:last_kt + 1], AX.X, ALU.max),
                after=[kn_done[last_kt]])
            tc1 = p.comp("vector", lambda e: e.tensor_mul(
                nrm_sb[0:1, 10:11], nrm_sb[0:1, 8:9], nrm_sb[0:1, 9:10]),
                after=[tqm, tkm])
            tc2 = p.comp("scalar", lambda e: e.activation(
                nrm_sb[0:1, 11:12], nrm_sb[0:1, 10:11], AF.Sqrt), after=[tc1])
            tc3 = p.comp("scalar", lambda e: e.mul(
                nrm_sb[0:1, 12:13], nrm_sb[0:1, 11:12], -1.001 * sm_scale),
                after=[tc2])
            tc4 = p.comp("tensor", lambda e: e.matmul(
                pA[4][:, 0:1], onesf_sb[0:1, :], nrm_sb[0:1, 12:13],
                start=True, stop=True), after=[tc3, prev_cx])
            tc5 = p.comp("scalar", lambda e: e.activation(
                cc_sb[:], pA[4][:, 0:1], AF.Copy), after=[tc4])

            ctx_head_toks = []
            for j in range(NH):
                n_t = len(tiles)
                exp_prev = [None, None]
                pt_readers = [None, None]
                pv_prev = None
                den_prev = None
                kreq = kwin_done[qw]
                for i, (k0, mid) in enumerate(tiles):
                    bk = i % 2
                    t_s = p.comp("tensor", lambda e, k0=k0, bk=bk, j=j: e.matmul(
                        pA[bk][:], K_sb[:, k0:k0 + 128],
                        q_sb[:, j * 512:(j + 1) * 512], start=True, stop=True),
                        after=[exp_prev[bk], q_toks[j], kreq] if i < 2
                        else [exp_prev[bk]])
                    t_e = p.comp("scalar", lambda e, bk=bk: e.activation(
                        pt_sb[:, bk * 512:(bk + 1) * 512], pA[bk][:], AF.Exp,
                        bias=cc_sb[:, 0:1], scale=sm_scale),
                        after=[t_s, tc5, pt_readers[bk]])
                    exp_prev[bk] = t_e
                    t_m = t_e
                    if mid is not None:
                        t_m = p.comp("vector", lambda e, bk=bk, mid=mid:
                                     e.tensor_mul(
                                         pt_sb[:, bk * 512:(bk + 1) * 512],
                                         pt_sb[:, bk * 512:(bk + 1) * 512],
                                         masks_sb[:, mid * 512:(mid + 1) * 512]),
                                     after=[t_e, t_const])
                    t_pv = p.comp("tensor", lambda e, k0=k0, bk=bk, i=i, n_t=n_t:
                                  e.matmul(pA[2][:], V_sb[:, k0:k0 + 128],
                                           pt_sb[:, bk * 512:(bk + 1) * 512],
                                           start=(i == 0), stop=(i == n_t - 1)),
                                  after=[t_m, prev_cf] if i == 0 else [t_m])
                    pv_prev = t_pv
                    den_prev = p.comp("tensor", lambda e, bk=bk, i=i, n_t=n_t:
                                      e.matmul(pA[3][0:1, :], onesb_sb[:, 0:1],
                                               pt_sb[:, bk * 512:(bk + 1) * 512],
                                               start=(i == 0), stop=(i == n_t - 1)),
                                      after=[t_m, t_onesb, prev_d1] if i == 0
                                      else [])
                    pt_readers[bk] = den_prev
                t_d1 = p.comp("scalar", lambda e: e.activation(
                    den_sb[:], pA[3][0:1, :], AF.Copy), after=[den_prev])
                prev_d1 = t_d1
                t_d2 = p.comp("vector", lambda e: e.reciprocal(
                    den_sb[:], den_sb[:]), after=[t_d1])
                t_d3 = p.comp("tensor", lambda e: e.matmul(
                    pA[4][:], onesf_sb[0:1, :], den_sb[:], start=True,
                    stop=True), after=[t_d2, prev_cx])
                prev_d3 = t_d3
                t_cf = p.comp("scalar", lambda e: e.activation(
                    ctxf_sb[:], pA[2][:], AF.Copy), after=[pv_prev, t_d1,
                                                           prev_cx])
                prev_cf = t_cf
                t_cx = p.comp("vector", lambda e, j=j: e.tensor_mul(
                    ctxh_sb[:, j * 512:(j + 1) * 512], ctxf_sb[:], pA[4][:]),
                    after=[t_cf, t_d3, prev_ctx_st])
                prev_cx = t_cx
                ctx_head_toks.append(t_cx)
            st_toks = []
            wslot = [None, None]
            wcv = [None, None]
            pe_prev = [None, None]
            for ot in range(NDT):
                sl = ot % 2
                t_wf = p.dma("sync", lambda e, ot=ot, sl=sl: e.dma_start(
                    out=wotf_sb[:, sl * 512:(sl + 1) * 512].rearrange(
                        "p (j o) -> p j o", o=128),
                    in_=wo_d[:, ot * 128:(ot + 1) * 128].rearrange(
                        "(j p) o -> p j o", p=128)),
                    after=[wcv[sl] if True else None])
                t_wl = p.comp("vector", lambda e, sl=sl: e.tensor_copy(
                    wot_sb[:, sl * 512:(sl + 1) * 512],
                    wotf_sb[:, sl * 512:(sl + 1) * 512]),
                    after=[t_wf, wslot[sl]])
                wcv[sl] = t_wl
                mo = None
                for j in range(4):
                    mo = p.comp("tensor", lambda e, sl=sl, j=j: e.matmul(
                        pb[:, sl * 512:(sl + 1) * 512],
                        wot_sb[:, sl * 512 + j * 128:sl * 512 + (j + 1) * 128],
                        ctxh_sb[:, j * 512:(j + 1) * 512],
                        start=(j == 0), stop=(j == 3)),
                        after=[t_wl, pe_prev[sl]] + ctx_head_toks
                        if j == 0 else [])
                wslot[sl] = mo
                t_pe = p.comp("scalar", lambda e, sl=sl: e.activation(
                    po_sb[:, sl * 512:(sl + 1) * 512],
                    pb[:, sl * 512:(sl + 1) * 512], AF.Copy),
                    after=[mo, st_toks[-2] if len(st_toks) >= 2 else None])
                pe_prev[sl] = t_pe
                t_st = p.dma("scalar", lambda e, qw=qw, ot=ot, sl=sl: e.dma_start(
                    out=rsi_d[(qw // 2) * D + ot * 128:
                              (qw // 2) * D + (ot + 1) * 128,
                              (qw % 2) * 512:(qw % 2 + 1) * 512],
                    in_=po_sb[:, sl * 512:(sl + 1) * 512]), after=[t_pe])
                st_toks.append(t_st)
            prev_ctx_st = st_toks[-1]
            xw_free = prev_ctx_st

        t_rs = p.add("gpsimd", lambda e: e.collective_compute(
            "ReduceScatter", ALU.add, rg, [rsi_d], [rso_d]),
            after=[prev_ctx_st], inc="ag", k=1)

        res_toks = []
        for half in range(2):
            for ot in range(NDT):
                sl = ot % 2
                t_x = p.dma("sync", lambda e, ot=ot, half=half, sl=sl:
                            e.dma_start(
                                out=xr_sb[:, sl * 512:(sl + 1) * 512],
                                in_=x_d[ot * 128:(ot + 1) * 128,
                                        half * 512:(half + 1) * 512]),
                            after=[res_toks[-2] if len(res_toks) >= 2 else None])
                t_p = p.dma("sync", lambda e, ot=ot, half=half, sl=sl:
                            e.dma_start(
                                out=po_sb[:, sl * 512:(sl + 1) * 512],
                                in_=rso_d[ot * 128:(ot + 1) * 128,
                                          half * 512:(half + 1) * 512]),
                            after=[t_rs,
                                   res_toks[-2] if len(res_toks) >= 2 else None])
                if layer == 0:
                    t_r = p.comp("vector", lambda e, ot=ot, sl=sl:
                                 e.scalar_tensor_tensor(
                                     xo_sb[:, sl * 512:(sl + 1) * 512],
                                     po_sb[:, sl * 512:(sl + 1) * 512],
                                     bo_sb[:, ot:ot + 1],
                                     xr_sb[:, sl * 512:(sl + 1) * 512],
                                     ALU.add, ALU.add), after=[t_x, t_p, t_bo])
                else:
                    t_r = p.comp("vector", lambda e, sl=sl: e.tensor_add(
                        xo_sb[:, sl * 512:(sl + 1) * 512],
                        po_sb[:, sl * 512:(sl + 1) * 512],
                        xr_sb[:, sl * 512:(sl + 1) * 512]), after=[t_x, t_p])
                t_o = p.dma("scalar", lambda e, ot=ot, half=half, sl=sl:
                            e.dma_start(
                                out=x1_d[ot * 128:(ot + 1) * 128,
                                         half * 512:(half + 1) * 512],
                                in_=xo_sb[:, sl * 512:(sl + 1) * 512]),
                            after=[t_r])
                res_toks.append(t_o)

    return res_toks[-1]




def prep_attn_maps(maps, inputs, layer, pre):
    if layer == 0:
        wqkvT = np.ascontiguousarray(np.asarray(inputs["L0_attn_w_qkv"]).T)
        bqkv = np.asarray(inputs["L0_attn_b_qkv"])
        wdT = np.asarray(inputs["L0_attn_w_dense"]).T
        boT = np.ascontiguousarray(
            np.asarray(inputs["L0_attn_b_dense"]).reshape(32, 128).T)
        cos, sinS = rope_tables()
    else:
        wq = np.asarray(inputs["L1_attn_wq"])
        wk = np.asarray(inputs["L1_attn_wk"])
        wv = np.asarray(inputs["L1_attn_wv"])
        wo1 = np.asarray(inputs["L1_attn_wo"])
        wkv = np.ascontiguousarray(np.concatenate([wk, wv], axis=1))
    _plan, masks = build_tile_plan(layer)
    for c in range(NCORES):
        m = maps[c]
        wsl = wdT if layer == 0 else wo1
        m[pre + "wo"] = np.ascontiguousarray(wsl[512 * c:512 * (c + 1), :])
        m[pre + "masks"] = masks
        if layer == 0:
            m[pre + "wq"] = np.ascontiguousarray(wqkvT[:, 768 * c:768 * c + 512])
            m[pre + "wkv"] = np.ascontiguousarray(
                wqkvT[:, 768 * c + 512:768 * (c + 1)])
            m[pre + "bq"] = np.ascontiguousarray(
                bqkv[768 * c:768 * (c + 1)].reshape(6, 128).T)
            m[pre + "bo"] = boT
            m["cosT"] = cos
            m["sinSW"] = np.ascontiguousarray(np.roll(sinS, 64, axis=0))
        else:
            m[pre + "wq"] = np.ascontiguousarray(wq[:, 512 * c:512 * (c + 1)])
            m[pre + "wkv"] = wkv




def emit_ln(p, nc, x_d, xt_sb, sq_sb, stat_sb, bc_sb, onesf_sb, pm, pb,
            t_ones, write_fn):
    TPCl = TPC
    mm_tok = None
    slot_free = [None, None]
    for ic in range(NDT):
        sl = ic % 2
        t_ld = p.dma("sync", lambda e, ic=ic, sl=sl: e.dma_start(
            out=xt_sb[:, sl * TPCl:(sl + 1) * TPCl],
            in_=x_d[ic * 128:(ic + 1) * 128, :]), after=[slot_free[sl]])
        t_sq = p.comp("scalar", lambda e, sl=sl: e.activation(
            sq_sb[:, 0:TPCl], xt_sb[:, sl * TPCl:(sl + 1) * TPCl], AF.Square),
            after=[t_ld, mm_tok])
        for h in range(2):
            p.comp("tensor", lambda e, sl=sl, h=h, ic=ic: e.matmul(
                pm[h][0:1, :], onesf_sb[:, 0:1],
                xt_sb[:, sl * TPCl + h * 512:sl * TPCl + (h + 1) * 512],
                start=(ic == 0), stop=(ic == NDT - 1)),
                after=[t_ld, t_ones])
            mm_tok = p.comp("tensor", lambda e, h=h, ic=ic: e.matmul(
                pm[2 + h][0:1, :], onesf_sb[:, 0:1],
                sq_sb[:, h * 512:(h + 1) * 512],
                start=(ic == 0), stop=(ic == NDT - 1)),
                after=[t_sq])
        slot_free[sl] = mm_tok
    m_ = stat_sb[0:1, 0:TPCl]
    msq = stat_sb[0:1, TPCl:2 * TPCl]
    m2 = stat_sb[0:1, 2 * TPCl:3 * TPCl]
    var = stat_sb[0:1, 3 * TPCl:4 * TPCl]
    r_ = stat_sb[0:1, 3 * TPCl:4 * TPCl]
    mr = stat_sb[0:1, 2 * TPCl:3 * TPCl]
    t1 = p.comp("scalar", lambda e: e.activation(
        m_[:, 0:512], pm[0][0:1, :], AF.Copy, scale=1.0 / D), after=[mm_tok])
    t1 = p.comp("scalar", lambda e: e.activation(
        m_[:, 512:1024], pm[1][0:1, :], AF.Copy, scale=1.0 / D), after=[t1])
    t2 = p.comp("scalar", lambda e: e.activation(
        msq[:, 0:512], pm[2][0:1, :], AF.Copy, scale=1.0 / D), after=[t1])
    t2 = p.comp("scalar", lambda e: e.activation(
        msq[:, 512:1024], pm[3][0:1, :], AF.Copy, scale=1.0 / D), after=[t2])
    t3 = p.comp("scalar", lambda e: e.activation(m2[:], m_[:], AF.Square),
                after=[t2])
    t4 = p.comp("vector", lambda e: e.tensor_sub(var[:], msq[:], m2[:]),
                after=[t3])
    t4b = p.comp("vector", lambda e: e.tensor_scalar_add(var[:], var[:], 1e-5),
                 after=[t4])
    t5 = p.comp("scalar", lambda e: e.activation(var[:], var[:], AF.Sqrt),
                after=[t4b])
    t6 = p.comp("vector", lambda e: e.reciprocal(r_[:], var[:]), after=[t5])
    t7 = p.comp("vector", lambda e: e.tensor_mul(mr[:], m_[:], r_[:]),
                after=[t6])
    bts = [p.comp("tensor", lambda e, h=h: e.matmul(
        pb[:, h * 512:(h + 1) * 512], onesf_sb[0:1, :],
        r_[:, h * 512:(h + 1) * 512], start=True, stop=True),
        after=[t6]) for h in range(2)]
    t8 = p.comp("scalar", lambda e: e.activation(bc_sb[:, 0:TPCl], pb[:],
                                                 AF.Copy), after=bts)
    bts2 = [p.comp("tensor", lambda e, h=h: e.matmul(
        pb[:, h * 512:(h + 1) * 512], onesf_sb[0:1, :],
        mr[:, h * 512:(h + 1) * 512], start=True, stop=True),
        after=[t7, t8]) for h in range(2)]
    t9 = p.comp("scalar", lambda e: e.activation(bc_sb[:, TPCl:2 * TPCl], pb[:],
                                                 AF.Copy), after=bts2)
    rb = bc_sb[:, 0:TPCl]
    mrb = bc_sb[:, TPCl:2 * TPCl]
    toks = []
    slot_free = [None, None]
    for ic in range(NDT):
        sl = ic % 2
        t_ld = p.dma("sync", lambda e, ic=ic, sl=sl: e.dma_start(
            out=xt_sb[:, sl * TPCl:(sl + 1) * TPCl],
            in_=x_d[ic * 128:(ic + 1) * 128, :]), after=[t8, slot_free[sl]])
        tm = p.comp("vector", lambda e, sl=sl: e.tensor_mul(
            xt_sb[:, sl * TPCl:(sl + 1) * TPCl],
            xt_sb[:, sl * TPCl:(sl + 1) * TPCl], rb), after=[t_ld, t8])
        tok = write_fn(ic, sl, tm, t9, mrb)
        toks.append(tok)
        slot_free[sl] = tok
    return toks, bts2[-1]




def emit_l1_ffn(p, io, pre):
    nc = p.nc
    x_d = io["x"]
    bw_d = io["bw"]
    bb_d = io["bb"]
    at_d = io["at"]
    bs_d = io["bs"]
    rw_d = io["rw"]
    eyeb_d = io["eyeb"]
    x1_d = io["xout"]

    ctx = contextlib.ExitStack()
    with ctx:
        sb = lambda name, shape, dt: ctx.enter_context(
            nc.sbuf_tensor(pre + name, shape, dt))
        xln_sb = sb("xln_sb", [128, NDT * TPC], BF16)
        bwb_sb = sb("bwb_sb", [128, 2 * NDT * 128], BF16)
        bsb_sb = sb("bsb_sb", [128, 2 * 8 * 128], BF16)
        atb_sb = sb("atb_sb", [128, NDT * 64], BF16)
        rwb_sb = sb("rwb_sb", [128, NDT * 16], BF16)
        eyeb_sb = sb("eyeb_sb", [128, 128], BF16)
        onesf_sb = sb("onesf_sb", [128, 128], F32)
        xt_sb = sb("xt_sb", [128, 2 * TPC], F32)
        sq_sb = sb("sq_sb", [128, TPC], F32)
        stat_sb = sb("stat_sb", [1, 4 * TPC], F32)
        bc_sb = sb("bc_sb", [128, 2 * TPC], F32)
        bb_sb = sb("bb_sb", [128, 32], F32)
        pr_sb = sb("pr_sb", [128, 16], F32)
        sc_sb = sb("sc_sb", [128, 8], F32)
        comba_sb = sb("comba_sb", [128, 8 * 16], BF16)
        rowE_sb = sb("rowE_sb", [1, TPC], F32)
        rhT_sb = sb("rhT_sb", [64, TPC], F32)
        RH_sb = sb("RH_sb", [128, 8 * TPC], BF16)
        xr_sb = sb("xr_sb", [128, 2 * 512], F32)
        xo_sb = sb("xo_sb", [128, 2 * 512], F32)

        psb = lambda name, shape, dt: ctx.enter_context(
            nc.psum_tensor(pre + name, shape, dt))
        pb = psb("pb", [128, TPC], F32)
        pA = [psb(f"pA{i}", [128, 512], F32) for i in range(4)]
        ptr = psb("ptr", [128, 512], BF16)

        t_eye = p.dma("sync", lambda e: e.dma_start(out=eyeb_sb[:], in_=eyeb_d))
        t_bb = p.dma("sync", lambda e: e.dma_start(out=bb_sb[:], in_=bb_d))
        t_ones = p.comp("vector", lambda e: e.memset(onesf_sb[:], 1.0))
        t_at = p.dma("gpsimd", lambda e: e.dma_start(
            out=atb_sb[:].rearrange("p (a o) -> p a o", o=64),
            in_=at_d.rearrange("(a p) o -> p a o", p=128)))
        t_rw = p.dma("gpsimd", lambda e: e.dma_start(
            out=rwb_sb[:].rearrange("p (a o) -> p a o", o=16),
            in_=rw_d.rearrange("(a p) o -> p a o", p=128)))

        def xw(ic, sl, tm, t9, mrb):
            return p.comp("vector", lambda e, ic=ic, sl=sl: e.tensor_sub(
                xln_sb[:, ic * TPC:(ic + 1) * TPC],
                xt_sb[:, sl * TPC:(sl + 1) * TPC], mrb), after=[tm, t9])

        xln_toks, t_lnmm = emit_ln(p, nc, x_d, xt_sb, sq_sb, stat_sb, bc_sb,
                                   onesf_sb, pA, pb, t_ones, xw)
        xln_done = xln_toks[-1]

        def xln(ic):
            return xln_sb[:, ic * TPC:(ic + 1) * TPC]

        mrh = None
        for h in range(2):
            for ic in range(NDT):
                mrh = p.comp("tensor", lambda e, ic=ic, h=h: e.matmul(
                    pb[0:64, h * 512:(h + 1) * 512],
                    atb_sb[:, ic * 64:(ic + 1) * 64],
                    xln(ic)[:, h * 512:(h + 1) * 512],
                    start=(ic == 0), stop=(ic == NDT - 1)),
                    after=[t_at, xln_done, t_lnmm] if (ic == 0 and h == 0)
                    else [])
        t_rh = p.comp("scalar", lambda e: e.activation(rhT_sb[:], pb[0:64, :],
                                                       AF.Copy), after=[mrh])

        tr_toks = []
        for tt in range(8):
            mro = None
            for ic in range(NDT):
                mro = p.comp("tensor", lambda e, ic=ic, tt=tt: e.matmul(
                    pA[0][:, 0:16],
                    xln_sb[:, ic * TPC + tt * 128:ic * TPC + (tt + 1) * 128],
                    rwb_sb[:, ic * 16:(ic + 1) * 16],
                    start=(ic == 0), stop=(ic == NDT - 1)),
                    after=[t_rw, xln_done, t_rh,
                           tr_toks[-1] if tr_toks else None]
                    if ic == 0 else [])
            c0 = p.comp("scalar", lambda e: e.activation(
                pr_sb[:], pA[0][:, 0:16], AF.Copy),
                after=[mro, tr_toks[-1] if tr_toks else None])
            c1 = p.comp("vector", lambda e: e.tensor_reduce(
                sc_sb[:, 0:1], pr_sb[:], AX.X, ALU.max), after=[c0])
            c2 = p.comp("vector", lambda e: e.tensor_scalar_mul(
                sc_sb[:, 1:2], sc_sb[:, 0:1], -1.0), after=[c1])
            c3 = p.comp("scalar", lambda e: e.activation(
                pr_sb[:], pr_sb[:], AF.Exp, bias=sc_sb[:, 1:2]), after=[c2])
            c4 = p.comp("vector", lambda e: e.tensor_reduce(
                sc_sb[:, 2:3], pr_sb[:], AX.X, ALU.add), after=[c3])
            c5 = p.comp("vector", lambda e: e.reciprocal(
                sc_sb[:, 3:4], sc_sb[:, 2:3]), after=[c4])
            c6 = p.comp("vector", lambda e: e.tensor_scalar_mul(
                pr_sb[:], pr_sb[:], sc_sb[:, 3:4]), after=[c5])
            c7 = p.comp("vector", lambda e: e.tensor_reduce(
                sc_sb[:, 0:1], pr_sb[:], AX.X, ALU.max), after=[c6])
            c8 = p.comp("vector", lambda e: e.tensor_scalar(
                sq_sb[:, 0:16], pr_sb[:], sc_sb[:, 0:1], None, ALU.is_equal),
                after=[c7])
            c9 = p.comp("vector", lambda e: e.scalar_tensor_tensor(
                sq_sb[:, 16:32], sq_sb[:, 0:16], -2.0, pr_sb[:],
                ALU.mult, ALU.add), after=[c8])
            c10 = p.comp("vector", lambda e: e.tensor_reduce(
                sc_sb[:, 1:2], sq_sb[:, 16:32], AX.X, ALU.max), after=[c9])
            c11 = p.comp("vector", lambda e: e.tensor_scalar(
                sq_sb[:, 32:48], pr_sb[:], sc_sb[:, 1:2], None, ALU.is_equal),
                after=[c10])
            c12 = p.comp("vector", lambda e: e.tensor_add(
                sq_sb[:, 0:16], sq_sb[:, 0:16], sq_sb[:, 32:48]), after=[c11])
            c13 = p.comp("vector", lambda e: e.tensor_add(
                sc_sb[:, 4:5], sc_sb[:, 0:1], sc_sb[:, 1:2]), after=[c10])
            c14 = p.comp("vector", lambda e: e.tensor_scalar_add(
                sc_sb[:, 4:5], sc_sb[:, 4:5], 1e-6), after=[c13])
            c15 = p.comp("vector", lambda e: e.reciprocal(
                sc_sb[:, 5:6], sc_sb[:, 4:5]), after=[c14])
            c16 = p.comp("vector", lambda e: e.tensor_scalar_mul(
                sc_sb[:, 5:6], sc_sb[:, 5:6], 0.5), after=[c15])
            c17 = p.comp("vector", lambda e: e.tensor_mul(
                sq_sb[:, 0:16], sq_sb[:, 0:16], pr_sb[:]), after=[c12, c6])
            c18 = p.comp("vector", lambda e, tt=tt: e.tensor_scalar_mul(
                comba_sb[:, tt * 16:(tt + 1) * 16], sq_sb[:, 0:16],
                sc_sb[:, 5:6]), after=[c16, c17])
            tr_toks.append(c18)

        rh_toks = []
        prev_tr = None
        prev_row = None
        for e_ in range(16):
            row_toks = []
            for tt in range(8):
                ta = p.comp("tensor", lambda e, e_=e_, tt=tt: e.transpose(
                    ptr[0:1, 0:128],
                    comba_sb[:, tt * 16 + e_:tt * 16 + e_ + 1], eyeb_sb[:]),
                    after=[tr_toks[tt], t_eye, prev_tr,
                           rh_toks[-1] if rh_toks else None])
                prev_tr = p.comp("scalar", lambda e, tt=tt: e.activation(
                    rowE_sb[0:1, tt * 128:(tt + 1) * 128], ptr[0:1, 0:128],
                    AF.Copy), after=[ta, prev_row])
                row_toks.append(prev_tr)
            bc1 = p.comp("tensor", lambda e: e.matmul(
                pA[1][0:64, :], onesf_sb[0:1, 0:64], rowE_sb[0:1, 0:512],
                start=True, stop=True), after=row_toks + [prev_row])
            bc2 = p.comp("tensor", lambda e: e.matmul(
                pA[2][0:64, :], onesf_sb[0:1, 0:64], rowE_sb[0:1, 512:1024],
                start=True, stop=True), after=row_toks)
            cc = e_ // 2
            half = (e_ % 2) * 64
            m1 = p.comp("vector", lambda e, cc=cc, half=half: e.tensor_mul(
                RH_sb[half:half + 64, cc * TPC:cc * TPC + 512],
                pA[1][0:64, :], rhT_sb[0:64, 0:512]), after=[bc1, t_rh])
            m2 = p.comp("vector", lambda e, cc=cc, half=half: e.tensor_mul(
                RH_sb[half:half + 64, cc * TPC + 512:(cc + 1) * TPC],
                pA[2][0:64, :], rhT_sb[0:64, 512:1024]), after=[bc2, t_rh])
            prev_row = m2
            rh_toks += [m1, m2]

        res_toks = []
        wslot = [None, None]
        bslot = [None, None]
        for ot in range(NDT):
            sl = ot % 2
            t_wl = p.dma("gpsimd", lambda e, ot=ot, sl=sl: e.dma_start(
                out=bwb_sb[:, sl * NDT * 128:(sl + 1) * NDT * 128].rearrange(
                    "p (a o) -> p a o", o=128),
                in_=bw_d[:, ot * 128:(ot + 1) * 128].rearrange(
                    "(a p) o -> p a o", p=128)), after=[wslot[sl]])
            t_bs = p.dma("gpsimd", lambda e, ot=ot, sl=sl: e.dma_start(
                out=bsb_sb[:, sl * 8 * 128:(sl + 1) * 8 * 128].rearrange(
                    "p (a o) -> p a o", o=128),
                in_=bs_d[:, ot * 128:(ot + 1) * 128].rearrange(
                    "(a p) o -> p a o", p=128)), after=[bslot[sl]])
            tok = None
            for h in range(2):
                for ic in range(NDT):
                    tok = p.comp("tensor", lambda e, sl=sl, ic=ic, h=h: e.matmul(
                        pb[:, h * 512:(h + 1) * 512],
                        bwb_sb[:, sl * NDT * 128 + ic * 128:
                               sl * NDT * 128 + (ic + 1) * 128],
                        xln(ic)[:, h * 512:(h + 1) * 512],
                        start=(ic == 0), stop=False),
                        after=[t_wl, xln_done,
                               res_toks[-2] if len(res_toks) >= 2 else None,
                               rh_toks[-1]]
                        if (ic == 0 and h == 0) else [])
                for cc in range(8):
                    tok = p.comp("tensor", lambda e, sl=sl, cc=cc, h=h: e.matmul(
                        pb[:, h * 512:(h + 1) * 512],
                        bsb_sb[:, sl * 8 * 128 + cc * 128:
                               sl * 8 * 128 + (cc + 1) * 128],
                        RH_sb[:, cc * TPC + h * 512:cc * TPC + (h + 1) * 512],
                        start=False, stop=(cc == 7)),
                        after=[t_bs, rh_toks[-1]] if (cc == 0 and h == 0)
                        else [])
            wslot[sl] = tok
            bslot[sl] = tok
            t_r_prev = None
            for half in range(2):
                t_x = p.dma("sync", lambda e, ot=ot, half=half, sl=sl:
                            e.dma_start(
                                out=xr_sb[:, sl * 512:(sl + 1) * 512],
                                in_=x_d[ot * 128:(ot + 1) * 128,
                                        half * 512:(half + 1) * 512]),
                            after=[t_r_prev,
                                   res_toks[-2] if len(res_toks) >= 2
                                   else None])
                t_r = p.comp("vector", lambda e, ot=ot, half=half, sl=sl:
                             e.scalar_tensor_tensor(
                                 xo_sb[:, sl * 512:(sl + 1) * 512],
                                 pb[:, half * 512:(half + 1) * 512],
                                 bb_sb[:, ot:ot + 1],
                                 xr_sb[:, sl * 512:(sl + 1) * 512],
                                 ALU.add, ALU.add),
                             after=[tok, t_x, t_bb,
                                    res_toks[-1] if res_toks else None])
                t_r_prev = t_r
                t_o = p.dma("gpsimd", lambda e, ot=ot, half=half, sl=sl:
                            e.dma_start(
                                out=x1_d[ot * 128:(ot + 1) * 128,
                                         half * 512:(half + 1) * 512],
                                in_=xo_sb[:, sl * 512:(sl + 1) * 512]),
                            after=[t_r])
                res_toks.append(t_o)
    return res_toks[-1]


def prep_l1_ffn_maps(maps, inputs, pre):
    bwT = np.ascontiguousarray(np.asarray(inputs["L1_ffn_base_w"]).T)
    bbT = np.ascontiguousarray(np.asarray(inputs["L1_ffn_base_b"]).reshape(32, 128).T)
    AT = np.ascontiguousarray(np.asarray(inputs["L1_ffn_A"]).T)
    BsT = np.ascontiguousarray(
        np.asarray(inputs["L1_ffn_B"]).transpose(0, 2, 1).reshape(1024, D))
    rwT = np.ascontiguousarray(np.asarray(inputs["L1_ffn_router_w"]).T)
    for c in range(NCORES):
        maps[c].update({pre + "bw": bwT, pre + "bb": bbT, pre + "at": AT,
                        pre + "bs": BsT, pre + "rw": rwT})



NCHUNK = 11
NREAL_IB = 86


NPAR = 3


def emit_l0_ffn(p, io, pre, n_chunks=NCHUNK):
    nc = p.nc
    x_d = io["x"]
    cb_d = io["cb"]
    iw_d = io["iw"]
    idm_d = io["idm"]
    x1_d = io["xout"]
    cbp_d = io["cbp"]
    wgst_d = io["wgst"]
    wust_d = io["wust"]
    wdst_d = io["wdst"]
    wgag_d = io["wgag"]
    wuag_d = io["wuag"]
    wdag_d = io["wdag"]

    ctx = contextlib.ExitStack()
    with ctx:
        sb = lambda name, shape, dt: ctx.enter_context(
            nc.sbuf_tensor(pre + name, shape, dt))
        xln_sb = sb("xln_sb", [128, NDT * TPC], BF16)
        ggb_sb = sb("ggb_sb", [128, 4096], BF16)
        gub_sb = sb("gub_sb", [128, 4096], BF16)
        dstf_sb = sb("dstf_sb", [128, 4, 32, 128], BF16)
        i16w_sb = sb("i16w_sb", [128, 4096], I16)
        i32w_sb = sb("i32w_sb", [128, 2048], I32)
        i32d_sb = sb("i32d_sb", [128, 512], I32)
        mk_sb = sb("mk_sb", [128, 3 * 512], F32)
        tmpf_sb = sb("tmpf_sb", [128, 32, 8], F32)
        wdT_sb = sb("wdT_sb", [128, 2 * 384], BF16)
        h_sb = sb("h_sb", [128, 8 * TPC], BF16)
        hg_sb = sb("hg_sb", [128, TPC], BF16)
        onesf_sb = sb("onesf_sb", [128, 128], F32)
        xt_sb = sb("xt_sb", [128, 2 * TPC], F32)
        lnctx = contextlib.ExitStack()
        sq_sb = lnctx.enter_context(nc.sbuf_tensor(pre + "sq_sb", [128, TPC], F32))
        stat_sb = lnctx.enter_context(
            nc.sbuf_tensor(pre + "stat_sb", [1, 4 * TPC], F32))
        bc_sb = lnctx.enter_context(
            nc.sbuf_tensor(pre + "bc_sb", [128, 2 * TPC], F32))
        lnctx.close()
        wgT_sb = sb("wgT_sb", [128, 2, NDT, 128], BF16)
        wuT_sb = sb("wuT_sb", [128, 2, NDT, 128], BF16)
        ggT_sb = sb("ggT_sb", [128, 32, 128], BF16)
        xr_sb = xt_sb[:, 0:TPC]
        xo_sb = xt_sb[:, TPC:2 * TPC]

        psb = lambda name, shape, dt: ctx.enter_context(
            nc.psum_tensor(pre + name, shape, dt))
        pb = psb("pb", [128, TPC], F32)
        pA = [psb(f"pA{i}", [128, 512], F32) for i in range(6)]

        rg = [list(range(NCORES))]

        t_ones = p.comp("vector", lambda e: e.memset(onesf_sb[:], 1.0))

        t_init = p.dma("sync", lambda e: e.dma_start(out=x1_d, in_=x_d))

        t_cbp = None
        for r0 in range(0, 32768, 4096):
            t_cbp = p.dma("gpsimd", lambda e, r0=r0: e.dma_start(
                out=cbp_d[r0:r0 + 4096, 0:16],
                in_=cb_d[2 * r0:2 * r0 + 8192, :].rearrange(
                    "(r t) e -> r (t e)", t=2)), after=[t_cbp])

        def xw(ic, sl, tm, t9, mrb):
            return p.comp("vector", lambda e, ic=ic, sl=sl: e.tensor_sub(
                xln_sb[:, ic * TPC:(ic + 1) * TPC],
                xt_sb[:, sl * TPC:(sl + 1) * TPC], mrb), after=[tm, t9])

        xln_toks, t_lnmm = emit_ln(p, nc, x_d, xt_sb, sq_sb, stat_sb, bc_sb,
                                   onesf_sb, pA, pb, t_ones, xw)
        xln_done = xln_toks[-1]

        def xln(ic):
            return xln_sb[:, ic * TPC:(ic + 1) * TPC]

        ag_toks = []
        xacc_tok = [t_init] * 64
        xacc_ld = [None] * 64
        compute_done = []

        NPC = 16
        regcell = {}

        def setup_reg(e):
            regcell["r"] = e.alloc_register(pre + "nidx")
            e.reg_mov(regcell["r"], 4096)

        p.add("gpsimd", setup_reg)
        t_i16_prev = None
        t_mb_prev = None
        sel_last = [None, None, None]
        tT_by_buf = {"g": None, "u": None}
        stage_last = None
        last_gq = [None] * 4
        sel_q = [None] * 4
        sel_prev = None
        for c in range(n_chunks):
            par = c % 3
            stage_toks = []
            for m in range(3):
                tgt_sb = gub_sb if m == 1 else ggb_sb
                t_i16 = None
                for hf in range(2):
                    t_lw = p.dma("scalar", lambda e, m=m, c=c, hf=hf:
                                 e.dma_start(
                                     out=i32w_sb[:],
                                     in_=iw_d[m, c, :,
                                              hf * 2048:(hf + 1) * 2048]),
                                 after=[t_i16_prev, t_i16])
                    t_sh = p.comp("vector", lambda e: e.tensor_scalar(
                        i32w_sb[:], i32w_sb[:], 1, None,
                        ALU.logical_shift_right), after=[t_lw])
                    t_i16 = p.comp("vector", lambda e, hf=hf: e.tensor_copy(
                        i16w_sb[:, hf * 2048:(hf + 1) * 2048], i32w_sb[:]),
                        after=[t_sh] + (list(last_gq) if hf == 0 else []))
                t_i16_prev = t_i16
                t_lm = p.dma("scalar", lambda e, m=m, c=c: e.dma_start(
                    out=i32d_sb[:], in_=idm_d[m, c]), after=[t_mb_prev])
                t_and = p.comp("vector", lambda e: e.tensor_scalar(
                    i32d_sb[:], i32d_sb[:], 1, None, ALU.bitwise_and),
                    after=[t_lm, t_mb_prev])
                t_mb = p.comp("vector", lambda e, m=m: e.tensor_copy(
                    mk_sb[:, m * 512:(m + 1) * 512], i32d_sb[:]),
                    after=[t_and, sel_last[m]])
                t_mb_prev = t_mb
                war = tT_by_buf["u" if m == 1 else "g"]
                for pi in range(NPC):
                    q = pi % 4
                    t_g = p.add("gpsimd", lambda e, pi=pi, q=q: e.dma_gather(
                        dstf_sb[:, q], cbp_d,
                        i16w_sb[:, pi * 256:(pi + 1) * 256],
                        4096, regcell["r"], 128, elem_step=128,
                        single_packet=False, queue_num=q),
                        after=([t_i16, t_cbp] if pi < 4 else []) + [sel_q[q]],
                        inc=f"gq{q}", k=16, serial=False)
                    last_gq[q] = t_g
                    s1 = p.comp("vector", lambda e, q=q: e.tensor_sub(
                        tmpf_sb[:], dstf_sb[:, q, :, 8:16],
                        dstf_sb[:, q, :, 0:8]), after=[t_g, sel_prev])
                    m8 = mk_sb[:, m * 512 + pi * 32:m * 512 + (pi + 1) * 32]
                    s2 = p.comp("vector", lambda e, m8=m8: e.tensor_mul(
                        tmpf_sb[:], tmpf_sb[:],
                        m8.rearrange("p (j o) -> p j o", o=1).broadcast_to(
                            [128, 32, 8])), after=[s1, t_mb])
                    sel_prev = p.comp(
                        "vector", lambda e, pi=pi, q=q, tgt_sb=tgt_sb:
                        e.tensor_add(
                            tgt_sb[:, pi * 256:(pi + 1) * 256]
                            .rearrange("p (j e) -> p j e", e=8),
                            tmpf_sb[:], dstf_sb[:, q, :, 0:8]),
                        after=[s2, war if pi == 0 else None])
                    sel_q[q] = sel_prev
                sel_last[m] = sel_prev
                tT = None
                for b in range(32):
                    slot = ((b % 8) * 4 + b // 8) if m == 2 else b
                    tT = p.dma("sync", lambda e, b=b, slot=slot,
                               tgt_sb=tgt_sb: e.dma_start(
                                   out=ggT_sb[:, slot, :],
                                   in_=tgt_sb[:, b * 128:(b + 1) * 128],
                                   transpose=True),
                               after=[sel_prev, stage_last] if b == 0 else [])
                tT_by_buf["u" if m == 1 else "g"] = tT
                agg = ag_toks[-3] if len(ag_toks) >= 3 else None
                if m == 0:
                    st = p.dma("gpsimd", lambda e, par=par: e.dma_start(
                        out=wgst_d[par].rearrange("(a p) q -> p a q", p=128),
                        in_=ggT_sb[:]), after=[tT, agg])
                elif m == 1:
                    st = p.dma("gpsimd", lambda e, par=par: e.dma_start(
                        out=wust_d[par].rearrange("(a p) q -> p a q", p=128),
                        in_=ggT_sb[:]), after=[tT, agg])
                else:
                    st = p.dma("gpsimd", lambda e, par=par: e.dma_start(
                        out=wdst_d[par].rearrange("(cb p) (s h) -> p cb s h",
                                                  p=128, h=128),
                        in_=ggT_sb[:].rearrange("p (cb s) h -> p cb s h",
                                                s=4)), after=[tT, agg])
                stage_last = st
                stage_toks.append(st)
            t_sg, t_su, t_sd = stage_toks
            cgate = compute_done[c - 3] if c >= 3 else None
            a1 = p.add("gpsimd", lambda e, par=par: e.collective_compute(
                "AllGather", ALU.bypass, rg, [wgst_d[par]], [wgag_d[par]]),
                after=[t_sg, cgate], inc="ag", k=1, serial=False)
            a2 = p.add("gpsimd", lambda e, par=par: e.collective_compute(
                "AllGather", ALU.bypass, rg, [wust_d[par]], [wuag_d[par]]),
                after=[t_su], inc="ag", k=1, serial=False)
            a3 = p.add("gpsimd", lambda e, par=par: e.collective_compute(
                "AllGather", ALU.bypass, rg, [wdst_d[par]], [wdag_d[par]]),
                after=[t_sd], inc="ag", k=1, serial=False)
            ag_toks.append(a3)

            n_ib = min(8, NREAL_IB - c * 8)
            h_toks = []
            prev_pair_mm = compute_done[-1] if compute_done else None
            up_last = None
            for pp in range((n_ib + 1) // 2):
                twg = []
                twu = []
                for jj in range(2):
                    t_ = pp * 2 + jj
                    twg.append(p.dma("sync", lambda e, par=par, t_=t_, jj=jj:
                                     e.dma_start(
                                         out=wgT_sb[:, jj],
                                         in_=wgag_d[par][t_ * 4096:
                                                         (t_ + 1) * 4096, :]
                                         .rearrange("(a p) q -> p a q", p=128)),
                                     after=[a1, prev_pair_mm, xln_done,
                                            t_lnmm]))
                    twu.append(p.dma("sync", lambda e, par=par, t_=t_, jj=jj:
                                     e.dma_start(
                                         out=wuT_sb[:, jj],
                                         in_=wuag_d[par][t_ * 4096:
                                                         (t_ + 1) * 4096, :]
                                         .rearrange("(a p) q -> p a q", p=128)),
                                     after=[a2, prev_pair_mm, xln_done,
                                            t_lnmm]))
                for jj in range(2):
                    ib = pp * 2 + jj
                    if ib >= n_ib:
                        continue
                    mg = mu = None
                    for h in range(2):
                        for ic in range(NDT):
                            mg = p.comp("tensor", lambda e, ic=ic, jj=jj, h=h:
                                        e.matmul(
                                            pb[:, h * 512:(h + 1) * 512],
                                            wgT_sb[:, jj, ic, :],
                                            xln(ic)[:, h * 512:(h + 1) * 512],
                                            start=(ic == 0),
                                            stop=(ic == NDT - 1)),
                                        after=[twg[jj], xln_done, t_lnmm,
                                               up_last]
                                        if (h == 0 and ic == 0) else [])
                    for h in range(2):
                        for ic in range(NDT):
                            mu = p.comp("tensor", lambda e, ic=ic, jj=jj, h=h:
                                        e.matmul(
                                            pA[h][:],
                                            wuT_sb[:, jj, ic, :],
                                            xln(ic)[:, h * 512:(h + 1) * 512],
                                            start=(ic == 0),
                                            stop=(ic == NDT - 1)),
                                        after=[twu[jj], mg, up_last]
                                        if (h == 0 and ic == 0) else
                                        ([mg] if ic == 0 else []))
                    ts1 = p.comp("scalar", lambda e: e.activation(
                        hg_sb[:, 0:512], pb[:, 0:512], AF.Silu), after=[mg, mu])
                    ts2 = p.comp("scalar", lambda e: e.activation(
                        hg_sb[:, 512:1024], pb[:, 512:1024], AF.Silu),
                        after=[ts1])
                    ts3 = p.comp("scalar", lambda e, ib=ib: e.activation(
                        h_sb[:, ib * TPC:ib * TPC + 512], pA[0][:], AF.Copy),
                        after=[ts2, h_toks[-1] if h_toks else None])
                    ts4 = p.comp("scalar", lambda e, ib=ib: e.activation(
                        h_sb[:, ib * TPC + 512:(ib + 1) * TPC], pA[1][:],
                        AF.Copy), after=[ts3])
                    tm = p.comp("vector", lambda e, ib=ib: e.tensor_mul(
                        h_sb[:, ib * TPC:(ib + 1) * TPC],
                        h_sb[:, ib * TPC:(ib + 1) * TPC], hg_sb[:]),
                        after=[ts4])
                    h_toks.append(tm)
                    up_last = tm
                prev_pair_mm = up_last
            dn_last = None
            ev_last = None
            for hg0 in range(0, NDT, 3):
                ntile = min(3, NDT - hg0)
                slot_mm = [None, None]
                twd = []
                for ib in range(n_ib):
                    t_wd = None
                    for tl in range(ntile):
                        tg = hg0 + tl
                        t_wd = p.dma(
                            "sync", lambda e, par=par, ib=ib, tl=tl, tg=tg:
                            e.dma_start(
                                out=wdT_sb[:, (ib % 2) * 384 + tl * 128:
                                           (ib % 2) * 384 + (tl + 1) * 128],
                                in_=wdag_d[par][
                                    (tg // 4) * 1024 + ib * 128:
                                    (tg // 4) * 1024 + (ib + 1) * 128,
                                    (tg % 4) * 128:(tg % 4 + 1) * 128]),
                            after=[a3, slot_mm[ib % 2],
                                   h_toks[-1] if ib == 0 else None,
                                   ev_last if ib == 0 else None]
                            if tl == 0 else [slot_mm[ib % 2]])
                    twd.append(t_wd)
                    for t in range(ntile):
                        for h in range(2):
                            dn_last = p.comp(
                                "tensor",
                                lambda e, ib=ib, t=t, h=h, n_ib=n_ib:
                                e.matmul(
                                    pA[t * 2 + h][:],
                                    wdT_sb[:, (ib % 2) * 384 + t * 128:
                                           (ib % 2) * 384 + (t + 1) * 128],
                                    h_sb[:, ib * TPC + h * 512:
                                         ib * TPC + (h + 1) * 512],
                                    start=(ib == 0), stop=(ib == n_ib - 1)),
                                after=[t_wd, h_toks[ib]] +
                                ([ev_last] if ib == 0 else [])
                                if (t == 0 and h == 0) else [])
                    slot_mm[ib % 2] = dn_last
                for t in range(ntile):
                    ot = hg0 + t
                    for h in range(2):
                        sl = (2 * t + h) % 2
                        ti = ot * 2 + h
                        t_x = p.dma("sync", lambda e, ot=ot, h=h, sl=sl:
                                    e.dma_start(
                                        out=xr_sb[:, sl * 512:(sl + 1) * 512],
                                        in_=x1_d[ot * 128:(ot + 1) * 128,
                                                 h * 512:(h + 1) * 512]),
                                    after=[xacc_tok[ti], dn_last,
                                           xacc_ld[ti]])
                        t_r = p.comp("vector", lambda e, t=t, h=h, sl=sl:
                                     e.tensor_add(
                                         xo_sb[:, sl * 512:(sl + 1) * 512],
                                         pA[t * 2 + h][:],
                                         xr_sb[:, sl * 512:(sl + 1) * 512]),
                                     after=[t_x, dn_last])
                        t_o = p.dma("sync", lambda e, ot=ot, h=h, sl=sl:
                                    e.dma_start(
                                        out=x1_d[ot * 128:(ot + 1) * 128,
                                                 h * 512:(h + 1) * 512],
                                        in_=xo_sb[:, sl * 512:(sl + 1) * 512]),
                                    after=[t_r])
                        xacc_tok[ti] = t_o
                        xacc_ld[ti] = t_r
                        ev_last = t_r
            compute_done.append(ev_last)

    return xacc_tok[-1]


def _wrap16(mat):
    lin = np.ascontiguousarray(mat.T).reshape(-1)
    w16 = np.ascontiguousarray(lin.reshape(-1, 16).T)
    return np.tile(w16, (8, 1))


def prep_l0_ffn_maps(maps, inputs, pre):
    cb = np.ascontiguousarray(np.asarray(inputs["L0_ffn_codebook"], np.float32))
    ig = np.asarray(inputs["L0_ffn_idx_gate"])
    iu = np.asarray(inputs["L0_ffn_idx_up"])
    idn = np.asarray(inputs["L0_ffn_idx_down"])
    for c in range(NCORES):
        iw = np.zeros((3, NCHUNK, 128, 4096), np.int32)
        idm = np.zeros((3, NCHUNK, 128, 512), np.int32)
        for ch in range(NCHUNK):
            tile = ch * 8 + c
            if tile < NREAL_IB:
                rows = slice(tile * 128, (tile + 1) * 128)
                idm[0, ch] = ig[rows]
                iw[0, ch] = _wrap16(ig[rows])
                idm[1, ch] = iu[rows]
                iw[1, ch] = _wrap16(iu[rows])
            gd0 = ch * 128
            gd1 = min(gd0 + 128, idn.shape[1])
            dtile = np.zeros((4, 128, 128), np.int32)
            for s_ in range(4):
                hr = slice(512 * c + s_ * 128, 512 * c + (s_ + 1) * 128)
                dtile[s_, :, 0:gd1 - gd0] = idn[hr, gd0:gd1]
            dmat = np.ascontiguousarray(
                dtile.transpose(1, 0, 2).reshape(128, 512))
            idm[2, ch] = dmat
            iw[2, ch] = _wrap16(dmat)
        maps[c].update({"cb": cb, pre + "iw": iw, pre + "idm": idm})




def build_mega():
    nc = bass.Bass("TRN2", target_bir_lowering=False, num_swdge_queues=4)

    def dt(name, shape, dtyp, kind):
        return nc.dram_tensor(name, shape, dtyp, kind=kind).ap()

    x_d = dt("x", [D, TPC], F32, "ExternalInput")
    eyeb_d = dt("eyeb", [128, 128], BF16, "ExternalInput")
    nm0 = build_tile_plan(0)[1].shape[0]
    nm1 = build_tile_plan(1)[1].shape[0]

    a0 = dict(
        x=x_d, eyeb=eyeb_d,
        wq=dt("a0_wq", [D, 512], F32, "ExternalInput"),
        wkv=dt("a0_wkv", [D, 256], F32, "ExternalInput"),
        wo=dt("a0_wo", [512, D], F32, "ExternalInput"),
        masks=dt("a0_masks", [nm0, 128, 512], BF16, "ExternalInput"),
        bq=dt("a0_bq", [128, 6], F32, "ExternalInput"),
        bo=dt("a0_bo", [128, 32], F32, "ExternalInput"),
        cos=dt("cosT", [128, S], F32, "ExternalInput"),
        sin=dt("sinSW", [128, S], F32, "ExternalInput"),
        xout=dt("xa", [D, TPC], F32, "Internal"),
        xstage=dt("a0_xstage", [D, TPC], BF16, "Internal"),
        xag=dt("a0_xag", [NCORES * D, TPC], BF16, "Internal"),
        rsi=dt("a0_rsi", [NCORES * D, TPC], F32, "Internal"),
        rso=dt("a0_rso", [D, TPC], F32, "Internal"),
    )
    f0 = dict(
        x=a0["xout"],
        cb=dt("cb", [65536, 8], F32, "ExternalInput"),
        iw=dt("f0_iw", [3, NCHUNK, 128, 4096], I32, "ExternalInput"),
        idm=dt("f0_idm", [3, NCHUNK, 128, 512], I32, "ExternalInput"),
        cbp=dt("f0_cbp", [32768, 128], BF16, "Internal"),
        xout=dt("xb", [D, TPC], F32, "Internal"),
        wgst=[dt(f"f0_wgst{i}", [4096, 128], BF16, "Internal")
              for i in range(NPAR)],
        wust=[dt(f"f0_wust{i}", [4096, 128], BF16, "Internal")
              for i in range(NPAR)],
        wdst=[dt(f"f0_wdst{i}", [1024, 512], BF16, "Internal")
              for i in range(NPAR)],
        wgag=[dt(f"f0_wgag{i}", [32768, 128], BF16, "Internal")
              for i in range(NPAR)],
        wuag=[dt(f"f0_wuag{i}", [32768, 128], BF16, "Internal")
              for i in range(NPAR)],
        wdag=[dt(f"f0_wdag{i}", [8192, 512], BF16, "Internal")
              for i in range(NPAR)],
    )
    a1 = dict(
        x=f0["xout"], eyeb=eyeb_d,
        wq=dt("a1_wq", [D, 512], F32, "ExternalInput"),
        wkv=dt("a1_wkv", [D, 256], F32, "ExternalInput"),
        wo=dt("a1_wo", [512, D], F32, "ExternalInput"),
        masks=dt("a1_masks", [nm1, 128, 512], BF16, "ExternalInput"),
        xout=dt("xc", [D, TPC], F32, "Internal"),
        xstage=dt("a1_xstage", [D, TPC], BF16, "Internal"),
        xag=dt("a1_xag", [NCORES * D, TPC], BF16, "Internal"),
        rsi=dt("a1_rsi", [NCORES * D, TPC], F32, "Internal"),
        rso=dt("a1_rso", [D, TPC], F32, "Internal"),
        kvs=dt("a1_kvs", [256, TPC], BF16, "Internal"),
        kvag=dt("a1_kvag", [NCORES * 256, TPC], BF16, "Internal"),
    )
    f1 = dict(
        x=a1["xout"], eyeb=eyeb_d,
        bw=dt("f1_bw", [D, D], F32, "ExternalInput"),
        bb=dt("f1_bb", [128, 32], F32, "ExternalInput"),
        at=dt("f1_at", [D, 64], F32, "ExternalInput"),
        bs=dt("f1_bs", [1024, D], F32, "ExternalInput"),
        rw=dt("f1_rw", [D, 16], F32, "ExternalInput"),
        xout=dt("x1", [D, TPC], F32, "ExternalOutput"),
    )

    ctx = contextlib.ExitStack()
    with ctx:
        p = Prog(nc, ctx)
        blk = ctx.enter_context(nc.Block())
        p.add("gpsimd", lambda e: e.load_library(mlp))
        emit_attn(p, a0, 0, "a0")
        p.stage_gate()
        emit_l0_ffn(p, f0, "f0")
        p.stage_gate()
        emit_attn(p, a1, 1, "a1")
        p.stage_gate()
        last = emit_l1_ffn(p, f1, "f1")
        p.add("gpsimd", lambda e: e.engine_nop(), after=[last])
        p.emit(blk)
    mybir.codegen_inst_isa_subclasses(nc)
    return nc




def kernel(**inputs):
    x0 = np.asarray(inputs["L0_attn_hidden_states"], np.float32).reshape(S, D)
    eyeb = np.eye(128, dtype=np.float32).astype(BF)
    maps = [{"x": np.ascontiguousarray(x0[core_token_ids(c)].T), "eyeb": eyeb}
            for c in range(NCORES)]
    prep_attn_maps(maps, inputs, 0, "a0_")
    prep_l0_ffn_maps(maps, inputs, "f0_")
    prep_attn_maps(maps, inputs, 1, "a1_")
    prep_l1_ffn_maps(maps, inputs, "f1_")
    nc = build_mega()
    res = run_bass_kernel_spmd(nc, maps, core_ids=list(range(NCORES)))
    out = np.zeros((S, D), dtype=np.float32)
    for c in range(NCORES):
        out[core_token_ids(c)] = res.results[c]["x1"].T
    return out.reshape(1, S, D)


# version 09
# speedup vs baseline: 1.3035x; 1.0190x over previous
import contextlib
import numpy as np
import ml_dtypes
import concourse.bass as bass
import concourse.mybir as mybir
from concourse.bass_utils import run_bass_kernel_spmd
from concourse.library_config import mlp

F32 = mybir.dt.float32
BF16 = mybir.dt.bfloat16
I32 = mybir.dt.int32
I16 = mybir.dt.int16
AF = mybir.ActivationFunctionType
ALU = mybir.AluOpType
AX = mybir.AxisListType

NCORES = 8
S = 8192
D = 4096
TPC = 1024
NDT = D // 128
BF = ml_dtypes.bfloat16
NQW = S // 512


def core_token_ids(c):
    return np.arange(TPC * c, TPC * (c + 1))




def l0_tile_mask(q0, k0):
    k = np.arange(k0, k0 + 128)[:, None]
    q = np.arange(q0, q0 + 512)[None, :]
    qb = q // 64
    kb = k // 64
    act = (kb <= qb) & ((qb - kb < 16) | ((kb + 1) % 8 == 0))
    return act & (k <= q)


def l1_tile_mask(q0, k0):
    k = np.arange(k0, k0 + 128)[:, None]
    q = np.arange(q0, q0 + 512)[None, :]
    return k <= q


def build_tile_plan(layer):
    plan = []
    masks = {}
    order = []
    for qw in range(NQW):
        q0 = 512 * qw
        tiles = []
        for k0 in range(0, q0 + 512, 128):
            m = l0_tile_mask(q0, k0) if layer == 0 else l1_tile_mask(q0, k0)
            if not m.any():
                continue
            if m.all():
                tiles.append((k0, None))
            else:
                key = m.tobytes()
                if key not in masks:
                    masks[key] = len(order)
                    order.append(m)
                tiles.append((k0, masks[key]))
        plan.append(tiles)
    marr = np.stack(order).astype(BF) if order else np.zeros((1, 128, 512), BF)
    return plan, marr


def rope_tables():
    hd = 128
    inv = 1.0 / (1000000.0 ** (np.arange(0, hd, 2, dtype=np.float64) / hd))
    t = np.arange(S, dtype=np.float64)
    fr = np.outer(t, inv)
    emb = np.concatenate([fr, fr], axis=1)
    cos = np.cos(emb).T.astype(np.float32)
    sin = np.sin(emb).T.astype(np.float32)
    sinS = sin.copy()
    sinS[:64] *= -1.0
    return np.ascontiguousarray(cos), np.ascontiguousarray(sinS)




class Sem:
    def __init__(self, stack, nc, name):
        self.h = stack.enter_context(nc.semaphore(name))
        self.n = 0


class Prog:
    ENGS = ("sync", "tensor", "vector", "scalar", "gpsimd")

    def __init__(self, nc, stack):
        self.nc = nc
        self.stack = stack
        self.ops = {e: [] for e in self.ENGS}
        self.sems = {}
        self.gate = ()
        self.gated = set()

    def sem(self, name):
        if name not in self.sems:
            self.sems[name] = Sem(self.stack, self.nc, name)
        return self.sems[name]

    def stage_gate(self):
        self.gate = tuple((nm, s.n) for nm, s in self.sems.items() if s.n > 0)
        self.gated = set()

    def add(self, eng, fn, after=(), inc=None, k=1, serial=True):
        if self.gate and eng not in self.gated:
            self.gated.add(eng)
            after = tuple(t for t in after if t is not None) + self.gate
        s = self.sem(inc) if inc else None
        tok = None
        extra = ()
        if s is not None:
            if serial and s.n > 0:
                extra = ((inc, s.n),)
            s.n += k
            tok = (inc, s.n)
        after = tuple(t for t in after if t is not None) + extra
        self.ops[eng].append((after, fn, s, k))
        return tok

    def dma(self, eng, fn, after=()):
        return self.add(eng, fn, after, inc="dma_" + eng, k=16)

    def comp(self, eng, fn, after=()):
        return self.add(eng, fn, after, inc="c_" + eng, k=1)

    def barrier_tok(self, eng_sem):
        return (eng_sem, self.sem(eng_sem).n)

    def emit(self, blk):
        def make_body(name):
            def body(eng):
                wm = {}
                own = "c_" + name
                for (waits, fn, s, k) in self.ops[name]:
                    for (snm, val) in waits:
                        if name == "tensor" and snm == own:
                            continue
                        if wm.get(snm, 0) >= val:
                            continue
                        eng.wait_ge(self.sems[snm].h, val)
                        wm[snm] = val
                    ins = fn(eng)
                    if s is not None:
                        assert ins is not None, f"op on {name} returned None"
                        ins.then_inc(s.h, k)
            return body

        for name in self.ENGS:
            if self.ops[name]:
                getattr(blk, name)(make_body(name))




def emit_attn(p, io, layer, pre):
    plan, masks_np = build_tile_plan(layer)
    nm = masks_np.shape[0]
    sm_scale = (1.0 / 128.0) if layer == 0 else float(1.0 / np.sqrt(128.0))

    nc = p.nc
    x_d = io["x"]
    wq_d = io["wq"]
    wkv_d = io["wkv"]
    wo_d = io["wo"]
    masks_d = io["masks"]
    eyeb_d = io["eyeb"]
    x1_d = io["xout"]
    xstage_d = io["xstage"]
    xag_d = io["xag"]
    rsi_d = io["rsi"]
    rso_d = io["rso"]
    if layer == 0:
        bq_d = io["bq"]
        bo_d = io["bo"]
        cos_d = io["cos"]
        sin_d = io["sin"]
    else:
        kvstage_d = io["kvs"]
        kvag_d = io["kvag"]

    ctx = contextlib.ExitStack()
    with ctx:
        sb = lambda name, shape, dt: ctx.enter_context(
            nc.sbuf_tensor(pre + name, shape, dt))
        xw_sb = sb("xw_sb", [128, NDT * 512], BF16)
        K_sb = sb("K_sb", [128, S], BF16)
        V_sb = sb("V_sb", [128, S], BF16)
        wqb_sb = sb("wqb_sb", [128, NDT * 512], BF16)
        wkvb_sb = sb("wkvb_sb", [128, NDT * 256], BF16)
        masks_sb = sb("masks_sb", [128, nm * 512], BF16)
        eyeb_sb = sb("eyeb_sb", [128, 128], BF16)
        onesf_sb = sb("onesf_sb", [128, 128], F32)
        onesb_sb = sb("onesb_sb", [128, 4], BF16)
        xt_sb = sb("xt_sb", [128, 2 * TPC], F32)
        xb_sb = sb("xb_sb", [128, 2 * TPC], BF16)
        sq_sb = sb("sq_sb", [128, TPC], F32)
        stat_sb = sb("stat_sb", [1, 4 * TPC], F32)
        bc_sb = sb("bc_sb", [128, 2 * TPC], F32)
        if layer == 0:
            bq_sb = sb("bq_sb", [128, 6], F32)
            bo_sb = sb("bo_sb", [128, 32], F32)
            cosw_sb = sb("cosw_sb", [128, 512], F32)
            sinw_sb = sb("sinw_sb", [128, 512], F32)
        wot_sb = sb("wot_sb", [128, 2 * 512], BF16)
        wotf_sb = sb("wotf_sb", [128, 2 * 512], F32)
        po_sb = sb("po_sb", [128, 2 * 512], F32)
        ctxh_sb = sb("ctxh_sb", [128, 4 * 512], BF16)
        kvf_sb = sb("kvf_sb", [128, 2 * 512], F32)
        vb_sb = sb("vb_sb", [128, 512], BF16)
        q_sb = sb("q_sb", [128, 4 * 512], BF16)
        qf_sb = sb("qf_sb", [128, 512], F32)
        kn_sb = sb("kn_sb", [1, 64], F32)
        nrm_sb = sb("nrm_sb", [1, 16], F32)
        cc_sb = sb("cc_sb", [128, 1], F32)
        pt_sb = sb("pt_sb", [128, 2 * 512], BF16)
        den_sb = sb("den_sb", [1, 512], F32)
        ctxf_sb = sb("ctxf_sb", [128, 512], F32)
        xr_sb = sb("xr_sb", [128, 2 * 512], F32)
        xo_sb = sb("xo_sb", [128, 2 * 512], F32)

        psb = lambda name, shape, dt: ctx.enter_context(
            nc.psum_tensor(pre + name, shape, dt))
        pb = psb("pb", [128, TPC], F32)
        pA = [psb(f"pA{i}", [128, 512], F32) for i in range(5)]
        ptr = psb("ptr", [128, 512], BF16)

        rg = [list(range(NCORES))]

        for n in range(nm):
            p.dma("sync", lambda e, n=n: e.dma_start(
                out=masks_sb[:, n * 512:(n + 1) * 512], in_=masks_d[n]))
        t_eye = p.dma("sync", lambda e: e.dma_start(out=eyeb_sb[:], in_=eyeb_d))
        if layer == 0:
            p.dma("sync", lambda e: e.dma_start(out=bq_sb[:], in_=bq_d))
            t_bo = p.dma("sync", lambda e: e.dma_start(out=bo_sb[:], in_=bo_d))
        else:
            t_bo = None
        t_const = p.barrier_tok("dma_sync")
        t_ones = p.comp("vector", lambda e: e.memset(onesf_sb[:], 1.0))
        t_onesb = p.comp("vector", lambda e: e.memset(onesb_sb[:], 1.0))

        pm0, pm1, pv0, pv1 = pA[0], pA[1], pA[2], pA[3]
        mm_tok = None
        slot_free = [None, None]
        for ic in range(NDT):
            sl = ic % 2
            t_ld = p.dma("sync", lambda e, ic=ic, sl=sl: e.dma_start(
                out=xt_sb[:, sl * TPC:(sl + 1) * TPC],
                in_=x_d[ic * 128:(ic + 1) * 128, :]), after=[slot_free[sl]])
            t_sq = p.comp("scalar", lambda e, sl=sl: e.activation(
                sq_sb[:], xt_sb[:, sl * TPC:(sl + 1) * TPC], AF.Square),
                after=[t_ld, mm_tok])
            for h in range(2):
                p.comp("tensor", lambda e, sl=sl, h=h, ic=ic: e.matmul(
                    (pm0 if h == 0 else pm1)[0:1, :], onesf_sb[:, 0:1],
                    xt_sb[:, sl * TPC + h * 512:sl * TPC + (h + 1) * 512],
                    start=(ic == 0), stop=(ic == NDT - 1)),
                    after=[t_ld, t_ones])
                mm_tok = p.comp("tensor", lambda e, h=h, ic=ic: e.matmul(
                    (pv0 if h == 0 else pv1)[0:1, :], onesf_sb[:, 0:1],
                    sq_sb[:, h * 512:(h + 1) * 512],
                    start=(ic == 0), stop=(ic == NDT - 1)),
                    after=[t_sq])
            slot_free[sl] = mm_tok
        m_ = stat_sb[0:1, 0:TPC]
        msq = stat_sb[0:1, TPC:2 * TPC]
        m2 = stat_sb[0:1, 2 * TPC:3 * TPC]
        var = stat_sb[0:1, 3 * TPC:4 * TPC]
        r_ = stat_sb[0:1, 3 * TPC:4 * TPC]
        mr = stat_sb[0:1, 2 * TPC:3 * TPC]
        t1 = p.comp("scalar", lambda e: e.activation(
            m_[:, 0:512], pm0[0:1, :], AF.Copy, scale=1.0 / D), after=[mm_tok])
        t1 = p.comp("scalar", lambda e: e.activation(
            m_[:, 512:1024], pm1[0:1, :], AF.Copy, scale=1.0 / D), after=[t1])
        t2 = p.comp("scalar", lambda e: e.activation(
            msq[:, 0:512], pv0[0:1, :], AF.Copy, scale=1.0 / D), after=[t1])
        t2 = p.comp("scalar", lambda e: e.activation(
            msq[:, 512:1024], pv1[0:1, :], AF.Copy, scale=1.0 / D), after=[t2])
        t3 = p.comp("scalar", lambda e: e.activation(m2[:], m_[:], AF.Square),
                    after=[t2])
        t4 = p.comp("vector", lambda e: e.tensor_sub(var[:], msq[:], m2[:]),
                    after=[t3])
        t4b = p.comp("vector", lambda e: e.tensor_scalar_add(var[:], var[:],
                                                             1e-5), after=[t4])
        t5 = p.comp("scalar", lambda e: e.activation(var[:], var[:], AF.Sqrt),
                    after=[t4b])
        t6 = p.comp("vector", lambda e: e.reciprocal(r_[:], var[:]), after=[t5])
        t7 = p.comp("vector", lambda e: e.tensor_mul(mr[:], m_[:], r_[:]),
                    after=[t6])
        bts = [p.comp("tensor", lambda e, h=h: e.matmul(
            pb[:, h * 512:(h + 1) * 512], onesf_sb[0:1, :],
            r_[:, h * 512:(h + 1) * 512], start=True, stop=True),
            after=[t6]) for h in range(2)]
        t8 = p.comp("scalar", lambda e: e.activation(bc_sb[:, 0:TPC], pb[:],
                                                     AF.Copy), after=bts)
        bts2 = [p.comp("tensor", lambda e, h=h: e.matmul(
            pb[:, h * 512:(h + 1) * 512], onesf_sb[0:1, :],
            mr[:, h * 512:(h + 1) * 512], start=True, stop=True),
            after=[t7, t8]) for h in range(2)]
        t9 = p.comp("scalar", lambda e: e.activation(bc_sb[:, TPC:2 * TPC], pb[:],
                                                     AF.Copy), after=bts2)
        rb = bc_sb[:, 0:TPC]
        mrb = bc_sb[:, TPC:2 * TPC]
        xst_toks = []
        slot_free = [None, None]
        for ic in range(NDT):
            sl = ic % 2
            t_ld = p.dma("sync", lambda e, ic=ic, sl=sl: e.dma_start(
                out=xt_sb[:, sl * TPC:(sl + 1) * TPC],
                in_=x_d[ic * 128:(ic + 1) * 128, :]), after=[t8, slot_free[sl]])
            tm = p.comp("vector", lambda e, sl=sl: e.tensor_mul(
                xt_sb[:, sl * TPC:(sl + 1) * TPC],
                xt_sb[:, sl * TPC:(sl + 1) * TPC], rb), after=[t_ld, t8])
            tsb = p.comp("vector", lambda e, sl=sl: e.tensor_sub(
                xb_sb[:, sl * TPC:(sl + 1) * TPC],
                xt_sb[:, sl * TPC:(sl + 1) * TPC], mrb), after=[tm, t9])
            tst = p.dma("scalar", lambda e, ic=ic, sl=sl: e.dma_start(
                out=xstage_d[ic * 128:(ic + 1) * 128, :],
                in_=xb_sb[:, sl * TPC:(sl + 1) * TPC]), after=[tsb])
            xst_toks.append(tst)
            slot_free[sl] = tst
        t_agx = p.add("gpsimd", lambda e: e.collective_compute(
            "AllGather", ALU.bypass, rg, [xstage_d], [xag_d]),
            after=[xst_toks[-1]], inc="ag", k=1)
        t_ln_done = t9

        def stream_slab_to(w_dram, col0, ncols, dst_sb, dst_off, after=()):
            return p.dma("gpsimd", lambda e: e.dma_start(
                out=dst_sb[:, dst_off:dst_off + NDT * ncols].rearrange(
                    "p (a o) -> p a o", o=ncols),
                in_=w_dram[:, col0:col0 + ncols].rearrange(
                    "(a p) o -> p a o", p=128)), after=after)

        wq_toks = []
        prev = None
        for j in range(4):
            prev = stream_slab_to(wq_d, j * 128, 128, wqb_sb, j * NDT * 128,
                                  after=[prev])
            wq_toks.append(prev)
        t_wkv = stream_slab_to(wkv_d, 0, 256, wkvb_sb, 0, after=[prev])

        def wq_tile(j, ic):
            return wqb_sb[:, j * NDT * 128 + ic * 128:j * NDT * 128 + (ic + 1) * 128]

        def wkv_tile(which, ic):
            o = ic * 256 + which * 128
            return wkvb_sb[:, o:o + 128]

        kv_ready = None
        kn_done = {}
        xw_init_gate = None
        if layer == 1:
            own_done = []
            ldt_last = None
            for h in range(2):
                ldt = []
                for ic in range(NDT):
                    ldt.append(p.dma("sync", lambda e, ic=ic, h=h: e.dma_start(
                        out=xw_sb[:, ic * 512:(ic + 1) * 512],
                        in_=xstage_d[ic * 128:(ic + 1) * 128,
                                     h * 512:(h + 1) * 512]),
                        after=[xst_toks[ic],
                               own_done[-1] if own_done else None]))
                mk = mv = None
                for ic in range(NDT):
                    mk = p.comp("tensor", lambda e, ic=ic: e.matmul(
                        pA[0][:], wkv_tile(0, ic),
                        xw_sb[:, ic * 512:(ic + 1) * 512],
                        start=(ic == 0), stop=(ic == NDT - 1)),
                        after=[t_wkv, t_ln_done, ldt[ic]] if ic == 0
                        else [ldt[ic]])
                for ic in range(NDT):
                    mv = p.comp("tensor", lambda e, ic=ic: e.matmul(
                        pA[1][:], wkv_tile(1, ic),
                        xw_sb[:, ic * 512:(ic + 1) * 512],
                        start=(ic == 0), stop=(ic == NDT - 1)), after=[])
                tk = p.comp("scalar", lambda e, h=h: e.activation(
                    xb_sb[:, h * 512:(h + 1) * 512], pA[0][:], AF.Copy),
                    after=[mk, mv])
                tv = p.comp("scalar", lambda e, h=h: e.activation(
                    xb_sb[:, TPC + h * 512:TPC + (h + 1) * 512], pA[1][:],
                    AF.Copy), after=[tk])
                own_done.append(tv)
                ldt_last = ldt[-1]
            tsk = p.dma("scalar", lambda e: e.dma_start(
                out=kvstage_d[0:128, :], in_=xb_sb[:, 0:TPC]), after=own_done)
            tsv = p.dma("scalar", lambda e: e.dma_start(
                out=kvstage_d[128:256, :], in_=xb_sb[:, TPC:2 * TPC]),
                after=own_done)
            t_agkv = p.add("gpsimd", lambda e: e.collective_compute(
                "AllGather", ALU.bypass, rg, [kvstage_d], [kvag_d]),
                after=[tsk, tsv], inc="ag", k=1)
            kl = []
            for r in range(NCORES):
                kl.append(p.dma("sync", lambda e, r=r: e.dma_start(
                    out=K_sb[:, r * TPC:(r + 1) * TPC],
                    in_=kvag_d[r * 256:r * 256 + 128, :]), after=[t_agkv]))
                kl.append(p.dma("sync", lambda e, r=r: e.dma_start(
                    out=V_sb[:, r * TPC:(r + 1) * TPC],
                    in_=kvag_d[r * 256 + 128:r * 256 + 256, :]),
                    after=[t_agkv]))
            tprev = None
            for kt in range(64):
                ta = p.comp("tensor", lambda e, kt=kt: e.transpose(
                    ptr[:, 0:128], V_sb[:, kt * 128:(kt + 1) * 128], eyeb_sb[:]),
                    after=[kl[2 * (kt // 8) + 1], t_eye, tprev])
                tprev = p.comp("vector", lambda e, kt=kt: e.tensor_copy(
                    V_sb[:, kt * 128:(kt + 1) * 128], ptr[:, 0:128]), after=[ta])
            kv_ready = tprev
            knt = None
            for kt in range(64):
                tsq2 = p.comp("scalar", lambda e, kt=kt: e.activation(
                    sq_sb[:, 0:128], K_sb[:, kt * 128:(kt + 1) * 128],
                    AF.Square), after=[kl[2 * (kt // 8)], knt])
                tm1 = p.comp("tensor", lambda e: e.matmul(
                    pA[4][0:1, 0:128], onesf_sb[:, 0:1], sq_sb[:, 0:128],
                    start=True, stop=True), after=[tsq2])
                knt = p.comp("vector", lambda e, kt=kt: e.tensor_reduce(
                    kn_sb[0:1, kt:kt + 1], pA[4][0:1, 0:128], AX.X, ALU.max),
                    after=[tm1])
                kn_done[kt] = knt
            xw_init_gate = ldt_last
            xw_init_gate = tsv

        NH = 4
        kwin_done = {}
        if layer == 1:
            for w in range(NQW):
                kwin_done[w] = kv_ready
        xw_free = xw_init_gate
        prev_qnorm = None
        prev_qproj = None
        qevac_prev = [None, None]
        prev_cx0 = kn_done.get(63) if layer == 1 else None
        prev_ctx_st = None
        prev_cf = None
        prev_cx = kn_done.get(63) if layer == 1 else None
        prev_d1 = None
        prev_d3 = None
        for qw in range(NQW):
            r = qw // 2
            hh = qw % 2
            t_ldw = p.dma("sync", lambda e, r=r, hh=hh: e.dma_start(
                out=xw_sb[:].rearrange("p (a t) -> p a t", t=512),
                in_=xag_d[r * D:(r + 1) * D,
                          hh * 512:(hh + 1) * 512].rearrange(
                    "(a p) t -> p a t", p=128)),
                after=[t_agx, xw_free])
            ldw = [t_ldw] * NDT
            if layer == 0:
                t_cs = p.dma("sync", lambda e, qw=qw: e.dma_start(
                    out=cosw_sb[:], in_=cos_d[:, qw * 512:(qw + 1) * 512]),
                    after=[prev_qproj, t_const])
                t_sn = p.dma("sync", lambda e, qw=qw: e.dma_start(
                    out=sinw_sb[:], in_=sin_d[:, qw * 512:(qw + 1) * 512]),
                    after=[prev_qproj])
                mk = mv = None
                for ic in range(NDT):
                    mk = p.comp("tensor", lambda e, ic=ic: e.matmul(
                        pA[0][:], wkv_tile(0, ic),
                        xw_sb[:, ic * 512:(ic + 1) * 512],
                        start=(ic == 0), stop=(ic == NDT - 1)),
                        after=[t_wkv, t_ln_done, ldw[ic], prev_ctx_st]
                        if ic == 0 else [ldw[ic]])
                for ic in range(NDT):
                    mv = p.comp("tensor", lambda e, ic=ic: e.matmul(
                        pA[1][:], wkv_tile(1, ic),
                        xw_sb[:, ic * 512:(ic + 1) * 512],
                        start=(ic == 0), stop=(ic == NDT - 1)), after=[])
                tke = p.comp("scalar", lambda e: e.activation(
                    kvf_sb[:, 0:512], pA[0][:], AF.Identity, bias=bq_sb[:, 4:5]),
                    after=[mk, mv])
                tr1 = p.comp("vector", lambda e: e.tensor_mul(
                    kvf_sb[:, 512:1024], kvf_sb[:, 0:512], cosw_sb[:]),
                    after=[tke, t_cs])
                tr2 = p.comp("vector", lambda e: e.tensor_mul(
                    qf_sb[0:64, :], kvf_sb[64:128, 0:512], sinw_sb[64:128, :]),
                    after=[tke, t_sn])
                tr3 = p.comp("vector", lambda e: e.tensor_mul(
                    qf_sb[64:128, :], kvf_sb[0:64, 0:512], sinw_sb[0:64, :]),
                    after=[tke])
                tk_bf = p.comp("vector", lambda e, qw=qw: e.tensor_add(
                    K_sb[:, qw * 512:(qw + 1) * 512], kvf_sb[:, 512:1024],
                    qf_sb[:, 0:512]), after=[tr1, tr2, tr3])
                tve = p.comp("scalar", lambda e: e.activation(
                    vb_sb[:], pA[1][:], AF.Identity, bias=bq_sb[:, 5:6]),
                    after=[tke])
                tprev = None
                for j in range(4):
                    ta = p.comp("tensor", lambda e, j=j: e.transpose(
                        ptr[:, 0:128], vb_sb[:, j * 128:(j + 1) * 128],
                        eyeb_sb[:]), after=[tve, t_eye, tprev])
                    tprev = p.comp("vector", lambda e, j=j, qw=qw: e.tensor_copy(
                        V_sb[:, qw * 512 + j * 128:qw * 512 + (j + 1) * 128],
                        ptr[:, 0:128]), after=[ta])
                for j in range(4):
                    kt = qw * 4 + j
                    tsq2 = p.comp("scalar", lambda e, qw=qw, j=j: e.activation(
                        sq_sb[:, 0:128],
                        K_sb[:, qw * 512 + j * 128:qw * 512 + (j + 1) * 128],
                        AF.Square),
                        after=[tk_bf, kn_done.get(kt - 1), prev_qnorm])
                    tm1 = p.comp("tensor", lambda e: e.matmul(
                        pA[4][0:1, 0:128], onesf_sb[:, 0:1], sq_sb[:, 0:128],
                        start=True, stop=True), after=[tsq2, prev_cx])
                    kn_done[kt] = p.comp(
                        "vector", lambda e, kt=kt: e.tensor_reduce(
                            kn_sb[0:1, kt:kt + 1], pA[4][0:1, 0:128], AX.X,
                            ALU.max), after=[tm1])
                kwin_done[qw] = tprev

            q_toks = []
            qn_toks = []
            for j in range(NH):
                qh = j % 2
                mq = None
                for ic in range(NDT):
                    mq = p.comp("tensor", lambda e, j=j, ic=ic, qh=qh: e.matmul(
                        pb[:, qh * 512:(qh + 1) * 512], wq_tile(j, ic),
                        xw_sb[:, ic * 512:(ic + 1) * 512],
                        start=(ic == 0), stop=(ic == NDT - 1)),
                        after=[wq_toks[j], t_ln_done, ldw[ic], prev_qproj,
                               qevac_prev[qh]]
                        if ic == 0 else [ldw[ic]])
                if layer == 0:
                    tqe = p.comp("scalar", lambda e, j=j, qh=qh: e.activation(
                        qf_sb[:], pb[:, qh * 512:(qh + 1) * 512], AF.Identity,
                        bias=bq_sb[:, j:j + 1]),
                        after=[mq])
                    qevac_prev[qh] = tqe
                    r1 = p.comp("vector", lambda e: e.tensor_mul(
                        kvf_sb[:, 0:512], qf_sb[:], cosw_sb[:]), after=[tqe])
                    r2 = p.comp("vector", lambda e: e.tensor_mul(
                        kvf_sb[0:64, 512:1024], qf_sb[64:128, :],
                        sinw_sb[64:128, :]), after=[tqe])
                    r3 = p.comp("vector", lambda e: e.tensor_mul(
                        kvf_sb[64:128, 512:1024], qf_sb[0:64, :],
                        sinw_sb[0:64, :]), after=[tqe])
                    tq = p.comp("vector", lambda e, j=j: e.tensor_add(
                        q_sb[:, j * 512:(j + 1) * 512], kvf_sb[:, 0:512],
                        kvf_sb[:, 512:1024]), after=[r1, r2, r3])
                else:
                    tq = p.comp("scalar", lambda e, j=j, qh=qh: e.activation(
                        q_sb[:, j * 512:(j + 1) * 512],
                        pb[:, qh * 512:(qh + 1) * 512], AF.Copy),
                        after=[mq])
                    qevac_prev[qh] = tq
                q_toks.append(tq)
                tsq3 = p.comp("scalar", lambda e, j=j: e.activation(
                    sq_sb[:, 128:640], q_sb[:, j * 512:(j + 1) * 512],
                    AF.Square), after=[tq, qn_toks[-1] if qn_toks else
                                       prev_qnorm])
                tm1 = p.comp("tensor", lambda e: e.matmul(
                    pA[4][0:1, 0:512], onesf_sb[:, 0:1], sq_sb[:, 128:640],
                    start=True, stop=True), after=[tsq3, prev_cx])
                qn_toks.append(p.comp("vector", lambda e, j=j: e.tensor_reduce(
                    nrm_sb[0:1, j:j + 1], pA[4][0:1, 0:512], AX.X, ALU.max),
                    after=[tm1]))
            prev_qproj = q_toks[-1]
            prev_qnorm = qn_toks[-1]

            tiles = plan[qw]
            last_kt = max(k0 for (k0, _m) in tiles) // 128
            tqm = p.comp("vector", lambda e: e.tensor_reduce(
                nrm_sb[0:1, 8:9], nrm_sb[0:1, 0:4], AX.X, ALU.max),
                after=qn_toks)
            tkm = p.comp("vector", lambda e, last_kt=last_kt: e.tensor_reduce(
                nrm_sb[0:1, 9:10], kn_sb[0:1, 0:last_kt + 1], AX.X, ALU.max),
                after=[kn_done[last_kt]])
            tc1 = p.comp("vector", lambda e: e.tensor_mul(
                nrm_sb[0:1, 10:11], nrm_sb[0:1, 8:9], nrm_sb[0:1, 9:10]),
                after=[tqm, tkm])
            tc2 = p.comp("scalar", lambda e: e.activation(
                nrm_sb[0:1, 11:12], nrm_sb[0:1, 10:11], AF.Sqrt), after=[tc1])
            tc3 = p.comp("scalar", lambda e: e.mul(
                nrm_sb[0:1, 12:13], nrm_sb[0:1, 11:12], -1.001 * sm_scale),
                after=[tc2])
            tc4 = p.comp("tensor", lambda e: e.matmul(
                pA[4][:, 0:1], onesf_sb[0:1, :], nrm_sb[0:1, 12:13],
                start=True, stop=True), after=[tc3, prev_cx])
            tc5 = p.comp("scalar", lambda e: e.activation(
                cc_sb[:], pA[4][:, 0:1], AF.Copy), after=[tc4])

            ctx_head_toks = []
            for j in range(NH):
                n_t = len(tiles)
                exp_prev = [None, None]
                pt_readers = [None, None]
                pv_prev = None
                den_prev = None
                kreq = kwin_done[qw]
                for i, (k0, mid) in enumerate(tiles):
                    bk = i % 2
                    t_s = p.comp("tensor", lambda e, k0=k0, bk=bk, j=j: e.matmul(
                        pA[bk][:], K_sb[:, k0:k0 + 128],
                        q_sb[:, j * 512:(j + 1) * 512], start=True, stop=True),
                        after=[exp_prev[bk], q_toks[j], kreq] if i < 2
                        else [exp_prev[bk]])
                    t_e = p.comp("scalar", lambda e, bk=bk: e.activation(
                        pt_sb[:, bk * 512:(bk + 1) * 512], pA[bk][:], AF.Exp,
                        bias=cc_sb[:, 0:1], scale=sm_scale),
                        after=[t_s, tc5, pt_readers[bk]])
                    exp_prev[bk] = t_e
                    t_m = t_e
                    if mid is not None:
                        t_m = p.comp("vector", lambda e, bk=bk, mid=mid:
                                     e.tensor_mul(
                                         pt_sb[:, bk * 512:(bk + 1) * 512],
                                         pt_sb[:, bk * 512:(bk + 1) * 512],
                                         masks_sb[:, mid * 512:(mid + 1) * 512]),
                                     after=[t_e, t_const])
                    t_pv = p.comp("tensor", lambda e, k0=k0, bk=bk, i=i, n_t=n_t:
                                  e.matmul(pA[2][:], V_sb[:, k0:k0 + 128],
                                           pt_sb[:, bk * 512:(bk + 1) * 512],
                                           start=(i == 0), stop=(i == n_t - 1)),
                                  after=[t_m, prev_cf] if i == 0 else [t_m])
                    pv_prev = t_pv
                    den_prev = p.comp("tensor", lambda e, bk=bk, i=i, n_t=n_t:
                                      e.matmul(pA[3][0:1, :], onesb_sb[:, 0:1],
                                               pt_sb[:, bk * 512:(bk + 1) * 512],
                                               start=(i == 0), stop=(i == n_t - 1)),
                                      after=[t_m, t_onesb, prev_d1] if i == 0
                                      else [])
                    pt_readers[bk] = den_prev
                t_d1 = p.comp("scalar", lambda e: e.activation(
                    den_sb[:], pA[3][0:1, :], AF.Copy), after=[den_prev])
                prev_d1 = t_d1
                t_d2 = p.comp("vector", lambda e: e.reciprocal(
                    den_sb[:], den_sb[:]), after=[t_d1])
                t_d3 = p.comp("tensor", lambda e: e.matmul(
                    pA[4][:], onesf_sb[0:1, :], den_sb[:], start=True,
                    stop=True), after=[t_d2, prev_cx])
                prev_d3 = t_d3
                t_cf = p.comp("scalar", lambda e: e.activation(
                    ctxf_sb[:], pA[2][:], AF.Copy), after=[pv_prev, t_d1,
                                                           prev_cx])
                prev_cf = t_cf
                t_cx = p.comp("vector", lambda e, j=j: e.tensor_mul(
                    ctxh_sb[:, j * 512:(j + 1) * 512], ctxf_sb[:], pA[4][:]),
                    after=[t_cf, t_d3, prev_ctx_st])
                prev_cx = t_cx
                ctx_head_toks.append(t_cx)
            st_toks = []
            wslot = [None, None]
            wcv = [None, None]
            pe_prev = [None, None]
            for ot in range(NDT):
                sl = ot % 2
                t_wf = p.dma("sync", lambda e, ot=ot, sl=sl: e.dma_start(
                    out=wotf_sb[:, sl * 512:(sl + 1) * 512].rearrange(
                        "p (j o) -> p j o", o=128),
                    in_=wo_d[:, ot * 128:(ot + 1) * 128].rearrange(
                        "(j p) o -> p j o", p=128)),
                    after=[wcv[sl] if True else None])
                t_wl = p.comp("vector", lambda e, sl=sl: e.tensor_copy(
                    wot_sb[:, sl * 512:(sl + 1) * 512],
                    wotf_sb[:, sl * 512:(sl + 1) * 512]),
                    after=[t_wf, wslot[sl]])
                wcv[sl] = t_wl
                mo = None
                for j in range(4):
                    mo = p.comp("tensor", lambda e, sl=sl, j=j: e.matmul(
                        pb[:, sl * 512:(sl + 1) * 512],
                        wot_sb[:, sl * 512 + j * 128:sl * 512 + (j + 1) * 128],
                        ctxh_sb[:, j * 512:(j + 1) * 512],
                        start=(j == 0), stop=(j == 3)),
                        after=[t_wl, pe_prev[sl]] + ctx_head_toks
                        if j == 0 else [])
                wslot[sl] = mo
                t_pe = p.comp("scalar", lambda e, sl=sl: e.activation(
                    po_sb[:, sl * 512:(sl + 1) * 512],
                    pb[:, sl * 512:(sl + 1) * 512], AF.Copy),
                    after=[mo, st_toks[-2] if len(st_toks) >= 2 else None])
                pe_prev[sl] = t_pe
                t_st = p.dma("scalar", lambda e, qw=qw, ot=ot, sl=sl: e.dma_start(
                    out=rsi_d[(qw // 2) * D + ot * 128:
                              (qw // 2) * D + (ot + 1) * 128,
                              (qw % 2) * 512:(qw % 2 + 1) * 512],
                    in_=po_sb[:, sl * 512:(sl + 1) * 512]), after=[t_pe])
                st_toks.append(t_st)
            prev_ctx_st = st_toks[-1]
            xw_free = prev_ctx_st

        t_rs = p.add("gpsimd", lambda e: e.collective_compute(
            "ReduceScatter", ALU.add, rg, [rsi_d], [rso_d]),
            after=[prev_ctx_st], inc="ag", k=1)

        res_toks = []
        for half in range(2):
            for ot in range(NDT):
                sl = ot % 2
                t_x = p.dma("sync", lambda e, ot=ot, half=half, sl=sl:
                            e.dma_start(
                                out=xr_sb[:, sl * 512:(sl + 1) * 512],
                                in_=x_d[ot * 128:(ot + 1) * 128,
                                        half * 512:(half + 1) * 512]),
                            after=[res_toks[-2] if len(res_toks) >= 2 else None])
                t_p = p.dma("sync", lambda e, ot=ot, half=half, sl=sl:
                            e.dma_start(
                                out=po_sb[:, sl * 512:(sl + 1) * 512],
                                in_=rso_d[ot * 128:(ot + 1) * 128,
                                          half * 512:(half + 1) * 512]),
                            after=[t_rs,
                                   res_toks[-2] if len(res_toks) >= 2 else None])
                if layer == 0:
                    t_r = p.comp("vector", lambda e, ot=ot, sl=sl:
                                 e.scalar_tensor_tensor(
                                     xo_sb[:, sl * 512:(sl + 1) * 512],
                                     po_sb[:, sl * 512:(sl + 1) * 512],
                                     bo_sb[:, ot:ot + 1],
                                     xr_sb[:, sl * 512:(sl + 1) * 512],
                                     ALU.add, ALU.add), after=[t_x, t_p, t_bo])
                else:
                    t_r = p.comp("vector", lambda e, sl=sl: e.tensor_add(
                        xo_sb[:, sl * 512:(sl + 1) * 512],
                        po_sb[:, sl * 512:(sl + 1) * 512],
                        xr_sb[:, sl * 512:(sl + 1) * 512]), after=[t_x, t_p])
                t_o = p.dma("scalar", lambda e, ot=ot, half=half, sl=sl:
                            e.dma_start(
                                out=x1_d[ot * 128:(ot + 1) * 128,
                                         half * 512:(half + 1) * 512],
                                in_=xo_sb[:, sl * 512:(sl + 1) * 512]),
                            after=[t_r])
                res_toks.append(t_o)

    return res_toks[-1]




def prep_attn_maps(maps, inputs, layer, pre):
    if layer == 0:
        wqkvT = np.ascontiguousarray(np.asarray(inputs["L0_attn_w_qkv"]).T)
        bqkv = np.asarray(inputs["L0_attn_b_qkv"])
        wdT = np.asarray(inputs["L0_attn_w_dense"]).T
        boT = np.ascontiguousarray(
            np.asarray(inputs["L0_attn_b_dense"]).reshape(32, 128).T)
        cos, sinS = rope_tables()
    else:
        wq = np.asarray(inputs["L1_attn_wq"])
        wk = np.asarray(inputs["L1_attn_wk"])
        wv = np.asarray(inputs["L1_attn_wv"])
        wo1 = np.asarray(inputs["L1_attn_wo"])
        wkv = np.ascontiguousarray(np.concatenate([wk, wv], axis=1))
    _plan, masks = build_tile_plan(layer)
    for c in range(NCORES):
        m = maps[c]
        wsl = wdT if layer == 0 else wo1
        m[pre + "wo"] = np.ascontiguousarray(wsl[512 * c:512 * (c + 1), :])
        m[pre + "masks"] = masks
        if layer == 0:
            m[pre + "wq"] = np.ascontiguousarray(wqkvT[:, 768 * c:768 * c + 512])
            m[pre + "wkv"] = np.ascontiguousarray(
                wqkvT[:, 768 * c + 512:768 * (c + 1)])
            m[pre + "bq"] = np.ascontiguousarray(
                bqkv[768 * c:768 * (c + 1)].reshape(6, 128).T)
            m[pre + "bo"] = boT
            m["cosT"] = cos
            m["sinSW"] = np.ascontiguousarray(np.roll(sinS, 64, axis=0))
        else:
            m[pre + "wq"] = np.ascontiguousarray(wq[:, 512 * c:512 * (c + 1)])
            m[pre + "wkv"] = wkv




def emit_ln(p, nc, x_d, xt_sb, sq_sb, stat_sb, bc_sb, onesf_sb, pm, pb,
            t_ones, write_fn):
    TPCl = TPC
    mm_tok = None
    slot_free = [None, None]
    for ic in range(NDT):
        sl = ic % 2
        t_ld = p.dma("sync", lambda e, ic=ic, sl=sl: e.dma_start(
            out=xt_sb[:, sl * TPCl:(sl + 1) * TPCl],
            in_=x_d[ic * 128:(ic + 1) * 128, :]), after=[slot_free[sl]])
        t_sq = p.comp("scalar", lambda e, sl=sl: e.activation(
            sq_sb[:, 0:TPCl], xt_sb[:, sl * TPCl:(sl + 1) * TPCl], AF.Square),
            after=[t_ld, mm_tok])
        for h in range(2):
            p.comp("tensor", lambda e, sl=sl, h=h, ic=ic: e.matmul(
                pm[h][0:1, :], onesf_sb[:, 0:1],
                xt_sb[:, sl * TPCl + h * 512:sl * TPCl + (h + 1) * 512],
                start=(ic == 0), stop=(ic == NDT - 1)),
                after=[t_ld, t_ones])
            mm_tok = p.comp("tensor", lambda e, h=h, ic=ic: e.matmul(
                pm[2 + h][0:1, :], onesf_sb[:, 0:1],
                sq_sb[:, h * 512:(h + 1) * 512],
                start=(ic == 0), stop=(ic == NDT - 1)),
                after=[t_sq])
        slot_free[sl] = mm_tok
    m_ = stat_sb[0:1, 0:TPCl]
    msq = stat_sb[0:1, TPCl:2 * TPCl]
    m2 = stat_sb[0:1, 2 * TPCl:3 * TPCl]
    var = stat_sb[0:1, 3 * TPCl:4 * TPCl]
    r_ = stat_sb[0:1, 3 * TPCl:4 * TPCl]
    mr = stat_sb[0:1, 2 * TPCl:3 * TPCl]
    t1 = p.comp("scalar", lambda e: e.activation(
        m_[:, 0:512], pm[0][0:1, :], AF.Copy, scale=1.0 / D), after=[mm_tok])
    t1 = p.comp("scalar", lambda e: e.activation(
        m_[:, 512:1024], pm[1][0:1, :], AF.Copy, scale=1.0 / D), after=[t1])
    t2 = p.comp("scalar", lambda e: e.activation(
        msq[:, 0:512], pm[2][0:1, :], AF.Copy, scale=1.0 / D), after=[t1])
    t2 = p.comp("scalar", lambda e: e.activation(
        msq[:, 512:1024], pm[3][0:1, :], AF.Copy, scale=1.0 / D), after=[t2])
    t3 = p.comp("scalar", lambda e: e.activation(m2[:], m_[:], AF.Square),
                after=[t2])
    t4 = p.comp("vector", lambda e: e.tensor_sub(var[:], msq[:], m2[:]),
                after=[t3])
    t4b = p.comp("vector", lambda e: e.tensor_scalar_add(var[:], var[:], 1e-5),
                 after=[t4])
    t5 = p.comp("scalar", lambda e: e.activation(var[:], var[:], AF.Sqrt),
                after=[t4b])
    t6 = p.comp("vector", lambda e: e.reciprocal(r_[:], var[:]), after=[t5])
    t7 = p.comp("vector", lambda e: e.tensor_mul(mr[:], m_[:], r_[:]),
                after=[t6])
    bts = [p.comp("tensor", lambda e, h=h: e.matmul(
        pb[:, h * 512:(h + 1) * 512], onesf_sb[0:1, :],
        r_[:, h * 512:(h + 1) * 512], start=True, stop=True),
        after=[t6]) for h in range(2)]
    t8 = p.comp("scalar", lambda e: e.activation(bc_sb[:, 0:TPCl], pb[:],
                                                 AF.Copy), after=bts)
    bts2 = [p.comp("tensor", lambda e, h=h: e.matmul(
        pb[:, h * 512:(h + 1) * 512], onesf_sb[0:1, :],
        mr[:, h * 512:(h + 1) * 512], start=True, stop=True),
        after=[t7, t8]) for h in range(2)]
    t9 = p.comp("scalar", lambda e: e.activation(bc_sb[:, TPCl:2 * TPCl], pb[:],
                                                 AF.Copy), after=bts2)
    rb = bc_sb[:, 0:TPCl]
    mrb = bc_sb[:, TPCl:2 * TPCl]
    toks = []
    slot_free = [None, None]
    for ic in range(NDT):
        sl = ic % 2
        t_ld = p.dma("sync", lambda e, ic=ic, sl=sl: e.dma_start(
            out=xt_sb[:, sl * TPCl:(sl + 1) * TPCl],
            in_=x_d[ic * 128:(ic + 1) * 128, :]), after=[t8, slot_free[sl]])
        tm = p.comp("vector", lambda e, sl=sl: e.tensor_mul(
            xt_sb[:, sl * TPCl:(sl + 1) * TPCl],
            xt_sb[:, sl * TPCl:(sl + 1) * TPCl], rb), after=[t_ld, t8])
        tok = write_fn(ic, sl, tm, t9, mrb)
        toks.append(tok)
        slot_free[sl] = tok
    return toks, bts2[-1]




def emit_l1_ffn(p, io, pre):
    nc = p.nc
    x_d = io["x"]
    bw_d = io["bw"]
    bb_d = io["bb"]
    at_d = io["at"]
    bs_d = io["bs"]
    rw_d = io["rw"]
    eyeb_d = io["eyeb"]
    x1_d = io["xout"]

    ctx = contextlib.ExitStack()
    with ctx:
        sb = lambda name, shape, dt: ctx.enter_context(
            nc.sbuf_tensor(pre + name, shape, dt))
        xln_sb = sb("xln_sb", [128, NDT * TPC], BF16)
        bwb_sb = sb("bwb_sb", [128, 2 * NDT * 128], BF16)
        bsb_sb = sb("bsb_sb", [128, 2 * 8 * 128], BF16)
        atb_sb = sb("atb_sb", [128, NDT * 64], BF16)
        rwb_sb = sb("rwb_sb", [128, NDT * 16], BF16)
        eyeb_sb = sb("eyeb_sb", [128, 128], BF16)
        onesf_sb = sb("onesf_sb", [128, 128], F32)
        xt_sb = sb("xt_sb", [128, 2 * TPC], F32)
        sq_sb = sb("sq_sb", [128, TPC], F32)
        stat_sb = sb("stat_sb", [1, 4 * TPC], F32)
        bc_sb = sb("bc_sb", [128, 2 * TPC], F32)
        bb_sb = sb("bb_sb", [128, 32], F32)
        pr_sb = sb("pr_sb", [128, 16], F32)
        sc_sb = sb("sc_sb", [128, 8], F32)
        comba_sb = sb("comba_sb", [128, 8 * 16], BF16)
        rowE_sb = sb("rowE_sb", [1, TPC], F32)
        rhT_sb = sb("rhT_sb", [64, TPC], F32)
        RH_sb = sb("RH_sb", [128, 8 * TPC], BF16)
        xr_sb = sb("xr_sb", [128, 2 * 512], F32)
        xo_sb = sb("xo_sb", [128, 2 * 512], F32)

        psb = lambda name, shape, dt: ctx.enter_context(
            nc.psum_tensor(pre + name, shape, dt))
        pb = psb("pb", [128, TPC], F32)
        pA = [psb(f"pA{i}", [128, 512], F32) for i in range(4)]
        ptr = psb("ptr", [128, 512], BF16)

        t_eye = p.dma("sync", lambda e: e.dma_start(out=eyeb_sb[:], in_=eyeb_d))
        t_bb = p.dma("sync", lambda e: e.dma_start(out=bb_sb[:], in_=bb_d))
        t_ones = p.comp("vector", lambda e: e.memset(onesf_sb[:], 1.0))
        t_at = p.dma("gpsimd", lambda e: e.dma_start(
            out=atb_sb[:].rearrange("p (a o) -> p a o", o=64),
            in_=at_d.rearrange("(a p) o -> p a o", p=128)))
        t_rw = p.dma("gpsimd", lambda e: e.dma_start(
            out=rwb_sb[:].rearrange("p (a o) -> p a o", o=16),
            in_=rw_d.rearrange("(a p) o -> p a o", p=128)))

        def xw(ic, sl, tm, t9, mrb):
            return p.comp("vector", lambda e, ic=ic, sl=sl: e.tensor_sub(
                xln_sb[:, ic * TPC:(ic + 1) * TPC],
                xt_sb[:, sl * TPC:(sl + 1) * TPC], mrb), after=[tm, t9])

        xln_toks, t_lnmm = emit_ln(p, nc, x_d, xt_sb, sq_sb, stat_sb, bc_sb,
                                   onesf_sb, pA, pb, t_ones, xw)
        xln_done = xln_toks[-1]

        def xln(ic):
            return xln_sb[:, ic * TPC:(ic + 1) * TPC]

        mrh = None
        for h in range(2):
            for ic in range(NDT):
                mrh = p.comp("tensor", lambda e, ic=ic, h=h: e.matmul(
                    pb[0:64, h * 512:(h + 1) * 512],
                    atb_sb[:, ic * 64:(ic + 1) * 64],
                    xln(ic)[:, h * 512:(h + 1) * 512],
                    start=(ic == 0), stop=(ic == NDT - 1)),
                    after=[t_at, xln_done, t_lnmm] if (ic == 0 and h == 0)
                    else [])
        t_rh = p.comp("scalar", lambda e: e.activation(rhT_sb[:], pb[0:64, :],
                                                       AF.Copy), after=[mrh])

        tr_toks = []
        for tt in range(8):
            mro = None
            for ic in range(NDT):
                mro = p.comp("tensor", lambda e, ic=ic, tt=tt: e.matmul(
                    pA[0][:, 0:16],
                    xln_sb[:, ic * TPC + tt * 128:ic * TPC + (tt + 1) * 128],
                    rwb_sb[:, ic * 16:(ic + 1) * 16],
                    start=(ic == 0), stop=(ic == NDT - 1)),
                    after=[t_rw, xln_done, t_rh,
                           tr_toks[-1] if tr_toks else None]
                    if ic == 0 else [])
            c0 = p.comp("scalar", lambda e: e.activation(
                pr_sb[:], pA[0][:, 0:16], AF.Copy),
                after=[mro, tr_toks[-1] if tr_toks else None])
            c1 = p.comp("vector", lambda e: e.tensor_reduce(
                sc_sb[:, 0:1], pr_sb[:], AX.X, ALU.max), after=[c0])
            c2 = p.comp("vector", lambda e: e.tensor_scalar_mul(
                sc_sb[:, 1:2], sc_sb[:, 0:1], -1.0), after=[c1])
            c3 = p.comp("scalar", lambda e: e.activation(
                pr_sb[:], pr_sb[:], AF.Exp, bias=sc_sb[:, 1:2]), after=[c2])
            c4 = p.comp("vector", lambda e: e.tensor_reduce(
                sc_sb[:, 2:3], pr_sb[:], AX.X, ALU.add), after=[c3])
            c5 = p.comp("vector", lambda e: e.reciprocal(
                sc_sb[:, 3:4], sc_sb[:, 2:3]), after=[c4])
            c6 = p.comp("vector", lambda e: e.tensor_scalar_mul(
                pr_sb[:], pr_sb[:], sc_sb[:, 3:4]), after=[c5])
            c7 = p.comp("vector", lambda e: e.tensor_reduce(
                sc_sb[:, 0:1], pr_sb[:], AX.X, ALU.max), after=[c6])
            c8 = p.comp("vector", lambda e: e.tensor_scalar(
                sq_sb[:, 0:16], pr_sb[:], sc_sb[:, 0:1], None, ALU.is_equal),
                after=[c7])
            c9 = p.comp("vector", lambda e: e.scalar_tensor_tensor(
                sq_sb[:, 16:32], sq_sb[:, 0:16], -2.0, pr_sb[:],
                ALU.mult, ALU.add), after=[c8])
            c10 = p.comp("vector", lambda e: e.tensor_reduce(
                sc_sb[:, 1:2], sq_sb[:, 16:32], AX.X, ALU.max), after=[c9])
            c11 = p.comp("vector", lambda e: e.tensor_scalar(
                sq_sb[:, 32:48], pr_sb[:], sc_sb[:, 1:2], None, ALU.is_equal),
                after=[c10])
            c12 = p.comp("vector", lambda e: e.tensor_add(
                sq_sb[:, 0:16], sq_sb[:, 0:16], sq_sb[:, 32:48]), after=[c11])
            c13 = p.comp("vector", lambda e: e.tensor_add(
                sc_sb[:, 4:5], sc_sb[:, 0:1], sc_sb[:, 1:2]), after=[c10])
            c14 = p.comp("vector", lambda e: e.tensor_scalar_add(
                sc_sb[:, 4:5], sc_sb[:, 4:5], 1e-6), after=[c13])
            c15 = p.comp("vector", lambda e: e.reciprocal(
                sc_sb[:, 5:6], sc_sb[:, 4:5]), after=[c14])
            c16 = p.comp("vector", lambda e: e.tensor_scalar_mul(
                sc_sb[:, 5:6], sc_sb[:, 5:6], 0.5), after=[c15])
            c17 = p.comp("vector", lambda e: e.tensor_mul(
                sq_sb[:, 0:16], sq_sb[:, 0:16], pr_sb[:]), after=[c12, c6])
            c18 = p.comp("vector", lambda e, tt=tt: e.tensor_scalar_mul(
                comba_sb[:, tt * 16:(tt + 1) * 16], sq_sb[:, 0:16],
                sc_sb[:, 5:6]), after=[c16, c17])
            tr_toks.append(c18)

        rh_toks = []
        prev_tr = None
        prev_row = None
        for e_ in range(16):
            row_toks = []
            for tt in range(8):
                ta = p.comp("tensor", lambda e, e_=e_, tt=tt: e.transpose(
                    ptr[0:1, 0:128],
                    comba_sb[:, tt * 16 + e_:tt * 16 + e_ + 1], eyeb_sb[:]),
                    after=[tr_toks[tt], t_eye, prev_tr,
                           rh_toks[-1] if rh_toks else None])
                prev_tr = p.comp("scalar", lambda e, tt=tt: e.activation(
                    rowE_sb[0:1, tt * 128:(tt + 1) * 128], ptr[0:1, 0:128],
                    AF.Copy), after=[ta, prev_row])
                row_toks.append(prev_tr)
            bc1 = p.comp("tensor", lambda e: e.matmul(
                pA[1][0:64, :], onesf_sb[0:1, 0:64], rowE_sb[0:1, 0:512],
                start=True, stop=True), after=row_toks + [prev_row])
            bc2 = p.comp("tensor", lambda e: e.matmul(
                pA[2][0:64, :], onesf_sb[0:1, 0:64], rowE_sb[0:1, 512:1024],
                start=True, stop=True), after=row_toks)
            cc = e_ // 2
            half = (e_ % 2) * 64
            m1 = p.comp("vector", lambda e, cc=cc, half=half: e.tensor_mul(
                RH_sb[half:half + 64, cc * TPC:cc * TPC + 512],
                pA[1][0:64, :], rhT_sb[0:64, 0:512]), after=[bc1, t_rh])
            m2 = p.comp("vector", lambda e, cc=cc, half=half: e.tensor_mul(
                RH_sb[half:half + 64, cc * TPC + 512:(cc + 1) * TPC],
                pA[2][0:64, :], rhT_sb[0:64, 512:1024]), after=[bc2, t_rh])
            prev_row = m2
            rh_toks += [m1, m2]

        res_toks = []
        wslot = [None, None]
        bslot = [None, None]
        for ot in range(NDT):
            sl = ot % 2
            t_wl = p.dma("gpsimd", lambda e, ot=ot, sl=sl: e.dma_start(
                out=bwb_sb[:, sl * NDT * 128:(sl + 1) * NDT * 128].rearrange(
                    "p (a o) -> p a o", o=128),
                in_=bw_d[:, ot * 128:(ot + 1) * 128].rearrange(
                    "(a p) o -> p a o", p=128)), after=[wslot[sl]])
            t_bs = p.dma("gpsimd", lambda e, ot=ot, sl=sl: e.dma_start(
                out=bsb_sb[:, sl * 8 * 128:(sl + 1) * 8 * 128].rearrange(
                    "p (a o) -> p a o", o=128),
                in_=bs_d[:, ot * 128:(ot + 1) * 128].rearrange(
                    "(a p) o -> p a o", p=128)), after=[bslot[sl]])
            tok = None
            for h in range(2):
                for ic in range(NDT):
                    tok = p.comp("tensor", lambda e, sl=sl, ic=ic, h=h: e.matmul(
                        pb[:, h * 512:(h + 1) * 512],
                        bwb_sb[:, sl * NDT * 128 + ic * 128:
                               sl * NDT * 128 + (ic + 1) * 128],
                        xln(ic)[:, h * 512:(h + 1) * 512],
                        start=(ic == 0), stop=False),
                        after=[t_wl, xln_done,
                               res_toks[-2] if len(res_toks) >= 2 else None,
                               rh_toks[-1]]
                        if (ic == 0 and h == 0) else [])
                for cc in range(8):
                    tok = p.comp("tensor", lambda e, sl=sl, cc=cc, h=h: e.matmul(
                        pb[:, h * 512:(h + 1) * 512],
                        bsb_sb[:, sl * 8 * 128 + cc * 128:
                               sl * 8 * 128 + (cc + 1) * 128],
                        RH_sb[:, cc * TPC + h * 512:cc * TPC + (h + 1) * 512],
                        start=False, stop=(cc == 7)),
                        after=[t_bs, rh_toks[-1]] if (cc == 0 and h == 0)
                        else [])
            wslot[sl] = tok
            bslot[sl] = tok
            t_r_prev = None
            for half in range(2):
                t_x = p.dma("sync", lambda e, ot=ot, half=half, sl=sl:
                            e.dma_start(
                                out=xr_sb[:, sl * 512:(sl + 1) * 512],
                                in_=x_d[ot * 128:(ot + 1) * 128,
                                        half * 512:(half + 1) * 512]),
                            after=[t_r_prev,
                                   res_toks[-2] if len(res_toks) >= 2
                                   else None])
                t_r = p.comp("vector", lambda e, ot=ot, half=half, sl=sl:
                             e.scalar_tensor_tensor(
                                 xo_sb[:, sl * 512:(sl + 1) * 512],
                                 pb[:, half * 512:(half + 1) * 512],
                                 bb_sb[:, ot:ot + 1],
                                 xr_sb[:, sl * 512:(sl + 1) * 512],
                                 ALU.add, ALU.add),
                             after=[tok, t_x, t_bb,
                                    res_toks[-1] if res_toks else None])
                t_r_prev = t_r
                t_o = p.dma("gpsimd", lambda e, ot=ot, half=half, sl=sl:
                            e.dma_start(
                                out=x1_d[ot * 128:(ot + 1) * 128,
                                         half * 512:(half + 1) * 512],
                                in_=xo_sb[:, sl * 512:(sl + 1) * 512]),
                            after=[t_r])
                res_toks.append(t_o)
    return res_toks[-1]


def prep_l1_ffn_maps(maps, inputs, pre):
    bwT = np.ascontiguousarray(np.asarray(inputs["L1_ffn_base_w"]).T)
    bbT = np.ascontiguousarray(np.asarray(inputs["L1_ffn_base_b"]).reshape(32, 128).T)
    AT = np.ascontiguousarray(np.asarray(inputs["L1_ffn_A"]).T)
    BsT = np.ascontiguousarray(
        np.asarray(inputs["L1_ffn_B"]).transpose(0, 2, 1).reshape(1024, D))
    rwT = np.ascontiguousarray(np.asarray(inputs["L1_ffn_router_w"]).T)
    for c in range(NCORES):
        maps[c].update({pre + "bw": bwT, pre + "bb": bbT, pre + "at": AT,
                        pre + "bs": BsT, pre + "rw": rwT})



NCHUNK = 11
NREAL_IB = 86


NPAR = 3


def emit_l0_ffn(p, io, pre, n_chunks=NCHUNK):
    nc = p.nc
    x_d = io["x"]
    cb_d = io["cb"]
    iw_d = io["iw"]
    idm_d = io["idm"]
    x1_d = io["xout"]
    cbp_d = io["cbp"]
    wgst_d = io["wgst"]
    wust_d = io["wust"]
    wdst_d = io["wdst"]
    wgag_d = io["wgag"]
    wuag_d = io["wuag"]
    wdag_d = io["wdag"]

    ctx = contextlib.ExitStack()
    with ctx:
        sb = lambda name, shape, dt: ctx.enter_context(
            nc.sbuf_tensor(pre + name, shape, dt))
        xln_sb = sb("xln_sb", [128, NDT * TPC], BF16)
        ggb_sb = sb("ggb_sb", [128, 4096], BF16)
        gub_sb = sb("gub_sb", [128, 4096], BF16)
        dstf_sb = sb("dstf_sb", [128, 4, 32, 128], BF16)
        i16w_sb = sb("i16w_sb", [128, 4096], I16)
        i32w_sb = sb("i32w_sb", [128, 2048], I32)
        i32d_sb = sb("i32d_sb", [128, 512], I32)
        mk_sb = sb("mk_sb", [128, 3 * 512], F32)
        tmpf_sb = sb("tmpf_sb", [128, 32, 8], F32)
        wdT_sb = sb("wdT_sb", [128, 2 * 384], BF16)
        h_sb = sb("h_sb", [128, 8 * TPC], BF16)
        hg_sb = sb("hg_sb", [128, TPC], BF16)
        onesf_sb = sb("onesf_sb", [128, 128], F32)
        xt_sb = sb("xt_sb", [128, 2 * TPC], F32)
        lnctx = contextlib.ExitStack()
        sq_sb = lnctx.enter_context(nc.sbuf_tensor(pre + "sq_sb", [128, TPC], F32))
        stat_sb = lnctx.enter_context(
            nc.sbuf_tensor(pre + "stat_sb", [1, 4 * TPC], F32))
        bc_sb = lnctx.enter_context(
            nc.sbuf_tensor(pre + "bc_sb", [128, 2 * TPC], F32))
        lnctx.close()
        wgT_sb = sb("wgT_sb", [128, 2, NDT, 128], BF16)
        wuT_sb = sb("wuT_sb", [128, 2, NDT, 128], BF16)
        ggT_sb = sb("ggT_sb", [128, 32, 128], BF16)
        xr_sb = xt_sb[:, 0:TPC]
        xo_sb = xt_sb[:, TPC:2 * TPC]

        psb = lambda name, shape, dt: ctx.enter_context(
            nc.psum_tensor(pre + name, shape, dt))
        pb = psb("pb", [128, TPC], F32)
        pA = [psb(f"pA{i}", [128, 512], F32) for i in range(6)]

        rg = [list(range(NCORES))]

        t_ones = p.comp("vector", lambda e: e.memset(onesf_sb[:], 1.0))

        t_init = p.dma("sync", lambda e: e.dma_start(out=x1_d, in_=x_d))

        t_cbp = None
        for r0 in range(0, 32768, 4096):
            t_cbp = p.dma("gpsimd", lambda e, r0=r0: e.dma_start(
                out=cbp_d[r0:r0 + 4096, 0:16],
                in_=cb_d[2 * r0:2 * r0 + 8192, :].rearrange(
                    "(r t) e -> r (t e)", t=2)), after=[t_cbp])

        def xw(ic, sl, tm, t9, mrb):
            return p.comp("vector", lambda e, ic=ic, sl=sl: e.tensor_sub(
                xln_sb[:, ic * TPC:(ic + 1) * TPC],
                xt_sb[:, sl * TPC:(sl + 1) * TPC], mrb), after=[tm, t9])

        xln_toks, t_lnmm = emit_ln(p, nc, x_d, xt_sb, sq_sb, stat_sb, bc_sb,
                                   onesf_sb, pA, pb, t_ones, xw)
        xln_done = xln_toks[-1]

        def xln(ic):
            return xln_sb[:, ic * TPC:(ic + 1) * TPC]

        ag_toks = []
        xacc_tok = [t_init] * 64
        xacc_ld = [None] * 64
        compute_done = []

        NPC = 16
        regcell = {}

        def setup_reg(e):
            regcell["r"] = e.alloc_register(pre + "nidx")
            e.reg_mov(regcell["r"], 4096)

        p.add("gpsimd", setup_reg)
        t_i16_prev = None
        t_mb_prev = None
        sel_last = [None, None, None]
        tT_by_buf = {"g": None, "u": None}
        stage_last = None
        last_gq = [None] * 4
        sel_q = [None] * 4
        sel_prev = None
        ag_abc = [None] * n_chunks

        def phaseA(c):
            nonlocal t_i16_prev, t_mb_prev, stage_last, sel_prev
            par = c % 3
            stage_toks = []
            for m in range(3):
                tgt_sb = gub_sb if m == 1 else ggb_sb
                t_i16 = None
                for hf in range(2):
                    t_lw = p.dma("scalar", lambda e, m=m, c=c, hf=hf:
                                 e.dma_start(
                                     out=i32w_sb[:],
                                     in_=iw_d[m, c, :,
                                              hf * 2048:(hf + 1) * 2048]),
                                 after=[t_i16_prev, t_i16])
                    t_sh = p.comp("vector", lambda e: e.tensor_scalar(
                        i32w_sb[:], i32w_sb[:], 1, None,
                        ALU.logical_shift_right), after=[t_lw])
                    t_i16 = p.comp("vector", lambda e, hf=hf: e.tensor_copy(
                        i16w_sb[:, hf * 2048:(hf + 1) * 2048], i32w_sb[:]),
                        after=[t_sh] + (list(last_gq) if hf == 0 else []))
                t_i16_prev = t_i16
                t_lm = p.dma("scalar", lambda e, m=m, c=c: e.dma_start(
                    out=i32d_sb[:], in_=idm_d[m, c]), after=[t_mb_prev])
                t_and = p.comp("vector", lambda e: e.tensor_scalar(
                    i32d_sb[:], i32d_sb[:], 1, None, ALU.bitwise_and),
                    after=[t_lm, t_mb_prev])
                t_mb = p.comp("vector", lambda e, m=m: e.tensor_copy(
                    mk_sb[:, m * 512:(m + 1) * 512], i32d_sb[:]),
                    after=[t_and, sel_last[m]])
                t_mb_prev = t_mb
                war = tT_by_buf["u" if m == 1 else "g"]
                for pi in range(NPC):
                    q = pi % 4
                    t_g = p.add("gpsimd", lambda e, pi=pi, q=q: e.dma_gather(
                        dstf_sb[:, q], cbp_d,
                        i16w_sb[:, pi * 256:(pi + 1) * 256],
                        4096, regcell["r"], 128, elem_step=128,
                        single_packet=False, queue_num=q),
                        after=([t_i16, t_cbp] if pi < 4 else []) + [sel_q[q]],
                        inc=f"gq{q}", k=16, serial=False)
                    last_gq[q] = t_g
                    s1 = p.comp("vector", lambda e, q=q: e.tensor_sub(
                        tmpf_sb[:], dstf_sb[:, q, :, 8:16],
                        dstf_sb[:, q, :, 0:8]), after=[t_g, sel_prev])
                    m8 = mk_sb[:, m * 512 + pi * 32:m * 512 + (pi + 1) * 32]
                    s2 = p.comp("vector", lambda e, m8=m8: e.tensor_mul(
                        tmpf_sb[:], tmpf_sb[:],
                        m8.rearrange("p (j o) -> p j o", o=1).broadcast_to(
                            [128, 32, 8])), after=[s1, t_mb])
                    sel_prev = p.comp(
                        "vector", lambda e, pi=pi, q=q, tgt_sb=tgt_sb:
                        e.tensor_add(
                            tgt_sb[:, pi * 256:(pi + 1) * 256]
                            .rearrange("p (j e) -> p j e", e=8),
                            tmpf_sb[:], dstf_sb[:, q, :, 0:8]),
                        after=[s2, war if pi == 0 else None])
                    sel_q[q] = sel_prev
                sel_last[m] = sel_prev
                tT = None
                for b in range(32):
                    slot = ((b % 8) * 4 + b // 8) if m == 2 else b
                    tT = p.dma("sync", lambda e, b=b, slot=slot,
                               tgt_sb=tgt_sb: e.dma_start(
                                   out=ggT_sb[:, slot, :],
                                   in_=tgt_sb[:, b * 128:(b + 1) * 128],
                                   transpose=True),
                               after=[sel_prev, stage_last] if b == 0 else [])
                tT_by_buf["u" if m == 1 else "g"] = tT
                agg = ag_toks[-3] if len(ag_toks) >= 3 else None
                if m == 0:
                    st = p.dma("gpsimd", lambda e, par=par: e.dma_start(
                        out=wgst_d[par].rearrange("(a p) q -> p a q", p=128),
                        in_=ggT_sb[:]), after=[tT, agg])
                elif m == 1:
                    st = p.dma("gpsimd", lambda e, par=par: e.dma_start(
                        out=wust_d[par].rearrange("(a p) q -> p a q", p=128),
                        in_=ggT_sb[:]), after=[tT, agg])
                else:
                    st = p.dma("gpsimd", lambda e, par=par: e.dma_start(
                        out=wdst_d[par].rearrange("(cb p) (s h) -> p cb s h",
                                                  p=128, h=128),
                        in_=ggT_sb[:].rearrange("p (cb s) h -> p cb s h",
                                                s=4)), after=[tT, agg])
                stage_last = st
                stage_toks.append(st)
            t_sg, t_su, t_sd = stage_toks
            cgate = compute_done[c - 3] if c >= 3 else None
            a1 = p.add("gpsimd", lambda e, par=par: e.collective_compute(
                "AllGather", ALU.bypass, rg, [wgst_d[par]], [wgag_d[par]]),
                after=[t_sg, cgate], inc="ag", k=1, serial=False)
            a2 = p.add("gpsimd", lambda e, par=par: e.collective_compute(
                "AllGather", ALU.bypass, rg, [wust_d[par]], [wuag_d[par]]),
                after=[t_su], inc="ag", k=1, serial=False)
            a3 = p.add("gpsimd", lambda e, par=par: e.collective_compute(
                "AllGather", ALU.bypass, rg, [wdst_d[par]], [wdag_d[par]]),
                after=[t_sd], inc="ag", k=1, serial=False)
            ag_toks.append(a3)
            ag_abc[c] = (a1, a2, a3)

        def phaseB(c):
            par = c % 3
            a1, a2, a3 = ag_abc[c]
            n_ib = min(8, NREAL_IB - c * 8)
            h_toks = []
            prev_pair_mm = compute_done[-1] if compute_done else None
            up_last = None
            for pp in range((n_ib + 1) // 2):
                twg = []
                twu = []
                for jj in range(2):
                    t_ = pp * 2 + jj
                    twg.append(p.dma("sync", lambda e, par=par, t_=t_, jj=jj:
                                     e.dma_start(
                                         out=wgT_sb[:, jj],
                                         in_=wgag_d[par][t_ * 4096:
                                                         (t_ + 1) * 4096, :]
                                         .rearrange("(a p) q -> p a q", p=128)),
                                     after=[a1, prev_pair_mm, xln_done,
                                            t_lnmm]))
                    twu.append(p.dma("sync", lambda e, par=par, t_=t_, jj=jj:
                                     e.dma_start(
                                         out=wuT_sb[:, jj],
                                         in_=wuag_d[par][t_ * 4096:
                                                         (t_ + 1) * 4096, :]
                                         .rearrange("(a p) q -> p a q", p=128)),
                                     after=[a2, prev_pair_mm, xln_done,
                                            t_lnmm]))
                for jj in range(2):
                    ib = pp * 2 + jj
                    if ib >= n_ib:
                        continue
                    mg = mu = None
                    for h in range(2):
                        for ic in range(NDT):
                            mg = p.comp("tensor", lambda e, ic=ic, jj=jj, h=h:
                                        e.matmul(
                                            pb[:, h * 512:(h + 1) * 512],
                                            wgT_sb[:, jj, ic, :],
                                            xln(ic)[:, h * 512:(h + 1) * 512],
                                            start=(ic == 0),
                                            stop=(ic == NDT - 1)),
                                        after=[twg[jj], xln_done, t_lnmm,
                                               up_last]
                                        if (h == 0 and ic == 0) else [])
                    for h in range(2):
                        for ic in range(NDT):
                            mu = p.comp("tensor", lambda e, ic=ic, jj=jj, h=h:
                                        e.matmul(
                                            pA[h][:],
                                            wuT_sb[:, jj, ic, :],
                                            xln(ic)[:, h * 512:(h + 1) * 512],
                                            start=(ic == 0),
                                            stop=(ic == NDT - 1)),
                                        after=[twu[jj], mg, up_last]
                                        if (h == 0 and ic == 0) else
                                        ([mg] if ic == 0 else []))
                    ts1 = p.comp("scalar", lambda e: e.activation(
                        hg_sb[:, 0:512], pb[:, 0:512], AF.Silu), after=[mg, mu])
                    ts2 = p.comp("scalar", lambda e: e.activation(
                        hg_sb[:, 512:1024], pb[:, 512:1024], AF.Silu),
                        after=[ts1])
                    ts3 = p.comp("scalar", lambda e, ib=ib: e.activation(
                        h_sb[:, ib * TPC:ib * TPC + 512], pA[0][:], AF.Copy),
                        after=[ts2, h_toks[-1] if h_toks else None])
                    ts4 = p.comp("scalar", lambda e, ib=ib: e.activation(
                        h_sb[:, ib * TPC + 512:(ib + 1) * TPC], pA[1][:],
                        AF.Copy), after=[ts3])
                    tm = p.comp("vector", lambda e, ib=ib: e.tensor_mul(
                        h_sb[:, ib * TPC:(ib + 1) * TPC],
                        h_sb[:, ib * TPC:(ib + 1) * TPC], hg_sb[:]),
                        after=[ts4])
                    h_toks.append(tm)
                    up_last = tm
                prev_pair_mm = up_last
            dn_last = None
            ev_last = None
            for hg0 in range(0, NDT, 3):
                ntile = min(3, NDT - hg0)
                slot_mm = [None, None]
                twd = []
                for ib in range(n_ib):
                    t_wd = None
                    for tl in range(ntile):
                        tg = hg0 + tl
                        t_wd = p.dma(
                            "sync", lambda e, par=par, ib=ib, tl=tl, tg=tg:
                            e.dma_start(
                                out=wdT_sb[:, (ib % 2) * 384 + tl * 128:
                                           (ib % 2) * 384 + (tl + 1) * 128],
                                in_=wdag_d[par][
                                    (tg // 4) * 1024 + ib * 128:
                                    (tg // 4) * 1024 + (ib + 1) * 128,
                                    (tg % 4) * 128:(tg % 4 + 1) * 128]),
                            after=[a3, slot_mm[ib % 2],
                                   h_toks[-1] if ib == 0 else None,
                                   ev_last if ib == 0 else None]
                            if tl == 0 else [slot_mm[ib % 2]])
                    twd.append(t_wd)
                    for t in range(ntile):
                        for h in range(2):
                            dn_last = p.comp(
                                "tensor",
                                lambda e, ib=ib, t=t, h=h, n_ib=n_ib:
                                e.matmul(
                                    pA[t * 2 + h][:],
                                    wdT_sb[:, (ib % 2) * 384 + t * 128:
                                           (ib % 2) * 384 + (t + 1) * 128],
                                    h_sb[:, ib * TPC + h * 512:
                                         ib * TPC + (h + 1) * 512],
                                    start=(ib == 0), stop=(ib == n_ib - 1)),
                                after=[t_wd, h_toks[ib]] +
                                ([ev_last] if ib == 0 else [])
                                if (t == 0 and h == 0) else [])
                    slot_mm[ib % 2] = dn_last
                for t in range(ntile):
                    ot = hg0 + t
                    for h in range(2):
                        sl = (2 * t + h) % 2
                        ti = ot * 2 + h
                        t_x = p.dma("sync", lambda e, ot=ot, h=h, sl=sl:
                                    e.dma_start(
                                        out=xr_sb[:, sl * 512:(sl + 1) * 512],
                                        in_=x1_d[ot * 128:(ot + 1) * 128,
                                                 h * 512:(h + 1) * 512]),
                                    after=[xacc_tok[ti], dn_last,
                                           xacc_ld[ti]])
                        t_r = p.comp("vector", lambda e, t=t, h=h, sl=sl:
                                     e.tensor_add(
                                         xo_sb[:, sl * 512:(sl + 1) * 512],
                                         pA[t * 2 + h][:],
                                         xr_sb[:, sl * 512:(sl + 1) * 512]),
                                     after=[t_x, dn_last])
                        t_o = p.dma("sync", lambda e, ot=ot, h=h, sl=sl:
                                    e.dma_start(
                                        out=x1_d[ot * 128:(ot + 1) * 128,
                                                 h * 512:(h + 1) * 512],
                                        in_=xo_sb[:, sl * 512:(sl + 1) * 512]),
                                    after=[t_r])
                        xacc_tok[ti] = t_o
                        xacc_ld[ti] = t_r
                        ev_last = t_r
            compute_done.append(ev_last)

        for cc in range(n_chunks + 1):
            if cc < n_chunks:
                phaseA(cc)
            if cc >= 1:
                phaseB(cc - 1)

    return xacc_tok[-1]


def _wrap16(mat):
    lin = np.ascontiguousarray(mat.T).reshape(-1)
    w16 = np.ascontiguousarray(lin.reshape(-1, 16).T)
    return np.tile(w16, (8, 1))


def prep_l0_ffn_maps(maps, inputs, pre):
    cb = np.ascontiguousarray(np.asarray(inputs["L0_ffn_codebook"], np.float32))
    ig = np.asarray(inputs["L0_ffn_idx_gate"])
    iu = np.asarray(inputs["L0_ffn_idx_up"])
    idn = np.asarray(inputs["L0_ffn_idx_down"])
    for c in range(NCORES):
        iw = np.zeros((3, NCHUNK, 128, 4096), np.int32)
        idm = np.zeros((3, NCHUNK, 128, 512), np.int32)
        for ch in range(NCHUNK):
            tile = ch * 8 + c
            if tile < NREAL_IB:
                rows = slice(tile * 128, (tile + 1) * 128)
                idm[0, ch] = ig[rows]
                iw[0, ch] = _wrap16(ig[rows])
                idm[1, ch] = iu[rows]
                iw[1, ch] = _wrap16(iu[rows])
            gd0 = ch * 128
            gd1 = min(gd0 + 128, idn.shape[1])
            dtile = np.zeros((4, 128, 128), np.int32)
            for s_ in range(4):
                hr = slice(512 * c + s_ * 128, 512 * c + (s_ + 1) * 128)
                dtile[s_, :, 0:gd1 - gd0] = idn[hr, gd0:gd1]
            dmat = np.ascontiguousarray(
                dtile.transpose(1, 0, 2).reshape(128, 512))
            idm[2, ch] = dmat
            iw[2, ch] = _wrap16(dmat)
        maps[c].update({"cb": cb, pre + "iw": iw, pre + "idm": idm})




def build_mega():
    nc = bass.Bass("TRN2", target_bir_lowering=False, num_swdge_queues=4)

    def dt(name, shape, dtyp, kind):
        return nc.dram_tensor(name, shape, dtyp, kind=kind).ap()

    x_d = dt("x", [D, TPC], F32, "ExternalInput")
    eyeb_d = dt("eyeb", [128, 128], BF16, "ExternalInput")
    nm0 = build_tile_plan(0)[1].shape[0]
    nm1 = build_tile_plan(1)[1].shape[0]

    a0 = dict(
        x=x_d, eyeb=eyeb_d,
        wq=dt("a0_wq", [D, 512], F32, "ExternalInput"),
        wkv=dt("a0_wkv", [D, 256], F32, "ExternalInput"),
        wo=dt("a0_wo", [512, D], F32, "ExternalInput"),
        masks=dt("a0_masks", [nm0, 128, 512], BF16, "ExternalInput"),
        bq=dt("a0_bq", [128, 6], F32, "ExternalInput"),
        bo=dt("a0_bo", [128, 32], F32, "ExternalInput"),
        cos=dt("cosT", [128, S], F32, "ExternalInput"),
        sin=dt("sinSW", [128, S], F32, "ExternalInput"),
        xout=dt("xa", [D, TPC], F32, "Internal"),
        xstage=dt("a0_xstage", [D, TPC], BF16, "Internal"),
        xag=dt("a0_xag", [NCORES * D, TPC], BF16, "Internal"),
        rsi=dt("a0_rsi", [NCORES * D, TPC], F32, "Internal"),
        rso=dt("a0_rso", [D, TPC], F32, "Internal"),
    )
    f0 = dict(
        x=a0["xout"],
        cb=dt("cb", [65536, 8], F32, "ExternalInput"),
        iw=dt("f0_iw", [3, NCHUNK, 128, 4096], I32, "ExternalInput"),
        idm=dt("f0_idm", [3, NCHUNK, 128, 512], I32, "ExternalInput"),
        cbp=dt("f0_cbp", [32768, 128], BF16, "Internal"),
        xout=dt("xb", [D, TPC], F32, "Internal"),
        wgst=[dt(f"f0_wgst{i}", [4096, 128], BF16, "Internal")
              for i in range(NPAR)],
        wust=[dt(f"f0_wust{i}", [4096, 128], BF16, "Internal")
              for i in range(NPAR)],
        wdst=[dt(f"f0_wdst{i}", [1024, 512], BF16, "Internal")
              for i in range(NPAR)],
        wgag=[dt(f"f0_wgag{i}", [32768, 128], BF16, "Internal")
              for i in range(NPAR)],
        wuag=[dt(f"f0_wuag{i}", [32768, 128], BF16, "Internal")
              for i in range(NPAR)],
        wdag=[dt(f"f0_wdag{i}", [8192, 512], BF16, "Internal")
              for i in range(NPAR)],
    )
    a1 = dict(
        x=f0["xout"], eyeb=eyeb_d,
        wq=dt("a1_wq", [D, 512], F32, "ExternalInput"),
        wkv=dt("a1_wkv", [D, 256], F32, "ExternalInput"),
        wo=dt("a1_wo", [512, D], F32, "ExternalInput"),
        masks=dt("a1_masks", [nm1, 128, 512], BF16, "ExternalInput"),
        xout=dt("xc", [D, TPC], F32, "Internal"),
        xstage=dt("a1_xstage", [D, TPC], BF16, "Internal"),
        xag=dt("a1_xag", [NCORES * D, TPC], BF16, "Internal"),
        rsi=dt("a1_rsi", [NCORES * D, TPC], F32, "Internal"),
        rso=dt("a1_rso", [D, TPC], F32, "Internal"),
        kvs=dt("a1_kvs", [256, TPC], BF16, "Internal"),
        kvag=dt("a1_kvag", [NCORES * 256, TPC], BF16, "Internal"),
    )
    f1 = dict(
        x=a1["xout"], eyeb=eyeb_d,
        bw=dt("f1_bw", [D, D], F32, "ExternalInput"),
        bb=dt("f1_bb", [128, 32], F32, "ExternalInput"),
        at=dt("f1_at", [D, 64], F32, "ExternalInput"),
        bs=dt("f1_bs", [1024, D], F32, "ExternalInput"),
        rw=dt("f1_rw", [D, 16], F32, "ExternalInput"),
        xout=dt("x1", [D, TPC], F32, "ExternalOutput"),
    )

    ctx = contextlib.ExitStack()
    with ctx:
        p = Prog(nc, ctx)
        blk = ctx.enter_context(nc.Block())
        p.add("gpsimd", lambda e: e.load_library(mlp))
        emit_attn(p, a0, 0, "a0")
        p.stage_gate()
        emit_l0_ffn(p, f0, "f0")
        p.stage_gate()
        emit_attn(p, a1, 1, "a1")
        p.stage_gate()
        last = emit_l1_ffn(p, f1, "f1")
        p.add("gpsimd", lambda e: e.engine_nop(), after=[last])
        p.emit(blk)
    mybir.codegen_inst_isa_subclasses(nc)
    return nc




def kernel(**inputs):
    x0 = np.asarray(inputs["L0_attn_hidden_states"], np.float32).reshape(S, D)
    eyeb = np.eye(128, dtype=np.float32).astype(BF)
    maps = [{"x": np.ascontiguousarray(x0[core_token_ids(c)].T), "eyeb": eyeb}
            for c in range(NCORES)]
    prep_attn_maps(maps, inputs, 0, "a0_")
    prep_l0_ffn_maps(maps, inputs, "f0_")
    prep_attn_maps(maps, inputs, 1, "a1_")
    prep_l1_ffn_maps(maps, inputs, "f1_")
    nc = build_mega()
    res = run_bass_kernel_spmd(nc, maps, core_ids=list(range(NCORES)))
    out = np.zeros((S, D), dtype=np.float32)
    for c in range(NCORES):
        out[core_token_ids(c)] = res.results[c]["x1"].T
    return out.reshape(1, S, D)
